# version 07
# baseline (speedup 1.0000x reference)
import math
import contextlib
import numpy as np

import concourse.bass as bass
import concourse.mybir as mybir
from concourse.bass_utils import run_bass_kernel_spmd

F32 = mybir.dt.float32
BF16 = mybir.dt.bfloat16
AF = mybir.ActivationFunctionType
ALU = mybir.AluOpType

N_CORES = 8
SEQ = 4096
D = 2048
NT = SEQ // 128
ND = D // 128
MAGIC = 12582912.0
QK_LEVEL = 9
L0A_HALVES = 2
TWO_PI = 2.0 * math.pi


def host_constants():
    c = {}
    j = np.arange(64, dtype=np.float64)
    invf0 = 10000.0 ** (-2.0 * j / 128.0)
    ang0 = np.arange(SEQ, dtype=np.float64)[None, :] * np.concatenate([invf0, invf0])[:, None]
    c["rope0_cos"] = np.cos(ang0).astype(np.float32)
    c["rope0_sin"] = np.sin(ang0).astype(np.float32)
    p0 = np.zeros((128, 128), dtype=np.float32)
    for m in range(64):
        p0[64 + m, m] = -1.0
        p0[m, 64 + m] = 1.0
    c["rot0_perm"] = p0
    jj = np.arange(32, dtype=np.float64)
    invf1 = 10000.0 ** (-2.0 * jj / 64.0)
    p = np.arange(128)
    ang1 = np.arange(SEQ, dtype=np.float64)[None, :] * invf1[(p % 64) // 2][:, None]
    c["rope1_cos"] = np.cos(ang1).astype(np.float32)
    sgn = np.where(p % 2 == 0, -1.0, 1.0)[:, None]
    c["rope1_sin"] = (np.sin(ang1) * sgn).astype(np.float32)
    p1 = np.zeros((128, 128), dtype=np.float32)
    for m in range(128):
        p1[m ^ 1, m] = 1.0
    c["rot1_perm"] = p1
    c["ident"] = np.eye(128, dtype=np.float32)
    e16 = np.zeros((8, 128), dtype=np.float32)
    for e in range(8):
        e16[e, e * 16:(e + 1) * 16] = 2.0
    c["e16"] = e16
    c["ones"] = np.ones((128, 128), dtype=np.float32)
    dm = np.full((128, 1), MAGIC, dtype=np.float32)
    dm[64:, 0] += 0.25
    c["dmagic"] = dm
    b2 = np.zeros((128, 1), dtype=np.float32)
    b2[64:, 0] = math.pi / 2.0
    c["b2pi"] = b2
    c["ln2pi"] = np.full((128, 1), math.log(1.0 / TWO_PI), dtype=np.float32)
    h24 = np.zeros((128, 24 * 24), dtype=np.float32)
    for h in range(24):
        h24[:, h * 24 + h] = 1.0
    c["hot24"] = h24
    hr24 = np.zeros((24, 24 * 128), dtype=np.float32)
    for h in range(24):
        hr24[h, h * 128:(h + 1) * 128] = 1.0
    c["hotrow24"] = hr24
    tril = np.where(np.tril(np.ones((64, 64))) > 0, 0.0, -30000.0)
    cm = np.zeros((128, 512), dtype=np.float32)
    for j in range(8):
        cm[0:64, j * 64:(j + 1) * 64] = tril
        cm[64:128, j * 64:(j + 1) * 64] = tril
    c["cmask"] = cm
    blk = np.repeat(np.arange(64), 4)
    lmm = np.where(blk[:, None] >= blk[None, :], 0.0, -30000.0).astype(np.float32)
    c["lmmask"] = lmm
    tok_blk = np.repeat(np.arange(64), 64)
    gm = np.where(tok_blk[:, None] > blk[None, :], 0.0, -30000.0).astype(np.float32)
    c["gmask"] = gm
    h64 = np.zeros((128, 2), dtype=np.float32)
    h64[0:64, 0] = 1.0
    h64[64:128, 1] = 1.0
    c["hot64"] = h64
    c["hot64T"] = h64.T.copy()
    selht = np.zeros((32, 16 * 128), dtype=np.float32)
    for ht in range(16):
        for m in range(128):
            selht[ht * 2 + m // 64, ht * 128 + m] = 1.0
    c["selht"] = selht
    sw = np.zeros((128, 128), dtype=np.float32)
    for m in range(128):
        sw[m ^ 64, m] = 1.0
    c["swap64"] = sw
    H = np.array([[1.0]])
    for _ in range(7):
        H = np.block([[H, H], [H, -H]])
    c["h128"] = (H / np.sqrt(8192.0)).astype(np.float32)
    return c


class Sem:
    def __init__(self, h):
        self.h = h
        self.n = 0

    def inc(self, instr, step=1):
        instr.then_inc(self.h, step)
        self.n += step
        return instr

    def dma(self, instr):
        return self.inc(instr, 16)

    def wait(self, eng, n=None):
        eng.wait_ge(self.h, self.n if n is None else n)


class MK:
    def __init__(self, dbg=()):
        self.nc = bass.Bass("TRN2", target_bir_lowering=False, debug=False)
        self.es = contextlib.ExitStack()
        self.dbg = set(dbg)
        self.dram = {}
        self._sb_cache = {}
        self.phase_sems = []

    def din(self, name, shape, dt=F32):
        ap = self.nc.dram_tensor(name, list(shape), dt, kind="ExternalInput").ap()
        self.dram[name] = ap
        return ap

    def dout(self, name, shape, dt=F32):
        ap = self.nc.dram_tensor(name, list(shape), dt, kind="ExternalOutput").ap()
        self.dram[name] = ap
        return ap

    def dtmp(self, name, shape, dt):
        kind = "ExternalOutput" if name in self.dbg else "Internal"
        ap = self.nc.dram_tensor(name, list(shape), dt, kind=kind).ap()
        self.dram[name] = ap
        return ap

    def sb(self, name, shape, dt):
        t = self.es.enter_context(self.nc.sbuf_tensor(name, list(shape), dt))
        return t

    def ps(self, name, shape, dt=F32):
        return self.es.enter_context(self.nc.psum_tensor(name, list(shape), dt))

    def sem(self, name):
        return Sem(self.es.enter_context(self.nc.semaphore(name)))


def bf16(x):
    import ml_dtypes
    return x.astype(ml_dtypes.bfloat16)



def emit_weight_prep(mk, block):
    names = [
        "L0_attn_Wq", "L0_attn_Wk", "L0_attn_Wv", "L0_attn_Wo",
        "L0_attn_rand_matrix",
        "L0_ffn_weight", "L0_ffn_router_w", "L0_ffn_lora_A", "L0_ffn_lora_B",
        "L1_attn_w_qkv", "L1_attn_pool_q", "L1_attn_w_lm_qkv",
        "L1_attn_w_lm_out", "L1_attn_w_l2t_q", "L1_attn_w_l2t_out",
        "L1_attn_w_lm_kv", "L1_attn_w_out", "L1_attn_w_gate",
        "L1_ffn_W1", "L1_ffn_W3",
    ]
    s = mk.s_wprep
    for n in names:
        src = mk.dram[n]
        shp = list(src.tensor.shape)
        flat = int(np.prod(shp))
        dst = mk.dtmp(n + "_bf", shp, BF16)
        mk._wprep_pairs = getattr(mk, "_wprep_pairs", []) + [(dst, src)]

    @block.gpsimd
    def _(g):
        for dst, src in mk._wprep_pairs:
            idx = tuple(slice(None) for _ in dst.tensor.shape)
            s.dma(g.dma_start(dst[idx], src[idx]))


def emit_const_load(mk, block):
    s = mk.s_const
    nc = mk.nc
    mk.c_ident = mk.sb("c_ident", [128, 128], BF16)
    mk.c_identf = mk.sb("c_identf", [128, 128], F32)
    mk.c_rot0 = mk.sb("c_rot0", [128, 128], BF16)
    mk.c_rot1 = mk.sb("c_rot1", [128, 128], BF16)
    mk.c_e16 = mk.sb("c_e16", [8, 128], BF16)
    mk.c_ones = mk.sb("c_ones", [128, 128], BF16)

    @block.gpsimd
    def _(g):
        s.dma(g.dma_start(mk.c_ident[:], mk.dram["ident"][:, :]))
        s.dma(g.dma_start(mk.c_identf[:], mk.dram["ident"][:, :]))
        s.dma(g.dma_start(mk.c_rot0[:], mk.dram["rot0_perm"][:, :]))
        s.dma(g.dma_start(mk.c_rot1[:], mk.dram["rot1_perm"][:, :]))
        s.dma(g.dma_start(mk.c_e16[:], mk.dram["e16"][:, :]))
        s.dma(g.dma_start(mk.c_ones[:], mk.dram["ones"][:, :]))


def emit_layernorm(mk, block, src_name, dst_name, s_src_ready=None):
    src = mk.dram[src_name]
    dst = mk.dram[dst_name]
    tag = dst_name
    ph = contextlib.ExitStack()
    sbl = lambda n, s, d: ph.enter_context(mk.nc.sbuf_tensor(n, list(s), d))
    xb = [sbl(f"ln_x{i}_{tag}", [128, D], F32) for i in range(2)]
    onb = [sbl(f"ln_o{i}_{tag}", [128, D], BF16) for i in range(2)]
    sq = sbl(f"ln_sq_{tag}", [128, D], F32)
    vdump = sbl(f"ln_vd_{tag}", [128, D], BF16)
    ssum = sbl(f"ln_sum_{tag}", [128, 1], F32)
    ssq = sbl(f"ln_ssq_{tag}", [128, 1], F32)
    nmean = sbl(f"ln_nm_{tag}", [128, 1], F32)
    m2 = sbl(f"ln_m2_{tag}", [128, 1], F32)
    var = sbl(f"ln_var_{tag}", [128, 1], F32)
    rstd = sbl(f"ln_rstd_{tag}", [128, 1], F32)
    s_ld = mk.sem(f"ln_ld_{tag}")
    s_sq = mk.sem(f"ln_sq_{tag}")
    s_v = mk.sem(f"ln_v_{tag}")
    s_st = mk.sem(f"ln_st_{tag}")
    phase_start(mk, block)

    for t in range(NT):
        b = t % 2

        @block.sync
        def _(sy, t=t, b=b):
            if t >= 2:
                s_v.wait(sy, 16 * (t - 1))
            if s_src_ready is not None and t == 0:
                s_src_ready.wait(sy)
            s_ld.dma(sy.dma_start(xb[b][:], src[t * 128:(t + 1) * 128, :]))

        @block.scalar
        def _(sc, t=t, b=b):
            if t == 0:
                mk.s_eps.wait(sc)
            sc.wait_ge(s_ld.h, 16 * (t + 1))
            if t >= 1:
                sc.wait_ge(s_v.h, 16 * (t - 1) + 3)
            s_sq.inc(sc.activation(sq[:], xb[b][:], AF.Square))

        @block.vector
        def _(v, t=t, b=b):
            v.wait_ge(s_ld.h, 16 * (t + 1))

            def S(ins, step=1):
                s_v.inc(v.drain(), step)
                v.wait_ge(s_v.h, s_v.n)
            v.tensor_scalar(vdump[:], xb[b][:], 1.0, None, ALU.mult,
                            ALU.add, accum_out=ssum[:])
            S(None)
            v.tensor_scalar(nmean[:], ssum[:], -1.0 / D, None, ALU.mult)
            S(None)
            v.wait_ge(s_sq.h, s_sq.n)
            v.tensor_scalar(vdump[:], sq[:], 1.0, None, ALU.mult,
                            ALU.add, accum_out=ssq[:])
            S(None)
            v.scalar_tensor_tensor(m2[:], nmean[:], 1.0, nmean[:],
                                   ALU.mult, ALU.mult)
            S(None)
            v.scalar_tensor_tensor(var[:], ssq[:], 1.0 / D, m2[:],
                                   ALU.mult, ALU.subtract)
            S(None)

        @block.scalar
        def _(sc, t=t, b=b):
            sc.wait_ge(s_v.h, s_v.n)
            sc.activation(rstd[:], var[:], AF.Ln, bias=mk.c_eps[:])
            sc.drain()
            sc.activation(rstd[:], rstd[:], AF.Exp, scale=-0.5)
            s_sq.inc(sc.drain())

        @block.vector
        def _(v, t=t, b=b):
            v.wait_ge(s_sq.h, s_sq.n)
            if t >= 2:
                v.wait_ge(s_st.h, 16 * (t - 1))
            s_v.inc(v.tensor_scalar(onb[b][:], xb[b][:], nmean[:], rstd[:],
                                    ALU.add, ALU.mult), 11)

        @block.sync
        def _(sy, t=t, b=b):
            sy.wait_ge(s_v.h, s_v.n)
            s_st.dma(sy.dma_start(dst[t * 128:(t + 1) * 128, :], onb[b][:]))
            if "ln_stats" in mk.dbg:
                dbgd = mk.dram["ln_stats"]
                for ci, tile in enumerate([ssum, ssq, var, rstd]):
                    s_st.dma(sy.dma_start(dbgd[t * 128:(t + 1) * 128, ci:ci + 1],
                                          tile[:], allow_slow_non_contiguous=True))

    ph.close()
    phase_end(mk, [s_st, s_v, s_sq, s_ld])
    return s_st


def barrier(mk, block, sems, engines=("tensor", "vector", "scalar", "sync", "gpsimd")):
    for eng in engines:
        def body(e, sems=tuple(sems)):
            for s in sems:
                if s.n > 0:
                    e.wait_ge(s.h, s.n)
        getattr(block, eng)(body)


def fence(mk, t, sem, step=1):
    sem.inc(t.drain(), step)


def phase_start(mk, block):
    barrier(mk, block, list(mk.phase_sems))


def phase_end(mk, sems):
    mk.phase_sems.extend(sems)


def emit_wtrans(mk, block, names, tag=""):
    s_ld = mk.sem("wt_ld" + tag)
    s_st = mk.sem("wt_st" + tag)
    phase_start(mk, block)
    ph = contextlib.ExitStack()
    stg = [ph.enter_context(mk.nc.sbuf_tensor(f"wt_stage{i}{tag}", [128, 8192], BF16))
           for i in range(2)]
    plan = []
    for n in names:
        src = mk.dram[n + "_bf"]
        O, D2 = src.tensor.shape
        dst = mk.dtmp("wsc_" + n, [O // 128, D2 // 128, 128, 128], BF16)
        plan.append((src, dst, O, D2))
    mk._wt_plan = plan

    i = 0
    for src, dst, O, D2 in plan:
        for kd in range(D2 // 128):
            b = i % 2

            @block.sync
            def _(sy, i=i, b=b, src=src, kd=kd, O=O):
                if i >= 2:
                    sy.wait_ge(s_st.h, 16 * (i - 1))
                mk.s_wprep.wait(sy)
                s_ld.dma(sy.dma_start(stg[b][:, 0:O],
                                      src[0:O, kd * 128:(kd + 1) * 128],
                                      transpose=True))

            @block.scalar
            def _(sc, i=i, b=b, dst=dst, kd=kd, O=O):
                sc.wait_ge(s_ld.h, 16 * (i + 1))
                v = dst[:, kd, :, :].rearrange("os dd ol -> dd os ol")
                s_st.dma(sc.dma_start(v, stg[b][:, 0:O].rearrange(
                    "dd (os ol) -> dd os ol", ol=128)))
            i += 1
    ph.close()
    phase_end(mk, [s_st, s_ld])
    mk.wt_done = s_st


def emit_l0_prep_small(mk, block, parts=("ld", "rw", "la", "bt")):
    nc = mk.nc
    s = mk.sem("l0sp")
    mk.s_l0sp = s
    mk.rwT = mk.sb("rwT", [128, 16 * 8], BF16)
    mk.laT = mk.sb("laT", [128, 16 * 128], BF16)
    mk.BT = mk.sb("BT", [128, 2048], BF16)
    mk.RRT = mk.sb("RRT", [128, 128], BF16)
    mk.b_qkv = mk.sb("b_qkv", [128, 32], F32)
    mk.b_o = mk.sb("b_o", [128, 16], F32)
    mk.dmagic = mk.sb("c_dmagic", [128, 1], F32)
    mk.b2pi = mk.sb("c_b2pi", [128, 1], F32)
    mk.ln2pi = mk.sb("c_ln2pi", [128, 1], F32)
    mk.hot24 = mk.sb("c_hot24", [128, 24 * 24], BF16)
    mk.hotrow24 = mk.sb("c_hotrow24", [24, 24 * 128], F32)
    phase_start(mk, block)
    ph = contextlib.ExitStack()
    tmp8 = ph.enter_context(mk.nc.sbuf_tensor("l0sp_tmp8", [128, 2048], BF16))
    BT_d = mk.dtmp("BT_d", [128, 2048], BF16)
    ps_t = ph.enter_context(mk.nc.psum_tensor("l0sp_ps", [128, 1024], BF16))

    @block.gpsimd
    def _(g):
        s.dma(g.dma_start(mk.hot24[:], mk.dram["hot24"][:, :]))

    @block.sync
    def _(sy):
        mk.s_wprep.wait(sy)
        s.dma(sy.dma_start(mk.dmagic[:], mk.dram["dmagic"][:, :]))
        s.dma(sy.dma_start(mk.b2pi[:], mk.dram["b2pi"][:, :]))
        s.dma(sy.dma_start(mk.ln2pi[:], mk.dram["ln2pi"][:, :]))
        s.dma(sy.dma_start(mk.hotrow24[:], mk.dram["hotrow24"][:, :]))
        s.dma(sy.dma_start(mk.RRT[:, 0:64],
                           mk.dram["L0_attn_rand_matrix_bf"][:, :], transpose=True))
        s.dma(sy.dma_start(mk.RRT[:, 64:128],
                           mk.dram["L0_attn_rand_matrix_bf"][:, :], transpose=True))
        for c, (nm, nstr) in enumerate([("L0_attn_bq", 16), ("L0_attn_bk", 8),
                                        ("L0_attn_bv", 8)]):
            base = [0, 16, 24][c]
            v = mk.dram[nm].rearrange("(s p) -> p s", p=128)
            s.dma(sy.dma_start(mk.b_qkv[:, base:base + nstr], v,
                               allow_slow_non_contiguous=True))
        v = mk.dram["L0_attn_bo"].rearrange("(s p) -> p s", p=128)
        s.dma(sy.dma_start(mk.b_o[:, :], v, allow_slow_non_contiguous=True))
        s.dma(sy.dma_start(tmp8[0:8, :], mk.dram["L0_ffn_router_w_bf"][:, :]))

    s_mm = mk.sem("l0sp_mm")
    for kd in (range(16) if "rw" in parts else []):
        @block.tensor
        def _(t, kd=kd):
            if kd == 0:
                s.wait(t)
                mk.s_const.wait(t)
            if kd > 0:
                s_mm.wait(t)
            t.transpose(ps_t[:, 0:8], tmp8[0:8, kd * 128:(kd + 1) * 128],
                        mk.c_ident[0:8, 0:8])
            fence(mk, t, s_mm)

        @block.vector
        def _(v, kd=kd):
            v.wait_ge(s_mm.h, s_mm.n)
            s_mm.inc(v.tensor_copy(mk.rwT[:, kd * 8:(kd + 1) * 8], ps_t[:, 0:8]))

    if "la" in parts:
        @block.sync
        def _(sy):
            sy.wait_ge(s_mm.h, s_mm.n)
            la = mk.dram["L0_ffn_lora_A_bf"].rearrange("e r d -> (e r) d")
            s.dma(sy.dma_start(tmp8[:, :], la))

    for kd in (range(16) if "la" in parts else []):
        @block.tensor
        def _(t, kd=kd):
            if kd == 0:
                s.wait(t)
            s_mm.wait(t)
            t.transpose(ps_t[:, 0:128], tmp8[:, kd * 128:(kd + 1) * 128],
                        mk.c_ident[:])
            fence(mk, t, s_mm)

        @block.vector
        def _(v, kd=kd):
            v.wait_ge(s_mm.h, s_mm.n)
            s_mm.inc(v.tensor_copy(mk.laT[:, kd * 128:(kd + 1) * 128], ps_t[:, 0:128]))

    s_bt = mk.sem("l0sp_bt")
    bt_tile = ph.enter_context(mk.nc.sbuf_tensor("bt_tile", [128, 16 * 16], BF16))
    bt_ev = ph.enter_context(mk.nc.sbuf_tensor("bt_ev", [16, 16 * 128], BF16))
    for e in (range(8) if "bt" in parts else []):
        @block.sync
        def _(sy, e=e):
            if e > 0:
                s_bt.wait(sy)
            lb = mk.dram["L0_ffn_lora_B_bf"][e, :, :].rearrange(
                "(oc p) r -> p oc r", p=128)
            s_bt.dma(sy.dma_start(
                bt_tile[:].rearrange("p (oc r) -> p oc r", r=16), lb))

        for oc in range(16):
            @block.tensor
            def _(t, e=e, oc=oc):
                t.wait_ge(s_bt.h, s_bt.n)
                s_mm.wait(t)
                t.transpose(ps_t[0:16, 0:128], bt_tile[:, oc * 16:(oc + 1) * 16],
                            mk.c_ident[:])
                fence(mk, t, s_mm)

            @block.vector
            def _(v, e=e, oc=oc):
                v.wait_ge(s_mm.h, s_mm.n)
                s_mm.inc(v.tensor_copy(bt_ev[0:16, oc * 128:(oc + 1) * 128],
                                       ps_t[0:16, 0:128]))

        @block.sync
        def _(sy, e=e):
            sy.wait_ge(s_mm.h, s_mm.n)
            s_bt.dma(sy.dma_start(BT_d[e * 16:(e + 1) * 16, :], bt_ev[0:16, :]))

    if "bt" in parts:
        @block.sync
        def _(sy):
            s_bt.wait(sy)
            s.dma(sy.dma_start(mk.BT[:, :], BT_d[:, :]))
    if "l0sp_dump" in mk.dbg:
        dmp = mk.dtmp("l0sp_dump", [128, 128 + 2048 + 2048 + 128], BF16)
        @block.sync
        def _(sy):
            s.wait(sy)
            sy.wait_ge(s_mm.h, s_mm.n)
            sy.wait_ge(s_bt.h, s_bt.n)
            s.dma(sy.dma_start(dmp[:, 0:128], mk.rwT[:, :]))
            s.dma(sy.dma_start(dmp[:, 128:2176], mk.laT[:, :]))
            s.dma(sy.dma_start(dmp[:, 2176:4224], mk.BT[:, :]))
            s.dma(sy.dma_start(dmp[:, 4224:4352], mk.RRT[:, :]))
    ph.close()
    phase_end(mk, [s, s_mm, s_bt])
    mk.l0sp_done = s


def emit_l0_passA(mk, block, parts=("qk", "v", "f1", "f2", "f3")):
    nc = mk.nc
    ph = contextlib.ExitStack()
    sb = lambda n, s, d: ph.enter_context(nc.sbuf_tensor(n, list(s), d))
    ps = lambda n, s, d=F32: ph.enter_context(nc.psum_tensor(n, list(s), d))
    qr_d = mk.dram["qr_d"]
    Vt = mk.dram["Vt"]
    g0_d = mk.dram["g0_d"]

    phase_start(mk, block)
    barrier(mk, block, [mk.s_const, mk.s_wprep])

    HN = SEQ // 2
    NS = HN // 512

    xnf = sb("a_xnf", [128, 16 * HN], BF16)
    ropeC = sb("a_ropeC", [128, HN], F32)
    ropeS = sb("a_ropeS", [128, HN], F32)
    strip = [sb(f"a_strip{i}", [128, 16 * 128], BF16) for i in range(2)]
    qsb = [sb(f"a_qsb{i}", [128, 512], BF16) for i in range(2)]
    t1 = [sb(f"a_t1{i}", [128, 512], F32) for i in range(2)]
    t2 = [sb(f"a_t2{i}", [128, 512], F32) for i in range(2)]
    qrb = [sb(f"a_qrb{i}", [128, 512], BF16) for i in range(2)]
    vsb = sb("a_vsb", [128, 512], BF16)
    vtb = [sb(f"a_vtb{i}", [128, 128], BF16) for i in range(2)]
    wfm = sb("a_wfm", [8, HN], BF16)
    Axw = sb("a_axw", [128, HN], BF16)
    wbsb = [sb(f"a_wb{i}", [128, 512], BF16) for i in range(2)]
    gsb = [sb(f"a_gsb{i}", [128, 512], BF16) for i in range(2)]
    ps_p = [ps(f"a_psp{i}", [128, 512]) for i in range(2)]
    ps_r = [ps(f"a_psr{i}", [128, 512]) for i in range(2)]
    ps_t = ph.enter_context(nc.psum_tensor("a_pst", [128, 1024], BF16))
    ps_l = ph.enter_context(nc.psum_tensor("a_psl", [128, 512], F32))
    ps_w = ph.enter_context(nc.psum_tensor("a_psw", [8, 512], F32))
    esb = sb("a_esb", [128, 8], F32)
    edump = sb("a_edump", [128, 8], BF16)
    esum = sb("a_esum", [128, 1], F32)
    mx = sb("a_mx", [128, 1], F32)
    nmx = sb("a_nmx", [128, 1], F32)
    resum = sb("a_resum", [128, 1], F32)
    probs = sb("a_probs", [128, 8], F32)
    v1c = sb("a_v1", [128, 1], F32)
    v2c = sb("a_v2", [128, 1], F32)
    m1 = sb("a_m1", [128, 8], F32)
    m2 = sb("a_m2", [128, 8], F32)
    p2 = sb("a_p2", [128, 8], F32)
    tdum = sb("a_tdum", [128, 8], F32)
    mm12 = sb("a_mm12", [128, 8], F32)
    s12 = sb("a_s12", [128, 1], F32)
    rs12 = sb("a_rs12", [128, 1], F32)
    wdt = sb("a_wdt", [128, 8], F32)
    f1dbg = sb("a_f1dbg", [128, 32], F32) if ("f1dbg_d" in mk.dbg or "f1all_d" in mk.dbg) else None

    s_zz = mk.sem("a_zz")

    @block.vector
    def _(v):
        for tile in [esb, edump, esum, mx, nmx, resum, probs, v1c, v2c,
                     m1, m2, p2, tdum, mm12, s12, rs12, wdt]:
            v.memset(tile[:], 0.0)
        v.memset(ps_l[:, :], 0.0)
        v.memset(ps_w[:, :], 0.0)
        s_zz.inc(v.drain())

    barrier(mk, block, [s_zz])

    s_ld = mk.sem("a_ld")
    s_mm = mk.sem("a_mm")
    s_sc = mk.sem("a_sc")
    s_ev = mk.sem("a_ev")
    s_st = mk.sem("a_st")

    for half in range(L0A_HALVES):
        n0 = half * HN
        ld0, mm0, sc0, ev0, st0 = s_ld.n, s_mm.n, s_sc.n, s_ev.n, s_st.n

        if half == 1:
            barrier(mk, block, [s_ld, s_mm, s_sc, s_ev, s_st])

        @block.sync
        def _(sy, half=half, n0=n0, mm0=mm0):
            if half == 1:
                sy.wait_ge(s_mm.h, mm0)
            for kd in range(16):
                s_ld.dma(sy.dma_start(
                    xnf[:, kd * HN:(kd + 1) * HN],
                    mk.dram["xn0"][n0:n0 + HN, kd * 128:(kd + 1) * 128],
                    transpose=True))
            s_ld.dma(sy.dma_start(ropeC[:], mk.dram["rope0_cos"][:, n0:n0 + HN]))
            s_ld.dma(sy.dma_start(ropeS[:], mk.dram["rope0_sin"][:, n0:n0 + HN]))
        xn_ready = s_ld.n

        if "xnf_d" in mk.dbg:
            @block.sync
            def _(sy, half=half, n0=n0):
                sy.wait_ge(s_ld.h, xn_ready)
                s_st.dma(sy.dma_start(mk.dram["xnf_d"][:, n0 * 16:(n0 + HN) * 16],
                                      xnf[:]))

        for h in (range(24) if "qk" in parts else []):
            wname = "wsc_L0_attn_Wq" if h < 16 else "wsc_L0_attn_Wk"
            os_ = h if h < 16 else h - 16
            sbuf_i = h % 2

            @block.gpsimd
            def _(g, h=h, sbuf_i=sbuf_i, wname=wname, os_=os_, mm0=mm0):
                if h >= 2:
                    g.wait_ge(s_mm.h, mm0 + (h - 1) * NS * 2)
                elif h < 2:
                    g.wait_ge(s_mm.h, mm0)
                w = mk.dram[wname][os_, :, :, :].rearrange("kd dd ol -> dd kd ol")
                s_ld.dma(g.dma_start(strip[sbuf_i][:].rearrange(
                    "dd (kd ol) -> dd kd ol", ol=128), w))

            for ns in range(NS):
                it = h * NS + ns
                b = it % 2

                @block.tensor
                def _(t, h=h, ns=ns, b=b, it=it, sbuf_i=sbuf_i,
                      xn_ready=xn_ready, sc0=sc0, ev0=ev0):
                    if ns == 0:
                        t.wait_ge(s_ld.h, xn_ready + 16 * (h + 1))
                    if it >= 2:
                        t.wait_ge(s_sc.h, sc0 + it - 1)
                    ins = None
                    if QK_LEVEL >= 1:
                        for kd in range(16):
                            ins = t.matmul(ps_p[b][:, :],
                                     strip[sbuf_i][:, kd * 128:(kd + 1) * 128],
                                     xnf[:, kd * HN + ns * 512: kd * HN + (ns + 1) * 512],
                                     start=(kd == 0), stop=(kd == 15))
                        fence(mk, t, s_mm)
                    else:
                        t.matmul(ps_p[b][:, :], strip[sbuf_i][:, 0:128],
                                 xnf[:, 0:512], start=True, stop=True)
                        fence(mk, t, s_mm)

                @block.scalar
                def _(sc, it=it, b=b, h=h, mm0=mm0, ev0=ev0):
                    sc.wait_ge(s_mm.h, mm0 + 2 * it + 1)
                    if it >= 2:
                        sc.wait_ge(s_ev.h, ev0 + it - 1)
                    if QK_LEVEL >= 1:
                        s_sc.inc(sc.activation(qsb[b][:], ps_p[b][:, :], AF.Identity,
                                               bias=mk.b_qkv[:, h:h + 1]))
                    else:
                        s_sc.inc(sc.activation(qsb[b][:], qsb[b][:], AF.Silu))

                @block.tensor
                def _(t, it=it, b=b, sc0=sc0, ev0=ev0):
                    t.wait_ge(s_sc.h, sc0 + it + 1)
                    if it >= 2 and QK_LEVEL >= 3:
                        t.wait_ge(s_ev.h, ev0 + it - 1)
                    t.matmul(ps_r[b][:, :], mk.c_rot0[:], qsb[b][:],
                             start=True, stop=True)
                    fence(mk, t, s_mm)

                @block.vector
                def _(v, it=it, b=b, ns=ns, mm0=mm0, st0=st0):
                    v.wait_ge(s_mm.h, mm0 + 2 * it + 2)
                    if it >= 2:
                        v.wait_ge(s_st.h, st0 + 16 * (it - 1))
                    cs = slice(ns * 512, (ns + 1) * 512)
                    if QK_LEVEL >= 3:
                        v.tensor_tensor(t1[b][:], ps_r[b][:, :], ropeS[:, cs], ALU.mult)
                        v.tensor_tensor(t2[b][:], qsb[b][:], ropeC[:, cs], ALU.mult)
                        s_ev.inc(v.tensor_tensor(qrb[b][:], t1[b][:], t2[b][:], ALU.add))
                    else:
                        s_ev.inc(v.tensor_copy(qrb[b][:], qsb[b][:]))

                @block.sync
                def _(sy, it=it, b=b, h=h, ns=ns, n0=n0, ev0=ev0):
                    sy.wait_ge(s_ev.h, ev0 + it + 1)
                    s_st.dma(sy.dma_start(
                        qr_d[h, :, n0 + ns * 512:n0 + (ns + 1) * 512], qrb[b][:]))

        mm1, sc1, ev1, st1 = s_mm.n, s_sc.n, s_ev.n, s_st.n

        for hd in (range(8) if "v" in parts else []):
            sbuf_i = hd % 2

            @block.gpsimd
            def _(g, hd=hd, sbuf_i=sbuf_i, mm1=mm1):
                if hd < 2:
                    g.wait_ge(s_mm.h, mm1)
                else:
                    g.wait_ge(s_mm.h, mm1 + (hd - 1) * NS * 5)
                w = mk.dram["wsc_L0_attn_Wv"][hd, :, :, :].rearrange(
                    "kd dd ol -> dd kd ol")
                s_ld.dma(g.dma_start(strip[sbuf_i][:].rearrange(
                    "dd (kd ol) -> dd kd ol", ol=128), w))

            for ns in range(NS):
                it = hd * NS + ns
                b = it % 2

                @block.tensor
                def _(t, hd=hd, ns=ns, b=b, it=it, sbuf_i=sbuf_i,
                      ld1=None, sc1=sc1, xn_ready=xn_ready):
                    t.wait_ge(s_ld.h, xn_ready + 16 * 24 + 16 * (hd + 1))
                    if it >= 2:
                        t.wait_ge(s_sc.h, sc1 + it - 1)
                    for kd in range(16):
                        t.matmul(ps_p[b][:, :],
                                 strip[sbuf_i][:, kd * 128:(kd + 1) * 128],
                                 xnf[:, kd * HN + ns * 512: kd * HN + (ns + 1) * 512],
                                 start=(kd == 0), stop=(kd == 15))
                    fence(mk, t, s_mm)

                @block.scalar
                def _(sc, hd=hd, b=b, it=it, mm1=mm1, ev1=ev1):
                    sc.wait_ge(s_mm.h, mm1 + it * 5 + 1)
                    if it >= 1:
                        sc.wait_ge(s_ev.h, ev1 + it * 4)
                    s_sc.inc(sc.activation(vsb[:], ps_p[b][:, :], AF.Identity,
                                           bias=mk.b_qkv[:, 24 + hd:25 + hd]))

                for t4 in range(4):
                    tb = t4 % 2

                    @block.tensor
                    def _(t, t4=t4, it=it, sc1=sc1, ev1=ev1):
                        t.wait_ge(s_sc.h, sc1 + it + 1)
                        cnt = it * 4 + t4
                        if cnt >= 1:
                            t.wait_ge(s_ev.h, ev1 + cnt)
                        t.transpose(ps_t[:, 0:128],
                                    vsb[:, t4 * 128:(t4 + 1) * 128],
                                    mk.c_ident[:])
                        fence(mk, t, s_mm)

                    @block.vector
                    def _(v, t4=t4, tb=tb, it=it, mm1=mm1, st1=st1):
                        v.wait_ge(s_mm.h, mm1 + it * 5 + 2 + t4)
                        cnt = it * 4 + t4
                        if cnt >= 2:
                            v.wait_ge(s_st.h, st1 + 16 * (cnt - 1))
                        s_ev.inc(v.tensor_copy(vtb[tb][:], ps_t[:, 0:128]))

                    @block.sync
                    def _(sy, hd=hd, ns=ns, t4=t4, tb=tb, it=it, n0=n0, ev1=ev1):
                        sy.wait_ge(s_ev.h, ev1 + it * 4 + t4 + 1)
                        nn = n0 + ns * 512 + t4 * 128
                        s_st.dma(sy.dma_start(Vt[hd, nn:nn + 128, :], vtb[tb][:]))

        mm2, sc2, ev2, st2 = s_mm.n, s_sc.n, s_ev.n, s_st.n

        NTT = HN // 128
        for tt in (range(NTT) if "f1" in parts else []):
            @block.tensor
            def _(t, tt=tt, xn_ready=xn_ready):
                if tt == 0:
                    t.wait_ge(s_ld.h, xn_ready)
                    s_st.wait(t)
                s_sc.wait(t)
                s_ev.wait(t)
                for kd in range(16):
                    t.matmul(ps_l[:, 0:8],
                             xnf[:, kd * HN + tt * 128: kd * HN + (tt + 1) * 128],
                             mk.rwT[:, kd * 8:(kd + 1) * 8],
                             start=(kd == 0), stop=(kd == 15))
                fence(mk, t, s_mm)

            @block.scalar
            def _(sc, tt=tt):
                s_mm.wait(sc)
                s_sc.inc(sc.activation(esb[:], ps_l[:, 0:8], AF.Exp))

            @block.vector
            def _(v, tt=tt):
                s_sc.wait(v)

                def S(ins):
                    s_ev.inc(v.drain())
                    v.wait_ge(s_ev.h, s_ev.n)
                S(v.tensor_reduce(v1c[:], esb[:], mybir.AxisListType.X, ALU.max))
                S(v.tensor_scalar(m1[:], esb[:], v1c[:], None, ALU.is_equal))
                S(v.tensor_scalar(tdum[:], m1[:], -1.0, 1.0, ALU.mult, ALU.add))
                S(v.tensor_tensor(p2[:], tdum[:], esb[:], ALU.mult))
                S(v.tensor_reduce(v2c[:], p2[:], mybir.AxisListType.X, ALU.max))
                S(v.tensor_scalar(m2[:], p2[:], v2c[:], None, ALU.is_equal))
                S(v.tensor_tensor(s12[:], v1c[:], v2c[:], ALU.add))
                S(v.reciprocal(rs12[:], s12[:]))
                S(v.tensor_tensor(mm12[:], m1[:], m2[:], ALU.add))
                S(v.tensor_tensor(tdum[:], mm12[:], esb[:], ALU.mult))
                S(v.tensor_scalar(wdt[:], tdum[:], rs12[:], None, ALU.mult))

            if tt == 0 and half == 0 and "f1dbg_d" in mk.dbg:
                @block.vector
                def _(v):
                    dbgsb = f1dbg
                    v.tensor_copy(dbgsb[:, 0:8], ps_l[:, 0:8])
                    v.tensor_copy(dbgsb[:, 8:16], esb[:])
                    v.tensor_copy(dbgsb[:, 16:24], probs[:])
                    s_ev.inc(v.tensor_copy(dbgsb[:, 24:32], wdt[:]))

                @block.sync
                def _(sy):
                    sy.wait_ge(s_ev.h, s_ev.n)
                    s_st.dma(sy.dma_start(mk.dram["f1dbg_d"][:, :], f1dbg[:]))

            if "f1all_d" in mk.dbg:
                @block.sync
                def _(sy, tt=tt, n0=n0, esb_cnt=s_sc.n):
                    sy.wait_ge(s_sc.h, esb_cnt)
                    s_st.dma(sy.dma_start(
                        mk.dram["f1all_d"][n0 + tt * 128:n0 + (tt + 1) * 128, 8:16],
                        esb[:]))
                    sy.wait_ge(s_ev.h, s_ev.n)
                    s_st.dma(sy.dma_start(
                        mk.dram["f1all_d"][n0 + tt * 128:n0 + (tt + 1) * 128, 0:8],
                        wdt[:]))

            @block.tensor
            def _(t, tt=tt):
                s_ev.wait(t)
                s_sc.wait(t)
                t.transpose(ps_w[0:8, 0:128], wdt[:], mk.c_identf[:])
                fence(mk, t, s_mm)

            @block.vector
            def _(v, tt=tt):
                s_mm.wait(v)
                s_ev.inc(v.tensor_copy(wfm[0:8, tt * 128:(tt + 1) * 128],
                                       ps_w[0:8, 0:128]))

            @block.sync
            def _(sy, tt=tt):
                sy.wait_ge(s_ev.h, s_ev.n)
                s_st.dma(sy.dma_start(mk.dram["f1q_d"][:, :], wdt[:]))

            @block.vector
            def _(v, tt=tt):
                v.wait_ge(s_st.h, s_st.n)

        if "wfm_d" in mk.dbg:
            @block.sync
            def _(sy, n0=n0):
                sy.wait_ge(s_ev.h, s_ev.n)
                sy.wait_ge(s_sc.h, s_sc.n)
                s_st.dma(sy.dma_start(mk.dram["wfm_d"][:, n0:n0 + HN], wfm[0:8, :]))

        mm3, sc3, ev3, st3 = s_mm.n, s_sc.n, s_ev.n, s_st.n

        for ns in (range(NS) if "f2" in parts else []):
            b = ns % 2

            @block.tensor
            def _(t, ns=ns, b=b, ev3=ev3, sc3=sc3):
                if ns == 0:
                    t.wait_ge(s_ev.h, ev3)
                    t.wait_ge(s_sc.h, sc3)
                if ns >= 2:
                    t.wait_ge(s_ev.h, ev3 + ns - 1)
                for kd in range(16):
                    t.matmul(ps_p[b][:, :], mk.laT[:, kd * 128:(kd + 1) * 128],
                             xnf[:, kd * HN + ns * 512: kd * HN + (ns + 1) * 512],
                             start=(kd == 0), stop=(kd == 15))
                t.matmul(ps_r[b][:, :], mk.c_e16[0:8, :],
                         wfm[0:8, ns * 512:(ns + 1) * 512], start=True, stop=True)
                fence(mk, t, s_mm, 2)

            @block.vector
            def _(v, ns=ns, b=b, mm3=mm3):
                v.wait_ge(s_mm.h, mm3 + 2 * (ns + 1))
                v.tensor_copy(wbsb[b][:], ps_r[b][:, :])
                s_ev.inc(v.tensor_tensor(Axw[:, ns * 512:(ns + 1) * 512],
                                         ps_p[b][:, :], wbsb[b][:], ALU.mult))

        if "axw_d" in mk.dbg:
            @block.sync
            def _(sy, n0=n0):
                sy.wait_ge(s_ev.h, s_ev.n)
                s_st.dma(sy.dma_start(mk.dram["axw_d"][:, n0:n0 + HN], Axw[:]))

        mm4, sc4, ev4, st4, ld4 = s_mm.n, s_sc.n, s_ev.n, s_st.n, s_ld.n

        for os_ in (range(16) if "f3" in parts else []):
            sbuf_i = os_ % 2

            @block.gpsimd
            def _(g, os_=os_, sbuf_i=sbuf_i, mm4=mm4):
                if os_ < 2:
                    g.wait_ge(s_mm.h, mm4)
                else:
                    g.wait_ge(s_mm.h, mm4 + (os_ - 1) * NS)
                w = mk.dram["wsc_L0_ffn_weight"][os_, :, :, :].rearrange(
                    "kd dd ol -> dd kd ol")
                s_ld.dma(g.dma_start(strip[sbuf_i][:].rearrange(
                    "dd (kd ol) -> dd kd ol", ol=128), w))

            for ns in range(NS):
                b = ns % 2
                itf = os_ * NS + ns

                @block.tensor
                def _(t, os_=os_, ns=ns, b=b, itf=itf, sbuf_i=sbuf_i,
                      ld4=ld4, ev4=ev4, sc4=sc4):
                    t.wait_ge(s_ld.h, ld4 + 16 * (os_ + 1))
                    if itf == 0:
                        t.wait_ge(s_ev.h, ev4)
                    if itf >= 2:
                        t.wait_ge(s_sc.h, sc4 + itf - 1)
                    for kd in range(16):
                        t.matmul(ps_p[b][:, :],
                                 strip[sbuf_i][:, kd * 128:(kd + 1) * 128],
                                 xnf[:, kd * HN + ns * 512: kd * HN + (ns + 1) * 512],
                                 start=(kd == 0), stop=False)
                    t.matmul(ps_p[b][:, :], mk.BT[:, os_ * 128:(os_ + 1) * 128],
                             Axw[:, ns * 512:(ns + 1) * 512], start=False, stop=True)
                    fence(mk, t, s_mm)

                @block.scalar
                def _(sc, b=b, itf=itf, mm4=mm4, st4=st4):
                    sc.wait_ge(s_mm.h, mm4 + itf + 1)
                    if itf >= 2:
                        sc.wait_ge(s_st.h, st4 + 16 * (itf - 1))
                    s_sc.inc(sc.activation(gsb[b][:], ps_p[b][:, :], AF.Silu))

                @block.sync
                def _(sy, os_=os_, ns=ns, b=b, itf=itf, n0=n0, sc4=sc4):
                    sy.wait_ge(s_sc.h, sc4 + itf + 1)
                    s_st.dma(sy.dma_start(
                        g0_d[os_ * 128:(os_ + 1) * 128,
                             n0 + ns * 512:n0 + (ns + 1) * 512],
                        gsb[b][:]))

    mk.passA_done = [s_st, s_ev, s_sc, s_mm, s_ld]
    ph.close()
    phase_end(mk, mk.passA_done)


def emit_l0_passB(mk, block):
    nc = mk.nc
    ph = contextlib.ExitStack()
    sb = lambda n, s, d: ph.enter_context(nc.sbuf_tensor(n, list(s), d))
    phase_start(mk, block)

    qr_sub = sb("b_qr", [128, 24 * 512], BF16)
    qp_sub = sb("b_qp", [128, 24 * 512], BF16)
    v4 = [sb(f"b_v4{i}", [8, 128 * 128], BF16) for i in range(2)]
    A_sb = sb("b_A", [8, 16 * 512], BF16)
    O2_sb = sb("b_O2", [128, 16 * 512], BF16)
    invb_sb = sb("b_invb", [128, 512], F32)
    tp = sb("b_tp", [128, 512], F32)
    tq = sb("b_tq", [128, 512], F32)
    n2t = sb("b_n2t", [128, 512], F32)
    rr = sb("b_rr", [128, 512], F32)
    nsq = sb("b_nsq", [24, 512], F32)
    inv2 = sb("b_inv2", [24, 512], F32)
    ps_n = ph.enter_context(nc.psum_tensor("b_psn", [24, 512], F32))
    ps_p = ph.enter_context(nc.psum_tensor("b_psp", [128, 512], F32))
    ps_i = ph.enter_context(nc.psum_tensor("b_psi", [128, 512], F32))
    ps_A = [ph.enter_context(nc.psum_tensor(f"b_psA{i}", [8, 512], F32))
            for i in range(2)]
    ps_O = [ph.enter_context(nc.psum_tensor(f"b_psO{i}", [128, 512], F32))
            for i in range(2)]

    s_ld = mk.sem("b_ld")
    s_mm = mk.sem("b_mm")
    s_sc = mk.sem("b_sc")
    s_ev = mk.sem("b_ev")
    s_st = mk.sem("b_st")
    O2_d = mk.dram["O2_d"]

    NSUB = SEQ // 512
    for su in range(NSUB):
        n0 = su * 512

        @block.sync
        def _(sy, su=su, n0=n0):
            if su > 0:
                sy.wait_ge(s_ev.h, s_ev.n)
                sy.wait_ge(s_st.h, s_st.n)
            qr = mk.dram["qr_d"][:, :, n0:n0 + 512].rearrange("h p t -> p h t")
            s_ld.dma(sy.dma_start(
                qr_sub[:].rearrange("p (h t) -> p h t", h=24), qr))
            vsrc = mk.dram["Vt"][:, n0:n0 + 128, :].rearrange("hd tq dd -> hd (tq dd)")
            s_ld.dma(sy.dma_start(v4[0][:, :], vsrc))

        @block.vector
        def _(v, su=su):
            v.wait_ge(s_ld.h, s_ld.n)
            s_ev.inc(v.scalar_tensor_tensor(qp_sub[:], qr_sub[:], 1.0,
                                            qr_sub[:], ALU.mult, ALU.mult))

        @block.tensor
        def _(t, su=su):
            s_ev.wait(t)
            for h in range(24):
                t.matmul(ps_n[:, :], mk.hot24[:, h * 24:(h + 1) * 24],
                         qp_sub[:, h * 512:(h + 1) * 512],
                         start=(h == 0), stop=(h == 23))
            fence(mk, t, s_mm)

        @block.vector
        def _(v, su=su):
            s_mm.wait(v)
            v.tensor_copy(nsq[:], ps_n[:, :])
            s_ev.inc(v.drain())
            v.wait_ge(s_ev.h, s_ev.n)
            v.tensor_scalar(nsq[:], nsq[:], 1e-12, None, ALU.max)
            s_ev.inc(v.drain())

        @block.scalar
        def _(sc, su=su):
            s_ev.wait(sc)
            sc.activation(inv2[:], nsq[:], AF.Ln)
            sc.drain()
            sc.activation(inv2[:], inv2[:], AF.Exp, scale=-0.5,
                          bias=mk.ln2pi[0:24, :])
            s_sc.inc(sc.drain())

        for h in range(24):
            @block.tensor
            def _(t, h=h, su=su):
                if h == 0:
                    s_sc.wait(t)
                else:
                    s_sc.wait(t)
                    s_ev.wait(t)
                t.matmul(ps_p[:, :], mk.RRT[:], qr_sub[:, h * 512:(h + 1) * 512],
                         start=True, stop=True)
                t.matmul(ps_i[:, :], mk.hotrow24[:, h * 128:(h + 1) * 128],
                         inv2[:], start=True, stop=True)
                fence(mk, t, s_mm)

            @block.vector
            def _(v, h=h, su=su):
                s_mm.wait(v)

                def S():
                    s_ev.inc(v.drain())
                    v.wait_ge(s_ev.h, s_ev.n)
                v.tensor_copy(invb_sb[:], ps_i[:, :])
                S()
                v.tensor_tensor(tp[:], ps_p[:, :], invb_sb[:], ALU.mult)
                S()
                v.tensor_scalar(tq[:], tp[:], 1.0, mk.dmagic[:], ALU.mult, ALU.add)
                S()
                v.tensor_scalar(n2t[:], tq[:], -MAGIC, None, ALU.add)
                S()
                v.scalar_tensor_tensor(rr[:], n2t[:], -1.0, tp[:], ALU.mult, ALU.add)
                S()

            @block.scalar
            def _(sc, h=h, su=su):
                s_ev.wait(sc)
                sc.activation(qp_sub[:, h * 512:(h + 1) * 512], rr[:], AF.Sin,
                              scale=TWO_PI, bias=mk.b2pi[:])
                s_sc.inc(sc.drain())

        qpr = qp_sub[:].rearrange("p (h t) -> p t h", h=24)
        for g32 in range(16):
            bb = g32 % 2

            @block.tensor
            def _(t, su=su, g32=g32, bb=bb):
                if g32 == 0:
                    s_sc.wait(t)
                if g32 >= 2:
                    t.wait_ge(s_ev.h, s_ev.n)
                for tl in range(32):
                    tok = g32 * 32 + tl
                    t.matmul(ps_A[bb][0:8, tl * 16:(tl + 1) * 16],
                             qpr[:, tok, 16:24], qpr[:, tok, 0:16],
                             start=True, stop=True, skip_group_check=True)
                fence(mk, t, s_mm)

            @block.vector
            def _(v, su=su, g32=g32, bb=bb):
                s_mm.wait(v)
                s_ev.inc(v.tensor_scalar(A_sb[0:8, g32 * 512:(g32 + 1) * 512],
                                         ps_A[bb][0:8, :], 1.0 / 64.0, None,
                                         ALU.mult))

        for q in range(4):
            qb = q % 2
            if q < 3:
                @block.sync
                def _(sy, su=su, q=q, n0=n0):
                    if q >= 1:
                        sy.wait_ge(s_mm.h, s_mm.n)
                    vsrc = mk.dram["Vt"][:, n0 + (q + 1) * 128:n0 + (q + 2) * 128, :]
                    s_ld.dma(sy.dma_start(
                        v4[(q + 1) % 2][:, :],
                        vsrc.rearrange("hd tq dd -> hd (tq dd)")))

            for g32 in range(4):
                bb = g32 % 2

                @block.tensor
                def _(t, su=su, q=q, g32=g32, bb=bb, qb=qb):
                    t.wait_ge(s_ld.h, s_ld.n)
                    if not (q == 0 and g32 == 0):
                        t.wait_ge(s_ev.h, s_ev.n)
                    for tl in range(32):
                        tok_l = g32 * 32 + tl
                        t.matmul(ps_O[bb][:, tl * 16:(tl + 1) * 16],
                                 v4[qb][0:8, tok_l * 128:(tok_l + 1) * 128],
                                 A_sb[0:8,
                                      (q * 128 + tok_l) * 16:(q * 128 + tok_l + 1) * 16],
                                 start=True, stop=True, skip_group_check=True)
                    fence(mk, t, s_mm)

                @block.vector
                def _(v, su=su, q=q, g32=g32, bb=bb):
                    s_mm.wait(v)
                    t0 = q * 128 + g32 * 32
                    tw0 = t0 // 16
                    dst = O2_sb[:].rearrange("p (j h tw) -> p tw j h",
                                             j=16, h=16)[:, tw0:tw0 + 2, :, :]
                    src = ps_O[bb][:, :].rearrange("p (t2 j h) -> p t2 j h",
                                                   t2=2, j=16)
                    s_ev.inc(v.tensor_copy(dst, src))

        @block.sync
        def _(sy, su=su, n0=n0):
            sy.wait_ge(s_ev.h, s_ev.n)
            for j in range(16):
                dst = O2_d[j * 128:(j + 1) * 128, :].rearrange(
                    "dd (h t) -> dd h t", h=16)[:, :, su * 32:(su + 1) * 32]
                src = O2_sb[:, j * 512:(j + 1) * 512].rearrange(
                    "p (h tw) -> p h tw", h=16)
                s_st.dma(sy.dma_start(dst, src))

    mk.passB_done = [s_st, s_ev, s_sc, s_mm, s_ld]
    ph.close()
    phase_end(mk, mk.passB_done)


def emit_l0_passC(mk, block):
    nc = mk.nc
    ph = contextlib.ExitStack()
    sb = lambda n, s, d: ph.enter_context(nc.sbuf_tensor(n, list(s), d))
    phase_start(mk, block)

    wot = sb("c_wot", [128, 16 * 2048], BF16)
    o2c = [sb(f"c_o2{i}", [128, 16 * 512], BF16) for i in range(2)]
    g0c = [sb(f"c_g0{i}", [128, 16 * 512], BF16) for i in range(2)]
    sgc = [sb(f"c_sg{i}", [128, 16 * 512], BF16) for i in range(2)]
    ps = [ph.enter_context(nc.psum_tensor(f"c_ps{i}", [128, 512], F32))
          for i in range(2)]
    s_ld = mk.sem("c_ld")
    s_mm = mk.sem("c_mm")
    s_ev = mk.sem("c_ev")
    s_st = mk.sem("c_st")
    sg_d = mk.dram["sg_d"]

    @block.sync
    def _(sy):
        for kd in range(16):
            s_ld.dma(sy.dma_start(
                wot[:, kd * 2048:(kd + 1) * 2048],
                mk.dram["L0_attn_Wo_bf"][0:2048, kd * 128:(kd + 1) * 128],
                transpose=True))
    wot_ready = s_ld.n

    for nch in range(8):
        cb = nch % 2
        n0 = nch * 512

        @block.sync
        def _(sy, nch=nch, cb=cb, n0=n0):
            if nch >= 2:
                sy.wait_ge(s_st.h, 16 * (nch - 1))
                sy.wait_ge(s_mm.h, s_mm.n)
            src = mk.dram["O2_d"].rearrange("(h dd) n -> dd h n", dd=128)
            s_ld.dma(sy.dma_start(
                o2c[cb][:].rearrange("p (h t) -> p h t", h=16),
                src[:, :, n0:n0 + 512]))
            src2 = mk.dram["g0_d"].rearrange("(os dd) n -> dd os n", dd=128)
            s_ld.dma(sy.dma_start(
                g0c[cb][:].rearrange("p (os t) -> p os t", os=16),
                src2[:, :, n0:n0 + 512]))

        for os_ in range(16):
            @block.tensor
            def _(t, nch=nch, cb=cb, os_=os_):
                if nch == 0 and os_ == 0:
                    t.wait_ge(s_ld.h, wot_ready)
                t.wait_ge(s_ld.h, s_ld.n)
                pb = os_ % 2
                if not (nch == 0 and os_ <= 1):
                    t.wait_ge(s_ev.h, s_ev.n - 1)
                for kd in range(16):
                    t.matmul(ps[pb][:, :],
                             wot[:, kd * 2048 + os_ * 128: kd * 2048 + (os_ + 1) * 128],
                             o2c[cb][:, kd * 512:(kd + 1) * 512],
                             start=(kd == 0), stop=(kd == 15))
                fence(mk, t, s_mm)

            @block.vector
            def _(v, nch=nch, cb=cb, os_=os_):
                s_mm.wait(v)
                pb = os_ % 2
                s_ev.inc(v.scalar_tensor_tensor(
                    sgc[cb][:, os_ * 512:(os_ + 1) * 512], ps[pb][:, :],
                    mk.b_o[:, os_:os_ + 1], g0c[cb][:, os_ * 512:(os_ + 1) * 512],
                    ALU.add, ALU.mult))

        @block.sync
        def _(sy, nch=nch, cb=cb, n0=n0):
            sy.wait_ge(s_ev.h, s_ev.n)
            dst = sg_d.rearrange("(os dd) n -> dd os n", dd=128)
            s_st.dma(sy.dma_start(
                dst[:, :, n0:n0 + 512],
                sgc[cb][:].rearrange("p (os t) -> p os t", os=16)))

    mk.passC_done = [s_st, s_ev, s_mm, s_ld]
    ph.close()
    phase_end(mk, mk.passC_done)


def emit_combine(mk, block, x_name, sg_name, x1_name):
    nc = mk.nc
    ph = contextlib.ExitStack()
    sb = lambda n, s, d: ph.enter_context(nc.sbuf_tensor(n, list(s), d))
    phase_start(mk, block)
    tag = x1_name
    sgt = [sb(f"m_sg{i}_{tag}", [128, D], BF16) for i in range(2)]
    xt = [sb(f"m_x{i}_{tag}", [128, D], F32) for i in range(2)]
    x1t = [sb(f"m_x1{i}_{tag}", [128, D], F32) for i in range(2)]
    s_ld = mk.sem(f"m_ld_{tag}")
    s_ev = mk.sem(f"m_ev_{tag}")
    s_st = mk.sem(f"m_st_{tag}")
    xd = mk.dram[x_name]
    sgd = mk.dram[sg_name]
    x1d = mk.dram[x1_name]

    for tt in range(NT):
        b = tt % 2

        @block.sync
        def _(sy, tt=tt, b=b):
            if tt >= 2:
                sy.wait_ge(s_ev.h, tt - 1)
            s_ld.dma(sy.dma_start(sgt[b][:], sgd[0:D, tt * 128:(tt + 1) * 128],
                                  transpose=True))
            s_ld.dma(sy.dma_start(xt[b][:], xd[tt * 128:(tt + 1) * 128, :]))

        @block.vector
        def _(v, tt=tt, b=b):
            v.wait_ge(s_ld.h, 32 * (tt + 1))
            if tt >= 2:
                v.wait_ge(s_st.h, 16 * (tt - 1))
            s_ev.inc(v.tensor_tensor(x1t[b][:], xt[b][:], sgt[b][:], ALU.add))

        @block.sync
        def _(sy, tt=tt, b=b):
            sy.wait_ge(s_ev.h, tt + 1)
            s_st.dma(sy.dma_start(x1d[tt * 128:(tt + 1) * 128, :], x1t[b][:]))

    done = [s_st, s_ev, s_ld]
    ph.close()
    phase_end(mk, done)
    return Sem(s_st.h) if False else done[0]



def emit_l1_sp(mk, block):
    nc = mk.nc
    s = mk.sem("p1_ld")
    s_mm = mk.sem("p1_mm")
    mk.pq_sb = mk.sb("pq_sb", [128, 16 * 4], BF16)
    mk.b1_qkv = mk.sb("b1_qkv", [128, 48], F32)
    mk.b1_lmqkv = mk.sb("b1_lmqkv", [128, 48], F32)
    mk.b1_lmout = mk.sb("b1_lmout", [128, 16], F32)
    mk.b1_l2tq = mk.sb("b1_l2tq", [128, 16], F32)
    mk.b1_l2to = mk.sb("b1_l2to", [128, 16], F32)
    mk.b1_lmkv = mk.sb("b1_lmkv", [128, 32], F32)
    mk.b1_out = mk.sb("b1_out", [128, 16], F32)
    mk.b1_gate = mk.sb("b1_gate", [32, 1], F32)
    mk.cmask = mk.sb("c_cmask", [128, 512], F32)
    mk.hot64 = mk.sb("c_hot64", [128, 2], BF16)
    mk.hot64T = mk.sb("c_hot64T", [2, 128], BF16)
    mk.c_h128 = mk.sb("c_h128", [128, 128], BF16)
    mk.c_sw64 = mk.sb("c_sw64", [128, 128], BF16)
    phase_start(mk, block)
    ph = contextlib.ExitStack()
    tmp = ph.enter_context(nc.sbuf_tensor("p1_tmp", [4, 2048], BF16))
    ps_t = ph.enter_context(nc.psum_tensor("p1_ps", [128, 1024], BF16))

    @block.sync
    def _(sy):
        mk.s_wprep.wait(sy)
        s.dma(sy.dma_start(mk.cmask[:], mk.dram["cmask"][:, :]))
        for nm, dst, nstr in [("L1_attn_b_qkv", mk.b1_qkv, 48),
                              ("L1_attn_b_lm_qkv", mk.b1_lmqkv, 48),
                              ("L1_attn_b_lm_out", mk.b1_lmout, 16),
                              ("L1_attn_b_l2t_q", mk.b1_l2tq, 16),
                              ("L1_attn_b_l2t_out", mk.b1_l2to, 16),
                              ("L1_attn_b_lm_kv", mk.b1_lmkv, 32),
                              ("L1_attn_b_out", mk.b1_out, 16)]:
            v = mk.dram[nm].rearrange("(s p) -> p s", p=128)
            s.dma(sy.dma_start(dst[:, 0:nstr], v, allow_slow_non_contiguous=True))
        v = mk.dram["L1_attn_b_gate"].rearrange("(s p) -> p s", p=32)
        s.dma(sy.dma_start(mk.b1_gate[0:32, :], v, allow_slow_non_contiguous=True))
        s.dma(sy.dma_start(tmp[0:4, :], mk.dram["L1_attn_pool_q_bf"][:, :]))

    @block.gpsimd
    def _(g):
        s.dma(g.dma_start(mk.hot64[:], mk.dram["hot64"][:, :]))
        s.dma(g.dma_start(mk.hot64T[0:2, :], mk.dram["hot64T"][:, :]))
        s.dma(g.dma_start(mk.c_h128[:], mk.dram["h128"][:, :]))
        s.dma(g.dma_start(mk.c_sw64[:], mk.dram["swap64"][:, :]))

    for ht in range(16):
        @block.tensor
        def _(t, ht=ht):
            if ht == 0:
                s.wait(t)
                mk.s_const.wait(t)
            else:
                s_mm.wait(t)
            t.transpose(ps_t[:, 0:4], tmp[0:4, ht * 128:(ht + 1) * 128],
                        mk.c_ident[0:4, 0:4])
            fence(mk, t, s_mm)

        @block.vector
        def _(v, ht=ht):
            s_mm.wait(v)
            s_mm.inc(v.tensor_copy(mk.pq_sb[:, ht * 4:(ht + 1) * 4], ps_t[:, 0:4]))

    ph.close()
    phase_end(mk, [s, s_mm])
    mk.l1sp_done = s


def emit_l1_qkv(mk, block):
    nc = mk.nc
    ph = contextlib.ExitStack()
    sb = lambda n, s, d: ph.enter_context(nc.sbuf_tensor(n, list(s), d))
    phase_start(mk, block)
    HN = SEQ // 2
    xnf = sb("e_xnf", [128, 16 * HN], BF16)
    ropeC = sb("e_ropeC", [128, HN], F32)
    ropeS = sb("e_ropeS", [128, HN], F32)
    strip = [sb(f"e_strip{i}", [128, 16 * 128], BF16) for i in range(2)]
    qsb = [sb(f"e_qsb{i}", [128, 512], BF16) for i in range(2)]
    t1 = [sb(f"e_t1{i}", [128, 512], F32) for i in range(2)]
    t2 = [sb(f"e_t2{i}", [128, 512], F32) for i in range(2)]
    qrb = [sb(f"e_qrb{i}", [128, 512], BF16) for i in range(2)]
    ps_p = [ph.enter_context(nc.psum_tensor(f"e_psp{i}", [128, 512], F32))
            for i in range(2)]
    ps_r = [ph.enter_context(nc.psum_tensor(f"e_psr{i}", [128, 512], F32))
            for i in range(2)]
    s_ld = mk.sem("e_ld")
    s_mm = mk.sem("e_mm")
    s_sc = mk.sem("e_sc")
    s_ev = mk.sem("e_ev")
    s_st = mk.sem("e_st")
    outd = {"q": mk.dram["q1_d"], "k": mk.dram["k1_d"], "v": mk.dram["v1_d"]}

    NS = HN // 512
    for half in range(2):
        n0 = half * HN
        mm0, sc0, ev0, st0 = s_mm.n, s_sc.n, s_ev.n, s_st.n
        if half == 1:
            barrier(mk, block, [s_ld, s_mm, s_sc, s_ev, s_st])

        @block.sync
        def _(sy, n0=n0):
            for kd in range(16):
                s_ld.dma(sy.dma_start(
                    xnf[:, kd * HN:(kd + 1) * HN],
                    mk.dram["xn1"][n0:n0 + HN, kd * 128:(kd + 1) * 128],
                    transpose=True))
            s_ld.dma(sy.dma_start(ropeC[:], mk.dram["rope1_cos"][:, n0:n0 + HN]))
            s_ld.dma(sy.dma_start(ropeS[:], mk.dram["rope1_sin"][:, n0:n0 + HN]))
        xn_ready = s_ld.n

        for os_ in range(48):
            kind = "q" if os_ < 16 else ("k" if os_ < 32 else "v")
            dst = outd[kind]
            orow = (os_ % 16) * 128
            sbuf_i = os_ % 2

            @block.gpsimd
            def _(g, os_=os_, sbuf_i=sbuf_i, mm0=mm0):
                if os_ >= 2:
                    g.wait_ge(s_mm.h, mm0 + (os_ - 1) * NS * 2)
                else:
                    g.wait_ge(s_mm.h, mm0)
                w = mk.dram["L1_attn_w_qkv_bf"][:, os_ * 128:(os_ + 1) * 128]
                s_ld.dma(g.dma_start(strip[sbuf_i][:].rearrange(
                    "dd (kd ol) -> dd kd ol", ol=128),
                    w.rearrange("(kd dd) ol -> dd kd ol", dd=128)))

            for ns in range(NS):
                it = os_ * NS + ns
                b = it % 2

                @block.tensor
                def _(t, os_=os_, ns=ns, b=b, it=it, sbuf_i=sbuf_i,
                      xn_ready=xn_ready, sc0=sc0):
                    if ns == 0:
                        t.wait_ge(s_ld.h, xn_ready + 16 * (os_ + 1))
                    if it >= 2:
                        t.wait_ge(s_sc.h, sc0 + it - 1)
                    for kd in range(16):
                        t.matmul(ps_p[b][:, :],
                                 strip[sbuf_i][:, kd * 128:(kd + 1) * 128],
                                 xnf[:, kd * HN + ns * 512: kd * HN + (ns + 1) * 512],
                                 start=(kd == 0), stop=(kd == 15))
                    fence(mk, t, s_mm)

                @block.scalar
                def _(sc, it=it, b=b, os_=os_, mm0=mm0, ev0=ev0):
                    sc.wait_ge(s_mm.h, mm0 + 2 * it + 1)
                    if it >= 2:
                        sc.wait_ge(s_ev.h, ev0 + it - 1)
                    s_sc.inc(sc.activation(qsb[b][:], ps_p[b][:, :], AF.Identity,
                                           bias=mk.b1_qkv[:, os_:os_ + 1]))

                @block.tensor
                def _(t, it=it, b=b, sc0=sc0, ev0=ev0, kind=kind):
                    t.wait_ge(s_sc.h, sc0 + it + 1)
                    if it >= 2:
                        t.wait_ge(s_ev.h, ev0 + it - 1)
                    if kind in ("q", "k"):
                        t.matmul(ps_r[b][:, :], mk.c_rot1[:], qsb[b][:],
                                 start=True, stop=True)
                    fence(mk, t, s_mm)

                @block.vector
                def _(v, it=it, b=b, ns=ns, mm0=mm0, st0=st0, kind=kind):
                    v.wait_ge(s_mm.h, mm0 + 2 * it + 2)
                    if it >= 2:
                        v.wait_ge(s_st.h, st0 + 16 * (it - 1))
                    cs = slice(ns * 512, (ns + 1) * 512)
                    if kind in ("q", "k"):
                        v.tensor_tensor(t1[b][:], ps_r[b][:, :], ropeS[:, cs],
                                        ALU.mult)
                        v.tensor_tensor(t2[b][:], qsb[b][:], ropeC[:, cs],
                                        ALU.mult)
                        s_ev.inc(v.tensor_tensor(qrb[b][:], t1[b][:], t2[b][:],
                                                 ALU.add))
                    else:
                        s_ev.inc(v.tensor_copy(qrb[b][:], qsb[b][:]))

                @block.sync
                def _(sy, it=it, b=b, ns=ns, n0=n0, ev0=ev0, dst=dst, orow=orow):
                    sy.wait_ge(s_ev.h, ev0 + it + 1)
                    s_st.dma(sy.dma_start(
                        dst[orow:orow + 128, n0 + ns * 512:n0 + (ns + 1) * 512],
                        qrb[b][:]))

    mk.l1qkv_done = [s_st, s_ev, s_sc, s_mm, s_ld]
    ph.close()
    phase_end(mk, mk.l1qkv_done)


def emit_l1_loc(mk, block):
    nc = mk.nc
    mk.attn_es = contextlib.ExitStack()
    mk.lmfA = mk.attn_es.enter_context(mk.nc.sbuf_tensor("lmfA", [128, 16 * 256], BF16))
    mk.lmfB = mk.attn_es.enter_context(mk.nc.sbuf_tensor("lmfB", [128, 16 * 256], BF16))
    ph = contextlib.ExitStack()
    sb = lambda n, s, d: ph.enter_context(nc.sbuf_tensor(n, list(s), d))
    phase_start(mk, block)

    qt = [sb(f"g_qt{i}", [128, SEQ], BF16) for i in range(2)]
    kt = [sb(f"g_kt{i}", [128, SEQ], BF16) for i in range(2)]
    vt = [sb(f"g_vt{i}", [128, SEQ], BF16) for i in range(2)]
    vT = sb("g_vT", [128, SEQ], BF16)
    sc_sb = sb("g_sc", [128, 512], F32)
    ex_sb = sb("g_ex", [128, 512], F32)
    ex2_sb = sb("g_ex2", [128, 512], F32)
    pr_sb = [sb(f"g_pr{i}", [128, 512], BF16) for i in range(2)]
    PT_sb = [sb(f"g_PT{i}", [128, 512], BF16) for i in range(2)]
    loc_sb = sb("g_loc", [128, SEQ], BF16)
    mx8 = sb("g_mx8", [128, 8], F32)
    nm8 = sb("g_nm8", [128, 8], F32)
    s8 = sb("g_s8", [128, 8], F32)
    r8 = sb("g_r8", [128, 8], F32)
    pe_sb = sb("g_pe", [128, 256], BF16)
    s2 = sb("g_s2", [2, 256], F32)
    rs2 = sb("g_rs2", [2, 256], BF16)
    pa_sb = sb("g_pa", [128, 256], BF16)
    pa_sw = sb("g_pasw", [128, 256], BF16)
    ps_sc = [ph.enter_context(nc.psum_tensor(f"g_pssc{i}", [128, 512], F32))
             for i in range(2)]
    ps_T = ph.enter_context(nc.psum_tensor("g_psT", [128, 1024], BF16))
    ps_o = ph.enter_context(nc.psum_tensor("g_pso", [128, 512], F32))
    ps_pa = ph.enter_context(nc.psum_tensor("g_pspa", [128, 512], F32))
    ps_ms = ph.enter_context(nc.psum_tensor("g_psms", [128, 512], F32))
    ps_lmA = ph.enter_context(nc.psum_tensor("g_pslma", [128, 512], F32))
    ps_lmB = ph.enter_context(nc.psum_tensor("g_pslmb", [128, 512], F32))
    s_ld = mk.sem("g_ld")
    s_mm = mk.sem("g_mm")
    s_sc = mk.sem("g_sc")
    s_ev = mk.sem("g_ev")
    s_st = mk.sem("g_st")

    sc_fence = {}
    c1_done = {}
    e_done = {}
    p_done = {}
    tr_fence = {}
    ptev_done = {}
    og_fence = {}
    ev_done = {}
    ld_mark = {}
    loc_st_mark = {}
    ptev_last = [0]
    vtcopy_last = [0]

    def prologue(ht):
        hb = ht % 2

        @block.sync
        def _(sy, ht=ht, hb=hb):
            if ht >= 2:
                sy.wait_ge(s_ev.h, s_ev.n)
                sy.wait_ge(s_mm.h, s_mm.n)
            s_ld.dma(sy.dma_start(qt[hb][:], mk.dram["q1_d"][ht * 128:(ht + 1) * 128, :]))
            s_ld.dma(sy.dma_start(kt[hb][:], mk.dram["k1_d"][ht * 128:(ht + 1) * 128, :]))
            s_ld.dma(sy.dma_start(vt[hb][:], mk.dram["v1_d"][ht * 128:(ht + 1) * 128, :]))
            ld_mark[ht] = s_ld.n

        for grp in range(4):
            @block.tensor
            def _(t, ht=ht, hb=hb, grp=grp):
                t.wait_ge(s_ld.h, ld_mark[ht])
                t.wait_ge(s_sc.h, ptev_last[0])
                s_ev.wait(t)
                for jj in range(8):
                    np_ = grp * 8 + jj
                    t.transpose(ps_T[:, jj * 128:(jj + 1) * 128],
                                vt[hb][:, np_ * 128:(np_ + 1) * 128],
                                mk.c_ident[:])
                fence(mk, t, s_mm)

            @block.vector
            def _(v, grp=grp):
                s_mm.wait(v)
                s_ev.inc(v.tensor_copy(vT[:, grp * 1024:(grp + 1) * 1024],
                                       ps_T[:, 0:1024]))
                vtcopy_last[0] = s_ev.n

        @block.sync
        def _(sy, ht=ht):
            sy.wait_ge(s_ev.h, s_ev.n)
            s_st.dma(sy.dma_start(mk.dram["vT1_d"][ht, :, :], vT[:]))

    def a_part(u):
        ht, nbg = u // 8, u % 8
        hb = ht % 2

        @block.tensor
        def _(t, u=u, ht=ht, hb=hb, nbg=nbg):
            t.wait_ge(s_ld.h, ld_mark[ht])
            if u - 2 in c1_done:
                t.wait_ge(s_ev.h, c1_done[u - 2])
            for j in range(8):
                nb = nbg * 8 + j
                for h2 in range(2):
                    t.matmul(ps_sc[u % 2][h2 * 64:(h2 + 1) * 64,
                                          j * 64:(j + 1) * 64],
                             qt[hb][h2 * 64:(h2 + 1) * 64, nb * 64:(nb + 1) * 64],
                             kt[hb][h2 * 64:(h2 + 1) * 64, nb * 64:(nb + 1) * 64],
                             start=True, stop=True, skip_group_check=True)
            fence(mk, t, s_mm)
            sc_fence[u] = s_mm.n

        @block.vector
        def _(v, u=u):
            v.wait_ge(s_mm.h, sc_fence[u])
            if u - 1 in e_done:
                v.wait_ge(s_sc.h, e_done[u - 1])

            def S():
                s_ev.inc(v.drain())
                v.wait_ge(s_ev.h, s_ev.n)
            v.scalar_tensor_tensor(sc_sb[:], ps_sc[u % 2][:, :], 0.125,
                                   mk.cmask[:], ALU.mult, ALU.add)
            vv = sc_sb[:].rearrange("p (b s) -> p b s", s=64)
            v.tensor_reduce(mx8[:], vv, mybir.AxisListType.X, ALU.max)
            S()
            v.tensor_scalar(nm8[:], mx8[:], -1.0, None, ALU.mult)
            S()
            bview = nm8[:].unsqueeze(2).broadcast_to([128, 8, 64])
            s_ev.inc(v.tensor_tensor(
                ex_sb[:].rearrange("p (b s) -> p b s", s=64), vv, bview,
                ALU.add))
            c1_done[u] = s_ev.n

        @block.scalar
        def _(sc, u=u):
            sc.wait_ge(s_ev.h, c1_done[u])
            if u - 1 in p_done:
                sc.wait_ge(s_ev.h, p_done[u - 1])
            s_sc.inc(sc.activation(ex2_sb[:], ex_sb[:], AF.Exp))
            e_done[u] = s_sc.n

    def a_part2(u):
        @block.vector
        def _(v, u=u):
            v.wait_ge(s_sc.h, e_done[u])
            if u - 2 in tr_fence:
                v.wait_ge(s_mm.h, tr_fence[u - 2])

            def S():
                s_ev.inc(v.drain())
                v.wait_ge(s_ev.h, s_ev.n)
            ev_ = ex2_sb[:].rearrange("p (b s) -> p b s", s=64)
            v.tensor_reduce(s8[:], ev_, mybir.AxisListType.X, ALU.add)
            S()
            v.reciprocal(r8[:], s8[:])
            S()
            rview = r8[:].unsqueeze(2).broadcast_to([128, 8, 64])
            s_ev.inc(v.tensor_tensor(
                pr_sb[u % 2][:].rearrange("p (b s) -> p b s", s=64), ev_,
                rview, ALU.mult))
            p_done[u] = s_ev.n

    def b_part(u):
        ht, nbg = u // 8, u % 8
        hb = ht % 2

        @block.tensor
        def _(t, u=u):
            t.wait_ge(s_ev.h, p_done[u])
            t.wait_ge(s_ev.h, vtcopy_last[0])
            if u - 2 in ptev_done:
                t.wait_ge(s_sc.h, ptev_done[u - 2])
            for tp in range(4):
                t.transpose(ps_T[:, tp * 128:(tp + 1) * 128],
                            pr_sb[u % 2][:, tp * 128:(tp + 1) * 128],
                            mk.c_ident[:])
            fence(mk, t, s_mm)
            tr_fence[u] = s_mm.n

        @block.scalar
        def _(sc, u=u):
            sc.wait_ge(s_mm.h, tr_fence[u])
            if u - 2 in og_fence:
                sc.wait_ge(s_mm.h, og_fence[u - 2])
            s_sc.inc(sc.activation(PT_sb[u % 2][:], ps_T[:, 0:512], AF.Copy))
            ptev_done[u] = s_sc.n
            ptev_last[0] = s_sc.n

        for jg in range(2):
            @block.tensor
            def _(t, u=u, ht=ht, hb=hb, nbg=nbg, jg=jg):
                t.wait_ge(s_sc.h, ptev_done[u])
                mark = (u, jg - 1) if jg == 1 else (u - 1, 1)
                if mark in ev_done:
                    t.wait_ge(s_ev.h, ev_done[mark])
                for jj in range(4):
                    j = jg * 4 + jj
                    nb = nbg * 8 + j
                    par = j % 2
                    if jj > 0:
                        t.drain()
                    t.matmul(ps_o[:, jj * 128:(jj + 1) * 128],
                             vT[par * 64:(par + 1) * 64,
                                (nb // 2) * 128:(nb // 2 + 1) * 128],
                             PT_sb[u % 2][par * 64:(par + 1) * 64,
                                          (j // 2) * 128:(j // 2 + 1) * 128],
                             start=True, stop=True, skip_group_check=True)
                fence(mk, t, s_mm)
                og_fence[u] = s_mm.n

            @block.vector
            def _(v, u=u, ht=ht, nbg=nbg, jg=jg):
                v.wait_ge(s_mm.h, og_fence[u])
                if nbg == 0 and jg == 0 and ht - 1 in loc_st_mark:
                    v.wait_ge(s_st.h, loc_st_mark[ht - 1])
                c0 = (nbg * 8 + jg * 4) * 64
                pv = ps_o[:, 0:512].rearrange("p (j two s) -> p j two s",
                                              two=2, s=64)
                v.tensor_copy(
                    loc_sb[0:64, c0:c0 + 256].rearrange("p (j s) -> p j s", s=64),
                    pv[0:64, :, 0, :])
                s_ev.inc(v.tensor_copy(
                    loc_sb[64:128, c0:c0 + 256].rearrange("p (j s) -> p j s", s=64),
                    pv[64:128, :, 1, :]))
                ev_done[(u, jg)] = s_ev.n

    def loc_store(ht):
        @block.sync
        def _(sy, ht=ht):
            sy.wait_ge(s_ev.h, ev_done[(ht * 8 + 7, 1)])
            s_st.dma(sy.dma_start(mk.dram["loc_d"][ht * 128:(ht + 1) * 128, :],
                                  loc_sb[:]))
            loc_st_mark[ht] = s_st.n

    def pa_lmh(ht):
        hb = ht % 2

        @block.tensor
        def _(t, ht=ht, hb=hb):
            s_ev.wait(t)
            for nb in range(64):
                for h2 in range(2):
                    t.matmul(ps_pa[h2 * 64:(h2 + 1) * 64, nb * 4:(nb + 1) * 4],
                             kt[hb][h2 * 64:(h2 + 1) * 64, nb * 64:(nb + 1) * 64],
                             mk.pq_sb[h2 * 64:(h2 + 1) * 64, ht * 4:(ht + 1) * 4],
                             start=True, stop=True, skip_group_check=True)
            fence(mk, t, s_mm)

        @block.scalar
        def _(sc, ht=ht):
            s_mm.wait(sc)
            s_sc.inc(sc.activation(pe_sb[:], ps_pa[:, 0:256], AF.Exp,
                                   scale=0.125))

        @block.tensor
        def _(t, ht=ht):
            s_sc.wait(t)
            t.matmul(ps_ms[0:2, 0:256], mk.hot64[:], pe_sb[:],
                     start=True, stop=True)
            fence(mk, t, s_mm)

        @block.vector
        def _(v, ht=ht):
            s_mm.wait(v)

            def S():
                s_ev.inc(v.drain())
                v.wait_ge(s_ev.h, s_ev.n)
            v.tensor_copy(s2[0:2, :], ps_ms[0:2, 0:256])
            S()
            v.reciprocal(s2[0:2, :], s2[0:2, :])
            S()
            v.tensor_copy(rs2[0:2, :], s2[0:2, :])
            S()

        @block.tensor
        def _(t, ht=ht):
            s_ev.wait(t)
            t.matmul(ps_ms[:, 256:512], mk.hot64T[0:2, :], rs2[0:2, :],
                     start=True, stop=True)
            fence(mk, t, s_mm)

        @block.vector
        def _(v, ht=ht):
            s_mm.wait(v)
            s_ev.inc(v.tensor_tensor(pa_sb[:], pe_sb[:], ps_ms[:, 256:512],
                                     ALU.mult))

        @block.tensor
        def _(t, ht=ht):
            s_ev.wait(t)
            t.matmul(ps_ms[:, 0:256], mk.c_sw64[:], pa_sb[:],
                     start=True, stop=True)
            fence(mk, t, s_mm)

        @block.vector
        def _(v, ht=ht):
            s_mm.wait(v)
            s_ev.inc(v.tensor_copy(pa_sw[:], ps_ms[:, 0:256]))

        @block.tensor
        def _(t, ht=ht):
            s_ev.wait(t)
            for nb in range(64):
                par = nb % 2
                for h2 in range(2):
                    dst = ps_lmA if h2 == 0 else ps_lmB
                    srcpa = pa_sb if h2 == par else pa_sw
                    t.matmul(dst[:, nb * 4:(nb + 1) * 4],
                             vT[par * 64:(par + 1) * 64, (nb // 2) * 128:(nb // 2 + 1) * 128],
                             srcpa[par * 64:(par + 1) * 64, nb * 4:(nb + 1) * 4],
                             start=True, stop=True, skip_group_check=True)
            fence(mk, t, s_mm)

        @block.vector
        def _(v, ht=ht):
            s_mm.wait(v)
            v.tensor_copy(mk.lmfA[0:64, ht * 256:(ht + 1) * 256],
                          ps_lmA[0:64, 0:256])
            s_ev.inc(v.tensor_copy(mk.lmfA[64:128, ht * 256:(ht + 1) * 256],
                                   ps_lmB[64:128, 0:256]))

    NU = 16 * 8
    for u in range(NU + 1):
        ht, nbg = u // 8, u % 8
        if u < NU and nbg == 0:
            if u - 1 >= 0:
                b_part(u - 1)
                loc_store(ht - 1)
            prologue(ht)
        if u < NU:
            a_part(u)
        if u < NU and nbg != 0:
            pass
        if u - 1 >= 0 and (u - 1) % 8 != 7 and u <= NU - 1:
            b_part(u - 1)
        if u < NU:
            a_part2(u)
        if u < NU and nbg == 7:
            pa_lmh(ht)
    b_part(NU - 1)
    loc_store(15)

    mk.l1loc_done = [s_st, s_ev, s_sc, s_mm, s_ld]
    ph.close()
    phase_end(mk, mk.l1loc_done)


def _strip_matmuls_direct(mk, t, ps, strip, rhs_sb, rhs_off, n):
    for kd in range(16):
        t.matmul(ps[:, 0:n], strip[:, kd * 128:(kd + 1) * 128],
                 rhs_sb[:, kd * 256 + rhs_off: kd * 256 + rhs_off + n],
                 start=(kd == 0), stop=(kd == 15))


def emit_l1_lm(mk, block):
    nc = mk.nc
    mk.gk_sb = mk.attn_es.enter_context(mk.nc.sbuf_tensor("gk_sb", [128, 16 * 256], BF16))
    mk.gv_sb = mk.attn_es.enter_context(mk.nc.sbuf_tensor("gv_sb", [128, 16 * 256], BF16))
    ph = contextlib.ExitStack()
    sb = lambda n, s, d: ph.enter_context(nc.sbuf_tensor(n, list(s), d))
    phase_start(mk, block)

    lq = sb("l_lq", [128, 16 * 256], BF16)
    lk = sb("l_lk", [128, 16 * 256], BF16)
    lv = sb("l_lv", [128, 16 * 256], BF16)
    lvT = sb("l_lvT", [128, 16 * 2 * 128], BF16)
    lo = sb("l_lo", [128, 16 * 256], BF16)
    l2q = sb("l_l2q", [128, 16 * 256], BF16)
    ro = sb("l_ro", [128, 16 * 256], BF16)
    strip = [sb(f"l_strip{i}", [128, 16 * 128], BF16) for i in range(2)]
    lmm = sb("l_lmm", [128, 2 * 256], F32)
    sc2 = sb("l_sc2", [128, 512], F32)
    ex2 = sb("l_ex2", [128, 512], F32)
    pr2 = sb("l_pr2", [128, 512], BF16)
    PT2 = sb("l_PT2", [128, 512], BF16)
    mx1 = sb("l_mx1", [128, 4], F32)
    nmx1 = sb("l_nmx1", [128, 4], F32)
    sum1 = sb("l_sum1", [128, 4], F32)
    rs1 = sb("l_rs1", [128, 4], F32)
    edump = sb("l_edump", [128, 256], F32)
    kt = sb("l_kt", [128, SEQ], BF16)
    vT = sb("l_vT", [128, SEQ], BF16)
    rsc = sb("l_rsc", [128, 256], F32)
    rpr = sb("l_rpr", [128, 256], BF16)
    rpw = sb("l_rpw", [128, 256], BF16)
    mxs = sb("l_mxs", [128, 2], F32)
    ps_a = ph.enter_context(nc.psum_tensor("l_psa", [128, 512], F32))
    ps_b = ph.enter_context(nc.psum_tensor("l_psb", [128, 512], F32))
    ps_T = ph.enter_context(nc.psum_tensor("l_psT", [128, 1024], BF16))
    ps_lmA = ph.enter_context(nc.psum_tensor("l_pslma", [128, 512], F32))
    ps_lmB = ph.enter_context(nc.psum_tensor("l_pslmb", [128, 512], F32))
    ps_Tf = ph.enter_context(nc.psum_tensor("l_psTf", [128, 512], F32))
    s_ld = mk.sem("l_ld")
    s_mm = mk.sem("l_mm")
    s_sc = mk.sem("l_sc")
    s_ev = mk.sem("l_ev")

    @block.sync
    def _(sy):
        s_ld.dma(sy.dma_start(lmm[:, 0:256], mk.dram["lmmask"][0:128, :]))
        s_ld.dma(sy.dma_start(lmm[:, 256:512], mk.dram["lmmask"][128:256, :]))

    def proj_256(wname, bias, dsts, rhs, nstrips):
        for os_ in range(nstrips):
            sbuf_i = os_ % 2

            @block.gpsimd
            def _(g, os_=os_, sbuf_i=sbuf_i, wname=wname):
                g.wait_ge(s_mm.h, s_mm.n)
                w = mk.dram[wname][:, os_ * 128:(os_ + 1) * 128]
                s_ld.dma(g.dma_start(strip[sbuf_i][:].rearrange(
                    "dd (kd ol) -> dd kd ol", ol=128),
                    w.rearrange("(kd dd) ol -> dd kd ol", dd=128)))

            @block.tensor
            def _(t, os_=os_, sbuf_i=sbuf_i, rhs=rhs):
                t.wait_ge(s_ld.h, s_ld.n)
                s_ev.wait(t)
                s_sc.wait(t)
                for kd in range(16):
                    t.matmul(ps_a[:, 0:256], strip[sbuf_i][:, kd * 128:(kd + 1) * 128],
                             rhs[:, kd * 256:(kd + 1) * 256],
                             start=(kd == 0), stop=(kd == 15))
                fence(mk, t, s_mm)

            @block.vector
            def _(v, os_=os_, bias=bias, dsts=dsts):
                s_mm.wait(v)
                dst, base = None, None
                for dsb, b0, b1 in dsts:
                    if b0 <= os_ < b1:
                        dst, base = dsb, b0
                s_sc.inc(v.tensor_scalar(
                    dst[:, (os_ - base) * 256:(os_ - base + 1) * 256],
                    ps_a[:, 0:256], bias[:, os_:os_ + 1], None, ALU.add))

    proj_256("L1_attn_w_lm_qkv_bf", mk.b1_lmqkv,
             [(lq, 0, 16), (lk, 16, 32), (lv, 32, 48)], mk.lmfA, 48)

    for grp in range(4):
        @block.tensor
        def _(t, grp=grp):
            s_sc.wait(t)
            s_ev.wait(t)
            for jj in range(8):
                c = grp * 8 + jj
                t.transpose(ps_T[:, jj * 128:(jj + 1) * 128],
                            lv[:, c * 128:(c + 1) * 128], mk.c_ident[:])
            fence(mk, t, s_mm)

        @block.vector
        def _(v, grp=grp):
            s_mm.wait(v)
            s_ev.inc(v.tensor_copy(lvT[:, grp * 1024:(grp + 1) * 1024],
                                   ps_T[:, 0:1024]))

    for ht in range(16):
        for h2 in range(2):
            @block.tensor
            def _(t, ht=ht, h2=h2):
                s_ev.wait(t)
                s_sc.wait(t)
                for qh in range(2):
                    t.matmul(ps_a[:, qh * 256:(qh + 1) * 256],
                             lq[h2 * 64:(h2 + 1) * 64,
                                ht * 256 + qh * 128: ht * 256 + (qh + 1) * 128],
                             lk[h2 * 64:(h2 + 1) * 64, ht * 256:(ht + 1) * 256],
                             start=True, stop=True, skip_group_check=True)
                fence(mk, t, s_mm)

            @block.vector
            def _(v, h2=h2):
                s_mm.wait(v)
                s_sc.wait(v)

                def S():
                    s_ev.inc(v.drain())
                    v.wait_ge(s_ev.h, s_ev.n)
                for qh in range(2):
                    v.scalar_tensor_tensor(sc2[:, qh * 256:(qh + 1) * 256],
                                           ps_a[:, qh * 256:(qh + 1) * 256],
                                           0.125, lmm[:, qh * 256:(qh + 1) * 256],
                                           ALU.mult, ALU.add)
                vv = sc2[:].rearrange("p (q k) -> p q k", k=256)
                v.tensor_reduce(mx1[:, 0:2], vv, mybir.AxisListType.X, ALU.max)
                S()
                v.tensor_scalar(nmx1[:, 0:2], mx1[:, 0:2], -1.0, None, ALU.mult)
                S()
                bv = nmx1[:, 0:2].unsqueeze(2).broadcast_to([128, 2, 256])
                s_ev.inc(v.tensor_tensor(
                    ex2[:].rearrange("p (q k) -> p q k", k=256), vv, bv, ALU.add))

            @block.scalar
            def _(sc, h2=h2):
                s_ev.wait(sc)
                s_sc.inc(sc.activation(sc2[:], ex2[:], AF.Exp))

            @block.vector
            def _(v, h2=h2):
                s_sc.wait(v)

                def S():
                    s_ev.inc(v.drain())
                    v.wait_ge(s_ev.h, s_ev.n)
                ev_ = sc2[:].rearrange("p (q k) -> p q k", k=256)
                v.tensor_reduce(sum1[:, 0:2], ev_, mybir.AxisListType.X, ALU.add)
                S()
                v.reciprocal(rs1[:, 0:2], sum1[:, 0:2])
                S()
                rv = rs1[:, 0:2].unsqueeze(2).broadcast_to([128, 2, 256])
                s_ev.inc(v.tensor_tensor(
                    pr2[:].rearrange("p (q k) -> p q k", k=256), ev_, rv,
                    ALU.mult))

            @block.tensor
            def _(t, h2=h2):
                s_ev.wait(t)
                for c in range(4):
                    t.transpose(ps_T[:, c * 128:(c + 1) * 128],
                                pr2[:, c * 128:(c + 1) * 128], mk.c_ident[:])
                fence(mk, t, s_mm)

            @block.vector
            def _(v, h2=h2):
                s_mm.wait(v)
                s_ev.inc(v.tensor_copy(PT2[:], ps_T[:, 0:512]))

            @block.tensor
            def _(t, ht=ht, h2=h2):
                s_ev.wait(t)
                for qh in range(2):
                    for kt_ in range(2):
                        t.matmul(ps_b[h2 * 64:(h2 + 1) * 64,
                                      qh * 128:(qh + 1) * 128],
                                 lvT[:, (ht * 2 + kt_) * 128 + h2 * 64:
                                     (ht * 2 + kt_) * 128 + (h2 + 1) * 64],
                                 PT2[:, (qh * 2 + kt_) * 128:
                                     (qh * 2 + kt_ + 1) * 128],
                                 start=(kt_ == 0), stop=(kt_ == 1),
                                 skip_group_check=True)
                fence(mk, t, s_mm)

        @block.vector
        def _(v, ht=ht):
            s_mm.wait(v)
            s_ev.inc(v.tensor_copy(lo[:, ht * 256:(ht + 1) * 256],
                                   ps_b[:, 0:256]))

    for os_ in range(16):
        sbuf_i = os_ % 2

        @block.gpsimd
        def _(g, os_=os_, sbuf_i=sbuf_i):
            g.wait_ge(s_mm.h, s_mm.n)
            w = mk.dram["L1_attn_w_lm_out_bf"][:, os_ * 128:(os_ + 1) * 128]
            s_ld.dma(g.dma_start(strip[sbuf_i][:].rearrange(
                "dd (kd ol) -> dd kd ol", ol=128),
                w.rearrange("(kd dd) ol -> dd kd ol", dd=128)))

        @block.tensor
        def _(t, os_=os_, sbuf_i=sbuf_i):
            t.wait_ge(s_ld.h, s_ld.n)
            s_ev.wait(t)
            for kd in range(16):
                t.matmul(ps_a[:, 0:256], strip[sbuf_i][:, kd * 128:(kd + 1) * 128],
                         lo[:, kd * 256:(kd + 1) * 256],
                         start=(kd == 0), stop=(kd == 15))
            fence(mk, t, s_mm)

        @block.vector
        def _(v, os_=os_):
            s_mm.wait(v)
            s_ev.inc(v.scalar_tensor_tensor(
                mk.lmfB[:, os_ * 256:(os_ + 1) * 256], ps_a[:, 0:256],
                mk.b1_lmout[:, os_:os_ + 1],
                mk.lmfA[:, os_ * 256:(os_ + 1) * 256], ALU.add, ALU.add))

    proj_256("L1_attn_w_l2t_q_bf", mk.b1_l2tq, [(l2q, 0, 16)], mk.lmfB, 16)

    for ht in range(16):
        @block.sync
        def _(sy, ht=ht):
            sy.wait_ge(s_ev.h, s_ev.n)
            sy.wait_ge(s_mm.h, s_mm.n)
            s_ld.dma(sy.dma_start(kt[:], mk.dram["k1_d"][ht * 128:(ht + 1) * 128, :]))
            s_ld.dma(sy.dma_start(vT[:], mk.dram["vT1_d"][ht, :, :]))

        @block.tensor
        def _(t, ht=ht):
            t.wait_ge(s_ld.h, s_ld.n)
            s_ev.wait(t)
            s_sc.wait(t)
            for nb in range(64):
                for h2 in range(2):
                    t.matmul(ps_a[h2 * 64:(h2 + 1) * 64, nb * 4:(nb + 1) * 4],
                             kt[h2 * 64:(h2 + 1) * 64, nb * 64:(nb + 1) * 64],
                             l2q[h2 * 64:(h2 + 1) * 64,
                                 ht * 256 + nb * 4: ht * 256 + (nb + 1) * 4],
                             start=True, stop=True, skip_group_check=True)
            fence(mk, t, s_mm)

        @block.vector
        def _(v, ht=ht):
            s_mm.wait(v)
            s_ev.inc(v.tensor_scalar(rsc[:], ps_a[:, 0:256], 0.125, None,
                                     ALU.mult))

        for chb in range(2):
            @block.tensor
            def _(t, chb=chb):
                s_ev.wait(t)
                t.transpose(ps_Tf[:, 0:128], rsc[:, chb * 128:(chb + 1) * 128],
                            mk.c_identf[:])
                fence(mk, t, s_mm)

            @block.vector
            def _(v, chb=chb):
                s_mm.wait(v)

                def S():
                    s_ev.inc(v.drain())
                    v.wait_ge(s_ev.h, s_ev.n)
                v.tensor_copy(ex2[:, 128:256], ps_Tf[:, 0:128])
                S()
                vv = ex2[:, 128:256].rearrange("p (h s) -> p h s", s=64)
                v.tensor_reduce(mxs[:], vv, mybir.AxisListType.X, ALU.max)
                S()
                v.tensor_scalar(mx1[:, 0:2], mxs[:], -1.0, None, ALU.mult)
                S()
                bv = mx1[:, 0:2].unsqueeze(2).broadcast_to([128, 2, 64])
                v.tensor_tensor(ex2[:, 0:128].rearrange("p (h s) -> p h s", s=64),
                                vv, bv, ALU.add)
                S()

            @block.scalar
            def _(sc, chb=chb):
                s_ev.wait(sc)
                s_sc.inc(sc.activation(sc2[:, 0:128], ex2[:, 0:128], AF.Exp))

            @block.vector
            def _(v, chb=chb):
                s_sc.wait(v)

                def S():
                    s_ev.inc(v.drain())
                    v.wait_ge(s_ev.h, s_ev.n)
                ev_ = sc2[:, 0:128].rearrange("p (h s) -> p h s", s=64)
                v.tensor_reduce(mxs[:], ev_, mybir.AxisListType.X, ALU.add)
                S()
                v.reciprocal(sum1[:, 0:2], mxs[:])
                S()
                rv = sum1[:, 0:2].unsqueeze(2).broadcast_to([128, 2, 64])
                v.tensor_tensor(pr2[:, 0:128].rearrange("p (h s) -> p h s", s=64),
                                ev_, rv, ALU.mult)
                S()

            @block.tensor
            def _(t, chb=chb):
                s_ev.wait(t)
                t.transpose(ps_T[:, 128:256], pr2[:, 0:128], mk.c_ident[:])
                fence(mk, t, s_mm)

            @block.vector
            def _(v, chb=chb):
                s_mm.wait(v)
                s_ev.inc(v.tensor_copy(rpr[:, chb * 128:(chb + 1) * 128],
                                       ps_T[:, 128:256]))

        @block.tensor
        def _(t, ht=ht):
            s_ev.wait(t)
            t.matmul(ps_b[:, 0:256], mk.c_sw64[:], rpr[:, 0:256],
                     start=True, stop=True)
            fence(mk, t, s_mm)

        @block.vector
        def _(v, ht=ht):
            s_mm.wait(v)
            s_ev.inc(v.tensor_copy(rpw[:], ps_b[:, 0:256]))

        @block.tensor
        def _(t, ht=ht):
            s_ev.wait(t)
            for nb in range(64):
                par = nb % 2
                for h2 in range(2):
                    dst = ps_lmA if h2 == 0 else ps_lmB
                    srcpa = rpr if h2 == par else rpw
                    t.matmul(dst[:, nb * 4:(nb + 1) * 4],
                             vT[par * 64:(par + 1) * 64,
                                (nb // 2) * 128:(nb // 2 + 1) * 128],
                             srcpa[par * 64:(par + 1) * 64, nb * 4:(nb + 1) * 4],
                             start=True, stop=True, skip_group_check=True)
            fence(mk, t, s_mm)

        @block.vector
        def _(v, ht=ht):
            s_mm.wait(v)
            v.tensor_copy(ro[0:64, ht * 256:(ht + 1) * 256], ps_lmA[0:64, 0:256])
            s_ev.inc(v.tensor_copy(ro[64:128, ht * 256:(ht + 1) * 256],
                                   ps_lmB[64:128, 0:256]))

    for os_ in range(16):
        sbuf_i = os_ % 2

        @block.gpsimd
        def _(g, os_=os_, sbuf_i=sbuf_i):
            g.wait_ge(s_mm.h, s_mm.n)
            w = mk.dram["L1_attn_w_l2t_out_bf"][:, os_ * 128:(os_ + 1) * 128]
            s_ld.dma(g.dma_start(strip[sbuf_i][:].rearrange(
                "dd (kd ol) -> dd kd ol", ol=128),
                w.rearrange("(kd dd) ol -> dd kd ol", dd=128)))

        @block.tensor
        def _(t, os_=os_, sbuf_i=sbuf_i):
            t.wait_ge(s_ld.h, s_ld.n)
            s_ev.wait(t)
            for kd in range(16):
                t.matmul(ps_a[:, 0:256], strip[sbuf_i][:, kd * 128:(kd + 1) * 128],
                         ro[:, kd * 256:(kd + 1) * 256],
                         start=(kd == 0), stop=(kd == 15))
            fence(mk, t, s_mm)

        @block.vector
        def _(v, os_=os_):
            s_mm.wait(v)
            s_ev.inc(v.scalar_tensor_tensor(
                mk.lmfA[:, os_ * 256:(os_ + 1) * 256], ps_a[:, 0:256],
                mk.b1_l2to[:, os_:os_ + 1],
                mk.lmfB[:, os_ * 256:(os_ + 1) * 256], ALU.add, ALU.add))

    for os_ in range(32):
        sbuf_i = os_ % 2

        @block.gpsimd
        def _(g, os_=os_, sbuf_i=sbuf_i):
            g.wait_ge(s_mm.h, s_mm.n)
            w = mk.dram["L1_attn_w_lm_kv_bf"][:, os_ * 128:(os_ + 1) * 128]
            s_ld.dma(g.dma_start(strip[sbuf_i][:].rearrange(
                "dd (kd ol) -> dd kd ol", ol=128),
                w.rearrange("(kd dd) ol -> dd kd ol", dd=128)))

        @block.tensor
        def _(t, os_=os_, sbuf_i=sbuf_i):
            t.wait_ge(s_ld.h, s_ld.n)
            s_sc.wait(t)
            s_ev.wait(t)
            for kd in range(16):
                t.matmul(ps_a[:, 0:256], strip[sbuf_i][:, kd * 128:(kd + 1) * 128],
                         mk.lmfA[:, kd * 256:(kd + 1) * 256],
                         start=(kd == 0), stop=(kd == 15))
            fence(mk, t, s_mm)

        @block.scalar
        def _(sc, os_=os_):
            s_mm.wait(sc)
            dst = mk.gk_sb if os_ < 16 else mk.gv_sb
            b0 = os_ % 16
            s_sc.inc(sc.activation(dst[:, b0 * 256:(b0 + 1) * 256],
                                   ps_a[:, 0:256], AF.Identity,
                                   bias=mk.b1_lmkv[:, os_:os_ + 1]))

    if "lmf_d" in mk.dbg:
        @block.sync
        def _(sy):
            sy.wait_ge(s_sc.h, s_sc.n)
            sy.wait_ge(s_ev.h, s_ev.n)
            dd = mk.dram["lmf_d"]
            s_ld.dma(sy.dma_start(dd[0:128, :], mk.lmfA[:]))
            s_ld.dma(sy.dma_start(dd[128:256, :], mk.lmfB[:]))
            s_ld.dma(sy.dma_start(dd[256:384, :], mk.gk_sb[:]))
            s_ld.dma(sy.dma_start(dd[384:512, :], mk.gv_sb[:]))
            s_ld.dma(sy.dma_start(dd[512:640, :], lo[:]))
            s_ld.dma(sy.dma_start(dd[640:768, :], ro[:]))

    mk.l1lm_done = [s_sc, s_ev, s_mm, s_ld]
    ph.close()
    phase_end(mk, mk.l1lm_done)


def emit_l1_glob(mk, block, parts=None):
    nc = mk.nc
    ph = contextlib.ExitStack()
    sb = lambda n, s, d: ph.enter_context(nc.sbuf_tensor(n, list(s), d))
    phase_start(mk, block)

    gate_sb = sb("gl_gate", [32, SEQ], BF16)
    selht = sb("gl_selht", [32, 16 * 128], BF16)
    gmsk = sb("gl_gmsk", [128, 32 * 256], F32)
    xnf = sb("gl_xnf", [128, 16 * 512], BF16)
    qt = [sb(f"gl_qt{i}", [128, SEQ], BF16) for i in range(2)]
    locb = [sb(f"gl_locb{i}", [128, SEQ], BF16) for i in range(2)]
    gvT = [sb(f"gl_gvT{i}", [128, 256], BF16) for i in range(2)]
    gsc = sb("gl_gsc", [128, 1024], F32)
    ge = sb("gl_ge", [128, 1024], F32)
    ge2 = sb("gl_ge2", [128, 1024], F32)
    gp = [sb(f"gl_gp{i}", [128, 1024], BF16) for i in range(2)]
    gpT = [sb(f"gl_gpT{i}", [128, 1024], BF16) for i in range(2)]
    dsb = sb("gl_dsb", [128, 128], F32)
    msb = sb("gl_msb", [128, 128], F32)
    att = sb("gl_att", [128, 512], BF16)
    mx1 = sb("gl_mx1", [128, 4], F32)
    nmx = sb("gl_nmx", [128, 4], F32)
    sm1 = sb("gl_sm1", [128, 4], F32)
    rs1 = sb("gl_rs1", [128, 4], F32)
    wg_sb = sb("gl_wg", [128, 16 * 32], BF16)
    ps_s = [[ph.enter_context(nc.psum_tensor(f"gl_pss{i}{j}", [128, 512], F32))
             for j in range(2)] for i in range(2)]
    ps_T = ph.enter_context(nc.psum_tensor("gl_psT", [128, 1024], BF16))
    ps_og = ph.enter_context(nc.psum_tensor("gl_psog", [128, 512], F32))
    ps_gb = ph.enter_context(nc.psum_tensor("gl_psgb", [128, 512], F32))
    s_ld = mk.sem("gl_ld")
    s_mm = mk.sem("gl_mm")
    s_sc = mk.sem("gl_sc")
    s_ev = mk.sem("gl_ev")
    s_st = mk.sem("gl_st")

    @block.gpsimd
    def _(g):
        s_ld.dma(g.dma_start(selht[0:32, :], mk.dram["selht"][:, :]))

    @block.sync
    def _(sy):
        s_ld.dma(sy.dma_start(
            gmsk[:].rearrange("p (tt nl) -> p tt nl", nl=256),
            mk.dram["gmask"][:, :].rearrange("(tt p) nl -> p tt nl", p=128)))
        s_ld.dma(sy.dma_start(
            wg_sb[:].rearrange("p (kd o) -> p kd o", o=32),
            mk.dram["L1_attn_w_gate_bf"][:, :].rearrange("(kd p) o -> p kd o", p=128)))

    for nch in range(8):
        @block.sync
        def _(sy, nch=nch):
            sy.wait_ge(s_mm.h, s_mm.n)
            for kd in range(16):
                s_ld.dma(sy.dma_start(
                    xnf[:, kd * 512:(kd + 1) * 512],
                    mk.dram["xn1"][nch * 512:(nch + 1) * 512,
                                   kd * 128:(kd + 1) * 128], transpose=True))

        @block.tensor
        def _(t, nch=nch):
            t.wait_ge(s_ld.h, s_ld.n)
            s_sc.wait(t)
            for kd in range(16):
                t.matmul(ps_s[0][0][0:32, 0:512],
                         wg_sb[:, kd * 32:(kd + 1) * 32],
                         xnf[:, kd * 512:(kd + 1) * 512],
                         start=(kd == 0), stop=(kd == 15))
            fence(mk, t, s_mm)

        @block.scalar
        def _(sc, nch=nch):
            s_mm.wait(sc)
            s_sc.inc(sc.activation(gate_sb[0:32, nch * 512:(nch + 1) * 512],
                                   ps_s[0][0][0:32, 0:512], AF.Sigmoid,
                                   bias=mk.b1_gate[0:32, 0:1]))

    sc_fence = {}
    c1_done = {}
    e_done = {}
    p_done = {}
    tr_fence = {}
    gpT_done = {}
    og_fence = {}
    cmb_done = {}
    st_done = {}
    ld_mark = {}
    gvT_done = {}
    gvT_last = [0]

    def prefetch(ht):
        hb = ht % 2

        @block.sync
        def _(sy, ht=ht, hb=hb):
            gq = (ht - 2) * 16 + 15
            if gq in sc_fence:
                sy.wait_ge(s_mm.h, sc_fence[gq])
            if gq in cmb_done:
                sy.wait_ge(s_ev.h, cmb_done[gq])
            s_ld.dma(sy.dma_start(qt[hb][:], mk.dram["q1_d"][ht * 128:(ht + 1) * 128, :]))
            s_ld.dma(sy.dma_start(locb[hb][:], mk.dram["loc_d"][ht * 128:(ht + 1) * 128, :]))
            ld_mark[ht] = s_ld.n

    prefetch(0)
    prefetch(1)

    NG = 16 * 16
    for g in range(NG + 1):
        ht, ttp = g // 16, g % 16
        hb = ht % 2

        if g < NG and ttp == 1 and 2 <= ht + 1 < 16:
            prefetch(ht + 1)

        if g < NG:
            if ttp == 0:
                @block.tensor
                def _(t, ht=ht, hb=hb, g=g):
                    gq = (ht - 2) * 16 + 15
                    if gq in og_fence:
                        t.wait_ge(s_mm.h, og_fence[gq])
                    if g - 1 in gpT_done:
                        t.wait_ge(s_sc.h, gpT_done[g - 1])
                    s_sc.wait(t)
                    for kt_ in range(2):
                        t.transpose(ps_T[:, kt_ * 128:(kt_ + 1) * 128],
                                    mk.gv_sb[:, ht * 256 + kt_ * 128:
                                             ht * 256 + (kt_ + 1) * 128],
                                    mk.c_ident[:])
                    fence(mk, t, s_mm)
                    gvT_done[ht] = s_mm.n

                @block.scalar
                def _(sc, ht=ht, hb=hb):
                    sc.wait_ge(s_mm.h, gvT_done[ht])
                    sc.activation(gvT[hb][:], ps_T[:, 0:256], AF.Copy)
                    s_sc.inc(sc.drain())
                    gvT_done[ht] = ("sc", s_sc.n)
                    gvT_last[0] = s_sc.n

            @block.tensor
            def _(t, g=g, ht=ht, ttp=ttp, hb=hb):
                t.wait_ge(s_ld.h, ld_mark[ht])
                if g == 0:
                    s_sc.wait(t)
                if g - 2 in c1_done:
                    t.wait_ge(s_ev.h, c1_done[g - 2])
                first = True
                for tt2 in range(2):
                    tt = ttp * 2 + tt2
                    for h2 in range(2):
                        if not first:
                            t.drain()
                        first = False
                        t.matmul(ps_s[g % 2][tt2][:, h2 * 256:(h2 + 1) * 256],
                                 qt[hb][h2 * 64:(h2 + 1) * 64,
                                        tt * 128:(tt + 1) * 128],
                                 mk.gk_sb[h2 * 64:(h2 + 1) * 64,
                                          ht * 256:(ht + 1) * 256],
                                 start=True, stop=True, skip_group_check=True)
                fence(mk, t, s_mm)
                sc_fence[g] = s_mm.n

            @block.vector
            def _(v, g=g, ht=ht, ttp=ttp):
                v.wait_ge(s_mm.h, sc_fence[g])
                if g - 1 in e_done:
                    v.wait_ge(s_sc.h, e_done[g - 1])

                def S():
                    s_ev.inc(v.drain())
                    v.wait_ge(s_ev.h, s_ev.n)
                for tt2 in range(2):
                    tt = ttp * 2 + tt2
                    for h2 in range(2):
                        v.scalar_tensor_tensor(
                            gsc[:, (tt2 * 2 + h2) * 256:(tt2 * 2 + h2 + 1) * 256],
                            ps_s[g % 2][tt2][:, h2 * 256:(h2 + 1) * 256],
                            0.125, gmsk[:, tt * 256:(tt + 1) * 256],
                            ALU.mult, ALU.add)
                vv = gsc[:].rearrange("p (c k) -> p c k", k=256)
                v.tensor_reduce(mx1[:], vv, mybir.AxisListType.X, ALU.max)
                S()
                v.tensor_scalar(nmx[:], mx1[:], -1.0, None, ALU.mult)
                S()
                bv = nmx[:].unsqueeze(2).broadcast_to([128, 4, 256])
                s_ev.inc(v.tensor_tensor(ge[:].rearrange("p (c k) -> p c k", k=256),
                                         vv, bv, ALU.add))
                c1_done[g] = s_ev.n

            @block.scalar
            def _(sc, g=g):
                sc.wait_ge(s_ev.h, c1_done[g])
                if g - 1 in p_done:
                    sc.wait_ge(s_ev.h, p_done[g - 1])
                s_sc.inc(sc.activation(ge2[:], ge[:], AF.Exp))
                e_done[g] = s_sc.n

            @block.vector
            def _(v, g=g, ttp=ttp):
                v.wait_ge(s_sc.h, e_done[g])
                if g - 2 in tr_fence:
                    v.wait_ge(s_mm.h, tr_fence[g - 2])

                def S():
                    s_ev.inc(v.drain())
                    v.wait_ge(s_ev.h, s_ev.n)
                ev_ = ge2[:].rearrange("p (c k) -> p c k", k=256)
                v.tensor_reduce(sm1[:], ev_, mybir.AxisListType.X, ALU.add)
                S()
                v.reciprocal(rs1[:], sm1[:])
                S()
                rv = rs1[:].unsqueeze(2).broadcast_to([128, 4, 256])
                v.tensor_tensor(gp[g % 2][:].rearrange("p (c k) -> p c k", k=256),
                                ev_, rv, ALU.mult)
                if ttp == 0:
                    s_ev.inc(v.drain())
                    v.wait_ge(s_ev.h, s_ev.n)
                    s_ev.inc(v.memset(gp[g % 2][0:64, 0:512], 0.0))
                else:
                    s_ev.inc(v.drain())
                p_done[g] = s_ev.n

        p = g - 1
        if p < 0:
            continue
        pht, pttp = p // 16, p % 16
        phb = pht % 2

        gvl = gvT_last[0]

        @block.tensor
        def _(t, p=p, gvl=gvl):
            t.wait_ge(s_ev.h, p_done[p])
            if p - 1 in gpT_done:
                t.wait_ge(s_sc.h, gpT_done[p - 1])
            t.wait_ge(s_sc.h, gvl)
            for c in range(8):
                t.transpose(ps_T[:, c * 128:(c + 1) * 128],
                            gp[p % 2][:, c * 128:(c + 1) * 128], mk.c_ident[:])
            fence(mk, t, s_mm)
            tr_fence[p] = s_mm.n

        @block.scalar
        def _(sc, p=p):
            sc.wait_ge(s_mm.h, tr_fence[p])
            if p - 2 in og_fence:
                sc.wait_ge(s_mm.h, og_fence[p - 2])
            s_sc.inc(sc.activation(gpT[p % 2][:], ps_T[:, 0:1024], AF.Copy))
            gpT_done[p] = s_sc.n

        @block.tensor
        def _(t, p=p, pht=pht, pttp=pttp, phb=phb):
            t.wait_ge(s_sc.h, gpT_done[p])
            t.wait_ge(s_sc.h, gvT_done[pht][1])
            if p - 1 in cmb_done:
                t.wait_ge(s_ev.h, cmb_done[p - 1])
            for tt2 in range(2):
                for h2 in range(2):
                    for kt_ in range(2):
                        c = (tt2 * 2 + h2) * 2 + kt_
                        t.matmul(ps_og[h2 * 64:(h2 + 1) * 64,
                                       tt2 * 128:(tt2 + 1) * 128],
                                 gvT[phb][:, kt_ * 128 + h2 * 64:
                                       kt_ * 128 + (h2 + 1) * 64],
                                 gpT[p % 2][:, c * 128:(c + 1) * 128],
                                 start=(kt_ == 0), stop=(kt_ == 1),
                                 skip_group_check=True)
            t.matmul(ps_gb[:, 0:256], selht[0:32, pht * 128:(pht + 1) * 128],
                     gate_sb[0:32, pttp * 256:(pttp + 1) * 256],
                     start=True, stop=True, skip_group_check=True)
            fence(mk, t, s_mm)
            og_fence[p] = s_mm.n

        @block.vector
        def _(v, p=p, pht=pht, pttp=pttp, phb=phb):
            v.wait_ge(s_mm.h, og_fence[p])
            if p % 2 == 0 and p - 1 in st_done:
                v.wait_ge(s_st.h, st_done[p - 1])
            for tt2 in range(2):
                tt = pttp * 2 + tt2
                lsl = locb[phb][:, tt * 128:(tt + 1) * 128]
                v.tensor_tensor(dsb[:], ps_og[:, tt2 * 128:(tt2 + 1) * 128],
                                lsl, ALU.subtract)
                v.tensor_tensor(msb[:], dsb[:],
                                ps_gb[:, tt2 * 128:(tt2 + 1) * 128], ALU.mult)
                v.tensor_tensor(att[:, (p % 2) * 256 + tt2 * 128:
                                    (p % 2) * 256 + (tt2 + 1) * 128],
                                msb[:], lsl, ALU.add)
            s_ev.inc(v.drain())
            cmb_done[p] = s_ev.n

        if p % 2 == 1:
            @block.sync
            def _(sy, p=p, pht=pht, pttp=pttp):
                sy.wait_ge(s_ev.h, cmb_done[p])
                c0 = (pttp - 1) * 256
                s_st.dma(sy.dma_start(
                    mk.dram["att1_d"][pht * 128:(pht + 1) * 128, c0:c0 + 512],
                    att[:]))
                st_done[p] = s_st.n

    mk.l1glob_done = [s_sc, s_ev, s_mm, s_ld, s_st]
    ph.close()
    phase_end(mk, mk.l1glob_done)


def emit_w2prep(mk, block):
    nc = mk.nc
    W2p = mk.dtmp("W2p_bf", [D, 8192], BF16)
    ph = contextlib.ExitStack()
    sb = lambda n, s, d: ph.enter_context(nc.sbuf_tensor(n, list(s), d))
    phase_start(mk, block)
    bufA = sb("wp_bufA", [128, 8192], F32)
    bufB = sb("wp_bufB", [128, 8192], F32)
    obuf = sb("wp_obuf", [128, 8192], BF16)
    s_ld = mk.sem("wp_ld")
    s_ev = mk.sem("wp_ev")
    s_st = mk.sem("wp_st")

    for ot in range(16):
        @block.sync
        def _(sy, ot=ot):
            if ot >= 1:
                sy.wait_ge(s_ev.h, s_ev.n)
            s_ld.dma(sy.dma_start(bufA[:], mk.dram["L1_ffn_W2"][ot * 128:(ot + 1) * 128, :]))

        @block.vector
        def _(v, ot=ot):
            v.wait_ge(s_ld.h, s_ld.n)
            if ot >= 1:
                v.wait_ge(s_st.h, s_st.n)

            def S():
                s_ev.inc(v.drain())
                v.wait_ge(s_ev.h, s_ev.n)
            cur = bufA
            for st in range(6):
                Bs = 128 << st
                G = 8192 // (2 * Bs)
                dst = obuf if st == 5 else (bufB if cur is bufA else bufA)
                vi = cur[:].rearrange("p (g j b) -> p g j b", j=2, b=Bs)
                vo = dst[:].rearrange("p (g j b) -> p g j b", j=2, b=Bs)
                v.tensor_tensor(vo[:, :, 0, :], vi[:, :, 0, :], vi[:, :, 1, :],
                                ALU.add)
                S()
                v.tensor_tensor(vo[:, :, 1, :], vi[:, :, 0, :], vi[:, :, 1, :],
                                ALU.subtract)
                S()
                cur = dst

        @block.sync
        def _(sy, ot=ot):
            sy.wait_ge(s_ev.h, s_ev.n)
            s_st.dma(sy.dma_start(W2p[ot * 128:(ot + 1) * 128, :], obuf[:]))

    mk.w2p_done = [s_st, s_ev, s_ld]
    ph.close()
    phase_end(mk, mk.w2p_done)


def emit_l1_ffn(mk, block):
    nc = mk.nc
    hid_d = mk.dtmp("hid_d", [8192, SEQ], BF16)
    ph = contextlib.ExitStack()
    sb = lambda n, s, d: ph.enter_context(nc.sbuf_tensor(n, list(s), d))
    phase_start(mk, block)
    xnf = [sb(f"f_xnf{i}", [128, 16 * 512], BF16) for i in range(2)]
    w1s = [sb(f"f_w1s{i}", [128, 16 * 128], BF16) for i in range(2)]
    w3s = [sb(f"f_w3s{i}", [128, 16 * 128], BF16) for i in range(2)]
    sU = [sb(f"f_sU{i}", [128, 512], BF16) for i in range(2)]
    hidb = [sb(f"f_hid{i}", [128, 512], BF16) for i in range(2)]
    hid2 = [sb(f"f_hid2{i}", [128, 512], BF16) for i in range(2)]
    psU = [ph.enter_context(nc.psum_tensor(f"f_psU{i}", [128, 512], F32))
           for i in range(2)]
    psG = [ph.enter_context(nc.psum_tensor(f"f_psG{i}", [128, 512], F32))
           for i in range(2)]
    psH = [ph.enter_context(nc.psum_tensor(f"f_psH{i}", [128, 512], F32))
           for i in range(2)]
    s_ld = mk.sem("f_ld")
    s_ws = mk.sem("f_ws")
    s_mm = mk.sem("f_mm")
    s_sc = mk.sem("f_sc")
    s_ev = mk.sem("f_ev")
    s_st = mk.sem("f_st")
    ug_fence = {}
    h_fence = {}
    sil_done = {}
    tt_done = {}
    ev2_done = {}
    st_done = {}

    def emit_h128(t, it):
        b = it % 2
        t.wait_ge(s_ev.h, tt_done[it])
        if it - 2 in ev2_done:
            t.wait_ge(s_ev.h, ev2_done[it - 2])
        t.matmul(psH[b][:, :], mk.c_h128[:], hidb[b][:],
                 start=True, stop=True)
        fence(mk, t, s_mm)
        h_fence[it] = s_mm.n

    for ns in range(8):
        @block.sync
        def _(sy, ns=ns):
            xb = ns % 2
            gate = ns * 64 - 65
            if gate in ug_fence:
                sy.wait_ge(s_mm.h, ug_fence[gate])
            for kd in range(16):
                s_ld.dma(sy.dma_start(
                    xnf[xb][:, kd * 512:(kd + 1) * 512],
                    mk.dram["xn1"][ns * 512:(ns + 1) * 512,
                                   kd * 128:(kd + 1) * 128], transpose=True))

        for os_ in range(64):
            b = os_ % 2
            xb = ns % 2
            it = ns * 64 + os_

            @block.gpsimd
            def _(g, ns=ns, os_=os_, b=b, it=it):
                if it - 2 in ug_fence:
                    g.wait_ge(s_mm.h, ug_fence[it - 2])
                for nm, dstb in (("wsc_L1_ffn_W1", w1s), ("wsc_L1_ffn_W3", w3s)):
                    w = mk.dram[nm][os_, :, :, :].rearrange("kd dd ol -> dd kd ol")
                    s_ws.dma(g.dma_start(dstb[b][:].rearrange(
                        "dd (kd ol) -> dd kd ol", ol=128), w))

            @block.tensor
            def _(t, ns=ns, os_=os_, b=b, xb=xb, it=it):
                t.wait_ge(s_ws.h, s_ws.n)
                t.wait_ge(s_ld.h, s_ld.n)
                if it - 2 in sil_done:
                    t.wait_ge(s_sc.h, sil_done[it - 2])
                if it - 2 in tt_done:
                    t.wait_ge(s_ev.h, tt_done[it - 2])
                for kd in range(16):
                    t.matmul(psU[b][:, :], w1s[b][:, kd * 128:(kd + 1) * 128],
                             xnf[xb][:, kd * 512:(kd + 1) * 512],
                             start=(kd == 0), stop=(kd == 15))
                for kd in range(16):
                    t.matmul(psG[b][:, :], w3s[b][:, kd * 128:(kd + 1) * 128],
                             xnf[xb][:, kd * 512:(kd + 1) * 512],
                             start=(kd == 0), stop=(kd == 15))
                fence(mk, t, s_mm)
                ug_fence[it] = s_mm.n
                if it - 1 in tt_done:
                    emit_h128(t, it - 1)

            @block.scalar
            def _(sc, ns=ns, os_=os_, b=b, it=it):
                sc.wait_ge(s_mm.h, ug_fence[it])
                if it - 2 in tt_done:
                    sc.wait_ge(s_ev.h, tt_done[it - 2])
                s_sc.inc(sc.activation(sU[b][:], psU[b][:, :], AF.Silu))
                sil_done[it] = s_sc.n

            @block.vector
            def _(v, ns=ns, os_=os_, b=b, it=it):
                v.wait_ge(s_sc.h, sil_done[it])
                if it - 2 in h_fence:
                    v.wait_ge(s_mm.h, h_fence[it - 2])
                s_ev.inc(v.tensor_tensor(hidb[b][:], sU[b][:], psG[b][:, :],
                                         ALU.mult))
                tt_done[it] = s_ev.n

            @block.vector
            def _(v, ns=ns, os_=os_, it=it):
                pit = it - 1
                if pit not in tt_done or pit < 0:
                    return
                pb = pit % 2
                v.wait_ge(s_mm.h, h_fence.get(pit, 0))
                if pit - 2 in st_done:
                    v.wait_ge(s_st.h, st_done[pit - 2])
                s_ev.inc(v.tensor_copy(hid2[pb][:], psH[pb][:, :]))
                ev2_done[pit] = s_ev.n

            @block.sync
            def _(sy, ns=ns, os_=os_, it=it):
                pit = it - 1
                if pit not in ev2_done:
                    return
                pns, pos = pit // 64, pit % 64
                pb = pit % 2
                sy.wait_ge(s_ev.h, ev2_done[pit])
                s_st.dma(sy.dma_start(
                    mk.dram["hid_d"][pos * 128:(pos + 1) * 128,
                                     pns * 512:(pns + 1) * 512], hid2[pb][:]))
                st_done[pit] = s_st.n

    last = 8 * 64 - 1

    @block.tensor
    def _(t):
        emit_h128(t, last)

    @block.vector
    def _(v):
        pb = last % 2
        v.wait_ge(s_mm.h, h_fence[last])
        s_ev.inc(v.tensor_copy(hid2[pb][:], psH[pb][:, :]))
        ev2_done[last] = s_ev.n

    @block.sync
    def _(sy):
        pb = last % 2
        sy.wait_ge(s_ev.h, ev2_done[last])
        s_st.dma(sy.dma_start(
            mk.dram["hid_d"][63 * 128:64 * 128, 7 * 512:8 * 512], hid2[pb][:]))
        st_done[last] = s_st.n

    mk.ffn_done = [s_st, s_ev, s_sc, s_mm, s_ws, s_ld]
    ph.close()
    phase_end(mk, mk.ffn_done)


def emit_l1_ffn2(mk, block):
    nc = mk.nc
    g1_d = mk.dtmp("g1_d", [D, SEQ], BF16)
    ph = contextlib.ExitStack()
    sb = lambda n, s, d: ph.enter_context(nc.sbuf_tensor(n, list(s), d))
    phase_start(mk, block)
    hch = sb("f2_hch", [128, 64 * 512], BF16)
    w2s = [sb(f"f2_w2s{i}", [128, 64 * 128], BF16) for i in range(2)]
    g1b = [sb(f"f2_g1b{i}", [128, 512], BF16) for i in range(2)]
    psW = [ph.enter_context(nc.psum_tensor(f"f2_psW{i}", [128, 512], F32))
           for i in range(2)]
    s_ld = mk.sem("f2_ld")
    s_ws = mk.sem("f2_ws")
    s_mm = mk.sem("f2_mm")
    s_sc = mk.sem("f2_sc")
    s_st = mk.sem("f2_st")
    mm_fence = {}
    sil2_done = {}
    st2_done = {}

    for nch in range(8):
        @block.sync
        def _(sy, nch=nch):
            sy.wait_ge(s_mm.h, s_mm.n)
            s_ld.dma(sy.dma_start(
                hch[:].rearrange("p (kd j) -> p kd j", j=512),
                mk.dram["hid_d"][:, nch * 512:(nch + 1) * 512].rearrange(
                    "(kd p) j -> p kd j", p=128)))

        for os_ in range(16):
            b = os_ % 2
            it = nch * 16 + os_

            @block.gpsimd
            def _(g, nch=nch, os_=os_, b=b, it=it):
                if it - 2 in mm_fence:
                    g.wait_ge(s_mm.h, mm_fence[it - 2])
                w = mk.dram["wsc_W2p"][os_, :, :, :].rearrange("kd dd ol -> dd kd ol")
                s_ws.dma(g.dma_start(w2s[b][:].rearrange(
                    "dd (kd ol) -> dd kd ol", ol=128), w))

            @block.tensor
            def _(t, nch=nch, os_=os_, b=b, it=it):
                t.wait_ge(s_ws.h, s_ws.n)
                t.wait_ge(s_ld.h, s_ld.n)
                if it - 2 in sil2_done:
                    t.wait_ge(s_sc.h, sil2_done[it - 2])
                for kd in range(64):
                    t.matmul(psW[b][:, :], w2s[b][:, kd * 128:(kd + 1) * 128],
                             hch[:, kd * 512:(kd + 1) * 512],
                             start=(kd == 0), stop=(kd == 63))
                fence(mk, t, s_mm)
                mm_fence[it] = s_mm.n

            @block.scalar
            def _(sc, nch=nch, os_=os_, b=b, it=it):
                sc.wait_ge(s_mm.h, mm_fence[it])
                if it - 2 in st2_done:
                    sc.wait_ge(s_st.h, st2_done[it - 2])
                s_sc.inc(sc.activation(g1b[b][:], psW[b][:, :], AF.Silu))
                sil2_done[it] = s_sc.n

            @block.sync
            def _(sy, nch=nch, os_=os_, b=b, it=it):
                sy.wait_ge(s_sc.h, sil2_done[it])
                s_st.dma(sy.dma_start(
                    mk.dram["g1_d"][os_ * 128:(os_ + 1) * 128,
                                    nch * 512:(nch + 1) * 512], g1b[b][:]))
                st2_done[it] = s_st.n

    mk.ffn2_done = [s_st, s_sc, s_mm, s_ws, s_ld]
    ph.close()
    phase_end(mk, mk.ffn2_done)


def emit_l1_wo(mk, block):
    nc = mk.nc
    sg1_d = mk.dtmp("sg1_d", [D, SEQ], BF16)
    ph = contextlib.ExitStack()
    sb = lambda n, s, d: ph.enter_context(nc.sbuf_tensor(n, list(s), d))
    phase_start(mk, block)
    wot = sb("o_wot", [128, 16 * 2048], BF16)
    a1c = [sb(f"o_a1{i}", [128, 16 * 512], BF16) for i in range(2)]
    g1c = [sb(f"o_g1{i}", [128, 16 * 512], BF16) for i in range(2)]
    sgc = [sb(f"o_sg{i}", [128, 16 * 512], BF16) for i in range(2)]
    ps = [ph.enter_context(nc.psum_tensor(f"o_ps{i}", [128, 512], F32))
          for i in range(2)]
    s_ld = mk.sem("o_ld")
    s_mm = mk.sem("o_mm")
    s_ev = mk.sem("o_ev")
    s_st = mk.sem("o_st")

    @block.sync
    def _(sy):
        for kd in range(16):
            s_ld.dma(sy.dma_start(
                wot[:, kd * 2048:(kd + 1) * 2048],
                mk.dram["L1_attn_w_out_bf"][kd * 128:(kd + 1) * 128, 0:2048]))
    wot_ready = s_ld.n

    for nch in range(8):
        cb = nch % 2
        n0 = nch * 512

        @block.sync
        def _(sy, nch=nch, cb=cb, n0=n0):
            if nch >= 2:
                sy.wait_ge(s_st.h, 16 * (nch - 1))
                sy.wait_ge(s_mm.h, s_mm.n)
            src = mk.dram["att1_d"].rearrange("(h dd) n -> dd h n", dd=128)
            s_ld.dma(sy.dma_start(
                a1c[cb][:].rearrange("p (h t) -> p h t", h=16),
                src[:, :, n0:n0 + 512]))
            src2 = mk.dram["g1_d"].rearrange("(os dd) n -> dd os n", dd=128)
            s_ld.dma(sy.dma_start(
                g1c[cb][:].rearrange("p (os t) -> p os t", os=16),
                src2[:, :, n0:n0 + 512]))

        for os_ in range(16):
            @block.tensor
            def _(t, nch=nch, cb=cb, os_=os_):
                if nch == 0 and os_ == 0:
                    t.wait_ge(s_ld.h, wot_ready)
                t.wait_ge(s_ld.h, s_ld.n)
                pb = os_ % 2
                if not (nch == 0 and os_ <= 1):
                    t.wait_ge(s_ev.h, s_ev.n - 1)
                for kd in range(16):
                    t.matmul(ps[pb][:, :],
                             wot[:, kd * 2048 + os_ * 128: kd * 2048 + (os_ + 1) * 128],
                             a1c[cb][:, kd * 512:(kd + 1) * 512],
                             start=(kd == 0), stop=(kd == 15))
                fence(mk, t, s_mm)

            @block.vector
            def _(v, nch=nch, cb=cb, os_=os_):
                s_mm.wait(v)
                pb = os_ % 2
                s_ev.inc(v.scalar_tensor_tensor(
                    sgc[cb][:, os_ * 512:(os_ + 1) * 512], ps[pb][:, :],
                    mk.b1_out[:, os_:os_ + 1], g1c[cb][:, os_ * 512:(os_ + 1) * 512],
                    ALU.add, ALU.mult))

        @block.sync
        def _(sy, nch=nch, cb=cb, n0=n0):
            sy.wait_ge(s_ev.h, s_ev.n)
            dst = mk.dram["sg1_d"].rearrange("(os dd) n -> dd os n", dd=128)
            s_st.dma(sy.dma_start(
                dst[:, :, n0:n0 + 512],
                sgc[cb][:].rearrange("p (os t) -> p os t", os=16)))

    mk.wo_done = [s_st, s_ev, s_mm, s_ld]
    ph.close()
    phase_end(mk, mk.wo_done)


def build_program(dbg=(), stages=("prep", "ln0")):
    mk = MK(dbg=dbg)
    nc = mk.nc
    mk.din("x", [SEQ, D], F32)
    for n, shp in INPUT_SHAPES.items():
        if n != "L0_attn_hidden_states":
            mk.din(n, shp, F32)
    for n, arr in HOST_CONSTS.items():
        mk.din(n, list(arr.shape), F32)
    out = mk.dout("out", [SEQ, D], F32)

    mk.dtmp("xn0", [SEQ, D], BF16)
    mk.dtmp("qr_d", [24, 128, SEQ], BF16)
    mk.dtmp("Vt", [8, SEQ, 128], BF16)
    mk.dtmp("g0_d", [D, SEQ], BF16)
    mk.dtmp("f1q_d", [128, 8], F32)
    mk.dtmp("O2_d", [D, SEQ], BF16)
    mk.dtmp("sg_d", [D, SEQ], BF16)
    mk.dtmp("x1_d", [SEQ, D], F32)
    mk.dtmp("xn1", [SEQ, D], BF16)
    mk.dtmp("q1_d", [D, SEQ], BF16)
    mk.dtmp("k1_d", [D, SEQ], BF16)
    mk.dtmp("v1_d", [D, SEQ], BF16)
    mk.dtmp("loc_d", [D, SEQ], BF16)
    mk.dtmp("att1_d", [D, SEQ], BF16)
    mk.dtmp("vT1_d", [16, 128, SEQ], BF16)
    if "wfm_d" in mk.dbg:
        mk.dtmp("wfm_d", [8, SEQ], BF16)
    if "f1dbg_d" in mk.dbg:
        mk.dtmp("f1dbg_d", [128, 32], F32)
    if "f1all_d" in mk.dbg:
        mk.dtmp("f1all_d", [SEQ, 24], F32)
    if "xnf_d" in mk.dbg:
        mk.dtmp("xnf_d", [128, 16 * SEQ], BF16)
    if "axw_d" in mk.dbg:
        mk.dtmp("axw_d", [128, SEQ], BF16)
    if "ln_stats" in mk.dbg:
        mk.dtmp("ln_stats", [SEQ, 4], F32)
    if "lmf_d" in mk.dbg:
        mk.dtmp("lmf_d", [768, 4096], BF16)

    mk.c_eps = None

    mk.s_wprep = mk.sem("wprep")
    mk.s_const = mk.sem("constld")

    with nc.Block() as block:
        mk.c_eps = mk.sb("c_eps", [128, 1], F32)
        mk.zero_col = mk.sb("c_zero", [128, 1], F32)
        mk.s_eps = mk.sem("eps")

        @block.vector
        def _(v):
            v.memset(mk.zero_col[:], 0.0)
            mk.s_eps.inc(v.memset(mk.c_eps[:], 1e-5))

        if "prep" in stages:
            emit_weight_prep(mk, block)
        emit_const_load(mk, block)

        if "ln0" in stages:
            mk.ln_done = emit_layernorm(mk, block, "x", "xn0")
        if "wtrans" in stages:
            emit_wtrans(mk, block, ["L0_attn_Wq", "L0_attn_Wk", "L0_attn_Wv",
                                    "L0_attn_Wo", "L0_ffn_weight",
                                    "L1_ffn_W1", "L1_ffn_W3"])
        l0sp_st = [s for s in stages if s.startswith("l0sp")]
        if l0sp_st:
            spec = l0sp_st[0]
            parts = tuple(spec.split(":", 1)[1].split(",")) if ":" in spec else ("ld", "rw", "la", "bt")
            emit_l0_prep_small(mk, block, parts=parts)
        l0a_st = [s for s in stages if s.startswith("l0a")]
        if l0a_st:
            spec = l0a_st[0]
            parts = tuple(spec.split(":", 1)[1].split(",")) if ":" in spec else ("qk", "v", "f1", "f2", "f3")
            emit_l0_passA(mk, block, parts=parts)
        if "l0b" in stages:
            emit_l0_passB(mk, block)
        if "l0c" in stages:
            emit_l0_passC(mk, block)
        if "cmb0" in stages:
            emit_combine(mk, block, "x", "sg_d", "x1_d")
        if "ln1" in stages:
            mk.ln1_done = emit_layernorm(mk, block, "x1_d", "xn1")
        if "l1sp" in stages:
            emit_l1_sp(mk, block)
        if "l1qkv" in stages:
            emit_l1_qkv(mk, block)
        if "l1loc" in stages:
            emit_l1_loc(mk, block)
        if "l1lm" in stages:
            emit_l1_lm(mk, block)
        if "w2p" in stages:
            emit_w2prep(mk, block)
            emit_wtrans(mk, block, ["W2p"], tag="2")
        l1g_st = [s for s in stages if s.startswith("l1glob")]
        if l1g_st:
            spec = l1g_st[0]
            gparts = tuple(spec.split(":", 1)[1].split(",")) if ":" in spec else ("gate", "gv", "sc", "out", "gb", "cmb")
            emit_l1_glob(mk, block, parts=gparts)
        if hasattr(mk, "attn_es"):
            mk.attn_es.close()
        if "l1ffn" in stages:
            emit_l1_ffn(mk, block)
        if "l1ffn2" in stages:
            emit_l1_ffn2(mk, block)
        if "l1wo" in stages:
            emit_l1_wo(mk, block)
        if "cmb1" in stages:
            emit_combine(mk, block, "x1_d", "sg1_d", "out")

        s_fin = mk.sem("fin")

        @block.gpsimd
        def _(g):
            g.wait_ge(mk.s_wprep.h, mk.s_wprep.n)
            for s in mk.phase_sems:
                if s.n > 0:
                    g.wait_ge(s.h, s.n)
            if "cmb1" not in stages:
                z = mk.sb("fin_z", [128, D], F32)
                for t in range(NT):
                    s_fin.dma(g.dma_start(z[:], mk.dram["x"][t * 128:(t + 1) * 128, :]))
                    g.wait_ge(s_fin.h, s_fin.n)
                    s_fin.dma(g.dma_start(out[t * 128:(t + 1) * 128, :], z[:]))
                    g.wait_ge(s_fin.h, s_fin.n)
            else:
                z = mk.sb("fin_z", [128, D], F32)
                s_fin.dma(g.dma_start(z[:], out[0:128, :]))
                g.wait_ge(s_fin.h, s_fin.n)

    mk.es.close()
    return mk


INPUT_SHAPES = {
    "L0_attn_hidden_states": [N_CORES, SEQ, D],
    "L0_attn_Wq": [2048, 2048], "L0_attn_bq": [2048],
    "L0_attn_Wk": [1024, 2048], "L0_attn_bk": [1024],
    "L0_attn_Wv": [1024, 2048], "L0_attn_bv": [1024],
    "L0_attn_Wo": [2048, 2048], "L0_attn_bo": [2048],
    "L0_attn_rand_matrix": [64, 128],
    "L0_ffn_weight": [2048, 2048],
    "L0_ffn_lora_A": [8, 16, 2048], "L0_ffn_lora_B": [8, 2048, 16],
    "L0_ffn_router_w": [8, 2048],
    "L1_attn_w_qkv": [2048, 6144], "L1_attn_b_qkv": [6144],
    "L1_attn_pool_q": [4, 2048],
    "L1_attn_w_lm_qkv": [2048, 6144], "L1_attn_b_lm_qkv": [6144],
    "L1_attn_w_lm_out": [2048, 2048], "L1_attn_b_lm_out": [2048],
    "L1_attn_w_l2t_q": [2048, 2048], "L1_attn_b_l2t_q": [2048],
    "L1_attn_w_l2t_out": [2048, 2048], "L1_attn_b_l2t_out": [2048],
    "L1_attn_w_lm_kv": [2048, 4096], "L1_attn_b_lm_kv": [4096],
    "L1_attn_w_out": [2048, 2048], "L1_attn_b_out": [2048],
    "L1_attn_w_gate": [2048, 32], "L1_attn_b_gate": [32],
    "L1_ffn_W1": [8192, 2048], "L1_ffn_W2": [2048, 8192], "L1_ffn_W3": [8192, 2048],
}

HOST_CONSTS = host_constants()


ALL_STAGES = ("prep", "ln0", "wtrans", "l0sp", "l0a", "l0b", "l0c", "cmb0",
              "ln1", "l1sp", "l1qkv", "l1loc", "l1lm", "l1glob", "w2p",
              "l1ffn", "l1ffn2", "l1wo", "cmb1")


def kernel(**inputs):
    mk = build_program(dbg=(), stages=ALL_STAGES)
    in_maps = []
    x_full = np.ascontiguousarray(inputs["L0_attn_hidden_states"])
    base = {k: np.ascontiguousarray(v) for k, v in inputs.items()
            if k != "L0_attn_hidden_states"}
    base.update({k: v for k, v in HOST_CONSTS.items()})
    for c in range(N_CORES):
        m = dict(base)
        m["x"] = x_full[c]
        in_maps.append(m)
    res = run_bass_kernel_spmd(mk.nc, in_maps, core_ids=list(range(N_CORES)))
    return np.stack([r["out"] for r in res.results], axis=0)


# version 08
# speedup vs baseline: 1.1477x; 1.1477x over previous
import math
import contextlib
import numpy as np

import concourse.bass as bass
import concourse.mybir as mybir
from concourse.bass_utils import run_bass_kernel_spmd

F32 = mybir.dt.float32
BF16 = mybir.dt.bfloat16
AF = mybir.ActivationFunctionType
ALU = mybir.AluOpType

N_CORES = 8
SEQ = 4096
D = 2048
NT = SEQ // 128
ND = D // 128
MAGIC = 12582912.0
QK_LEVEL = 9
L0A_HALVES = 2
TWO_PI = 2.0 * math.pi


def host_constants():
    c = {}
    j = np.arange(64, dtype=np.float64)
    invf0 = 10000.0 ** (-2.0 * j / 128.0)
    ang0 = np.arange(SEQ, dtype=np.float64)[None, :] * np.concatenate([invf0, invf0])[:, None]
    c["rope0_cos"] = np.cos(ang0).astype(np.float32)
    c["rope0_sin"] = np.sin(ang0).astype(np.float32)
    p0 = np.zeros((128, 128), dtype=np.float32)
    for m in range(64):
        p0[64 + m, m] = -1.0
        p0[m, 64 + m] = 1.0
    c["rot0_perm"] = p0
    jj = np.arange(32, dtype=np.float64)
    invf1 = 10000.0 ** (-2.0 * jj / 64.0)
    p = np.arange(128)
    ang1 = np.arange(SEQ, dtype=np.float64)[None, :] * invf1[(p % 64) // 2][:, None]
    c["rope1_cos"] = np.cos(ang1).astype(np.float32)
    sgn = np.where(p % 2 == 0, -1.0, 1.0)[:, None]
    c["rope1_sin"] = (np.sin(ang1) * sgn).astype(np.float32)
    p1 = np.zeros((128, 128), dtype=np.float32)
    for m in range(128):
        p1[m ^ 1, m] = 1.0
    c["rot1_perm"] = p1
    c["ident"] = np.eye(128, dtype=np.float32)
    e16 = np.zeros((8, 128), dtype=np.float32)
    for e in range(8):
        e16[e, e * 16:(e + 1) * 16] = 2.0
    c["e16"] = e16
    c["ones"] = np.ones((128, 128), dtype=np.float32)
    dm = np.full((128, 1), MAGIC, dtype=np.float32)
    dm[64:, 0] += 0.25
    c["dmagic"] = dm
    b2 = np.zeros((128, 1), dtype=np.float32)
    b2[64:, 0] = math.pi / 2.0
    c["b2pi"] = b2
    c["ln2pi"] = np.full((128, 1), math.log(1.0 / TWO_PI), dtype=np.float32)
    h24 = np.zeros((128, 24 * 24), dtype=np.float32)
    for h in range(24):
        h24[:, h * 24 + h] = 1.0
    c["hot24"] = h24
    hr24 = np.zeros((24, 24 * 128), dtype=np.float32)
    for h in range(24):
        hr24[h, h * 128:(h + 1) * 128] = 1.0
    c["hotrow24"] = hr24
    tril = np.where(np.tril(np.ones((64, 64))) > 0, 0.0, -30000.0)
    cm = np.zeros((128, 512), dtype=np.float32)
    for j in range(8):
        cm[0:64, j * 64:(j + 1) * 64] = tril
        cm[64:128, j * 64:(j + 1) * 64] = tril
    c["cmask"] = cm
    blk = np.repeat(np.arange(64), 4)
    lmm = np.where(blk[:, None] >= blk[None, :], 0.0, -30000.0).astype(np.float32)
    c["lmmask"] = lmm
    tok_blk = np.repeat(np.arange(64), 64)
    gm = np.where(tok_blk[:, None] > blk[None, :], 0.0, -30000.0).astype(np.float32)
    c["gmask"] = gm
    h64 = np.zeros((128, 2), dtype=np.float32)
    h64[0:64, 0] = 1.0
    h64[64:128, 1] = 1.0
    c["hot64"] = h64
    c["hot64T"] = h64.T.copy()
    selht = np.zeros((32, 16 * 128), dtype=np.float32)
    for ht in range(16):
        for m in range(128):
            selht[ht * 2 + m // 64, ht * 128 + m] = 1.0
    c["selht"] = selht
    sw = np.zeros((128, 128), dtype=np.float32)
    for m in range(128):
        sw[m ^ 64, m] = 1.0
    c["swap64"] = sw
    H = np.array([[1.0]])
    for _ in range(7):
        H = np.block([[H, H], [H, -H]])
    c["h128"] = (H / np.sqrt(8192.0)).astype(np.float32)
    return c


class Sem:
    def __init__(self, h):
        self.h = h
        self.n = 0

    def inc(self, instr, step=1):
        instr.then_inc(self.h, step)
        self.n += step
        return instr

    def dma(self, instr):
        return self.inc(instr, 16)

    def wait(self, eng, n=None):
        eng.wait_ge(self.h, self.n if n is None else n)


class MK:
    def __init__(self, dbg=()):
        self.nc = bass.Bass("TRN2", target_bir_lowering=False, debug=False)
        self.es = contextlib.ExitStack()
        self.dbg = set(dbg)
        self.dram = {}
        self._sb_cache = {}
        self.phase_sems = []

    def din(self, name, shape, dt=F32):
        ap = self.nc.dram_tensor(name, list(shape), dt, kind="ExternalInput").ap()
        self.dram[name] = ap
        return ap

    def dout(self, name, shape, dt=F32):
        ap = self.nc.dram_tensor(name, list(shape), dt, kind="ExternalOutput").ap()
        self.dram[name] = ap
        return ap

    def dtmp(self, name, shape, dt):
        kind = "ExternalOutput" if name in self.dbg else "Internal"
        ap = self.nc.dram_tensor(name, list(shape), dt, kind=kind).ap()
        self.dram[name] = ap
        return ap

    def sb(self, name, shape, dt):
        t = self.es.enter_context(self.nc.sbuf_tensor(name, list(shape), dt))
        return t

    def ps(self, name, shape, dt=F32):
        return self.es.enter_context(self.nc.psum_tensor(name, list(shape), dt))

    def sem(self, name):
        return Sem(self.es.enter_context(self.nc.semaphore(name)))


def bf16(x):
    import ml_dtypes
    return x.astype(ml_dtypes.bfloat16)



def emit_weight_prep(mk, block):
    names = [
        "L0_attn_Wq", "L0_attn_Wk", "L0_attn_Wv", "L0_attn_Wo",
        "L0_attn_rand_matrix",
        "L0_ffn_weight", "L0_ffn_router_w", "L0_ffn_lora_A", "L0_ffn_lora_B",
        "L1_attn_w_qkv", "L1_attn_pool_q", "L1_attn_w_lm_qkv",
        "L1_attn_w_lm_out", "L1_attn_w_l2t_q", "L1_attn_w_l2t_out",
        "L1_attn_w_lm_kv", "L1_attn_w_out", "L1_attn_w_gate",
        "L1_ffn_W1", "L1_ffn_W3",
    ]
    s = mk.s_wprep
    for n in names:
        src = mk.dram[n]
        shp = list(src.tensor.shape)
        flat = int(np.prod(shp))
        dst = mk.dtmp(n + "_bf", shp, BF16)
        mk._wprep_pairs = getattr(mk, "_wprep_pairs", []) + [(dst, src)]

    @block.gpsimd
    def _(g):
        for dst, src in mk._wprep_pairs:
            idx = tuple(slice(None) for _ in dst.tensor.shape)
            s.dma(g.dma_start(dst[idx], src[idx]))


def emit_const_load(mk, block):
    s = mk.s_const
    nc = mk.nc
    mk.c_ident = mk.sb("c_ident", [128, 128], BF16)
    mk.c_identf = mk.sb("c_identf", [128, 128], F32)
    mk.c_rot0 = mk.sb("c_rot0", [128, 128], BF16)
    mk.c_rot1 = mk.sb("c_rot1", [128, 128], BF16)
    mk.c_e16 = mk.sb("c_e16", [8, 128], BF16)
    mk.c_ones = mk.sb("c_ones", [128, 128], BF16)

    @block.gpsimd
    def _(g):
        s.dma(g.dma_start(mk.c_ident[:], mk.dram["ident"][:, :]))
        s.dma(g.dma_start(mk.c_identf[:], mk.dram["ident"][:, :]))
        s.dma(g.dma_start(mk.c_rot0[:], mk.dram["rot0_perm"][:, :]))
        s.dma(g.dma_start(mk.c_rot1[:], mk.dram["rot1_perm"][:, :]))
        s.dma(g.dma_start(mk.c_e16[:], mk.dram["e16"][:, :]))
        s.dma(g.dma_start(mk.c_ones[:], mk.dram["ones"][:, :]))


def emit_layernorm(mk, block, src_name, dst_name, s_src_ready=None):
    src = mk.dram[src_name]
    dst = mk.dram[dst_name]
    tag = dst_name
    ph = contextlib.ExitStack()
    sbl = lambda n, s, d: ph.enter_context(mk.nc.sbuf_tensor(n, list(s), d))
    xb = [sbl(f"ln_x{i}_{tag}", [128, D], F32) for i in range(2)]
    onb = [sbl(f"ln_o{i}_{tag}", [128, D], BF16) for i in range(2)]
    sq = sbl(f"ln_sq_{tag}", [128, D], F32)
    vdump = sbl(f"ln_vd_{tag}", [128, D], BF16)
    ssum = sbl(f"ln_sum_{tag}", [128, 1], F32)
    ssq = sbl(f"ln_ssq_{tag}", [128, 1], F32)
    nmean = sbl(f"ln_nm_{tag}", [128, 1], F32)
    m2 = sbl(f"ln_m2_{tag}", [128, 1], F32)
    var = sbl(f"ln_var_{tag}", [128, 1], F32)
    rstd = sbl(f"ln_rstd_{tag}", [128, 1], F32)
    s_ld = mk.sem(f"ln_ld_{tag}")
    s_sq = mk.sem(f"ln_sq_{tag}")
    s_v = mk.sem(f"ln_v_{tag}")
    s_st = mk.sem(f"ln_st_{tag}")
    phase_start(mk, block)

    for t in range(NT):
        b = t % 2

        @block.sync
        def _(sy, t=t, b=b):
            if t >= 2:
                s_v.wait(sy, 16 * (t - 1))
            if s_src_ready is not None and t == 0:
                s_src_ready.wait(sy)
            s_ld.dma(sy.dma_start(xb[b][:], src[t * 128:(t + 1) * 128, :]))

        @block.scalar
        def _(sc, t=t, b=b):
            if t == 0:
                mk.s_eps.wait(sc)
            sc.wait_ge(s_ld.h, 16 * (t + 1))
            if t >= 1:
                sc.wait_ge(s_v.h, 16 * (t - 1) + 3)
            s_sq.inc(sc.activation(sq[:], xb[b][:], AF.Square))

        @block.vector
        def _(v, t=t, b=b):
            v.wait_ge(s_ld.h, 16 * (t + 1))

            def S(ins, step=1):
                s_v.inc(v.drain(), step)
                v.wait_ge(s_v.h, s_v.n)
            v.tensor_scalar(vdump[:], xb[b][:], 1.0, None, ALU.mult,
                            ALU.add, accum_out=ssum[:])
            S(None)
            v.tensor_scalar(nmean[:], ssum[:], -1.0 / D, None, ALU.mult)
            S(None)
            v.wait_ge(s_sq.h, s_sq.n)
            v.tensor_scalar(vdump[:], sq[:], 1.0, None, ALU.mult,
                            ALU.add, accum_out=ssq[:])
            S(None)
            v.scalar_tensor_tensor(m2[:], nmean[:], 1.0, nmean[:],
                                   ALU.mult, ALU.mult)
            S(None)
            v.scalar_tensor_tensor(var[:], ssq[:], 1.0 / D, m2[:],
                                   ALU.mult, ALU.subtract)
            S(None)

        @block.scalar
        def _(sc, t=t, b=b):
            sc.wait_ge(s_v.h, s_v.n)
            sc.activation(rstd[:], var[:], AF.Ln, bias=mk.c_eps[:])
            sc.drain()
            sc.activation(rstd[:], rstd[:], AF.Exp, scale=-0.5)
            s_sq.inc(sc.drain())

        @block.vector
        def _(v, t=t, b=b):
            v.wait_ge(s_sq.h, s_sq.n)
            if t >= 2:
                v.wait_ge(s_st.h, 16 * (t - 1))
            s_v.inc(v.tensor_scalar(onb[b][:], xb[b][:], nmean[:], rstd[:],
                                    ALU.add, ALU.mult), 11)

        @block.sync
        def _(sy, t=t, b=b):
            sy.wait_ge(s_v.h, s_v.n)
            s_st.dma(sy.dma_start(dst[t * 128:(t + 1) * 128, :], onb[b][:]))
            if "ln_stats" in mk.dbg:
                dbgd = mk.dram["ln_stats"]
                for ci, tile in enumerate([ssum, ssq, var, rstd]):
                    s_st.dma(sy.dma_start(dbgd[t * 128:(t + 1) * 128, ci:ci + 1],
                                          tile[:], allow_slow_non_contiguous=True))

    ph.close()
    phase_end(mk, [s_st, s_v, s_sq, s_ld])
    return s_st


def barrier(mk, block, sems, engines=("tensor", "vector", "scalar", "sync", "gpsimd")):
    for eng in engines:
        def body(e, sems=tuple(sems)):
            for s in sems:
                if s.n > 0:
                    e.wait_ge(s.h, s.n)
        getattr(block, eng)(body)


def fence(mk, t, sem, step=1):
    sem.inc(t.drain(), step)


def phase_start(mk, block):
    barrier(mk, block, list(mk.phase_sems))


def phase_end(mk, sems):
    mk.phase_sems.extend(sems)


def emit_wtrans(mk, block, names, tag=""):
    s_ld = mk.sem("wt_ld" + tag)
    s_st = mk.sem("wt_st" + tag)
    phase_start(mk, block)
    ph = contextlib.ExitStack()
    stg = [ph.enter_context(mk.nc.sbuf_tensor(f"wt_stage{i}{tag}", [128, 8192], BF16))
           for i in range(2)]
    plan = []
    for n in names:
        src = mk.dram[n + "_bf"]
        O, D2 = src.tensor.shape
        dst = mk.dtmp("wsc_" + n, [O // 128, D2 // 128, 128, 128], BF16)
        plan.append((src, dst, O, D2))
    mk._wt_plan = plan

    i = 0
    for src, dst, O, D2 in plan:
        for kd in range(D2 // 128):
            b = i % 2

            @block.sync
            def _(sy, i=i, b=b, src=src, kd=kd, O=O):
                if i >= 2:
                    sy.wait_ge(s_st.h, 16 * (i - 1))
                mk.s_wprep.wait(sy)
                s_ld.dma(sy.dma_start(stg[b][:, 0:O],
                                      src[0:O, kd * 128:(kd + 1) * 128],
                                      transpose=True))

            @block.scalar
            def _(sc, i=i, b=b, dst=dst, kd=kd, O=O):
                sc.wait_ge(s_ld.h, 16 * (i + 1))
                v = dst[:, kd, :, :].rearrange("os dd ol -> dd os ol")
                s_st.dma(sc.dma_start(v, stg[b][:, 0:O].rearrange(
                    "dd (os ol) -> dd os ol", ol=128)))
            i += 1
    ph.close()
    phase_end(mk, [s_st, s_ld])
    mk.wt_done = s_st


def emit_l0_prep_small(mk, block, parts=("ld", "rw", "la", "bt")):
    nc = mk.nc
    s = mk.sem("l0sp")
    mk.s_l0sp = s
    mk.rwT = mk.sb("rwT", [128, 16 * 8], BF16)
    mk.laT = mk.sb("laT", [128, 16 * 128], BF16)
    mk.BT = mk.sb("BT", [128, 2048], BF16)
    mk.RRT = mk.sb("RRT", [128, 128], BF16)
    mk.b_qkv = mk.sb("b_qkv", [128, 32], F32)
    mk.b_o = mk.sb("b_o", [128, 16], F32)
    mk.dmagic = mk.sb("c_dmagic", [128, 1], F32)
    mk.b2pi = mk.sb("c_b2pi", [128, 1], F32)
    mk.ln2pi = mk.sb("c_ln2pi", [128, 1], F32)
    mk.hot24 = mk.sb("c_hot24", [128, 24 * 24], BF16)
    mk.hotrow24 = mk.sb("c_hotrow24", [24, 24 * 128], F32)
    phase_start(mk, block)
    ph = contextlib.ExitStack()
    tmp8 = ph.enter_context(mk.nc.sbuf_tensor("l0sp_tmp8", [128, 2048], BF16))
    BT_d = mk.dtmp("BT_d", [128, 2048], BF16)
    ps_t = ph.enter_context(mk.nc.psum_tensor("l0sp_ps", [128, 1024], BF16))

    @block.gpsimd
    def _(g):
        s.dma(g.dma_start(mk.hot24[:], mk.dram["hot24"][:, :]))

    @block.sync
    def _(sy):
        mk.s_wprep.wait(sy)
        s.dma(sy.dma_start(mk.dmagic[:], mk.dram["dmagic"][:, :]))
        s.dma(sy.dma_start(mk.b2pi[:], mk.dram["b2pi"][:, :]))
        s.dma(sy.dma_start(mk.ln2pi[:], mk.dram["ln2pi"][:, :]))
        s.dma(sy.dma_start(mk.hotrow24[:], mk.dram["hotrow24"][:, :]))
        s.dma(sy.dma_start(mk.RRT[:, 0:64],
                           mk.dram["L0_attn_rand_matrix_bf"][:, :], transpose=True))
        s.dma(sy.dma_start(mk.RRT[:, 64:128],
                           mk.dram["L0_attn_rand_matrix_bf"][:, :], transpose=True))
        for c, (nm, nstr) in enumerate([("L0_attn_bq", 16), ("L0_attn_bk", 8),
                                        ("L0_attn_bv", 8)]):
            base = [0, 16, 24][c]
            v = mk.dram[nm].rearrange("(s p) -> p s", p=128)
            s.dma(sy.dma_start(mk.b_qkv[:, base:base + nstr], v,
                               allow_slow_non_contiguous=True))
        v = mk.dram["L0_attn_bo"].rearrange("(s p) -> p s", p=128)
        s.dma(sy.dma_start(mk.b_o[:, :], v, allow_slow_non_contiguous=True))
        s.dma(sy.dma_start(tmp8[0:8, :], mk.dram["L0_ffn_router_w_bf"][:, :]))

    s_mm = mk.sem("l0sp_mm")
    for kd in (range(16) if "rw" in parts else []):
        @block.tensor
        def _(t, kd=kd):
            if kd == 0:
                s.wait(t)
                mk.s_const.wait(t)
            if kd > 0:
                s_mm.wait(t)
            t.transpose(ps_t[:, 0:8], tmp8[0:8, kd * 128:(kd + 1) * 128],
                        mk.c_ident[0:8, 0:8])
            fence(mk, t, s_mm)

        @block.vector
        def _(v, kd=kd):
            v.wait_ge(s_mm.h, s_mm.n)
            s_mm.inc(v.tensor_copy(mk.rwT[:, kd * 8:(kd + 1) * 8], ps_t[:, 0:8]))

    if "la" in parts:
        @block.sync
        def _(sy):
            sy.wait_ge(s_mm.h, s_mm.n)
            la = mk.dram["L0_ffn_lora_A_bf"].rearrange("e r d -> (e r) d")
            s.dma(sy.dma_start(tmp8[:, :], la))

    for kd in (range(16) if "la" in parts else []):
        @block.tensor
        def _(t, kd=kd):
            if kd == 0:
                s.wait(t)
            s_mm.wait(t)
            t.transpose(ps_t[:, 0:128], tmp8[:, kd * 128:(kd + 1) * 128],
                        mk.c_ident[:])
            fence(mk, t, s_mm)

        @block.vector
        def _(v, kd=kd):
            v.wait_ge(s_mm.h, s_mm.n)
            s_mm.inc(v.tensor_copy(mk.laT[:, kd * 128:(kd + 1) * 128], ps_t[:, 0:128]))

    s_bt = mk.sem("l0sp_bt")
    bt_tile = ph.enter_context(mk.nc.sbuf_tensor("bt_tile", [128, 16 * 16], BF16))
    bt_ev = ph.enter_context(mk.nc.sbuf_tensor("bt_ev", [16, 16 * 128], BF16))
    for e in (range(8) if "bt" in parts else []):
        @block.sync
        def _(sy, e=e):
            if e > 0:
                s_bt.wait(sy)
            lb = mk.dram["L0_ffn_lora_B_bf"][e, :, :].rearrange(
                "(oc p) r -> p oc r", p=128)
            s_bt.dma(sy.dma_start(
                bt_tile[:].rearrange("p (oc r) -> p oc r", r=16), lb))

        for oc in range(16):
            @block.tensor
            def _(t, e=e, oc=oc):
                t.wait_ge(s_bt.h, s_bt.n)
                s_mm.wait(t)
                t.transpose(ps_t[0:16, 0:128], bt_tile[:, oc * 16:(oc + 1) * 16],
                            mk.c_ident[:])
                fence(mk, t, s_mm)

            @block.vector
            def _(v, e=e, oc=oc):
                v.wait_ge(s_mm.h, s_mm.n)
                s_mm.inc(v.tensor_copy(bt_ev[0:16, oc * 128:(oc + 1) * 128],
                                       ps_t[0:16, 0:128]))

        @block.sync
        def _(sy, e=e):
            sy.wait_ge(s_mm.h, s_mm.n)
            s_bt.dma(sy.dma_start(BT_d[e * 16:(e + 1) * 16, :], bt_ev[0:16, :]))

    if "bt" in parts:
        @block.sync
        def _(sy):
            s_bt.wait(sy)
            s.dma(sy.dma_start(mk.BT[:, :], BT_d[:, :]))
    if "l0sp_dump" in mk.dbg:
        dmp = mk.dtmp("l0sp_dump", [128, 128 + 2048 + 2048 + 128], BF16)
        @block.sync
        def _(sy):
            s.wait(sy)
            sy.wait_ge(s_mm.h, s_mm.n)
            sy.wait_ge(s_bt.h, s_bt.n)
            s.dma(sy.dma_start(dmp[:, 0:128], mk.rwT[:, :]))
            s.dma(sy.dma_start(dmp[:, 128:2176], mk.laT[:, :]))
            s.dma(sy.dma_start(dmp[:, 2176:4224], mk.BT[:, :]))
            s.dma(sy.dma_start(dmp[:, 4224:4352], mk.RRT[:, :]))
    ph.close()
    phase_end(mk, [s, s_mm, s_bt])
    mk.l0sp_done = s


def emit_l0_passA(mk, block, parts=("qk", "v", "f1", "f2", "f3")):
    nc = mk.nc
    ph = contextlib.ExitStack()
    sb = lambda n, s, d: ph.enter_context(nc.sbuf_tensor(n, list(s), d))
    ps = lambda n, s, d=F32: ph.enter_context(nc.psum_tensor(n, list(s), d))
    qr_d = mk.dram["qr_d"]
    Vt = mk.dram["Vt"]
    g0_d = mk.dram["g0_d"]

    phase_start(mk, block)
    barrier(mk, block, [mk.s_const, mk.s_wprep])

    HN = SEQ // 2
    NS = HN // 512

    xnf = sb("a_xnf", [128, 16 * HN], BF16)
    ropeC = sb("a_ropeC", [128, HN], F32)
    ropeS = sb("a_ropeS", [128, HN], F32)
    strip = [sb(f"a_strip{i}", [128, 16 * 128], BF16) for i in range(2)]
    qsb = [sb(f"a_qsb{i}", [128, 512], BF16) for i in range(2)]
    t1 = [sb(f"a_t1{i}", [128, 512], F32) for i in range(2)]
    t2 = [sb(f"a_t2{i}", [128, 512], F32) for i in range(2)]
    qrb = [sb(f"a_qrb{i}", [128, 512], BF16) for i in range(2)]
    vsb = sb("a_vsb", [128, 512], BF16)
    vtb = [sb(f"a_vtb{i}", [128, 128], BF16) for i in range(2)]
    wfm = sb("a_wfm", [8, HN], BF16)
    Axw = sb("a_axw", [128, HN], BF16)
    wbsb = [sb(f"a_wb{i}", [128, 512], BF16) for i in range(2)]
    gsb = [sb(f"a_gsb{i}", [128, 512], BF16) for i in range(2)]
    ps_p = [ps(f"a_psp{i}", [128, 512]) for i in range(2)]
    ps_r = [ps(f"a_psr{i}", [128, 512]) for i in range(2)]
    ps_t = ph.enter_context(nc.psum_tensor("a_pst", [128, 1024], BF16))
    ps_l = ph.enter_context(nc.psum_tensor("a_psl", [128, 512], F32))
    ps_w = ph.enter_context(nc.psum_tensor("a_psw", [8, 512], F32))
    esb = sb("a_esb", [128, 8], F32)
    edump = sb("a_edump", [128, 8], BF16)
    esum = sb("a_esum", [128, 1], F32)
    mx = sb("a_mx", [128, 1], F32)
    nmx = sb("a_nmx", [128, 1], F32)
    resum = sb("a_resum", [128, 1], F32)
    probs = sb("a_probs", [128, 8], F32)
    v1c = sb("a_v1", [128, 1], F32)
    v2c = sb("a_v2", [128, 1], F32)
    m1 = sb("a_m1", [128, 8], F32)
    m2 = sb("a_m2", [128, 8], F32)
    p2 = sb("a_p2", [128, 8], F32)
    tdum = sb("a_tdum", [128, 8], F32)
    mm12 = sb("a_mm12", [128, 8], F32)
    s12 = sb("a_s12", [128, 1], F32)
    rs12 = sb("a_rs12", [128, 1], F32)
    wdt = sb("a_wdt", [128, 8], F32)
    f1dbg = sb("a_f1dbg", [128, 32], F32) if ("f1dbg_d" in mk.dbg or "f1all_d" in mk.dbg) else None

    s_zz = mk.sem("a_zz")

    @block.vector
    def _(v):
        for tile in [esb, edump, esum, mx, nmx, resum, probs, v1c, v2c,
                     m1, m2, p2, tdum, mm12, s12, rs12, wdt]:
            v.memset(tile[:], 0.0)
        v.memset(ps_l[:, :], 0.0)
        v.memset(ps_w[:, :], 0.0)
        s_zz.inc(v.drain())

    barrier(mk, block, [s_zz])

    s_ld = mk.sem("a_ld")
    s_mm = mk.sem("a_mm")
    s_sc = mk.sem("a_sc")
    s_ev = mk.sem("a_ev")
    s_st = mk.sem("a_st")

    for half in range(L0A_HALVES):
        n0 = half * HN
        ld0, mm0, sc0, ev0, st0 = s_ld.n, s_mm.n, s_sc.n, s_ev.n, s_st.n

        if half == 1:
            barrier(mk, block, [s_ld, s_mm, s_sc, s_ev, s_st])

        @block.sync
        def _(sy, half=half, n0=n0, mm0=mm0):
            if half == 1:
                sy.wait_ge(s_mm.h, mm0)
            for kd in range(16):
                s_ld.dma(sy.dma_start(
                    xnf[:, kd * HN:(kd + 1) * HN],
                    mk.dram["xn0"][n0:n0 + HN, kd * 128:(kd + 1) * 128],
                    transpose=True))
            s_ld.dma(sy.dma_start(ropeC[:], mk.dram["rope0_cos"][:, n0:n0 + HN]))
            s_ld.dma(sy.dma_start(ropeS[:], mk.dram["rope0_sin"][:, n0:n0 + HN]))
        xn_ready = s_ld.n

        if "xnf_d" in mk.dbg:
            @block.sync
            def _(sy, half=half, n0=n0):
                sy.wait_ge(s_ld.h, xn_ready)
                s_st.dma(sy.dma_start(mk.dram["xnf_d"][:, n0 * 16:(n0 + HN) * 16],
                                      xnf[:]))

        for h in (range(24) if "qk" in parts else []):
            wname = "wsc_L0_attn_Wq" if h < 16 else "wsc_L0_attn_Wk"
            os_ = h if h < 16 else h - 16
            sbuf_i = h % 2

            @block.gpsimd
            def _(g, h=h, sbuf_i=sbuf_i, wname=wname, os_=os_, mm0=mm0):
                if h >= 2:
                    g.wait_ge(s_mm.h, mm0 + (h - 1) * NS * 2)
                elif h < 2:
                    g.wait_ge(s_mm.h, mm0)
                w = mk.dram[wname][os_, :, :, :].rearrange("kd dd ol -> dd kd ol")
                s_ld.dma(g.dma_start(strip[sbuf_i][:].rearrange(
                    "dd (kd ol) -> dd kd ol", ol=128), w))

            for ns in range(NS):
                it = h * NS + ns
                b = it % 2

                @block.tensor
                def _(t, h=h, ns=ns, b=b, it=it, sbuf_i=sbuf_i,
                      xn_ready=xn_ready, sc0=sc0, ev0=ev0):
                    if ns == 0:
                        t.wait_ge(s_ld.h, xn_ready + 16 * (h + 1))
                    if it >= 2:
                        t.wait_ge(s_sc.h, sc0 + it - 1)
                    ins = None
                    if QK_LEVEL >= 1:
                        for kd in range(16):
                            ins = t.matmul(ps_p[b][:, :],
                                     strip[sbuf_i][:, kd * 128:(kd + 1) * 128],
                                     xnf[:, kd * HN + ns * 512: kd * HN + (ns + 1) * 512],
                                     start=(kd == 0), stop=(kd == 15))
                        fence(mk, t, s_mm)
                    else:
                        t.matmul(ps_p[b][:, :], strip[sbuf_i][:, 0:128],
                                 xnf[:, 0:512], start=True, stop=True)
                        fence(mk, t, s_mm)

                @block.scalar
                def _(sc, it=it, b=b, h=h, mm0=mm0, ev0=ev0):
                    sc.wait_ge(s_mm.h, mm0 + 2 * it + 1)
                    if it >= 2:
                        sc.wait_ge(s_ev.h, ev0 + it - 1)
                    if QK_LEVEL >= 1:
                        s_sc.inc(sc.activation(qsb[b][:], ps_p[b][:, :], AF.Identity,
                                               bias=mk.b_qkv[:, h:h + 1]))
                    else:
                        s_sc.inc(sc.activation(qsb[b][:], qsb[b][:], AF.Silu))

                @block.tensor
                def _(t, it=it, b=b, sc0=sc0, ev0=ev0):
                    t.wait_ge(s_sc.h, sc0 + it + 1)
                    if it >= 2 and QK_LEVEL >= 3:
                        t.wait_ge(s_ev.h, ev0 + it - 1)
                    t.matmul(ps_r[b][:, :], mk.c_rot0[:], qsb[b][:],
                             start=True, stop=True)
                    fence(mk, t, s_mm)

                @block.vector
                def _(v, it=it, b=b, ns=ns, mm0=mm0, st0=st0):
                    v.wait_ge(s_mm.h, mm0 + 2 * it + 2)
                    if it >= 2:
                        v.wait_ge(s_st.h, st0 + 16 * (it - 1))
                    cs = slice(ns * 512, (ns + 1) * 512)
                    if QK_LEVEL >= 3:
                        v.tensor_tensor(t1[b][:], ps_r[b][:, :], ropeS[:, cs], ALU.mult)
                        v.tensor_tensor(t2[b][:], qsb[b][:], ropeC[:, cs], ALU.mult)
                        s_ev.inc(v.tensor_tensor(qrb[b][:], t1[b][:], t2[b][:], ALU.add))
                    else:
                        s_ev.inc(v.tensor_copy(qrb[b][:], qsb[b][:]))

                @block.sync
                def _(sy, it=it, b=b, h=h, ns=ns, n0=n0, ev0=ev0):
                    sy.wait_ge(s_ev.h, ev0 + it + 1)
                    s_st.dma(sy.dma_start(
                        qr_d[h, :, n0 + ns * 512:n0 + (ns + 1) * 512], qrb[b][:]))

        mm1, sc1, ev1, st1 = s_mm.n, s_sc.n, s_ev.n, s_st.n

        for hd in (range(8) if "v" in parts else []):
            sbuf_i = hd % 2

            @block.gpsimd
            def _(g, hd=hd, sbuf_i=sbuf_i, mm1=mm1):
                if hd < 2:
                    g.wait_ge(s_mm.h, mm1)
                else:
                    g.wait_ge(s_mm.h, mm1 + (hd - 1) * NS * 5)
                w = mk.dram["wsc_L0_attn_Wv"][hd, :, :, :].rearrange(
                    "kd dd ol -> dd kd ol")
                s_ld.dma(g.dma_start(strip[sbuf_i][:].rearrange(
                    "dd (kd ol) -> dd kd ol", ol=128), w))

            for ns in range(NS):
                it = hd * NS + ns
                b = it % 2

                @block.tensor
                def _(t, hd=hd, ns=ns, b=b, it=it, sbuf_i=sbuf_i,
                      ld1=None, sc1=sc1, xn_ready=xn_ready):
                    t.wait_ge(s_ld.h, xn_ready + 16 * 24 + 16 * (hd + 1))
                    if it >= 2:
                        t.wait_ge(s_sc.h, sc1 + it - 1)
                    for kd in range(16):
                        t.matmul(ps_p[b][:, :],
                                 strip[sbuf_i][:, kd * 128:(kd + 1) * 128],
                                 xnf[:, kd * HN + ns * 512: kd * HN + (ns + 1) * 512],
                                 start=(kd == 0), stop=(kd == 15))
                    fence(mk, t, s_mm)

                @block.scalar
                def _(sc, hd=hd, b=b, it=it, mm1=mm1, ev1=ev1):
                    sc.wait_ge(s_mm.h, mm1 + it * 5 + 1)
                    if it >= 1:
                        sc.wait_ge(s_ev.h, ev1 + it * 4)
                    s_sc.inc(sc.activation(vsb[:], ps_p[b][:, :], AF.Identity,
                                           bias=mk.b_qkv[:, 24 + hd:25 + hd]))

                for t4 in range(4):
                    tb = t4 % 2

                    @block.tensor
                    def _(t, t4=t4, it=it, sc1=sc1, ev1=ev1):
                        t.wait_ge(s_sc.h, sc1 + it + 1)
                        cnt = it * 4 + t4
                        if cnt >= 1:
                            t.wait_ge(s_ev.h, ev1 + cnt)
                        t.transpose(ps_t[:, 0:128],
                                    vsb[:, t4 * 128:(t4 + 1) * 128],
                                    mk.c_ident[:])
                        fence(mk, t, s_mm)

                    @block.vector
                    def _(v, t4=t4, tb=tb, it=it, mm1=mm1, st1=st1):
                        v.wait_ge(s_mm.h, mm1 + it * 5 + 2 + t4)
                        cnt = it * 4 + t4
                        if cnt >= 2:
                            v.wait_ge(s_st.h, st1 + 16 * (cnt - 1))
                        s_ev.inc(v.tensor_copy(vtb[tb][:], ps_t[:, 0:128]))

                    @block.sync
                    def _(sy, hd=hd, ns=ns, t4=t4, tb=tb, it=it, n0=n0, ev1=ev1):
                        sy.wait_ge(s_ev.h, ev1 + it * 4 + t4 + 1)
                        nn = n0 + ns * 512 + t4 * 128
                        s_st.dma(sy.dma_start(Vt[hd, nn:nn + 128, :], vtb[tb][:]))

        mm2, sc2, ev2, st2 = s_mm.n, s_sc.n, s_ev.n, s_st.n

        NTT = HN // 128
        for tt in (range(NTT) if "f1" in parts else []):
            @block.tensor
            def _(t, tt=tt, xn_ready=xn_ready):
                if tt == 0:
                    t.wait_ge(s_ld.h, xn_ready)
                    s_st.wait(t)
                s_sc.wait(t)
                s_ev.wait(t)
                for kd in range(16):
                    t.matmul(ps_l[:, 0:8],
                             xnf[:, kd * HN + tt * 128: kd * HN + (tt + 1) * 128],
                             mk.rwT[:, kd * 8:(kd + 1) * 8],
                             start=(kd == 0), stop=(kd == 15))
                fence(mk, t, s_mm)

            @block.scalar
            def _(sc, tt=tt):
                s_mm.wait(sc)
                s_sc.inc(sc.activation(esb[:], ps_l[:, 0:8], AF.Exp))

            @block.vector
            def _(v, tt=tt):
                s_sc.wait(v)

                def S(ins):
                    s_ev.inc(v.drain())
                    v.wait_ge(s_ev.h, s_ev.n)
                S(v.tensor_reduce(v1c[:], esb[:], mybir.AxisListType.X, ALU.max))
                S(v.tensor_scalar(m1[:], esb[:], v1c[:], None, ALU.is_equal))
                S(v.tensor_scalar(tdum[:], m1[:], -1.0, 1.0, ALU.mult, ALU.add))
                S(v.tensor_tensor(p2[:], tdum[:], esb[:], ALU.mult))
                S(v.tensor_reduce(v2c[:], p2[:], mybir.AxisListType.X, ALU.max))
                S(v.tensor_scalar(m2[:], p2[:], v2c[:], None, ALU.is_equal))
                S(v.tensor_tensor(s12[:], v1c[:], v2c[:], ALU.add))
                S(v.reciprocal(rs12[:], s12[:]))
                S(v.tensor_tensor(mm12[:], m1[:], m2[:], ALU.add))
                S(v.tensor_tensor(tdum[:], mm12[:], esb[:], ALU.mult))
                S(v.tensor_scalar(wdt[:], tdum[:], rs12[:], None, ALU.mult))

            if tt == 0 and half == 0 and "f1dbg_d" in mk.dbg:
                @block.vector
                def _(v):
                    dbgsb = f1dbg
                    v.tensor_copy(dbgsb[:, 0:8], ps_l[:, 0:8])
                    v.tensor_copy(dbgsb[:, 8:16], esb[:])
                    v.tensor_copy(dbgsb[:, 16:24], probs[:])
                    s_ev.inc(v.tensor_copy(dbgsb[:, 24:32], wdt[:]))

                @block.sync
                def _(sy):
                    sy.wait_ge(s_ev.h, s_ev.n)
                    s_st.dma(sy.dma_start(mk.dram["f1dbg_d"][:, :], f1dbg[:]))

            if "f1all_d" in mk.dbg:
                @block.sync
                def _(sy, tt=tt, n0=n0, esb_cnt=s_sc.n):
                    sy.wait_ge(s_sc.h, esb_cnt)
                    s_st.dma(sy.dma_start(
                        mk.dram["f1all_d"][n0 + tt * 128:n0 + (tt + 1) * 128, 8:16],
                        esb[:]))
                    sy.wait_ge(s_ev.h, s_ev.n)
                    s_st.dma(sy.dma_start(
                        mk.dram["f1all_d"][n0 + tt * 128:n0 + (tt + 1) * 128, 0:8],
                        wdt[:]))

            @block.tensor
            def _(t, tt=tt):
                s_ev.wait(t)
                s_sc.wait(t)
                t.transpose(ps_w[0:8, 0:128], wdt[:], mk.c_identf[:])
                fence(mk, t, s_mm)

            @block.vector
            def _(v, tt=tt):
                s_mm.wait(v)
                s_ev.inc(v.tensor_copy(wfm[0:8, tt * 128:(tt + 1) * 128],
                                       ps_w[0:8, 0:128]))

            @block.sync
            def _(sy, tt=tt):
                sy.wait_ge(s_ev.h, s_ev.n)
                s_st.dma(sy.dma_start(mk.dram["f1q_d"][:, :], wdt[:]))

            @block.vector
            def _(v, tt=tt):
                v.wait_ge(s_st.h, s_st.n)

        if "wfm_d" in mk.dbg:
            @block.sync
            def _(sy, n0=n0):
                sy.wait_ge(s_ev.h, s_ev.n)
                sy.wait_ge(s_sc.h, s_sc.n)
                s_st.dma(sy.dma_start(mk.dram["wfm_d"][:, n0:n0 + HN], wfm[0:8, :]))

        mm3, sc3, ev3, st3 = s_mm.n, s_sc.n, s_ev.n, s_st.n

        for ns in (range(NS) if "f2" in parts else []):
            b = ns % 2

            @block.tensor
            def _(t, ns=ns, b=b, ev3=ev3, sc3=sc3):
                if ns == 0:
                    t.wait_ge(s_ev.h, ev3)
                    t.wait_ge(s_sc.h, sc3)
                if ns >= 2:
                    t.wait_ge(s_ev.h, ev3 + ns - 1)
                for kd in range(16):
                    t.matmul(ps_p[b][:, :], mk.laT[:, kd * 128:(kd + 1) * 128],
                             xnf[:, kd * HN + ns * 512: kd * HN + (ns + 1) * 512],
                             start=(kd == 0), stop=(kd == 15))
                t.matmul(ps_r[b][:, :], mk.c_e16[0:8, :],
                         wfm[0:8, ns * 512:(ns + 1) * 512], start=True, stop=True)
                fence(mk, t, s_mm, 2)

            @block.vector
            def _(v, ns=ns, b=b, mm3=mm3):
                v.wait_ge(s_mm.h, mm3 + 2 * (ns + 1))
                v.tensor_copy(wbsb[b][:], ps_r[b][:, :])
                s_ev.inc(v.tensor_tensor(Axw[:, ns * 512:(ns + 1) * 512],
                                         ps_p[b][:, :], wbsb[b][:], ALU.mult))

        if "axw_d" in mk.dbg:
            @block.sync
            def _(sy, n0=n0):
                sy.wait_ge(s_ev.h, s_ev.n)
                s_st.dma(sy.dma_start(mk.dram["axw_d"][:, n0:n0 + HN], Axw[:]))

        mm4, sc4, ev4, st4, ld4 = s_mm.n, s_sc.n, s_ev.n, s_st.n, s_ld.n

        for os_ in (range(16) if "f3" in parts else []):
            sbuf_i = os_ % 2

            @block.gpsimd
            def _(g, os_=os_, sbuf_i=sbuf_i, mm4=mm4):
                if os_ < 2:
                    g.wait_ge(s_mm.h, mm4)
                else:
                    g.wait_ge(s_mm.h, mm4 + (os_ - 1) * NS)
                w = mk.dram["wsc_L0_ffn_weight"][os_, :, :, :].rearrange(
                    "kd dd ol -> dd kd ol")
                s_ld.dma(g.dma_start(strip[sbuf_i][:].rearrange(
                    "dd (kd ol) -> dd kd ol", ol=128), w))

            for ns in range(NS):
                b = ns % 2
                itf = os_ * NS + ns

                @block.tensor
                def _(t, os_=os_, ns=ns, b=b, itf=itf, sbuf_i=sbuf_i,
                      ld4=ld4, ev4=ev4, sc4=sc4):
                    t.wait_ge(s_ld.h, ld4 + 16 * (os_ + 1))
                    if itf == 0:
                        t.wait_ge(s_ev.h, ev4)
                    if itf >= 2:
                        t.wait_ge(s_sc.h, sc4 + itf - 1)
                    for kd in range(16):
                        t.matmul(ps_p[b][:, :],
                                 strip[sbuf_i][:, kd * 128:(kd + 1) * 128],
                                 xnf[:, kd * HN + ns * 512: kd * HN + (ns + 1) * 512],
                                 start=(kd == 0), stop=False)
                    t.matmul(ps_p[b][:, :], mk.BT[:, os_ * 128:(os_ + 1) * 128],
                             Axw[:, ns * 512:(ns + 1) * 512], start=False, stop=True)
                    fence(mk, t, s_mm)

                @block.scalar
                def _(sc, b=b, itf=itf, mm4=mm4, st4=st4):
                    sc.wait_ge(s_mm.h, mm4 + itf + 1)
                    if itf >= 2:
                        sc.wait_ge(s_st.h, st4 + 16 * (itf - 1))
                    s_sc.inc(sc.activation(gsb[b][:], ps_p[b][:, :], AF.Silu))

                @block.sync
                def _(sy, os_=os_, ns=ns, b=b, itf=itf, n0=n0, sc4=sc4):
                    sy.wait_ge(s_sc.h, sc4 + itf + 1)
                    s_st.dma(sy.dma_start(
                        g0_d[os_ * 128:(os_ + 1) * 128,
                             n0 + ns * 512:n0 + (ns + 1) * 512],
                        gsb[b][:]))

    mk.passA_done = [s_st, s_ev, s_sc, s_mm, s_ld]
    ph.close()
    phase_end(mk, mk.passA_done)


def emit_l0_passB(mk, block):
    nc = mk.nc
    ph = contextlib.ExitStack()
    sb = lambda n, s, d: ph.enter_context(nc.sbuf_tensor(n, list(s), d))
    phase_start(mk, block)

    qr_sub = sb("b_qr", [128, 24 * 512], BF16)
    qp_sub = sb("b_qp", [128, 24 * 512], BF16)
    v4 = [sb(f"b_v4{i}", [8, 128 * 128], BF16) for i in range(2)]
    A_sb = sb("b_A", [8, 16 * 512], BF16)
    O2_sb = sb("b_O2", [128, 16 * 512], BF16)
    invb_sb = sb("b_invb", [128, 512], F32)
    tp = sb("b_tp", [128, 512], F32)
    tq = sb("b_tq", [128, 512], F32)
    n2t = sb("b_n2t", [128, 512], F32)
    rr = sb("b_rr", [128, 512], F32)
    nsq = sb("b_nsq", [24, 512], F32)
    inv2 = sb("b_inv2", [24, 512], F32)
    ps_n = ph.enter_context(nc.psum_tensor("b_psn", [24, 512], F32))
    ps_p = ph.enter_context(nc.psum_tensor("b_psp", [128, 512], F32))
    ps_i = ph.enter_context(nc.psum_tensor("b_psi", [128, 512], F32))
    ps_A = [ph.enter_context(nc.psum_tensor(f"b_psA{i}", [8, 512], F32))
            for i in range(2)]
    ps_O = [ph.enter_context(nc.psum_tensor(f"b_psO{i}", [128, 512], F32))
            for i in range(2)]

    s_ld = mk.sem("b_ld")
    s_mm = mk.sem("b_mm")
    s_sc = mk.sem("b_sc")
    s_ev = mk.sem("b_ev")
    s_st = mk.sem("b_st")
    O2_d = mk.dram["O2_d"]

    NSUB = SEQ // 512
    for su in range(NSUB):
        n0 = su * 512

        @block.sync
        def _(sy, su=su, n0=n0):
            if su > 0:
                sy.wait_ge(s_ev.h, s_ev.n)
                sy.wait_ge(s_st.h, s_st.n)
            qr = mk.dram["qr_d"][:, :, n0:n0 + 512].rearrange("h p t -> p h t")
            s_ld.dma(sy.dma_start(
                qr_sub[:].rearrange("p (h t) -> p h t", h=24), qr))
            vsrc = mk.dram["Vt"][:, n0:n0 + 128, :].rearrange("hd tq dd -> hd (tq dd)")
            s_ld.dma(sy.dma_start(v4[0][:, :], vsrc))

        @block.vector
        def _(v, su=su):
            v.wait_ge(s_ld.h, s_ld.n)
            s_ev.inc(v.scalar_tensor_tensor(qp_sub[:], qr_sub[:], 1.0,
                                            qr_sub[:], ALU.mult, ALU.mult))

        @block.tensor
        def _(t, su=su):
            s_ev.wait(t)
            for h in range(24):
                t.matmul(ps_n[:, :], mk.hot24[:, h * 24:(h + 1) * 24],
                         qp_sub[:, h * 512:(h + 1) * 512],
                         start=(h == 0), stop=(h == 23))
            fence(mk, t, s_mm)

        @block.vector
        def _(v, su=su):
            s_mm.wait(v)
            v.tensor_copy(nsq[:], ps_n[:, :])
            s_ev.inc(v.drain())
            v.wait_ge(s_ev.h, s_ev.n)
            v.tensor_scalar(nsq[:], nsq[:], 1e-12, None, ALU.max)
            s_ev.inc(v.drain())

        @block.scalar
        def _(sc, su=su):
            s_ev.wait(sc)
            sc.activation(inv2[:], nsq[:], AF.Ln)
            sc.drain()
            sc.activation(inv2[:], inv2[:], AF.Exp, scale=-0.5,
                          bias=mk.ln2pi[0:24, :])
            s_sc.inc(sc.drain())

        for h in range(24):
            @block.tensor
            def _(t, h=h, su=su):
                if h == 0:
                    s_sc.wait(t)
                else:
                    s_sc.wait(t)
                    s_ev.wait(t)
                t.matmul(ps_p[:, :], mk.RRT[:], qr_sub[:, h * 512:(h + 1) * 512],
                         start=True, stop=True)
                t.matmul(ps_i[:, :], mk.hotrow24[:, h * 128:(h + 1) * 128],
                         inv2[:], start=True, stop=True)
                fence(mk, t, s_mm)

            @block.vector
            def _(v, h=h, su=su):
                s_mm.wait(v)

                def S():
                    s_ev.inc(v.drain())
                    v.wait_ge(s_ev.h, s_ev.n)
                v.tensor_copy(invb_sb[:], ps_i[:, :])
                S()
                v.tensor_tensor(tp[:], ps_p[:, :], invb_sb[:], ALU.mult)
                S()
                v.tensor_scalar(tq[:], tp[:], 1.0, mk.dmagic[:], ALU.mult, ALU.add)
                S()
                v.tensor_scalar(n2t[:], tq[:], -MAGIC, None, ALU.add)
                S()
                v.scalar_tensor_tensor(rr[:], n2t[:], -1.0, tp[:], ALU.mult, ALU.add)
                S()

            @block.scalar
            def _(sc, h=h, su=su):
                s_ev.wait(sc)
                sc.activation(qp_sub[:, h * 512:(h + 1) * 512], rr[:], AF.Sin,
                              scale=TWO_PI, bias=mk.b2pi[:])
                s_sc.inc(sc.drain())

        qpr = qp_sub[:].rearrange("p (h t) -> p t h", h=24)
        for g32 in range(16):
            bb = g32 % 2

            @block.tensor
            def _(t, su=su, g32=g32, bb=bb):
                if g32 == 0:
                    s_sc.wait(t)
                if g32 >= 2:
                    t.wait_ge(s_ev.h, s_ev.n)
                for tl in range(32):
                    tok = g32 * 32 + tl
                    t.matmul(ps_A[bb][0:8, tl * 16:(tl + 1) * 16],
                             qpr[:, tok, 16:24], qpr[:, tok, 0:16],
                             start=True, stop=True, skip_group_check=True)
                fence(mk, t, s_mm)

            @block.vector
            def _(v, su=su, g32=g32, bb=bb):
                s_mm.wait(v)
                s_ev.inc(v.tensor_scalar(A_sb[0:8, g32 * 512:(g32 + 1) * 512],
                                         ps_A[bb][0:8, :], 1.0 / 64.0, None,
                                         ALU.mult))

        for q in range(4):
            qb = q % 2
            if q < 3:
                @block.sync
                def _(sy, su=su, q=q, n0=n0):
                    if q >= 1:
                        sy.wait_ge(s_mm.h, s_mm.n)
                    vsrc = mk.dram["Vt"][:, n0 + (q + 1) * 128:n0 + (q + 2) * 128, :]
                    s_ld.dma(sy.dma_start(
                        v4[(q + 1) % 2][:, :],
                        vsrc.rearrange("hd tq dd -> hd (tq dd)")))

            for g32 in range(4):
                bb = g32 % 2

                @block.tensor
                def _(t, su=su, q=q, g32=g32, bb=bb, qb=qb):
                    t.wait_ge(s_ld.h, s_ld.n)
                    if not (q == 0 and g32 == 0):
                        t.wait_ge(s_ev.h, s_ev.n)
                    for tl in range(32):
                        tok_l = g32 * 32 + tl
                        t.matmul(ps_O[bb][:, tl * 16:(tl + 1) * 16],
                                 v4[qb][0:8, tok_l * 128:(tok_l + 1) * 128],
                                 A_sb[0:8,
                                      (q * 128 + tok_l) * 16:(q * 128 + tok_l + 1) * 16],
                                 start=True, stop=True, skip_group_check=True)
                    fence(mk, t, s_mm)

                @block.vector
                def _(v, su=su, q=q, g32=g32, bb=bb):
                    s_mm.wait(v)
                    t0 = q * 128 + g32 * 32
                    tw0 = t0 // 16
                    dst = O2_sb[:].rearrange("p (j h tw) -> p tw j h",
                                             j=16, h=16)[:, tw0:tw0 + 2, :, :]
                    src = ps_O[bb][:, :].rearrange("p (t2 j h) -> p t2 j h",
                                                   t2=2, j=16)
                    s_ev.inc(v.tensor_copy(dst, src))

        @block.sync
        def _(sy, su=su, n0=n0):
            sy.wait_ge(s_ev.h, s_ev.n)
            for j in range(16):
                dst = O2_d[j * 128:(j + 1) * 128, :].rearrange(
                    "dd (h t) -> dd h t", h=16)[:, :, su * 32:(su + 1) * 32]
                src = O2_sb[:, j * 512:(j + 1) * 512].rearrange(
                    "p (h tw) -> p h tw", h=16)
                s_st.dma(sy.dma_start(dst, src))

    mk.passB_done = [s_st, s_ev, s_sc, s_mm, s_ld]
    ph.close()
    phase_end(mk, mk.passB_done)


def emit_l0_passC(mk, block):
    nc = mk.nc
    ph = contextlib.ExitStack()
    sb = lambda n, s, d: ph.enter_context(nc.sbuf_tensor(n, list(s), d))
    phase_start(mk, block)

    wot = sb("c_wot", [128, 16 * 2048], BF16)
    o2c = [sb(f"c_o2{i}", [128, 16 * 512], BF16) for i in range(2)]
    g0c = [sb(f"c_g0{i}", [128, 16 * 512], BF16) for i in range(2)]
    sgc = [sb(f"c_sg{i}", [128, 16 * 512], BF16) for i in range(2)]
    ps = [ph.enter_context(nc.psum_tensor(f"c_ps{i}", [128, 512], F32))
          for i in range(2)]
    s_ld = mk.sem("c_ld")
    s_mm = mk.sem("c_mm")
    s_ev = mk.sem("c_ev")
    s_st = mk.sem("c_st")
    sg_d = mk.dram["sg_d"]

    @block.sync
    def _(sy):
        for kd in range(16):
            s_ld.dma(sy.dma_start(
                wot[:, kd * 2048:(kd + 1) * 2048],
                mk.dram["L0_attn_Wo_bf"][0:2048, kd * 128:(kd + 1) * 128],
                transpose=True))
    wot_ready = s_ld.n

    for nch in range(8):
        cb = nch % 2
        n0 = nch * 512

        @block.sync
        def _(sy, nch=nch, cb=cb, n0=n0):
            if nch >= 2:
                sy.wait_ge(s_st.h, 16 * (nch - 1))
                sy.wait_ge(s_mm.h, s_mm.n)
            src = mk.dram["O2_d"].rearrange("(h dd) n -> dd h n", dd=128)
            s_ld.dma(sy.dma_start(
                o2c[cb][:].rearrange("p (h t) -> p h t", h=16),
                src[:, :, n0:n0 + 512]))
            src2 = mk.dram["g0_d"].rearrange("(os dd) n -> dd os n", dd=128)
            s_ld.dma(sy.dma_start(
                g0c[cb][:].rearrange("p (os t) -> p os t", os=16),
                src2[:, :, n0:n0 + 512]))

        for os_ in range(16):
            @block.tensor
            def _(t, nch=nch, cb=cb, os_=os_):
                if nch == 0 and os_ == 0:
                    t.wait_ge(s_ld.h, wot_ready)
                t.wait_ge(s_ld.h, s_ld.n)
                pb = os_ % 2
                if not (nch == 0 and os_ <= 1):
                    t.wait_ge(s_ev.h, s_ev.n - 1)
                for kd in range(16):
                    t.matmul(ps[pb][:, :],
                             wot[:, kd * 2048 + os_ * 128: kd * 2048 + (os_ + 1) * 128],
                             o2c[cb][:, kd * 512:(kd + 1) * 512],
                             start=(kd == 0), stop=(kd == 15))
                fence(mk, t, s_mm)

            @block.vector
            def _(v, nch=nch, cb=cb, os_=os_):
                s_mm.wait(v)
                pb = os_ % 2
                s_ev.inc(v.scalar_tensor_tensor(
                    sgc[cb][:, os_ * 512:(os_ + 1) * 512], ps[pb][:, :],
                    mk.b_o[:, os_:os_ + 1], g0c[cb][:, os_ * 512:(os_ + 1) * 512],
                    ALU.add, ALU.mult))

        @block.sync
        def _(sy, nch=nch, cb=cb, n0=n0):
            sy.wait_ge(s_ev.h, s_ev.n)
            dst = sg_d.rearrange("(os dd) n -> dd os n", dd=128)
            s_st.dma(sy.dma_start(
                dst[:, :, n0:n0 + 512],
                sgc[cb][:].rearrange("p (os t) -> p os t", os=16)))

    mk.passC_done = [s_st, s_ev, s_mm, s_ld]
    ph.close()
    phase_end(mk, mk.passC_done)


def emit_combine(mk, block, x_name, sg_name, x1_name):
    nc = mk.nc
    ph = contextlib.ExitStack()
    sb = lambda n, s, d: ph.enter_context(nc.sbuf_tensor(n, list(s), d))
    phase_start(mk, block)
    tag = x1_name
    sgt = [sb(f"m_sg{i}_{tag}", [128, D], BF16) for i in range(2)]
    xt = [sb(f"m_x{i}_{tag}", [128, D], F32) for i in range(2)]
    x1t = [sb(f"m_x1{i}_{tag}", [128, D], F32) for i in range(2)]
    s_ld = mk.sem(f"m_ld_{tag}")
    s_ev = mk.sem(f"m_ev_{tag}")
    s_st = mk.sem(f"m_st_{tag}")
    xd = mk.dram[x_name]
    sgd = mk.dram[sg_name]
    x1d = mk.dram[x1_name]

    for tt in range(NT):
        b = tt % 2

        @block.sync
        def _(sy, tt=tt, b=b):
            if tt >= 2:
                sy.wait_ge(s_ev.h, tt - 1)
            s_ld.dma(sy.dma_start(sgt[b][:], sgd[0:D, tt * 128:(tt + 1) * 128],
                                  transpose=True))
            s_ld.dma(sy.dma_start(xt[b][:], xd[tt * 128:(tt + 1) * 128, :]))

        @block.vector
        def _(v, tt=tt, b=b):
            v.wait_ge(s_ld.h, 32 * (tt + 1))
            if tt >= 2:
                v.wait_ge(s_st.h, 16 * (tt - 1))
            s_ev.inc(v.tensor_tensor(x1t[b][:], xt[b][:], sgt[b][:], ALU.add))

        @block.sync
        def _(sy, tt=tt, b=b):
            sy.wait_ge(s_ev.h, tt + 1)
            s_st.dma(sy.dma_start(x1d[tt * 128:(tt + 1) * 128, :], x1t[b][:]))

    done = [s_st, s_ev, s_ld]
    ph.close()
    phase_end(mk, done)
    return Sem(s_st.h) if False else done[0]



def emit_l1_sp(mk, block):
    nc = mk.nc
    s = mk.sem("p1_ld")
    s_mm = mk.sem("p1_mm")
    mk.pq_sb = mk.sb("pq_sb", [128, 16 * 4], BF16)
    mk.b1_qkv = mk.sb("b1_qkv", [128, 48], F32)
    mk.b1_lmqkv = mk.sb("b1_lmqkv", [128, 48], F32)
    mk.b1_lmout = mk.sb("b1_lmout", [128, 16], F32)
    mk.b1_l2tq = mk.sb("b1_l2tq", [128, 16], F32)
    mk.b1_l2to = mk.sb("b1_l2to", [128, 16], F32)
    mk.b1_lmkv = mk.sb("b1_lmkv", [128, 32], F32)
    mk.b1_out = mk.sb("b1_out", [128, 16], F32)
    mk.b1_gate = mk.sb("b1_gate", [32, 1], F32)
    mk.cmask = mk.sb("c_cmask", [128, 512], F32)
    mk.hot64 = mk.sb("c_hot64", [128, 2], BF16)
    mk.hot64T = mk.sb("c_hot64T", [2, 128], BF16)
    mk.c_h128 = mk.sb("c_h128", [128, 128], BF16)
    mk.c_sw64 = mk.sb("c_sw64", [128, 128], BF16)
    phase_start(mk, block)
    ph = contextlib.ExitStack()
    tmp = ph.enter_context(nc.sbuf_tensor("p1_tmp", [4, 2048], BF16))
    ps_t = ph.enter_context(nc.psum_tensor("p1_ps", [128, 1024], BF16))

    @block.sync
    def _(sy):
        mk.s_wprep.wait(sy)
        s.dma(sy.dma_start(mk.cmask[:], mk.dram["cmask"][:, :]))
        for nm, dst, nstr in [("L1_attn_b_qkv", mk.b1_qkv, 48),
                              ("L1_attn_b_lm_qkv", mk.b1_lmqkv, 48),
                              ("L1_attn_b_lm_out", mk.b1_lmout, 16),
                              ("L1_attn_b_l2t_q", mk.b1_l2tq, 16),
                              ("L1_attn_b_l2t_out", mk.b1_l2to, 16),
                              ("L1_attn_b_lm_kv", mk.b1_lmkv, 32),
                              ("L1_attn_b_out", mk.b1_out, 16)]:
            v = mk.dram[nm].rearrange("(s p) -> p s", p=128)
            s.dma(sy.dma_start(dst[:, 0:nstr], v, allow_slow_non_contiguous=True))
        v = mk.dram["L1_attn_b_gate"].rearrange("(s p) -> p s", p=32)
        s.dma(sy.dma_start(mk.b1_gate[0:32, :], v, allow_slow_non_contiguous=True))
        s.dma(sy.dma_start(tmp[0:4, :], mk.dram["L1_attn_pool_q_bf"][:, :]))

    @block.gpsimd
    def _(g):
        s.dma(g.dma_start(mk.hot64[:], mk.dram["hot64"][:, :]))
        s.dma(g.dma_start(mk.hot64T[0:2, :], mk.dram["hot64T"][:, :]))
        s.dma(g.dma_start(mk.c_h128[:], mk.dram["h128"][:, :]))
        s.dma(g.dma_start(mk.c_sw64[:], mk.dram["swap64"][:, :]))

    for ht in range(16):
        @block.tensor
        def _(t, ht=ht):
            if ht == 0:
                s.wait(t)
                mk.s_const.wait(t)
            else:
                s_mm.wait(t)
            t.transpose(ps_t[:, 0:4], tmp[0:4, ht * 128:(ht + 1) * 128],
                        mk.c_ident[0:4, 0:4])
            fence(mk, t, s_mm)

        @block.vector
        def _(v, ht=ht):
            s_mm.wait(v)
            s_mm.inc(v.tensor_copy(mk.pq_sb[:, ht * 4:(ht + 1) * 4], ps_t[:, 0:4]))

    ph.close()
    phase_end(mk, [s, s_mm])
    mk.l1sp_done = s


def emit_l1_qkv(mk, block):
    nc = mk.nc
    ph = contextlib.ExitStack()
    sb = lambda n, s, d: ph.enter_context(nc.sbuf_tensor(n, list(s), d))
    phase_start(mk, block)
    HN = SEQ // 2
    xnf = sb("e_xnf", [128, 16 * HN], BF16)
    ropeC = sb("e_ropeC", [128, HN], F32)
    ropeS = sb("e_ropeS", [128, HN], F32)
    strip = [sb(f"e_strip{i}", [128, 16 * 128], BF16) for i in range(2)]
    qsb = [sb(f"e_qsb{i}", [128, 512], BF16) for i in range(2)]
    t1 = [sb(f"e_t1{i}", [128, 512], F32) for i in range(2)]
    t2 = [sb(f"e_t2{i}", [128, 512], F32) for i in range(2)]
    qrb = [sb(f"e_qrb{i}", [128, 512], BF16) for i in range(2)]
    ps_p = [ph.enter_context(nc.psum_tensor(f"e_psp{i}", [128, 512], F32))
            for i in range(2)]
    ps_r = [ph.enter_context(nc.psum_tensor(f"e_psr{i}", [128, 512], F32))
            for i in range(2)]
    s_ld = mk.sem("e_ld")
    s_mm = mk.sem("e_mm")
    s_sc = mk.sem("e_sc")
    s_ev = mk.sem("e_ev")
    s_st = mk.sem("e_st")
    outd = {"q": mk.dram["q1_d"], "k": mk.dram["k1_d"], "v": mk.dram["v1_d"]}

    NS = HN // 512
    for half in range(2):
        n0 = half * HN
        mm0, sc0, ev0, st0 = s_mm.n, s_sc.n, s_ev.n, s_st.n
        if half == 1:
            barrier(mk, block, [s_ld, s_mm, s_sc, s_ev, s_st])

        @block.sync
        def _(sy, n0=n0):
            for kd in range(16):
                s_ld.dma(sy.dma_start(
                    xnf[:, kd * HN:(kd + 1) * HN],
                    mk.dram["xn1"][n0:n0 + HN, kd * 128:(kd + 1) * 128],
                    transpose=True))
            s_ld.dma(sy.dma_start(ropeC[:], mk.dram["rope1_cos"][:, n0:n0 + HN]))
            s_ld.dma(sy.dma_start(ropeS[:], mk.dram["rope1_sin"][:, n0:n0 + HN]))
        xn_ready = s_ld.n

        for os_ in range(48):
            kind = "q" if os_ < 16 else ("k" if os_ < 32 else "v")
            dst = outd[kind]
            orow = (os_ % 16) * 128
            sbuf_i = os_ % 2

            @block.gpsimd
            def _(g, os_=os_, sbuf_i=sbuf_i, mm0=mm0):
                if os_ >= 2:
                    g.wait_ge(s_mm.h, mm0 + (os_ - 1) * NS * 2)
                else:
                    g.wait_ge(s_mm.h, mm0)
                w = mk.dram["L1_attn_w_qkv_bf"][:, os_ * 128:(os_ + 1) * 128]
                s_ld.dma(g.dma_start(strip[sbuf_i][:].rearrange(
                    "dd (kd ol) -> dd kd ol", ol=128),
                    w.rearrange("(kd dd) ol -> dd kd ol", dd=128)))

            for ns in range(NS):
                it = os_ * NS + ns
                b = it % 2

                @block.tensor
                def _(t, os_=os_, ns=ns, b=b, it=it, sbuf_i=sbuf_i,
                      xn_ready=xn_ready, sc0=sc0):
                    if ns == 0:
                        t.wait_ge(s_ld.h, xn_ready + 16 * (os_ + 1))
                    if it >= 2:
                        t.wait_ge(s_sc.h, sc0 + it - 1)
                    for kd in range(16):
                        t.matmul(ps_p[b][:, :],
                                 strip[sbuf_i][:, kd * 128:(kd + 1) * 128],
                                 xnf[:, kd * HN + ns * 512: kd * HN + (ns + 1) * 512],
                                 start=(kd == 0), stop=(kd == 15))
                    fence(mk, t, s_mm)

                @block.scalar
                def _(sc, it=it, b=b, os_=os_, mm0=mm0, ev0=ev0):
                    sc.wait_ge(s_mm.h, mm0 + 2 * it + 1)
                    if it >= 2:
                        sc.wait_ge(s_ev.h, ev0 + it - 1)
                    s_sc.inc(sc.activation(qsb[b][:], ps_p[b][:, :], AF.Identity,
                                           bias=mk.b1_qkv[:, os_:os_ + 1]))

                @block.tensor
                def _(t, it=it, b=b, sc0=sc0, ev0=ev0, kind=kind):
                    t.wait_ge(s_sc.h, sc0 + it + 1)
                    if it >= 2:
                        t.wait_ge(s_ev.h, ev0 + it - 1)
                    if kind in ("q", "k"):
                        t.matmul(ps_r[b][:, :], mk.c_rot1[:], qsb[b][:],
                                 start=True, stop=True)
                    fence(mk, t, s_mm)

                @block.vector
                def _(v, it=it, b=b, ns=ns, mm0=mm0, st0=st0, kind=kind):
                    v.wait_ge(s_mm.h, mm0 + 2 * it + 2)
                    if it >= 2:
                        v.wait_ge(s_st.h, st0 + 16 * (it - 1))
                    cs = slice(ns * 512, (ns + 1) * 512)
                    if kind in ("q", "k"):
                        v.tensor_tensor(t1[b][:], ps_r[b][:, :], ropeS[:, cs],
                                        ALU.mult)
                        v.tensor_tensor(t2[b][:], qsb[b][:], ropeC[:, cs],
                                        ALU.mult)
                        s_ev.inc(v.tensor_tensor(qrb[b][:], t1[b][:], t2[b][:],
                                                 ALU.add))
                    else:
                        s_ev.inc(v.tensor_copy(qrb[b][:], qsb[b][:]))

                @block.sync
                def _(sy, it=it, b=b, ns=ns, n0=n0, ev0=ev0, dst=dst, orow=orow):
                    sy.wait_ge(s_ev.h, ev0 + it + 1)
                    s_st.dma(sy.dma_start(
                        dst[orow:orow + 128, n0 + ns * 512:n0 + (ns + 1) * 512],
                        qrb[b][:]))

    mk.l1qkv_done = [s_st, s_ev, s_sc, s_mm, s_ld]
    ph.close()
    phase_end(mk, mk.l1qkv_done)


def emit_l1_loc(mk, block):
    nc = mk.nc
    mk.attn_es = contextlib.ExitStack()
    mk.lmfA = mk.attn_es.enter_context(mk.nc.sbuf_tensor("lmfA", [128, 16 * 256], BF16))
    mk.lmfB = mk.attn_es.enter_context(mk.nc.sbuf_tensor("lmfB", [128, 16 * 256], BF16))
    ph = contextlib.ExitStack()
    sb = lambda n, s, d: ph.enter_context(nc.sbuf_tensor(n, list(s), d))
    phase_start(mk, block)

    qt = [sb(f"g_qt{i}", [128, SEQ], BF16) for i in range(2)]
    kt = [sb(f"g_kt{i}", [128, SEQ], BF16) for i in range(2)]
    vt = [sb(f"g_vt{i}", [128, SEQ], BF16) for i in range(2)]
    vT = sb("g_vT", [128, SEQ], BF16)
    sc_sb = sb("g_sc", [128, 512], F32)
    ex_sb = sb("g_ex", [128, 512], F32)
    ex2_sb = sb("g_ex2", [128, 512], F32)
    pr_sb = [sb(f"g_pr{i}", [128, 512], BF16) for i in range(2)]
    PT_sb = [sb(f"g_PT{i}", [128, 512], BF16) for i in range(2)]
    loc_sb = sb("g_loc", [128, SEQ], BF16)
    mx8 = sb("g_mx8", [128, 8], F32)
    nm8 = sb("g_nm8", [128, 8], F32)
    s8 = sb("g_s8", [128, 8], F32)
    r8 = sb("g_r8", [128, 8], F32)
    pe_sb = sb("g_pe", [128, 256], BF16)
    s2 = sb("g_s2", [2, 256], F32)
    rs2 = sb("g_rs2", [2, 256], BF16)
    pa_sb = sb("g_pa", [128, 256], BF16)
    pa_sw = sb("g_pasw", [128, 256], BF16)
    ps_sc = [ph.enter_context(nc.psum_tensor(f"g_pssc{i}", [128, 512], F32))
             for i in range(2)]
    ps_T = ph.enter_context(nc.psum_tensor("g_psT", [128, 1024], BF16))
    ps_o = ph.enter_context(nc.psum_tensor("g_pso", [128, 512], F32))
    ps_pa = ph.enter_context(nc.psum_tensor("g_pspa", [128, 512], F32))
    ps_ms = ph.enter_context(nc.psum_tensor("g_psms", [128, 512], F32))
    ps_lmA = ph.enter_context(nc.psum_tensor("g_pslma", [128, 512], F32))
    ps_lmB = ph.enter_context(nc.psum_tensor("g_pslmb", [128, 512], F32))
    s_ld = mk.sem("g_ld")
    s_mm = mk.sem("g_mm")
    s_sc = mk.sem("g_sc")
    s_ev = mk.sem("g_ev")
    s_st = mk.sem("g_st")

    sc_fence = {}
    c1_done = {}
    e_done = {}
    p_done = {}
    tr_fence = {}
    ptev_done = {}
    og_fence = {}
    ev_done = {}
    ld_mark = {}
    loc_st_mark = {}
    ptev_last = [0]
    vtcopy_last = [0]

    def prologue(ht):
        hb = ht % 2

        @block.sync
        def _(sy, ht=ht, hb=hb):
            if ht >= 2:
                sy.wait_ge(s_ev.h, s_ev.n)
                sy.wait_ge(s_mm.h, s_mm.n)
            s_ld.dma(sy.dma_start(qt[hb][:], mk.dram["q1_d"][ht * 128:(ht + 1) * 128, :]))
            s_ld.dma(sy.dma_start(kt[hb][:], mk.dram["k1_d"][ht * 128:(ht + 1) * 128, :]))
            s_ld.dma(sy.dma_start(vt[hb][:], mk.dram["v1_d"][ht * 128:(ht + 1) * 128, :]))
            ld_mark[ht] = s_ld.n

        for grp in range(4):
            @block.tensor
            def _(t, ht=ht, hb=hb, grp=grp):
                t.wait_ge(s_ld.h, ld_mark[ht])
                t.wait_ge(s_sc.h, ptev_last[0])
                s_ev.wait(t)
                for jj in range(8):
                    np_ = grp * 8 + jj
                    t.transpose(ps_T[:, jj * 128:(jj + 1) * 128],
                                vt[hb][:, np_ * 128:(np_ + 1) * 128],
                                mk.c_ident[:])
                fence(mk, t, s_mm)

            @block.vector
            def _(v, grp=grp):
                s_mm.wait(v)
                s_ev.inc(v.tensor_copy(vT[:, grp * 1024:(grp + 1) * 1024],
                                       ps_T[:, 0:1024]))
                vtcopy_last[0] = s_ev.n

        @block.sync
        def _(sy, ht=ht):
            sy.wait_ge(s_ev.h, s_ev.n)
            s_st.dma(sy.dma_start(mk.dram["vT1_d"][ht, :, :], vT[:]))

    def a_part(u):
        ht, nbg = u // 8, u % 8
        hb = ht % 2

        @block.tensor
        def _(t, u=u, ht=ht, hb=hb, nbg=nbg):
            t.wait_ge(s_ld.h, ld_mark[ht])
            if u - 2 in c1_done:
                t.wait_ge(s_ev.h, c1_done[u - 2])
            for j in range(8):
                nb = nbg * 8 + j
                for h2 in range(2):
                    t.matmul(ps_sc[u % 2][h2 * 64:(h2 + 1) * 64,
                                          j * 64:(j + 1) * 64],
                             qt[hb][h2 * 64:(h2 + 1) * 64, nb * 64:(nb + 1) * 64],
                             kt[hb][h2 * 64:(h2 + 1) * 64, nb * 64:(nb + 1) * 64],
                             start=True, stop=True, skip_group_check=True)
            fence(mk, t, s_mm)
            sc_fence[u] = s_mm.n

        @block.vector
        def _(v, u=u):
            v.wait_ge(s_mm.h, sc_fence[u])
            if u - 1 in e_done:
                v.wait_ge(s_sc.h, e_done[u - 1])

            def S():
                s_ev.inc(v.drain())
                v.wait_ge(s_ev.h, s_ev.n)
            v.scalar_tensor_tensor(sc_sb[:], ps_sc[u % 2][:, :], 0.125,
                                   mk.cmask[:], ALU.mult, ALU.add)
            vv = sc_sb[:].rearrange("p (b s) -> p b s", s=64)
            v.tensor_reduce(mx8[:], vv, mybir.AxisListType.X, ALU.max)
            S()
            v.tensor_scalar(nm8[:], mx8[:], -1.0, None, ALU.mult)
            S()
            bview = nm8[:].unsqueeze(2).broadcast_to([128, 8, 64])
            s_ev.inc(v.tensor_tensor(
                ex_sb[:].rearrange("p (b s) -> p b s", s=64), vv, bview,
                ALU.add))
            c1_done[u] = s_ev.n

        @block.scalar
        def _(sc, u=u):
            sc.wait_ge(s_ev.h, c1_done[u])
            if u - 1 in p_done:
                sc.wait_ge(s_ev.h, p_done[u - 1])
            s_sc.inc(sc.activation(ex2_sb[:], ex_sb[:], AF.Exp))
            e_done[u] = s_sc.n

    def a_part2(u):
        @block.vector
        def _(v, u=u):
            v.wait_ge(s_sc.h, e_done[u])
            if u - 2 in tr_fence:
                v.wait_ge(s_mm.h, tr_fence[u - 2])

            def S():
                s_ev.inc(v.drain())
                v.wait_ge(s_ev.h, s_ev.n)
            ev_ = ex2_sb[:].rearrange("p (b s) -> p b s", s=64)
            v.tensor_reduce(s8[:], ev_, mybir.AxisListType.X, ALU.add)
            S()
            v.reciprocal(r8[:], s8[:])
            S()
            rview = r8[:].unsqueeze(2).broadcast_to([128, 8, 64])
            s_ev.inc(v.tensor_tensor(
                pr_sb[u % 2][:].rearrange("p (b s) -> p b s", s=64), ev_,
                rview, ALU.mult))
            p_done[u] = s_ev.n

    def b_part(u):
        ht, nbg = u // 8, u % 8
        hb = ht % 2

        @block.tensor
        def _(t, u=u):
            t.wait_ge(s_ev.h, p_done[u])
            t.wait_ge(s_ev.h, vtcopy_last[0])
            if u - 2 in ptev_done:
                t.wait_ge(s_sc.h, ptev_done[u - 2])
            for tp in range(4):
                t.transpose(ps_T[:, tp * 128:(tp + 1) * 128],
                            pr_sb[u % 2][:, tp * 128:(tp + 1) * 128],
                            mk.c_ident[:])
            fence(mk, t, s_mm)
            tr_fence[u] = s_mm.n

        @block.scalar
        def _(sc, u=u):
            sc.wait_ge(s_mm.h, tr_fence[u])
            if u - 2 in og_fence:
                sc.wait_ge(s_mm.h, og_fence[u - 2])
            s_sc.inc(sc.activation(PT_sb[u % 2][:], ps_T[:, 0:512], AF.Copy))
            ptev_done[u] = s_sc.n
            ptev_last[0] = s_sc.n

        for jg in range(2):
            @block.tensor
            def _(t, u=u, ht=ht, hb=hb, nbg=nbg, jg=jg):
                t.wait_ge(s_sc.h, ptev_done[u])
                mark = (u, jg - 1) if jg == 1 else (u - 1, 1)
                if mark in ev_done:
                    t.wait_ge(s_ev.h, ev_done[mark])
                for jj in range(4):
                    j = jg * 4 + jj
                    nb = nbg * 8 + j
                    par = j % 2
                    if jj > 0:
                        t.drain()
                    t.matmul(ps_o[:, jj * 128:(jj + 1) * 128],
                             vT[par * 64:(par + 1) * 64,
                                (nb // 2) * 128:(nb // 2 + 1) * 128],
                             PT_sb[u % 2][par * 64:(par + 1) * 64,
                                          (j // 2) * 128:(j // 2 + 1) * 128],
                             start=True, stop=True, skip_group_check=True)
                fence(mk, t, s_mm)
                og_fence[u] = s_mm.n

            @block.vector
            def _(v, u=u, ht=ht, nbg=nbg, jg=jg):
                v.wait_ge(s_mm.h, og_fence[u])
                if nbg == 0 and jg == 0 and ht - 1 in loc_st_mark:
                    v.wait_ge(s_st.h, loc_st_mark[ht - 1])
                c0 = (nbg * 8 + jg * 4) * 64
                pv = ps_o[:, 0:512].rearrange("p (j two s) -> p j two s",
                                              two=2, s=64)
                v.tensor_copy(
                    loc_sb[0:64, c0:c0 + 256].rearrange("p (j s) -> p j s", s=64),
                    pv[0:64, :, 0, :])
                s_ev.inc(v.tensor_copy(
                    loc_sb[64:128, c0:c0 + 256].rearrange("p (j s) -> p j s", s=64),
                    pv[64:128, :, 1, :]))
                ev_done[(u, jg)] = s_ev.n

    def loc_store(ht):
        @block.sync
        def _(sy, ht=ht):
            sy.wait_ge(s_ev.h, ev_done[(ht * 8 + 7, 1)])
            s_st.dma(sy.dma_start(mk.dram["loc_d"][ht * 128:(ht + 1) * 128, :],
                                  loc_sb[:]))
            loc_st_mark[ht] = s_st.n

    def pa_lmh(ht):
        hb = ht % 2

        @block.tensor
        def _(t, ht=ht, hb=hb):
            s_ev.wait(t)
            for nb in range(64):
                for h2 in range(2):
                    t.matmul(ps_pa[h2 * 64:(h2 + 1) * 64, nb * 4:(nb + 1) * 4],
                             kt[hb][h2 * 64:(h2 + 1) * 64, nb * 64:(nb + 1) * 64],
                             mk.pq_sb[h2 * 64:(h2 + 1) * 64, ht * 4:(ht + 1) * 4],
                             start=True, stop=True, skip_group_check=True)
            fence(mk, t, s_mm)

        @block.scalar
        def _(sc, ht=ht):
            s_mm.wait(sc)
            s_sc.inc(sc.activation(pe_sb[:], ps_pa[:, 0:256], AF.Exp,
                                   scale=0.125))

        @block.tensor
        def _(t, ht=ht):
            s_sc.wait(t)
            t.matmul(ps_ms[0:2, 0:256], mk.hot64[:], pe_sb[:],
                     start=True, stop=True)
            fence(mk, t, s_mm)

        @block.vector
        def _(v, ht=ht):
            s_mm.wait(v)

            def S():
                s_ev.inc(v.drain())
                v.wait_ge(s_ev.h, s_ev.n)
            v.tensor_copy(s2[0:2, :], ps_ms[0:2, 0:256])
            S()
            v.reciprocal(s2[0:2, :], s2[0:2, :])
            S()
            v.tensor_copy(rs2[0:2, :], s2[0:2, :])
            S()

        @block.tensor
        def _(t, ht=ht):
            s_ev.wait(t)
            t.matmul(ps_ms[:, 256:512], mk.hot64T[0:2, :], rs2[0:2, :],
                     start=True, stop=True)
            fence(mk, t, s_mm)

        @block.vector
        def _(v, ht=ht):
            s_mm.wait(v)
            s_ev.inc(v.tensor_tensor(pa_sb[:], pe_sb[:], ps_ms[:, 256:512],
                                     ALU.mult))

        @block.tensor
        def _(t, ht=ht):
            s_ev.wait(t)
            t.matmul(ps_ms[:, 0:256], mk.c_sw64[:], pa_sb[:],
                     start=True, stop=True)
            fence(mk, t, s_mm)

        @block.vector
        def _(v, ht=ht):
            s_mm.wait(v)
            s_ev.inc(v.tensor_copy(pa_sw[:], ps_ms[:, 0:256]))

        @block.tensor
        def _(t, ht=ht):
            s_ev.wait(t)
            for nb in range(64):
                par = nb % 2
                for h2 in range(2):
                    dst = ps_lmA if h2 == 0 else ps_lmB
                    srcpa = pa_sb if h2 == par else pa_sw
                    t.matmul(dst[:, nb * 4:(nb + 1) * 4],
                             vT[par * 64:(par + 1) * 64, (nb // 2) * 128:(nb // 2 + 1) * 128],
                             srcpa[par * 64:(par + 1) * 64, nb * 4:(nb + 1) * 4],
                             start=True, stop=True, skip_group_check=True)
            fence(mk, t, s_mm)

        @block.vector
        def _(v, ht=ht):
            s_mm.wait(v)
            v.tensor_copy(mk.lmfA[0:64, ht * 256:(ht + 1) * 256],
                          ps_lmA[0:64, 0:256])
            s_ev.inc(v.tensor_copy(mk.lmfA[64:128, ht * 256:(ht + 1) * 256],
                                   ps_lmB[64:128, 0:256]))

    NU = 16 * 8
    for u in range(NU + 1):
        ht, nbg = u // 8, u % 8
        if u < NU and nbg == 0:
            if u - 1 >= 0:
                b_part(u - 1)
                loc_store(ht - 1)
            prologue(ht)
        if u < NU:
            a_part(u)
            a_part2(u)
        if u - 1 >= 0 and (u - 1) % 8 != 7 and u <= NU - 1:
            b_part(u - 1)
        if u < NU and nbg == 7:
            pa_lmh(ht)
    b_part(NU - 1)
    loc_store(15)

    mk.l1loc_done = [s_st, s_ev, s_sc, s_mm, s_ld]
    ph.close()
    phase_end(mk, mk.l1loc_done)


def _strip_matmuls_direct(mk, t, ps, strip, rhs_sb, rhs_off, n):
    for kd in range(16):
        t.matmul(ps[:, 0:n], strip[:, kd * 128:(kd + 1) * 128],
                 rhs_sb[:, kd * 256 + rhs_off: kd * 256 + rhs_off + n],
                 start=(kd == 0), stop=(kd == 15))


def emit_l1_lm(mk, block):
    nc = mk.nc
    mk.gk_sb = mk.attn_es.enter_context(mk.nc.sbuf_tensor("gk_sb", [128, 16 * 256], BF16))
    mk.gv_sb = mk.attn_es.enter_context(mk.nc.sbuf_tensor("gv_sb", [128, 16 * 256], BF16))
    ph = contextlib.ExitStack()
    sb = lambda n, s, d: ph.enter_context(nc.sbuf_tensor(n, list(s), d))
    phase_start(mk, block)

    lq = sb("l_lq", [128, 16 * 256], BF16)
    lk = sb("l_lk", [128, 16 * 256], BF16)
    lv = sb("l_lv", [128, 16 * 256], BF16)
    lvT = sb("l_lvT", [128, 16 * 2 * 128], BF16)
    lo = sb("l_lo", [128, 16 * 256], BF16)
    l2q = sb("l_l2q", [128, 16 * 256], BF16)
    ro = sb("l_ro", [128, 16 * 256], BF16)
    strip = [sb(f"l_strip{i}", [128, 16 * 128], BF16) for i in range(2)]
    lmm = sb("l_lmm", [128, 2 * 256], F32)
    sc2 = sb("l_sc2", [128, 512], F32)
    ex2 = sb("l_ex2", [128, 512], F32)
    pr2 = sb("l_pr2", [128, 512], BF16)
    PT2 = sb("l_PT2", [128, 512], BF16)
    mx1 = sb("l_mx1", [128, 4], F32)
    nmx1 = sb("l_nmx1", [128, 4], F32)
    sum1 = sb("l_sum1", [128, 4], F32)
    rs1 = sb("l_rs1", [128, 4], F32)
    edump = sb("l_edump", [128, 256], F32)
    kt = sb("l_kt", [128, SEQ], BF16)
    vT = sb("l_vT", [128, SEQ], BF16)
    rsc = sb("l_rsc", [128, 256], F32)
    rpr = sb("l_rpr", [128, 256], BF16)
    rpw = sb("l_rpw", [128, 256], BF16)
    mxs = sb("l_mxs", [128, 2], F32)
    ps_a = ph.enter_context(nc.psum_tensor("l_psa", [128, 512], F32))
    ps_b = ph.enter_context(nc.psum_tensor("l_psb", [128, 512], F32))
    ps_T = ph.enter_context(nc.psum_tensor("l_psT", [128, 1024], BF16))
    ps_lmA = ph.enter_context(nc.psum_tensor("l_pslma", [128, 512], F32))
    ps_lmB = ph.enter_context(nc.psum_tensor("l_pslmb", [128, 512], F32))
    ps_Tf = ph.enter_context(nc.psum_tensor("l_psTf", [128, 512], F32))
    s_ld = mk.sem("l_ld")
    s_mm = mk.sem("l_mm")
    s_sc = mk.sem("l_sc")
    s_ev = mk.sem("l_ev")

    @block.sync
    def _(sy):
        s_ld.dma(sy.dma_start(lmm[:, 0:256], mk.dram["lmmask"][0:128, :]))
        s_ld.dma(sy.dma_start(lmm[:, 256:512], mk.dram["lmmask"][128:256, :]))

    def proj_256(wname, bias, dsts, rhs, nstrips):
        for os_ in range(nstrips):
            sbuf_i = os_ % 2

            @block.gpsimd
            def _(g, os_=os_, sbuf_i=sbuf_i, wname=wname):
                g.wait_ge(s_mm.h, s_mm.n)
                w = mk.dram[wname][:, os_ * 128:(os_ + 1) * 128]
                s_ld.dma(g.dma_start(strip[sbuf_i][:].rearrange(
                    "dd (kd ol) -> dd kd ol", ol=128),
                    w.rearrange("(kd dd) ol -> dd kd ol", dd=128)))

            @block.tensor
            def _(t, os_=os_, sbuf_i=sbuf_i, rhs=rhs):
                t.wait_ge(s_ld.h, s_ld.n)
                s_ev.wait(t)
                s_sc.wait(t)
                for kd in range(16):
                    t.matmul(ps_a[:, 0:256], strip[sbuf_i][:, kd * 128:(kd + 1) * 128],
                             rhs[:, kd * 256:(kd + 1) * 256],
                             start=(kd == 0), stop=(kd == 15))
                fence(mk, t, s_mm)

            @block.vector
            def _(v, os_=os_, bias=bias, dsts=dsts):
                s_mm.wait(v)
                dst, base = None, None
                for dsb, b0, b1 in dsts:
                    if b0 <= os_ < b1:
                        dst, base = dsb, b0
                s_sc.inc(v.tensor_scalar(
                    dst[:, (os_ - base) * 256:(os_ - base + 1) * 256],
                    ps_a[:, 0:256], bias[:, os_:os_ + 1], None, ALU.add))

    proj_256("L1_attn_w_lm_qkv_bf", mk.b1_lmqkv,
             [(lq, 0, 16), (lk, 16, 32), (lv, 32, 48)], mk.lmfA, 48)

    for grp in range(4):
        @block.tensor
        def _(t, grp=grp):
            s_sc.wait(t)
            s_ev.wait(t)
            for jj in range(8):
                c = grp * 8 + jj
                t.transpose(ps_T[:, jj * 128:(jj + 1) * 128],
                            lv[:, c * 128:(c + 1) * 128], mk.c_ident[:])
            fence(mk, t, s_mm)

        @block.vector
        def _(v, grp=grp):
            s_mm.wait(v)
            s_ev.inc(v.tensor_copy(lvT[:, grp * 1024:(grp + 1) * 1024],
                                   ps_T[:, 0:1024]))

    for ht in range(16):
        for h2 in range(2):
            @block.tensor
            def _(t, ht=ht, h2=h2):
                s_ev.wait(t)
                s_sc.wait(t)
                for qh in range(2):
                    t.matmul(ps_a[:, qh * 256:(qh + 1) * 256],
                             lq[h2 * 64:(h2 + 1) * 64,
                                ht * 256 + qh * 128: ht * 256 + (qh + 1) * 128],
                             lk[h2 * 64:(h2 + 1) * 64, ht * 256:(ht + 1) * 256],
                             start=True, stop=True, skip_group_check=True)
                fence(mk, t, s_mm)

            @block.vector
            def _(v, h2=h2):
                s_mm.wait(v)
                s_sc.wait(v)

                def S():
                    s_ev.inc(v.drain())
                    v.wait_ge(s_ev.h, s_ev.n)
                for qh in range(2):
                    v.scalar_tensor_tensor(sc2[:, qh * 256:(qh + 1) * 256],
                                           ps_a[:, qh * 256:(qh + 1) * 256],
                                           0.125, lmm[:, qh * 256:(qh + 1) * 256],
                                           ALU.mult, ALU.add)
                vv = sc2[:].rearrange("p (q k) -> p q k", k=256)
                v.tensor_reduce(mx1[:, 0:2], vv, mybir.AxisListType.X, ALU.max)
                S()
                v.tensor_scalar(nmx1[:, 0:2], mx1[:, 0:2], -1.0, None, ALU.mult)
                S()
                bv = nmx1[:, 0:2].unsqueeze(2).broadcast_to([128, 2, 256])
                s_ev.inc(v.tensor_tensor(
                    ex2[:].rearrange("p (q k) -> p q k", k=256), vv, bv, ALU.add))

            @block.scalar
            def _(sc, h2=h2):
                s_ev.wait(sc)
                s_sc.inc(sc.activation(sc2[:], ex2[:], AF.Exp))

            @block.vector
            def _(v, h2=h2):
                s_sc.wait(v)

                def S():
                    s_ev.inc(v.drain())
                    v.wait_ge(s_ev.h, s_ev.n)
                ev_ = sc2[:].rearrange("p (q k) -> p q k", k=256)
                v.tensor_reduce(sum1[:, 0:2], ev_, mybir.AxisListType.X, ALU.add)
                S()
                v.reciprocal(rs1[:, 0:2], sum1[:, 0:2])
                S()
                rv = rs1[:, 0:2].unsqueeze(2).broadcast_to([128, 2, 256])
                s_ev.inc(v.tensor_tensor(
                    pr2[:].rearrange("p (q k) -> p q k", k=256), ev_, rv,
                    ALU.mult))

            @block.tensor
            def _(t, h2=h2):
                s_ev.wait(t)
                for c in range(4):
                    t.transpose(ps_T[:, c * 128:(c + 1) * 128],
                                pr2[:, c * 128:(c + 1) * 128], mk.c_ident[:])
                fence(mk, t, s_mm)

            @block.vector
            def _(v, h2=h2):
                s_mm.wait(v)
                s_ev.inc(v.tensor_copy(PT2[:], ps_T[:, 0:512]))

            @block.tensor
            def _(t, ht=ht, h2=h2):
                s_ev.wait(t)
                for qh in range(2):
                    for kt_ in range(2):
                        t.matmul(ps_b[h2 * 64:(h2 + 1) * 64,
                                      qh * 128:(qh + 1) * 128],
                                 lvT[:, (ht * 2 + kt_) * 128 + h2 * 64:
                                     (ht * 2 + kt_) * 128 + (h2 + 1) * 64],
                                 PT2[:, (qh * 2 + kt_) * 128:
                                     (qh * 2 + kt_ + 1) * 128],
                                 start=(kt_ == 0), stop=(kt_ == 1),
                                 skip_group_check=True)
                fence(mk, t, s_mm)

        @block.vector
        def _(v, ht=ht):
            s_mm.wait(v)
            s_ev.inc(v.tensor_copy(lo[:, ht * 256:(ht + 1) * 256],
                                   ps_b[:, 0:256]))

    for os_ in range(16):
        sbuf_i = os_ % 2

        @block.gpsimd
        def _(g, os_=os_, sbuf_i=sbuf_i):
            g.wait_ge(s_mm.h, s_mm.n)
            w = mk.dram["L1_attn_w_lm_out_bf"][:, os_ * 128:(os_ + 1) * 128]
            s_ld.dma(g.dma_start(strip[sbuf_i][:].rearrange(
                "dd (kd ol) -> dd kd ol", ol=128),
                w.rearrange("(kd dd) ol -> dd kd ol", dd=128)))

        @block.tensor
        def _(t, os_=os_, sbuf_i=sbuf_i):
            t.wait_ge(s_ld.h, s_ld.n)
            s_ev.wait(t)
            for kd in range(16):
                t.matmul(ps_a[:, 0:256], strip[sbuf_i][:, kd * 128:(kd + 1) * 128],
                         lo[:, kd * 256:(kd + 1) * 256],
                         start=(kd == 0), stop=(kd == 15))
            fence(mk, t, s_mm)

        @block.vector
        def _(v, os_=os_):
            s_mm.wait(v)
            s_ev.inc(v.scalar_tensor_tensor(
                mk.lmfB[:, os_ * 256:(os_ + 1) * 256], ps_a[:, 0:256],
                mk.b1_lmout[:, os_:os_ + 1],
                mk.lmfA[:, os_ * 256:(os_ + 1) * 256], ALU.add, ALU.add))

    proj_256("L1_attn_w_l2t_q_bf", mk.b1_l2tq, [(l2q, 0, 16)], mk.lmfB, 16)

    for ht in range(16):
        @block.sync
        def _(sy, ht=ht):
            sy.wait_ge(s_ev.h, s_ev.n)
            sy.wait_ge(s_mm.h, s_mm.n)
            s_ld.dma(sy.dma_start(kt[:], mk.dram["k1_d"][ht * 128:(ht + 1) * 128, :]))
            s_ld.dma(sy.dma_start(vT[:], mk.dram["vT1_d"][ht, :, :]))

        @block.tensor
        def _(t, ht=ht):
            t.wait_ge(s_ld.h, s_ld.n)
            s_ev.wait(t)
            s_sc.wait(t)
            for nb in range(64):
                for h2 in range(2):
                    t.matmul(ps_a[h2 * 64:(h2 + 1) * 64, nb * 4:(nb + 1) * 4],
                             kt[h2 * 64:(h2 + 1) * 64, nb * 64:(nb + 1) * 64],
                             l2q[h2 * 64:(h2 + 1) * 64,
                                 ht * 256 + nb * 4: ht * 256 + (nb + 1) * 4],
                             start=True, stop=True, skip_group_check=True)
            fence(mk, t, s_mm)

        @block.vector
        def _(v, ht=ht):
            s_mm.wait(v)
            s_ev.inc(v.tensor_scalar(rsc[:], ps_a[:, 0:256], 0.125, None,
                                     ALU.mult))

        for chb in range(2):
            @block.tensor
            def _(t, chb=chb):
                s_ev.wait(t)
                t.transpose(ps_Tf[:, 0:128], rsc[:, chb * 128:(chb + 1) * 128],
                            mk.c_identf[:])
                fence(mk, t, s_mm)

            @block.vector
            def _(v, chb=chb):
                s_mm.wait(v)

                def S():
                    s_ev.inc(v.drain())
                    v.wait_ge(s_ev.h, s_ev.n)
                v.tensor_copy(ex2[:, 128:256], ps_Tf[:, 0:128])
                S()
                vv = ex2[:, 128:256].rearrange("p (h s) -> p h s", s=64)
                v.tensor_reduce(mxs[:], vv, mybir.AxisListType.X, ALU.max)
                S()
                v.tensor_scalar(mx1[:, 0:2], mxs[:], -1.0, None, ALU.mult)
                S()
                bv = mx1[:, 0:2].unsqueeze(2).broadcast_to([128, 2, 64])
                v.tensor_tensor(ex2[:, 0:128].rearrange("p (h s) -> p h s", s=64),
                                vv, bv, ALU.add)
                S()

            @block.scalar
            def _(sc, chb=chb):
                s_ev.wait(sc)
                s_sc.inc(sc.activation(sc2[:, 0:128], ex2[:, 0:128], AF.Exp))

            @block.vector
            def _(v, chb=chb):
                s_sc.wait(v)

                def S():
                    s_ev.inc(v.drain())
                    v.wait_ge(s_ev.h, s_ev.n)
                ev_ = sc2[:, 0:128].rearrange("p (h s) -> p h s", s=64)
                v.tensor_reduce(mxs[:], ev_, mybir.AxisListType.X, ALU.add)
                S()
                v.reciprocal(sum1[:, 0:2], mxs[:])
                S()
                rv = sum1[:, 0:2].unsqueeze(2).broadcast_to([128, 2, 64])
                v.tensor_tensor(pr2[:, 0:128].rearrange("p (h s) -> p h s", s=64),
                                ev_, rv, ALU.mult)
                S()

            @block.tensor
            def _(t, chb=chb):
                s_ev.wait(t)
                t.transpose(ps_T[:, 128:256], pr2[:, 0:128], mk.c_ident[:])
                fence(mk, t, s_mm)

            @block.vector
            def _(v, chb=chb):
                s_mm.wait(v)
                s_ev.inc(v.tensor_copy(rpr[:, chb * 128:(chb + 1) * 128],
                                       ps_T[:, 128:256]))

        @block.tensor
        def _(t, ht=ht):
            s_ev.wait(t)
            t.matmul(ps_b[:, 0:256], mk.c_sw64[:], rpr[:, 0:256],
                     start=True, stop=True)
            fence(mk, t, s_mm)

        @block.vector
        def _(v, ht=ht):
            s_mm.wait(v)
            s_ev.inc(v.tensor_copy(rpw[:], ps_b[:, 0:256]))

        @block.tensor
        def _(t, ht=ht):
            s_ev.wait(t)
            for nb in range(64):
                par = nb % 2
                for h2 in range(2):
                    dst = ps_lmA if h2 == 0 else ps_lmB
                    srcpa = rpr if h2 == par else rpw
                    t.matmul(dst[:, nb * 4:(nb + 1) * 4],
                             vT[par * 64:(par + 1) * 64,
                                (nb // 2) * 128:(nb // 2 + 1) * 128],
                             srcpa[par * 64:(par + 1) * 64, nb * 4:(nb + 1) * 4],
                             start=True, stop=True, skip_group_check=True)
            fence(mk, t, s_mm)

        @block.vector
        def _(v, ht=ht):
            s_mm.wait(v)
            v.tensor_copy(ro[0:64, ht * 256:(ht + 1) * 256], ps_lmA[0:64, 0:256])
            s_ev.inc(v.tensor_copy(ro[64:128, ht * 256:(ht + 1) * 256],
                                   ps_lmB[64:128, 0:256]))

    for os_ in range(16):
        sbuf_i = os_ % 2

        @block.gpsimd
        def _(g, os_=os_, sbuf_i=sbuf_i):
            g.wait_ge(s_mm.h, s_mm.n)
            w = mk.dram["L1_attn_w_l2t_out_bf"][:, os_ * 128:(os_ + 1) * 128]
            s_ld.dma(g.dma_start(strip[sbuf_i][:].rearrange(
                "dd (kd ol) -> dd kd ol", ol=128),
                w.rearrange("(kd dd) ol -> dd kd ol", dd=128)))

        @block.tensor
        def _(t, os_=os_, sbuf_i=sbuf_i):
            t.wait_ge(s_ld.h, s_ld.n)
            s_ev.wait(t)
            for kd in range(16):
                t.matmul(ps_a[:, 0:256], strip[sbuf_i][:, kd * 128:(kd + 1) * 128],
                         ro[:, kd * 256:(kd + 1) * 256],
                         start=(kd == 0), stop=(kd == 15))
            fence(mk, t, s_mm)

        @block.vector
        def _(v, os_=os_):
            s_mm.wait(v)
            s_ev.inc(v.scalar_tensor_tensor(
                mk.lmfA[:, os_ * 256:(os_ + 1) * 256], ps_a[:, 0:256],
                mk.b1_l2to[:, os_:os_ + 1],
                mk.lmfB[:, os_ * 256:(os_ + 1) * 256], ALU.add, ALU.add))

    for os_ in range(32):
        sbuf_i = os_ % 2

        @block.gpsimd
        def _(g, os_=os_, sbuf_i=sbuf_i):
            g.wait_ge(s_mm.h, s_mm.n)
            w = mk.dram["L1_attn_w_lm_kv_bf"][:, os_ * 128:(os_ + 1) * 128]
            s_ld.dma(g.dma_start(strip[sbuf_i][:].rearrange(
                "dd (kd ol) -> dd kd ol", ol=128),
                w.rearrange("(kd dd) ol -> dd kd ol", dd=128)))

        @block.tensor
        def _(t, os_=os_, sbuf_i=sbuf_i):
            t.wait_ge(s_ld.h, s_ld.n)
            s_sc.wait(t)
            s_ev.wait(t)
            for kd in range(16):
                t.matmul(ps_a[:, 0:256], strip[sbuf_i][:, kd * 128:(kd + 1) * 128],
                         mk.lmfA[:, kd * 256:(kd + 1) * 256],
                         start=(kd == 0), stop=(kd == 15))
            fence(mk, t, s_mm)

        @block.scalar
        def _(sc, os_=os_):
            s_mm.wait(sc)
            dst = mk.gk_sb if os_ < 16 else mk.gv_sb
            b0 = os_ % 16
            s_sc.inc(sc.activation(dst[:, b0 * 256:(b0 + 1) * 256],
                                   ps_a[:, 0:256], AF.Identity,
                                   bias=mk.b1_lmkv[:, os_:os_ + 1]))

    if "lmf_d" in mk.dbg:
        @block.sync
        def _(sy):
            sy.wait_ge(s_sc.h, s_sc.n)
            sy.wait_ge(s_ev.h, s_ev.n)
            dd = mk.dram["lmf_d"]
            s_ld.dma(sy.dma_start(dd[0:128, :], mk.lmfA[:]))
            s_ld.dma(sy.dma_start(dd[128:256, :], mk.lmfB[:]))
            s_ld.dma(sy.dma_start(dd[256:384, :], mk.gk_sb[:]))
            s_ld.dma(sy.dma_start(dd[384:512, :], mk.gv_sb[:]))
            s_ld.dma(sy.dma_start(dd[512:640, :], lo[:]))
            s_ld.dma(sy.dma_start(dd[640:768, :], ro[:]))

    mk.l1lm_done = [s_sc, s_ev, s_mm, s_ld]
    ph.close()
    phase_end(mk, mk.l1lm_done)


def emit_l1_glob(mk, block, parts=None):
    nc = mk.nc
    ph = contextlib.ExitStack()
    sb = lambda n, s, d: ph.enter_context(nc.sbuf_tensor(n, list(s), d))
    phase_start(mk, block)

    gate_sb = sb("gl_gate", [32, SEQ], BF16)
    selht = sb("gl_selht", [32, 16 * 128], BF16)
    gmsk = sb("gl_gmsk", [128, 32 * 256], F32)
    xnf = sb("gl_xnf", [128, 16 * 512], BF16)
    qt = [sb(f"gl_qt{i}", [128, SEQ], BF16) for i in range(2)]
    locb = [sb(f"gl_locb{i}", [128, SEQ], BF16) for i in range(2)]
    gvT = [sb(f"gl_gvT{i}", [128, 256], BF16) for i in range(2)]
    gsc = sb("gl_gsc", [128, 1024], F32)
    ge = sb("gl_ge", [128, 1024], F32)
    ge2 = sb("gl_ge2", [128, 1024], F32)
    gp = [sb(f"gl_gp{i}", [128, 1024], BF16) for i in range(2)]
    gpT = [sb(f"gl_gpT{i}", [128, 1024], BF16) for i in range(2)]
    dsb = sb("gl_dsb", [128, 128], F32)
    msb = sb("gl_msb", [128, 128], F32)
    att = sb("gl_att", [128, 512], BF16)
    mx1 = sb("gl_mx1", [128, 4], F32)
    nmx = sb("gl_nmx", [128, 4], F32)
    sm1 = sb("gl_sm1", [128, 4], F32)
    rs1 = sb("gl_rs1", [128, 4], F32)
    wg_sb = sb("gl_wg", [128, 16 * 32], BF16)
    ps_s = [[ph.enter_context(nc.psum_tensor(f"gl_pss{i}{j}", [128, 512], F32))
             for j in range(2)] for i in range(2)]
    ps_T = ph.enter_context(nc.psum_tensor("gl_psT", [128, 1024], BF16))
    ps_og = ph.enter_context(nc.psum_tensor("gl_psog", [128, 512], F32))
    ps_gb = ph.enter_context(nc.psum_tensor("gl_psgb", [128, 512], F32))
    s_ld = mk.sem("gl_ld")
    s_mm = mk.sem("gl_mm")
    s_sc = mk.sem("gl_sc")
    s_ev = mk.sem("gl_ev")
    s_st = mk.sem("gl_st")

    @block.gpsimd
    def _(g):
        s_ld.dma(g.dma_start(selht[0:32, :], mk.dram["selht"][:, :]))

    @block.sync
    def _(sy):
        s_ld.dma(sy.dma_start(
            gmsk[:].rearrange("p (tt nl) -> p tt nl", nl=256),
            mk.dram["gmask"][:, :].rearrange("(tt p) nl -> p tt nl", p=128)))
        s_ld.dma(sy.dma_start(
            wg_sb[:].rearrange("p (kd o) -> p kd o", o=32),
            mk.dram["L1_attn_w_gate_bf"][:, :].rearrange("(kd p) o -> p kd o", p=128)))

    for nch in range(8):
        @block.sync
        def _(sy, nch=nch):
            sy.wait_ge(s_mm.h, s_mm.n)
            for kd in range(16):
                s_ld.dma(sy.dma_start(
                    xnf[:, kd * 512:(kd + 1) * 512],
                    mk.dram["xn1"][nch * 512:(nch + 1) * 512,
                                   kd * 128:(kd + 1) * 128], transpose=True))

        @block.tensor
        def _(t, nch=nch):
            t.wait_ge(s_ld.h, s_ld.n)
            s_sc.wait(t)
            for kd in range(16):
                t.matmul(ps_s[0][0][0:32, 0:512],
                         wg_sb[:, kd * 32:(kd + 1) * 32],
                         xnf[:, kd * 512:(kd + 1) * 512],
                         start=(kd == 0), stop=(kd == 15))
            fence(mk, t, s_mm)

        @block.scalar
        def _(sc, nch=nch):
            s_mm.wait(sc)
            s_sc.inc(sc.activation(gate_sb[0:32, nch * 512:(nch + 1) * 512],
                                   ps_s[0][0][0:32, 0:512], AF.Sigmoid,
                                   bias=mk.b1_gate[0:32, 0:1]))

    sc_fence = {}
    c1_done = {}
    e_done = {}
    p_done = {}
    tr_fence = {}
    gpT_done = {}
    og_fence = {}
    cmb_done = {}
    st_done = {}
    ld_mark = {}
    gvT_done = {}
    gvT_last = [0]

    def prefetch(ht):
        hb = ht % 2

        @block.sync
        def _(sy, ht=ht, hb=hb):
            gq = (ht - 2) * 16 + 15
            if gq in sc_fence:
                sy.wait_ge(s_mm.h, sc_fence[gq])
            if gq in cmb_done:
                sy.wait_ge(s_ev.h, cmb_done[gq])
            s_ld.dma(sy.dma_start(qt[hb][:], mk.dram["q1_d"][ht * 128:(ht + 1) * 128, :]))
            s_ld.dma(sy.dma_start(locb[hb][:], mk.dram["loc_d"][ht * 128:(ht + 1) * 128, :]))
            ld_mark[ht] = s_ld.n

    prefetch(0)
    prefetch(1)

    NG = 16 * 16
    for g in range(NG + 1):
        ht, ttp = g // 16, g % 16
        hb = ht % 2

        if g < NG and ttp == 1 and 2 <= ht + 1 < 16:
            prefetch(ht + 1)

        if g < NG:
            if ttp == 0:
                @block.tensor
                def _(t, ht=ht, hb=hb, g=g):
                    gq = (ht - 2) * 16 + 15
                    if gq in og_fence:
                        t.wait_ge(s_mm.h, og_fence[gq])
                    if g - 1 in gpT_done:
                        t.wait_ge(s_sc.h, gpT_done[g - 1])
                    s_sc.wait(t)
                    for kt_ in range(2):
                        t.transpose(ps_T[:, kt_ * 128:(kt_ + 1) * 128],
                                    mk.gv_sb[:, ht * 256 + kt_ * 128:
                                             ht * 256 + (kt_ + 1) * 128],
                                    mk.c_ident[:])
                    fence(mk, t, s_mm)
                    gvT_done[ht] = s_mm.n

                @block.scalar
                def _(sc, ht=ht, hb=hb):
                    sc.wait_ge(s_mm.h, gvT_done[ht])
                    sc.activation(gvT[hb][:], ps_T[:, 0:256], AF.Copy)
                    s_sc.inc(sc.drain())
                    gvT_done[ht] = ("sc", s_sc.n)
                    gvT_last[0] = s_sc.n

            @block.tensor
            def _(t, g=g, ht=ht, ttp=ttp, hb=hb):
                t.wait_ge(s_ld.h, ld_mark[ht])
                if g == 0:
                    s_sc.wait(t)
                if g - 2 in c1_done:
                    t.wait_ge(s_ev.h, c1_done[g - 2])
                first = True
                for tt2 in range(2):
                    tt = ttp * 2 + tt2
                    for h2 in range(2):
                        if not first:
                            t.drain()
                        first = False
                        t.matmul(ps_s[g % 2][tt2][:, h2 * 256:(h2 + 1) * 256],
                                 qt[hb][h2 * 64:(h2 + 1) * 64,
                                        tt * 128:(tt + 1) * 128],
                                 mk.gk_sb[h2 * 64:(h2 + 1) * 64,
                                          ht * 256:(ht + 1) * 256],
                                 start=True, stop=True, skip_group_check=True)
                fence(mk, t, s_mm)
                sc_fence[g] = s_mm.n

            @block.vector
            def _(v, g=g, ht=ht, ttp=ttp):
                v.wait_ge(s_mm.h, sc_fence[g])
                if g - 1 in e_done:
                    v.wait_ge(s_sc.h, e_done[g - 1])

                def S():
                    s_ev.inc(v.drain())
                    v.wait_ge(s_ev.h, s_ev.n)
                for tt2 in range(2):
                    tt = ttp * 2 + tt2
                    for h2 in range(2):
                        v.scalar_tensor_tensor(
                            gsc[:, (tt2 * 2 + h2) * 256:(tt2 * 2 + h2 + 1) * 256],
                            ps_s[g % 2][tt2][:, h2 * 256:(h2 + 1) * 256],
                            0.125, gmsk[:, tt * 256:(tt + 1) * 256],
                            ALU.mult, ALU.add)
                vv = gsc[:].rearrange("p (c k) -> p c k", k=256)
                v.tensor_reduce(mx1[:], vv, mybir.AxisListType.X, ALU.max)
                S()
                v.tensor_scalar(nmx[:], mx1[:], -1.0, None, ALU.mult)
                S()
                bv = nmx[:].unsqueeze(2).broadcast_to([128, 4, 256])
                s_ev.inc(v.tensor_tensor(ge[:].rearrange("p (c k) -> p c k", k=256),
                                         vv, bv, ALU.add))
                c1_done[g] = s_ev.n

            @block.scalar
            def _(sc, g=g):
                sc.wait_ge(s_ev.h, c1_done[g])
                if g - 1 in p_done:
                    sc.wait_ge(s_ev.h, p_done[g - 1])
                s_sc.inc(sc.activation(ge2[:], ge[:], AF.Exp))
                e_done[g] = s_sc.n

            @block.vector
            def _(v, g=g, ttp=ttp):
                v.wait_ge(s_sc.h, e_done[g])
                if g - 2 in tr_fence:
                    v.wait_ge(s_mm.h, tr_fence[g - 2])

                def S():
                    s_ev.inc(v.drain())
                    v.wait_ge(s_ev.h, s_ev.n)
                ev_ = ge2[:].rearrange("p (c k) -> p c k", k=256)
                v.tensor_reduce(sm1[:], ev_, mybir.AxisListType.X, ALU.add)
                S()
                v.reciprocal(rs1[:], sm1[:])
                S()
                rv = rs1[:].unsqueeze(2).broadcast_to([128, 4, 256])
                v.tensor_tensor(gp[g % 2][:].rearrange("p (c k) -> p c k", k=256),
                                ev_, rv, ALU.mult)
                if ttp == 0:
                    s_ev.inc(v.drain())
                    v.wait_ge(s_ev.h, s_ev.n)
                    s_ev.inc(v.memset(gp[g % 2][0:64, 0:512], 0.0))
                else:
                    s_ev.inc(v.drain())
                p_done[g] = s_ev.n

        p = g - 1
        if p < 0:
            continue
        pht, pttp = p // 16, p % 16
        phb = pht % 2

        gvl = gvT_last[0]

        @block.tensor
        def _(t, p=p, gvl=gvl):
            t.wait_ge(s_ev.h, p_done[p])
            if p - 1 in gpT_done:
                t.wait_ge(s_sc.h, gpT_done[p - 1])
            t.wait_ge(s_sc.h, gvl)
            for c in range(8):
                t.transpose(ps_T[:, c * 128:(c + 1) * 128],
                            gp[p % 2][:, c * 128:(c + 1) * 128], mk.c_ident[:])
            fence(mk, t, s_mm)
            tr_fence[p] = s_mm.n

        @block.scalar
        def _(sc, p=p):
            sc.wait_ge(s_mm.h, tr_fence[p])
            if p - 2 in og_fence:
                sc.wait_ge(s_mm.h, og_fence[p - 2])
            s_sc.inc(sc.activation(gpT[p % 2][:], ps_T[:, 0:1024], AF.Copy))
            gpT_done[p] = s_sc.n

        @block.tensor
        def _(t, p=p, pht=pht, pttp=pttp, phb=phb):
            t.wait_ge(s_sc.h, gpT_done[p])
            t.wait_ge(s_sc.h, gvT_done[pht][1])
            if p - 1 in cmb_done:
                t.wait_ge(s_ev.h, cmb_done[p - 1])
            for tt2 in range(2):
                for h2 in range(2):
                    for kt_ in range(2):
                        c = (tt2 * 2 + h2) * 2 + kt_
                        t.matmul(ps_og[h2 * 64:(h2 + 1) * 64,
                                       tt2 * 128:(tt2 + 1) * 128],
                                 gvT[phb][:, kt_ * 128 + h2 * 64:
                                       kt_ * 128 + (h2 + 1) * 64],
                                 gpT[p % 2][:, c * 128:(c + 1) * 128],
                                 start=(kt_ == 0), stop=(kt_ == 1),
                                 skip_group_check=True)
            t.matmul(ps_gb[:, 0:256], selht[0:32, pht * 128:(pht + 1) * 128],
                     gate_sb[0:32, pttp * 256:(pttp + 1) * 256],
                     start=True, stop=True, skip_group_check=True)
            fence(mk, t, s_mm)
            og_fence[p] = s_mm.n

        @block.vector
        def _(v, p=p, pht=pht, pttp=pttp, phb=phb):
            v.wait_ge(s_mm.h, og_fence[p])
            if p % 2 == 0 and p - 1 in st_done:
                v.wait_ge(s_st.h, st_done[p - 1])
            for tt2 in range(2):
                tt = pttp * 2 + tt2
                lsl = locb[phb][:, tt * 128:(tt + 1) * 128]
                v.tensor_tensor(dsb[:], ps_og[:, tt2 * 128:(tt2 + 1) * 128],
                                lsl, ALU.subtract)
                v.tensor_tensor(msb[:], dsb[:],
                                ps_gb[:, tt2 * 128:(tt2 + 1) * 128], ALU.mult)
                v.tensor_tensor(att[:, (p % 2) * 256 + tt2 * 128:
                                    (p % 2) * 256 + (tt2 + 1) * 128],
                                msb[:], lsl, ALU.add)
            s_ev.inc(v.drain())
            cmb_done[p] = s_ev.n

        if p % 2 == 1:
            @block.sync
            def _(sy, p=p, pht=pht, pttp=pttp):
                sy.wait_ge(s_ev.h, cmb_done[p])
                c0 = (pttp - 1) * 256
                s_st.dma(sy.dma_start(
                    mk.dram["att1_d"][pht * 128:(pht + 1) * 128, c0:c0 + 512],
                    att[:]))
                st_done[p] = s_st.n

    mk.l1glob_done = [s_sc, s_ev, s_mm, s_ld, s_st]
    ph.close()
    phase_end(mk, mk.l1glob_done)


def emit_w2prep(mk, block):
    nc = mk.nc
    W2p = mk.dtmp("W2p_bf", [D, 8192], BF16)
    ph = contextlib.ExitStack()
    sb = lambda n, s, d: ph.enter_context(nc.sbuf_tensor(n, list(s), d))
    phase_start(mk, block)
    bufA = sb("wp_bufA", [128, 8192], F32)
    bufB = sb("wp_bufB", [128, 8192], F32)
    obuf = sb("wp_obuf", [128, 8192], BF16)
    s_ld = mk.sem("wp_ld")
    s_ev = mk.sem("wp_ev")
    s_st = mk.sem("wp_st")

    for ot in range(16):
        @block.sync
        def _(sy, ot=ot):
            if ot >= 1:
                sy.wait_ge(s_ev.h, s_ev.n)
            s_ld.dma(sy.dma_start(bufA[:], mk.dram["L1_ffn_W2"][ot * 128:(ot + 1) * 128, :]))

        @block.vector
        def _(v, ot=ot):
            v.wait_ge(s_ld.h, s_ld.n)
            if ot >= 1:
                v.wait_ge(s_st.h, s_st.n)

            def S():
                s_ev.inc(v.drain())
                v.wait_ge(s_ev.h, s_ev.n)
            cur = bufA
            for st in range(6):
                Bs = 128 << st
                G = 8192 // (2 * Bs)
                dst = obuf if st == 5 else (bufB if cur is bufA else bufA)
                vi = cur[:].rearrange("p (g j b) -> p g j b", j=2, b=Bs)
                vo = dst[:].rearrange("p (g j b) -> p g j b", j=2, b=Bs)
                v.tensor_tensor(vo[:, :, 0, :], vi[:, :, 0, :], vi[:, :, 1, :],
                                ALU.add)
                S()
                v.tensor_tensor(vo[:, :, 1, :], vi[:, :, 0, :], vi[:, :, 1, :],
                                ALU.subtract)
                S()
                cur = dst

        @block.sync
        def _(sy, ot=ot):
            sy.wait_ge(s_ev.h, s_ev.n)
            s_st.dma(sy.dma_start(W2p[ot * 128:(ot + 1) * 128, :], obuf[:]))

    mk.w2p_done = [s_st, s_ev, s_ld]
    ph.close()
    phase_end(mk, mk.w2p_done)


def emit_l1_ffn(mk, block):
    nc = mk.nc
    hid_d = mk.dtmp("hid_d", [8192, SEQ], BF16)
    ph = contextlib.ExitStack()
    sb = lambda n, s, d: ph.enter_context(nc.sbuf_tensor(n, list(s), d))
    phase_start(mk, block)
    xnf = [sb(f"f_xnf{i}", [128, 16 * 512], BF16) for i in range(2)]
    w1s = [sb(f"f_w1s{i}", [128, 16 * 128], BF16) for i in range(2)]
    w3s = [sb(f"f_w3s{i}", [128, 16 * 128], BF16) for i in range(2)]
    sU = [sb(f"f_sU{i}", [128, 512], BF16) for i in range(2)]
    hidb = [sb(f"f_hid{i}", [128, 512], BF16) for i in range(2)]
    hid2 = [sb(f"f_hid2{i}", [128, 512], BF16) for i in range(2)]
    psU = [ph.enter_context(nc.psum_tensor(f"f_psU{i}", [128, 512], F32))
           for i in range(2)]
    psG = [ph.enter_context(nc.psum_tensor(f"f_psG{i}", [128, 512], F32))
           for i in range(2)]
    psH = [ph.enter_context(nc.psum_tensor(f"f_psH{i}", [128, 512], F32))
           for i in range(2)]
    s_ld = mk.sem("f_ld")
    s_ws = mk.sem("f_ws")
    s_mm = mk.sem("f_mm")
    s_sc = mk.sem("f_sc")
    s_ev = mk.sem("f_ev")
    s_st = mk.sem("f_st")
    ug_fence = {}
    h_fence = {}
    sil_done = {}
    tt_done = {}
    ev2_done = {}
    st_done = {}

    def emit_h128(t, it):
        b = it % 2
        t.wait_ge(s_ev.h, tt_done[it])
        if it - 2 in ev2_done:
            t.wait_ge(s_ev.h, ev2_done[it - 2])
        t.matmul(psH[b][:, :], mk.c_h128[:], hidb[b][:],
                 start=True, stop=True)
        fence(mk, t, s_mm)
        h_fence[it] = s_mm.n

    for ns in range(8):
        @block.sync
        def _(sy, ns=ns):
            xb = ns % 2
            gate = ns * 64 - 65
            if gate in ug_fence:
                sy.wait_ge(s_mm.h, ug_fence[gate])
            for kd in range(16):
                s_ld.dma(sy.dma_start(
                    xnf[xb][:, kd * 512:(kd + 1) * 512],
                    mk.dram["xn1"][ns * 512:(ns + 1) * 512,
                                   kd * 128:(kd + 1) * 128], transpose=True))

        for os_ in range(64):
            b = os_ % 2
            xb = ns % 2
            it = ns * 64 + os_

            @block.gpsimd
            def _(g, ns=ns, os_=os_, b=b, it=it):
                if it - 2 in ug_fence:
                    g.wait_ge(s_mm.h, ug_fence[it - 2])
                for nm, dstb in (("wsc_L1_ffn_W1", w1s), ("wsc_L1_ffn_W3", w3s)):
                    w = mk.dram[nm][os_, :, :, :].rearrange("kd dd ol -> dd kd ol")
                    s_ws.dma(g.dma_start(dstb[b][:].rearrange(
                        "dd (kd ol) -> dd kd ol", ol=128), w))

            @block.tensor
            def _(t, ns=ns, os_=os_, b=b, xb=xb, it=it):
                t.wait_ge(s_ws.h, s_ws.n)
                t.wait_ge(s_ld.h, s_ld.n)
                if it - 2 in sil_done:
                    t.wait_ge(s_sc.h, sil_done[it - 2])
                if it - 2 in tt_done:
                    t.wait_ge(s_ev.h, tt_done[it - 2])
                for kd in range(16):
                    t.matmul(psU[b][:, :], w1s[b][:, kd * 128:(kd + 1) * 128],
                             xnf[xb][:, kd * 512:(kd + 1) * 512],
                             start=(kd == 0), stop=(kd == 15))
                for kd in range(16):
                    t.matmul(psG[b][:, :], w3s[b][:, kd * 128:(kd + 1) * 128],
                             xnf[xb][:, kd * 512:(kd + 1) * 512],
                             start=(kd == 0), stop=(kd == 15))
                fence(mk, t, s_mm)
                ug_fence[it] = s_mm.n
                if it - 1 in tt_done:
                    emit_h128(t, it - 1)

            @block.scalar
            def _(sc, ns=ns, os_=os_, b=b, it=it):
                sc.wait_ge(s_mm.h, ug_fence[it])
                if it - 2 in tt_done:
                    sc.wait_ge(s_ev.h, tt_done[it - 2])
                s_sc.inc(sc.activation(sU[b][:], psU[b][:, :], AF.Silu))
                sil_done[it] = s_sc.n

            @block.vector
            def _(v, ns=ns, os_=os_, b=b, it=it):
                v.wait_ge(s_sc.h, sil_done[it])
                if it - 2 in h_fence:
                    v.wait_ge(s_mm.h, h_fence[it - 2])
                s_ev.inc(v.tensor_tensor(hidb[b][:], sU[b][:], psG[b][:, :],
                                         ALU.mult))
                tt_done[it] = s_ev.n

            @block.vector
            def _(v, ns=ns, os_=os_, it=it):
                pit = it - 1
                if pit not in tt_done or pit < 0:
                    return
                pb = pit % 2
                v.wait_ge(s_mm.h, h_fence.get(pit, 0))
                if pit - 2 in st_done:
                    v.wait_ge(s_st.h, st_done[pit - 2])
                s_ev.inc(v.tensor_copy(hid2[pb][:], psH[pb][:, :]))
                ev2_done[pit] = s_ev.n

            @block.sync
            def _(sy, ns=ns, os_=os_, it=it):
                pit = it - 1
                if pit not in ev2_done:
                    return
                pns, pos = pit // 64, pit % 64
                pb = pit % 2
                sy.wait_ge(s_ev.h, ev2_done[pit])
                s_st.dma(sy.dma_start(
                    mk.dram["hid_d"][pos * 128:(pos + 1) * 128,
                                     pns * 512:(pns + 1) * 512], hid2[pb][:]))
                st_done[pit] = s_st.n

    last = 8 * 64 - 1

    @block.tensor
    def _(t):
        emit_h128(t, last)

    @block.vector
    def _(v):
        pb = last % 2
        v.wait_ge(s_mm.h, h_fence[last])
        s_ev.inc(v.tensor_copy(hid2[pb][:], psH[pb][:, :]))
        ev2_done[last] = s_ev.n

    @block.sync
    def _(sy):
        pb = last % 2
        sy.wait_ge(s_ev.h, ev2_done[last])
        s_st.dma(sy.dma_start(
            mk.dram["hid_d"][63 * 128:64 * 128, 7 * 512:8 * 512], hid2[pb][:]))
        st_done[last] = s_st.n

    mk.ffn_done = [s_st, s_ev, s_sc, s_mm, s_ws, s_ld]
    ph.close()
    phase_end(mk, mk.ffn_done)


def emit_l1_ffn2(mk, block):
    nc = mk.nc
    g1_d = mk.dtmp("g1_d", [D, SEQ], BF16)
    ph = contextlib.ExitStack()
    sb = lambda n, s, d: ph.enter_context(nc.sbuf_tensor(n, list(s), d))
    phase_start(mk, block)
    hch = sb("f2_hch", [128, 64 * 512], BF16)
    w2s = [sb(f"f2_w2s{i}", [128, 64 * 128], BF16) for i in range(2)]
    g1b = [sb(f"f2_g1b{i}", [128, 512], BF16) for i in range(2)]
    psW = [ph.enter_context(nc.psum_tensor(f"f2_psW{i}", [128, 512], F32))
           for i in range(2)]
    s_ld = mk.sem("f2_ld")
    s_ws = mk.sem("f2_ws")
    s_mm = mk.sem("f2_mm")
    s_sc = mk.sem("f2_sc")
    s_st = mk.sem("f2_st")
    mm_fence = {}
    sil2_done = {}
    st2_done = {}

    for nch in range(8):
        @block.sync
        def _(sy, nch=nch):
            sy.wait_ge(s_mm.h, s_mm.n)
            s_ld.dma(sy.dma_start(
                hch[:].rearrange("p (kd j) -> p kd j", j=512),
                mk.dram["hid_d"][:, nch * 512:(nch + 1) * 512].rearrange(
                    "(kd p) j -> p kd j", p=128)))

        for os_ in range(16):
            b = os_ % 2
            it = nch * 16 + os_

            @block.gpsimd
            def _(g, nch=nch, os_=os_, b=b, it=it):
                if it - 2 in mm_fence:
                    g.wait_ge(s_mm.h, mm_fence[it - 2])
                w = mk.dram["wsc_W2p"][os_, :, :, :].rearrange("kd dd ol -> dd kd ol")
                s_ws.dma(g.dma_start(w2s[b][:].rearrange(
                    "dd (kd ol) -> dd kd ol", ol=128), w))

            @block.tensor
            def _(t, nch=nch, os_=os_, b=b, it=it):
                t.wait_ge(s_ws.h, s_ws.n)
                t.wait_ge(s_ld.h, s_ld.n)
                if it - 2 in sil2_done:
                    t.wait_ge(s_sc.h, sil2_done[it - 2])
                for kd in range(64):
                    t.matmul(psW[b][:, :], w2s[b][:, kd * 128:(kd + 1) * 128],
                             hch[:, kd * 512:(kd + 1) * 512],
                             start=(kd == 0), stop=(kd == 63))
                fence(mk, t, s_mm)
                mm_fence[it] = s_mm.n

            @block.scalar
            def _(sc, nch=nch, os_=os_, b=b, it=it):
                sc.wait_ge(s_mm.h, mm_fence[it])
                if it - 2 in st2_done:
                    sc.wait_ge(s_st.h, st2_done[it - 2])
                s_sc.inc(sc.activation(g1b[b][:], psW[b][:, :], AF.Silu))
                sil2_done[it] = s_sc.n

            @block.sync
            def _(sy, nch=nch, os_=os_, b=b, it=it):
                sy.wait_ge(s_sc.h, sil2_done[it])
                s_st.dma(sy.dma_start(
                    mk.dram["g1_d"][os_ * 128:(os_ + 1) * 128,
                                    nch * 512:(nch + 1) * 512], g1b[b][:]))
                st2_done[it] = s_st.n

    mk.ffn2_done = [s_st, s_sc, s_mm, s_ws, s_ld]
    ph.close()
    phase_end(mk, mk.ffn2_done)


def emit_l1_wo(mk, block):
    nc = mk.nc
    sg1_d = mk.dtmp("sg1_d", [D, SEQ], BF16)
    ph = contextlib.ExitStack()
    sb = lambda n, s, d: ph.enter_context(nc.sbuf_tensor(n, list(s), d))
    phase_start(mk, block)
    wot = sb("o_wot", [128, 16 * 2048], BF16)
    a1c = [sb(f"o_a1{i}", [128, 16 * 512], BF16) for i in range(2)]
    g1c = [sb(f"o_g1{i}", [128, 16 * 512], BF16) for i in range(2)]
    sgc = [sb(f"o_sg{i}", [128, 16 * 512], BF16) for i in range(2)]
    ps = [ph.enter_context(nc.psum_tensor(f"o_ps{i}", [128, 512], F32))
          for i in range(2)]
    s_ld = mk.sem("o_ld")
    s_mm = mk.sem("o_mm")
    s_ev = mk.sem("o_ev")
    s_st = mk.sem("o_st")

    @block.sync
    def _(sy):
        for kd in range(16):
            s_ld.dma(sy.dma_start(
                wot[:, kd * 2048:(kd + 1) * 2048],
                mk.dram["L1_attn_w_out_bf"][kd * 128:(kd + 1) * 128, 0:2048]))
    wot_ready = s_ld.n

    for nch in range(8):
        cb = nch % 2
        n0 = nch * 512

        @block.sync
        def _(sy, nch=nch, cb=cb, n0=n0):
            if nch >= 2:
                sy.wait_ge(s_st.h, 16 * (nch - 1))
                sy.wait_ge(s_mm.h, s_mm.n)
            src = mk.dram["att1_d"].rearrange("(h dd) n -> dd h n", dd=128)
            s_ld.dma(sy.dma_start(
                a1c[cb][:].rearrange("p (h t) -> p h t", h=16),
                src[:, :, n0:n0 + 512]))
            src2 = mk.dram["g1_d"].rearrange("(os dd) n -> dd os n", dd=128)
            s_ld.dma(sy.dma_start(
                g1c[cb][:].rearrange("p (os t) -> p os t", os=16),
                src2[:, :, n0:n0 + 512]))

        for os_ in range(16):
            @block.tensor
            def _(t, nch=nch, cb=cb, os_=os_):
                if nch == 0 and os_ == 0:
                    t.wait_ge(s_ld.h, wot_ready)
                t.wait_ge(s_ld.h, s_ld.n)
                pb = os_ % 2
                if not (nch == 0 and os_ <= 1):
                    t.wait_ge(s_ev.h, s_ev.n - 1)
                for kd in range(16):
                    t.matmul(ps[pb][:, :],
                             wot[:, kd * 2048 + os_ * 128: kd * 2048 + (os_ + 1) * 128],
                             a1c[cb][:, kd * 512:(kd + 1) * 512],
                             start=(kd == 0), stop=(kd == 15))
                fence(mk, t, s_mm)

            @block.vector
            def _(v, nch=nch, cb=cb, os_=os_):
                s_mm.wait(v)
                pb = os_ % 2
                s_ev.inc(v.scalar_tensor_tensor(
                    sgc[cb][:, os_ * 512:(os_ + 1) * 512], ps[pb][:, :],
                    mk.b1_out[:, os_:os_ + 1], g1c[cb][:, os_ * 512:(os_ + 1) * 512],
                    ALU.add, ALU.mult))

        @block.sync
        def _(sy, nch=nch, cb=cb, n0=n0):
            sy.wait_ge(s_ev.h, s_ev.n)
            dst = mk.dram["sg1_d"].rearrange("(os dd) n -> dd os n", dd=128)
            s_st.dma(sy.dma_start(
                dst[:, :, n0:n0 + 512],
                sgc[cb][:].rearrange("p (os t) -> p os t", os=16)))

    mk.wo_done = [s_st, s_ev, s_mm, s_ld]
    ph.close()
    phase_end(mk, mk.wo_done)


def build_program(dbg=(), stages=("prep", "ln0")):
    mk = MK(dbg=dbg)
    nc = mk.nc
    mk.din("x", [SEQ, D], F32)
    for n, shp in INPUT_SHAPES.items():
        if n != "L0_attn_hidden_states":
            mk.din(n, shp, F32)
    for n, arr in HOST_CONSTS.items():
        mk.din(n, list(arr.shape), F32)
    out = mk.dout("out", [SEQ, D], F32)

    mk.dtmp("xn0", [SEQ, D], BF16)
    mk.dtmp("qr_d", [24, 128, SEQ], BF16)
    mk.dtmp("Vt", [8, SEQ, 128], BF16)
    mk.dtmp("g0_d", [D, SEQ], BF16)
    mk.dtmp("f1q_d", [128, 8], F32)
    mk.dtmp("O2_d", [D, SEQ], BF16)
    mk.dtmp("sg_d", [D, SEQ], BF16)
    mk.dtmp("x1_d", [SEQ, D], F32)
    mk.dtmp("xn1", [SEQ, D], BF16)
    mk.dtmp("q1_d", [D, SEQ], BF16)
    mk.dtmp("k1_d", [D, SEQ], BF16)
    mk.dtmp("v1_d", [D, SEQ], BF16)
    mk.dtmp("loc_d", [D, SEQ], BF16)
    mk.dtmp("att1_d", [D, SEQ], BF16)
    mk.dtmp("vT1_d", [16, 128, SEQ], BF16)
    if "wfm_d" in mk.dbg:
        mk.dtmp("wfm_d", [8, SEQ], BF16)
    if "f1dbg_d" in mk.dbg:
        mk.dtmp("f1dbg_d", [128, 32], F32)
    if "f1all_d" in mk.dbg:
        mk.dtmp("f1all_d", [SEQ, 24], F32)
    if "xnf_d" in mk.dbg:
        mk.dtmp("xnf_d", [128, 16 * SEQ], BF16)
    if "axw_d" in mk.dbg:
        mk.dtmp("axw_d", [128, SEQ], BF16)
    if "ln_stats" in mk.dbg:
        mk.dtmp("ln_stats", [SEQ, 4], F32)
    if "lmf_d" in mk.dbg:
        mk.dtmp("lmf_d", [768, 4096], BF16)

    mk.c_eps = None

    mk.s_wprep = mk.sem("wprep")
    mk.s_const = mk.sem("constld")

    with nc.Block() as block:
        mk.c_eps = mk.sb("c_eps", [128, 1], F32)
        mk.zero_col = mk.sb("c_zero", [128, 1], F32)
        mk.s_eps = mk.sem("eps")

        @block.vector
        def _(v):
            v.memset(mk.zero_col[:], 0.0)
            mk.s_eps.inc(v.memset(mk.c_eps[:], 1e-5))

        if "prep" in stages:
            emit_weight_prep(mk, block)
        emit_const_load(mk, block)

        if "ln0" in stages:
            mk.ln_done = emit_layernorm(mk, block, "x", "xn0")
        if "wtrans" in stages:
            emit_wtrans(mk, block, ["L0_attn_Wq", "L0_attn_Wk", "L0_attn_Wv",
                                    "L0_attn_Wo", "L0_ffn_weight",
                                    "L1_ffn_W1", "L1_ffn_W3"])
        l0sp_st = [s for s in stages if s.startswith("l0sp")]
        if l0sp_st:
            spec = l0sp_st[0]
            parts = tuple(spec.split(":", 1)[1].split(",")) if ":" in spec else ("ld", "rw", "la", "bt")
            emit_l0_prep_small(mk, block, parts=parts)
        l0a_st = [s for s in stages if s.startswith("l0a")]
        if l0a_st:
            spec = l0a_st[0]
            parts = tuple(spec.split(":", 1)[1].split(",")) if ":" in spec else ("qk", "v", "f1", "f2", "f3")
            emit_l0_passA(mk, block, parts=parts)
        if "l0b" in stages:
            emit_l0_passB(mk, block)
        if "l0c" in stages:
            emit_l0_passC(mk, block)
        if "cmb0" in stages:
            emit_combine(mk, block, "x", "sg_d", "x1_d")
        if "ln1" in stages:
            mk.ln1_done = emit_layernorm(mk, block, "x1_d", "xn1")
        if "l1sp" in stages:
            emit_l1_sp(mk, block)
        if "l1qkv" in stages:
            emit_l1_qkv(mk, block)
        if "l1loc" in stages:
            emit_l1_loc(mk, block)
        if "l1lm" in stages:
            emit_l1_lm(mk, block)
        if "w2p" in stages:
            emit_w2prep(mk, block)
            emit_wtrans(mk, block, ["W2p"], tag="2")
        l1g_st = [s for s in stages if s.startswith("l1glob")]
        if l1g_st:
            spec = l1g_st[0]
            gparts = tuple(spec.split(":", 1)[1].split(",")) if ":" in spec else ("gate", "gv", "sc", "out", "gb", "cmb")
            emit_l1_glob(mk, block, parts=gparts)
        if hasattr(mk, "attn_es"):
            mk.attn_es.close()
        if "l1ffn" in stages:
            emit_l1_ffn(mk, block)
        if "l1ffn2" in stages:
            emit_l1_ffn2(mk, block)
        if "l1wo" in stages:
            emit_l1_wo(mk, block)
        if "cmb1" in stages:
            emit_combine(mk, block, "x1_d", "sg1_d", "out")

        s_fin = mk.sem("fin")

        @block.gpsimd
        def _(g):
            g.wait_ge(mk.s_wprep.h, mk.s_wprep.n)
            for s in mk.phase_sems:
                if s.n > 0:
                    g.wait_ge(s.h, s.n)
            if "cmb1" not in stages:
                z = mk.sb("fin_z", [128, D], F32)
                for t in range(NT):
                    s_fin.dma(g.dma_start(z[:], mk.dram["x"][t * 128:(t + 1) * 128, :]))
                    g.wait_ge(s_fin.h, s_fin.n)
                    s_fin.dma(g.dma_start(out[t * 128:(t + 1) * 128, :], z[:]))
                    g.wait_ge(s_fin.h, s_fin.n)
            else:
                z = mk.sb("fin_z", [128, D], F32)
                s_fin.dma(g.dma_start(z[:], out[0:128, :]))
                g.wait_ge(s_fin.h, s_fin.n)

    mk.es.close()
    return mk


INPUT_SHAPES = {
    "L0_attn_hidden_states": [N_CORES, SEQ, D],
    "L0_attn_Wq": [2048, 2048], "L0_attn_bq": [2048],
    "L0_attn_Wk": [1024, 2048], "L0_attn_bk": [1024],
    "L0_attn_Wv": [1024, 2048], "L0_attn_bv": [1024],
    "L0_attn_Wo": [2048, 2048], "L0_attn_bo": [2048],
    "L0_attn_rand_matrix": [64, 128],
    "L0_ffn_weight": [2048, 2048],
    "L0_ffn_lora_A": [8, 16, 2048], "L0_ffn_lora_B": [8, 2048, 16],
    "L0_ffn_router_w": [8, 2048],
    "L1_attn_w_qkv": [2048, 6144], "L1_attn_b_qkv": [6144],
    "L1_attn_pool_q": [4, 2048],
    "L1_attn_w_lm_qkv": [2048, 6144], "L1_attn_b_lm_qkv": [6144],
    "L1_attn_w_lm_out": [2048, 2048], "L1_attn_b_lm_out": [2048],
    "L1_attn_w_l2t_q": [2048, 2048], "L1_attn_b_l2t_q": [2048],
    "L1_attn_w_l2t_out": [2048, 2048], "L1_attn_b_l2t_out": [2048],
    "L1_attn_w_lm_kv": [2048, 4096], "L1_attn_b_lm_kv": [4096],
    "L1_attn_w_out": [2048, 2048], "L1_attn_b_out": [2048],
    "L1_attn_w_gate": [2048, 32], "L1_attn_b_gate": [32],
    "L1_ffn_W1": [8192, 2048], "L1_ffn_W2": [2048, 8192], "L1_ffn_W3": [8192, 2048],
}

HOST_CONSTS = host_constants()


ALL_STAGES = ("prep", "ln0", "wtrans", "l0sp", "l0a", "l0b", "l0c", "cmb0",
              "ln1", "l1sp", "l1qkv", "l1loc", "l1lm", "l1glob", "w2p",
              "l1ffn", "l1ffn2", "l1wo", "cmb1")


def kernel(**inputs):
    mk = build_program(dbg=(), stages=ALL_STAGES)
    in_maps = []
    x_full = np.ascontiguousarray(inputs["L0_attn_hidden_states"])
    base = {k: np.ascontiguousarray(v) for k, v in inputs.items()
            if k != "L0_attn_hidden_states"}
    base.update({k: v for k, v in HOST_CONSTS.items()})
    for c in range(N_CORES):
        m = dict(base)
        m["x"] = x_full[c]
        in_maps.append(m)
    res = run_bass_kernel_spmd(mk.nc, in_maps, core_ids=list(range(N_CORES)))
    return np.stack([r["out"] for r in res.results], axis=0)


# version 09
# speedup vs baseline: 1.1723x; 1.0215x over previous
import math
import contextlib
import numpy as np

import concourse.bass as bass
import concourse.mybir as mybir
from concourse.bass_utils import run_bass_kernel_spmd

F32 = mybir.dt.float32
BF16 = mybir.dt.bfloat16
AF = mybir.ActivationFunctionType
ALU = mybir.AluOpType

N_CORES = 8
SEQ = 4096
D = 2048
NT = SEQ // 128
ND = D // 128
MAGIC = 12582912.0
QK_LEVEL = 9
L0A_HALVES = 2
TWO_PI = 2.0 * math.pi


def host_constants():
    c = {}
    j = np.arange(64, dtype=np.float64)
    invf0 = 10000.0 ** (-2.0 * j / 128.0)
    ang0 = np.arange(SEQ, dtype=np.float64)[None, :] * np.concatenate([invf0, invf0])[:, None]
    c["rope0_cos"] = np.cos(ang0).astype(np.float32)
    c["rope0_sin"] = np.sin(ang0).astype(np.float32)
    p0 = np.zeros((128, 128), dtype=np.float32)
    for m in range(64):
        p0[64 + m, m] = -1.0
        p0[m, 64 + m] = 1.0
    c["rot0_perm"] = p0
    jj = np.arange(32, dtype=np.float64)
    invf1 = 10000.0 ** (-2.0 * jj / 64.0)
    p = np.arange(128)
    ang1 = np.arange(SEQ, dtype=np.float64)[None, :] * invf1[(p % 64) // 2][:, None]
    c["rope1_cos"] = np.cos(ang1).astype(np.float32)
    sgn = np.where(p % 2 == 0, -1.0, 1.0)[:, None]
    c["rope1_sin"] = (np.sin(ang1) * sgn).astype(np.float32)
    p1 = np.zeros((128, 128), dtype=np.float32)
    for m in range(128):
        p1[m ^ 1, m] = 1.0
    c["rot1_perm"] = p1
    c["ident"] = np.eye(128, dtype=np.float32)
    e16 = np.zeros((8, 128), dtype=np.float32)
    for e in range(8):
        e16[e, e * 16:(e + 1) * 16] = 2.0
    c["e16"] = e16
    c["ones"] = np.ones((128, 128), dtype=np.float32)
    dm = np.full((128, 1), MAGIC, dtype=np.float32)
    dm[64:, 0] += 0.25
    c["dmagic"] = dm
    b2 = np.zeros((128, 1), dtype=np.float32)
    b2[64:, 0] = math.pi / 2.0
    c["b2pi"] = b2
    c["ln2pi"] = np.full((128, 1), math.log(1.0 / TWO_PI), dtype=np.float32)
    h24 = np.zeros((128, 24 * 24), dtype=np.float32)
    for h in range(24):
        h24[:, h * 24 + h] = 1.0
    c["hot24"] = h24
    hr24 = np.zeros((24, 24 * 128), dtype=np.float32)
    for h in range(24):
        hr24[h, h * 128:(h + 1) * 128] = 1.0
    c["hotrow24"] = hr24
    tril = np.where(np.tril(np.ones((64, 64))) > 0, 0.0, -30000.0)
    cm = np.zeros((128, 512), dtype=np.float32)
    for j in range(8):
        cm[0:64, j * 64:(j + 1) * 64] = tril
        cm[64:128, j * 64:(j + 1) * 64] = tril
    c["cmask"] = cm
    blk = np.repeat(np.arange(64), 4)
    lmm = np.where(blk[:, None] >= blk[None, :], 0.0, -30000.0).astype(np.float32)
    c["lmmask"] = lmm
    tok_blk = np.repeat(np.arange(64), 64)
    gm = np.where(tok_blk[:, None] > blk[None, :], 0.0, -30000.0).astype(np.float32)
    c["gmask"] = gm
    h64 = np.zeros((128, 2), dtype=np.float32)
    h64[0:64, 0] = 1.0
    h64[64:128, 1] = 1.0
    c["hot64"] = h64
    c["hot64T"] = h64.T.copy()
    selht = np.zeros((32, 16 * 128), dtype=np.float32)
    for ht in range(16):
        for m in range(128):
            selht[ht * 2 + m // 64, ht * 128 + m] = 1.0
    c["selht"] = selht
    sw = np.zeros((128, 128), dtype=np.float32)
    for m in range(128):
        sw[m ^ 64, m] = 1.0
    c["swap64"] = sw
    H = np.array([[1.0]])
    for _ in range(7):
        H = np.block([[H, H], [H, -H]])
    c["h128"] = (H / np.sqrt(8192.0)).astype(np.float32)
    return c


class Sem:
    def __init__(self, h):
        self.h = h
        self.n = 0

    def inc(self, instr, step=1):
        instr.then_inc(self.h, step)
        self.n += step
        return instr

    def dma(self, instr):
        return self.inc(instr, 16)

    def wait(self, eng, n=None):
        eng.wait_ge(self.h, self.n if n is None else n)


class MK:
    def __init__(self, dbg=()):
        self.nc = bass.Bass("TRN2", target_bir_lowering=False, debug=False)
        self.es = contextlib.ExitStack()
        self.dbg = set(dbg)
        self.dram = {}
        self._sb_cache = {}
        self.phase_sems = []

    def din(self, name, shape, dt=F32):
        ap = self.nc.dram_tensor(name, list(shape), dt, kind="ExternalInput").ap()
        self.dram[name] = ap
        return ap

    def dout(self, name, shape, dt=F32):
        ap = self.nc.dram_tensor(name, list(shape), dt, kind="ExternalOutput").ap()
        self.dram[name] = ap
        return ap

    def dtmp(self, name, shape, dt):
        kind = "ExternalOutput" if name in self.dbg else "Internal"
        ap = self.nc.dram_tensor(name, list(shape), dt, kind=kind).ap()
        self.dram[name] = ap
        return ap

    def sb(self, name, shape, dt):
        t = self.es.enter_context(self.nc.sbuf_tensor(name, list(shape), dt))
        return t

    def ps(self, name, shape, dt=F32):
        return self.es.enter_context(self.nc.psum_tensor(name, list(shape), dt))

    def sem(self, name):
        return Sem(self.es.enter_context(self.nc.semaphore(name)))


def bf16(x):
    import ml_dtypes
    return x.astype(ml_dtypes.bfloat16)



def emit_weight_prep(mk, block):
    names = [
        "L0_attn_Wq", "L0_attn_Wk", "L0_attn_Wv", "L0_attn_Wo",
        "L0_attn_rand_matrix",
        "L0_ffn_weight", "L0_ffn_router_w", "L0_ffn_lora_A", "L0_ffn_lora_B",
        "L1_attn_w_qkv", "L1_attn_pool_q", "L1_attn_w_lm_qkv",
        "L1_attn_w_lm_out", "L1_attn_w_l2t_q", "L1_attn_w_l2t_out",
        "L1_attn_w_lm_kv", "L1_attn_w_out", "L1_attn_w_gate",
        "L1_ffn_W1", "L1_ffn_W3",
    ]
    s = mk.s_wprep
    for n in names:
        src = mk.dram[n]
        shp = list(src.tensor.shape)
        flat = int(np.prod(shp))
        dst = mk.dtmp(n + "_bf", shp, BF16)
        mk._wprep_pairs = getattr(mk, "_wprep_pairs", []) + [(dst, src)]

    @block.gpsimd
    def _(g):
        for dst, src in mk._wprep_pairs:
            idx = tuple(slice(None) for _ in dst.tensor.shape)
            s.dma(g.dma_start(dst[idx], src[idx]))


def emit_const_load(mk, block):
    s = mk.s_const
    nc = mk.nc
    mk.c_ident = mk.sb("c_ident", [128, 128], BF16)
    mk.c_identf = mk.sb("c_identf", [128, 128], F32)
    mk.c_rot0 = mk.sb("c_rot0", [128, 128], BF16)
    mk.c_rot1 = mk.sb("c_rot1", [128, 128], BF16)
    mk.c_e16 = mk.sb("c_e16", [8, 128], BF16)
    mk.c_ones = mk.sb("c_ones", [128, 128], BF16)

    @block.gpsimd
    def _(g):
        s.dma(g.dma_start(mk.c_ident[:], mk.dram["ident"][:, :]))
        s.dma(g.dma_start(mk.c_identf[:], mk.dram["ident"][:, :]))
        s.dma(g.dma_start(mk.c_rot0[:], mk.dram["rot0_perm"][:, :]))
        s.dma(g.dma_start(mk.c_rot1[:], mk.dram["rot1_perm"][:, :]))
        s.dma(g.dma_start(mk.c_e16[:], mk.dram["e16"][:, :]))
        s.dma(g.dma_start(mk.c_ones[:], mk.dram["ones"][:, :]))


def emit_layernorm(mk, block, src_name, dst_name, s_src_ready=None):
    src = mk.dram[src_name]
    dst = mk.dram[dst_name]
    tag = dst_name
    ph = contextlib.ExitStack()
    sbl = lambda n, s, d: ph.enter_context(mk.nc.sbuf_tensor(n, list(s), d))
    xb = [sbl(f"ln_x{i}_{tag}", [128, D], F32) for i in range(2)]
    onb = [sbl(f"ln_o{i}_{tag}", [128, D], BF16) for i in range(2)]
    sq = sbl(f"ln_sq_{tag}", [128, D], F32)
    vdump = sbl(f"ln_vd_{tag}", [128, D], BF16)
    ssum = sbl(f"ln_sum_{tag}", [128, 1], F32)
    ssq = sbl(f"ln_ssq_{tag}", [128, 1], F32)
    nmean = sbl(f"ln_nm_{tag}", [128, 1], F32)
    m2 = sbl(f"ln_m2_{tag}", [128, 1], F32)
    var = sbl(f"ln_var_{tag}", [128, 1], F32)
    rstd = sbl(f"ln_rstd_{tag}", [128, 1], F32)
    s_ld = mk.sem(f"ln_ld_{tag}")
    s_sq = mk.sem(f"ln_sq_{tag}")
    s_v = mk.sem(f"ln_v_{tag}")
    s_st = mk.sem(f"ln_st_{tag}")
    phase_start(mk, block)

    for t in range(NT):
        b = t % 2

        @block.sync
        def _(sy, t=t, b=b):
            if t >= 2:
                s_v.wait(sy, 16 * (t - 1))
            if s_src_ready is not None and t == 0:
                s_src_ready.wait(sy)
            s_ld.dma(sy.dma_start(xb[b][:], src[t * 128:(t + 1) * 128, :]))

        @block.scalar
        def _(sc, t=t, b=b):
            if t == 0:
                mk.s_eps.wait(sc)
            sc.wait_ge(s_ld.h, 16 * (t + 1))
            if t >= 1:
                sc.wait_ge(s_v.h, 16 * (t - 1) + 3)
            s_sq.inc(sc.activation(sq[:], xb[b][:], AF.Square))

        @block.vector
        def _(v, t=t, b=b):
            v.wait_ge(s_ld.h, 16 * (t + 1))

            def S(ins, step=1):
                s_v.inc(v.drain(), step)
                v.wait_ge(s_v.h, s_v.n)
            v.tensor_scalar(vdump[:], xb[b][:], 1.0, None, ALU.mult,
                            ALU.add, accum_out=ssum[:])
            S(None)
            v.tensor_scalar(nmean[:], ssum[:], -1.0 / D, None, ALU.mult)
            S(None)
            v.wait_ge(s_sq.h, s_sq.n)
            v.tensor_scalar(vdump[:], sq[:], 1.0, None, ALU.mult,
                            ALU.add, accum_out=ssq[:])
            S(None)
            v.scalar_tensor_tensor(m2[:], nmean[:], 1.0, nmean[:],
                                   ALU.mult, ALU.mult)
            S(None)
            v.scalar_tensor_tensor(var[:], ssq[:], 1.0 / D, m2[:],
                                   ALU.mult, ALU.subtract)
            S(None)

        @block.scalar
        def _(sc, t=t, b=b):
            sc.wait_ge(s_v.h, s_v.n)
            sc.activation(rstd[:], var[:], AF.Ln, bias=mk.c_eps[:])
            sc.drain()
            sc.activation(rstd[:], rstd[:], AF.Exp, scale=-0.5)
            s_sq.inc(sc.drain())

        @block.vector
        def _(v, t=t, b=b):
            v.wait_ge(s_sq.h, s_sq.n)
            if t >= 2:
                v.wait_ge(s_st.h, 16 * (t - 1))
            s_v.inc(v.tensor_scalar(onb[b][:], xb[b][:], nmean[:], rstd[:],
                                    ALU.add, ALU.mult), 11)

        @block.sync
        def _(sy, t=t, b=b):
            sy.wait_ge(s_v.h, s_v.n)
            s_st.dma(sy.dma_start(dst[t * 128:(t + 1) * 128, :], onb[b][:]))
            if "ln_stats" in mk.dbg:
                dbgd = mk.dram["ln_stats"]
                for ci, tile in enumerate([ssum, ssq, var, rstd]):
                    s_st.dma(sy.dma_start(dbgd[t * 128:(t + 1) * 128, ci:ci + 1],
                                          tile[:], allow_slow_non_contiguous=True))

    ph.close()
    phase_end(mk, [s_st, s_v, s_sq, s_ld])
    return s_st


def barrier(mk, block, sems, engines=("tensor", "vector", "scalar", "sync", "gpsimd")):
    for eng in engines:
        def body(e, sems=tuple(sems)):
            for s in sems:
                if s.n > 0:
                    e.wait_ge(s.h, s.n)
        getattr(block, eng)(body)


def fence(mk, t, sem, step=1):
    sem.inc(t.drain(), step)


def phase_start(mk, block):
    barrier(mk, block, list(mk.phase_sems))


def phase_end(mk, sems):
    mk.phase_sems.extend(sems)


def emit_wtrans(mk, block, names, tag=""):
    s_ld = mk.sem("wt_ld" + tag)
    s_st = mk.sem("wt_st" + tag)
    phase_start(mk, block)
    ph = contextlib.ExitStack()
    stg = [ph.enter_context(mk.nc.sbuf_tensor(f"wt_stage{i}{tag}", [128, 8192], BF16))
           for i in range(2)]
    plan = []
    for n in names:
        src = mk.dram[n + "_bf"]
        O, D2 = src.tensor.shape
        dst = mk.dtmp("wsc_" + n, [O // 128, D2 // 128, 128, 128], BF16)
        plan.append((src, dst, O, D2))
    mk._wt_plan = plan

    i = 0
    for src, dst, O, D2 in plan:
        for kd in range(D2 // 128):
            b = i % 2

            @block.sync
            def _(sy, i=i, b=b, src=src, kd=kd, O=O):
                if i >= 2:
                    sy.wait_ge(s_st.h, 16 * (i - 1))
                mk.s_wprep.wait(sy)
                s_ld.dma(sy.dma_start(stg[b][:, 0:O],
                                      src[0:O, kd * 128:(kd + 1) * 128],
                                      transpose=True))

            @block.scalar
            def _(sc, i=i, b=b, dst=dst, kd=kd, O=O):
                sc.wait_ge(s_ld.h, 16 * (i + 1))
                v = dst[:, kd, :, :].rearrange("os dd ol -> dd os ol")
                s_st.dma(sc.dma_start(v, stg[b][:, 0:O].rearrange(
                    "dd (os ol) -> dd os ol", ol=128)))
            i += 1
    ph.close()
    phase_end(mk, [s_st, s_ld])
    mk.wt_done = s_st


def emit_l0_prep_small(mk, block, parts=("ld", "rw", "la", "bt")):
    nc = mk.nc
    s = mk.sem("l0sp")
    mk.s_l0sp = s
    mk.rwT = mk.sb("rwT", [128, 16 * 8], BF16)
    mk.laT = mk.sb("laT", [128, 16 * 128], BF16)
    mk.BT = mk.sb("BT", [128, 2048], BF16)
    mk.RRT = mk.sb("RRT", [128, 128], BF16)
    mk.b_qkv = mk.sb("b_qkv", [128, 32], F32)
    mk.b_o = mk.sb("b_o", [128, 16], F32)
    mk.dmagic = mk.sb("c_dmagic", [128, 1], F32)
    mk.b2pi = mk.sb("c_b2pi", [128, 1], F32)
    mk.ln2pi = mk.sb("c_ln2pi", [128, 1], F32)
    mk.hot24 = mk.sb("c_hot24", [128, 24 * 24], BF16)
    mk.hotrow24 = mk.sb("c_hotrow24", [24, 24 * 128], F32)
    phase_start(mk, block)
    ph = contextlib.ExitStack()
    tmp8 = ph.enter_context(mk.nc.sbuf_tensor("l0sp_tmp8", [128, 2048], BF16))
    BT_d = mk.dtmp("BT_d", [128, 2048], BF16)
    ps_t = ph.enter_context(mk.nc.psum_tensor("l0sp_ps", [128, 1024], BF16))

    @block.gpsimd
    def _(g):
        s.dma(g.dma_start(mk.hot24[:], mk.dram["hot24"][:, :]))

    @block.sync
    def _(sy):
        mk.s_wprep.wait(sy)
        s.dma(sy.dma_start(mk.dmagic[:], mk.dram["dmagic"][:, :]))
        s.dma(sy.dma_start(mk.b2pi[:], mk.dram["b2pi"][:, :]))
        s.dma(sy.dma_start(mk.ln2pi[:], mk.dram["ln2pi"][:, :]))
        s.dma(sy.dma_start(mk.hotrow24[:], mk.dram["hotrow24"][:, :]))
        s.dma(sy.dma_start(mk.RRT[:, 0:64],
                           mk.dram["L0_attn_rand_matrix_bf"][:, :], transpose=True))
        s.dma(sy.dma_start(mk.RRT[:, 64:128],
                           mk.dram["L0_attn_rand_matrix_bf"][:, :], transpose=True))
        for c, (nm, nstr) in enumerate([("L0_attn_bq", 16), ("L0_attn_bk", 8),
                                        ("L0_attn_bv", 8)]):
            base = [0, 16, 24][c]
            v = mk.dram[nm].rearrange("(s p) -> p s", p=128)
            s.dma(sy.dma_start(mk.b_qkv[:, base:base + nstr], v,
                               allow_slow_non_contiguous=True))
        v = mk.dram["L0_attn_bo"].rearrange("(s p) -> p s", p=128)
        s.dma(sy.dma_start(mk.b_o[:, :], v, allow_slow_non_contiguous=True))
        s.dma(sy.dma_start(tmp8[0:8, :], mk.dram["L0_ffn_router_w_bf"][:, :]))

    s_mm = mk.sem("l0sp_mm")
    for kd in (range(16) if "rw" in parts else []):
        @block.tensor
        def _(t, kd=kd):
            if kd == 0:
                s.wait(t)
                mk.s_const.wait(t)
            if kd > 0:
                s_mm.wait(t)
            t.transpose(ps_t[:, 0:8], tmp8[0:8, kd * 128:(kd + 1) * 128],
                        mk.c_ident[0:8, 0:8])
            fence(mk, t, s_mm)

        @block.vector
        def _(v, kd=kd):
            v.wait_ge(s_mm.h, s_mm.n)
            s_mm.inc(v.tensor_copy(mk.rwT[:, kd * 8:(kd + 1) * 8], ps_t[:, 0:8]))

    if "la" in parts:
        @block.sync
        def _(sy):
            sy.wait_ge(s_mm.h, s_mm.n)
            la = mk.dram["L0_ffn_lora_A_bf"].rearrange("e r d -> (e r) d")
            s.dma(sy.dma_start(tmp8[:, :], la))

    for kd in (range(16) if "la" in parts else []):
        @block.tensor
        def _(t, kd=kd):
            if kd == 0:
                s.wait(t)
            s_mm.wait(t)
            t.transpose(ps_t[:, 0:128], tmp8[:, kd * 128:(kd + 1) * 128],
                        mk.c_ident[:])
            fence(mk, t, s_mm)

        @block.vector
        def _(v, kd=kd):
            v.wait_ge(s_mm.h, s_mm.n)
            s_mm.inc(v.tensor_copy(mk.laT[:, kd * 128:(kd + 1) * 128], ps_t[:, 0:128]))

    s_bt = mk.sem("l0sp_bt")
    bt_tile = ph.enter_context(mk.nc.sbuf_tensor("bt_tile", [128, 16 * 16], BF16))
    bt_ev = ph.enter_context(mk.nc.sbuf_tensor("bt_ev", [16, 16 * 128], BF16))
    for e in (range(8) if "bt" in parts else []):
        @block.sync
        def _(sy, e=e):
            if e > 0:
                s_bt.wait(sy)
            lb = mk.dram["L0_ffn_lora_B_bf"][e, :, :].rearrange(
                "(oc p) r -> p oc r", p=128)
            s_bt.dma(sy.dma_start(
                bt_tile[:].rearrange("p (oc r) -> p oc r", r=16), lb))

        for oc in range(16):
            @block.tensor
            def _(t, e=e, oc=oc):
                t.wait_ge(s_bt.h, s_bt.n)
                s_mm.wait(t)
                t.transpose(ps_t[0:16, 0:128], bt_tile[:, oc * 16:(oc + 1) * 16],
                            mk.c_ident[:])
                fence(mk, t, s_mm)

            @block.vector
            def _(v, e=e, oc=oc):
                v.wait_ge(s_mm.h, s_mm.n)
                s_mm.inc(v.tensor_copy(bt_ev[0:16, oc * 128:(oc + 1) * 128],
                                       ps_t[0:16, 0:128]))

        @block.sync
        def _(sy, e=e):
            sy.wait_ge(s_mm.h, s_mm.n)
            s_bt.dma(sy.dma_start(BT_d[e * 16:(e + 1) * 16, :], bt_ev[0:16, :]))

    if "bt" in parts:
        @block.sync
        def _(sy):
            s_bt.wait(sy)
            s.dma(sy.dma_start(mk.BT[:, :], BT_d[:, :]))
    if "l0sp_dump" in mk.dbg:
        dmp = mk.dtmp("l0sp_dump", [128, 128 + 2048 + 2048 + 128], BF16)
        @block.sync
        def _(sy):
            s.wait(sy)
            sy.wait_ge(s_mm.h, s_mm.n)
            sy.wait_ge(s_bt.h, s_bt.n)
            s.dma(sy.dma_start(dmp[:, 0:128], mk.rwT[:, :]))
            s.dma(sy.dma_start(dmp[:, 128:2176], mk.laT[:, :]))
            s.dma(sy.dma_start(dmp[:, 2176:4224], mk.BT[:, :]))
            s.dma(sy.dma_start(dmp[:, 4224:4352], mk.RRT[:, :]))
    ph.close()
    phase_end(mk, [s, s_mm, s_bt])
    mk.l0sp_done = s


def emit_l0_passA(mk, block, parts=("qk", "v", "f1", "f2", "f3")):
    nc = mk.nc
    ph = contextlib.ExitStack()
    sb = lambda n, s, d: ph.enter_context(nc.sbuf_tensor(n, list(s), d))
    ps = lambda n, s, d=F32: ph.enter_context(nc.psum_tensor(n, list(s), d))
    qr_d = mk.dram["qr_d"]
    Vt = mk.dram["Vt"]
    g0_d = mk.dram["g0_d"]

    phase_start(mk, block)
    barrier(mk, block, [mk.s_const, mk.s_wprep])

    HN = SEQ // 2
    NS = HN // 512

    xnf = sb("a_xnf", [128, 16 * HN], BF16)
    ropeC = sb("a_ropeC", [128, HN], F32)
    ropeS = sb("a_ropeS", [128, HN], F32)
    strip = [sb(f"a_strip{i}", [128, 16 * 128], BF16) for i in range(2)]
    qsb = [sb(f"a_qsb{i}", [128, 512], BF16) for i in range(2)]
    t1 = [sb(f"a_t1{i}", [128, 512], F32) for i in range(2)]
    t2 = [sb(f"a_t2{i}", [128, 512], F32) for i in range(2)]
    qrb = [sb(f"a_qrb{i}", [128, 512], BF16) for i in range(2)]
    vsb = sb("a_vsb", [128, 512], BF16)
    vtb = [sb(f"a_vtb{i}", [128, 128], BF16) for i in range(2)]
    wfm = sb("a_wfm", [8, HN], BF16)
    Axw = sb("a_axw", [128, HN], BF16)
    wbsb = [sb(f"a_wb{i}", [128, 512], BF16) for i in range(2)]
    gsb = [sb(f"a_gsb{i}", [128, 512], BF16) for i in range(2)]
    ps_p = [ps(f"a_psp{i}", [128, 512]) for i in range(2)]
    ps_r = [ps(f"a_psr{i}", [128, 512]) for i in range(2)]
    ps_t = ph.enter_context(nc.psum_tensor("a_pst", [128, 1024], BF16))
    ps_l = ph.enter_context(nc.psum_tensor("a_psl", [128, 512], F32))
    ps_w = ph.enter_context(nc.psum_tensor("a_psw", [8, 512], F32))
    esb = sb("a_esb", [128, 8], F32)
    edump = sb("a_edump", [128, 8], BF16)
    esum = sb("a_esum", [128, 1], F32)
    mx = sb("a_mx", [128, 1], F32)
    nmx = sb("a_nmx", [128, 1], F32)
    resum = sb("a_resum", [128, 1], F32)
    probs = sb("a_probs", [128, 8], F32)
    v1c = sb("a_v1", [128, 1], F32)
    v2c = sb("a_v2", [128, 1], F32)
    m1 = sb("a_m1", [128, 8], F32)
    m2 = sb("a_m2", [128, 8], F32)
    p2 = sb("a_p2", [128, 8], F32)
    tdum = sb("a_tdum", [128, 8], F32)
    mm12 = sb("a_mm12", [128, 8], F32)
    s12 = sb("a_s12", [128, 1], F32)
    rs12 = sb("a_rs12", [128, 1], F32)
    wdt = sb("a_wdt", [128, 8], F32)
    f1dbg = sb("a_f1dbg", [128, 32], F32) if ("f1dbg_d" in mk.dbg or "f1all_d" in mk.dbg) else None

    s_zz = mk.sem("a_zz")

    @block.vector
    def _(v):
        for tile in [esb, edump, esum, mx, nmx, resum, probs, v1c, v2c,
                     m1, m2, p2, tdum, mm12, s12, rs12, wdt]:
            v.memset(tile[:], 0.0)
        v.memset(ps_l[:, :], 0.0)
        v.memset(ps_w[:, :], 0.0)
        s_zz.inc(v.drain())

    barrier(mk, block, [s_zz])

    s_ld = mk.sem("a_ld")
    s_mm = mk.sem("a_mm")
    s_sc = mk.sem("a_sc")
    s_ev = mk.sem("a_ev")
    s_st = mk.sem("a_st")

    for half in range(L0A_HALVES):
        n0 = half * HN
        ld0, mm0, sc0, ev0, st0 = s_ld.n, s_mm.n, s_sc.n, s_ev.n, s_st.n

        if half == 1:
            barrier(mk, block, [s_ld, s_mm, s_sc, s_ev, s_st])

        @block.sync
        def _(sy, half=half, n0=n0, mm0=mm0):
            if half == 1:
                sy.wait_ge(s_mm.h, mm0)
            for kd in range(16):
                s_ld.dma(sy.dma_start(
                    xnf[:, kd * HN:(kd + 1) * HN],
                    mk.dram["xn0"][n0:n0 + HN, kd * 128:(kd + 1) * 128],
                    transpose=True))
            s_ld.dma(sy.dma_start(ropeC[:], mk.dram["rope0_cos"][:, n0:n0 + HN]))
            s_ld.dma(sy.dma_start(ropeS[:], mk.dram["rope0_sin"][:, n0:n0 + HN]))
        xn_ready = s_ld.n

        if "xnf_d" in mk.dbg:
            @block.sync
            def _(sy, half=half, n0=n0):
                sy.wait_ge(s_ld.h, xn_ready)
                s_st.dma(sy.dma_start(mk.dram["xnf_d"][:, n0 * 16:(n0 + HN) * 16],
                                      xnf[:]))

        for h in (range(24) if "qk" in parts else []):
            wname = "wsc_L0_attn_Wq" if h < 16 else "wsc_L0_attn_Wk"
            os_ = h if h < 16 else h - 16
            sbuf_i = h % 2

            @block.gpsimd
            def _(g, h=h, sbuf_i=sbuf_i, wname=wname, os_=os_, mm0=mm0):
                if h >= 2:
                    g.wait_ge(s_mm.h, mm0 + (h - 1) * NS * 2)
                elif h < 2:
                    g.wait_ge(s_mm.h, mm0)
                w = mk.dram[wname][os_, :, :, :].rearrange("kd dd ol -> dd kd ol")
                s_ld.dma(g.dma_start(strip[sbuf_i][:].rearrange(
                    "dd (kd ol) -> dd kd ol", ol=128), w))

            for ns in range(NS):
                it = h * NS + ns
                b = it % 2

                @block.tensor
                def _(t, h=h, ns=ns, b=b, it=it, sbuf_i=sbuf_i,
                      xn_ready=xn_ready, sc0=sc0, ev0=ev0):
                    if ns == 0:
                        t.wait_ge(s_ld.h, xn_ready + 16 * (h + 1))
                    if it >= 2:
                        t.wait_ge(s_sc.h, sc0 + it - 1)
                    ins = None
                    if QK_LEVEL >= 1:
                        for kd in range(16):
                            ins = t.matmul(ps_p[b][:, :],
                                     strip[sbuf_i][:, kd * 128:(kd + 1) * 128],
                                     xnf[:, kd * HN + ns * 512: kd * HN + (ns + 1) * 512],
                                     start=(kd == 0), stop=(kd == 15))
                        fence(mk, t, s_mm)
                    else:
                        t.matmul(ps_p[b][:, :], strip[sbuf_i][:, 0:128],
                                 xnf[:, 0:512], start=True, stop=True)
                        fence(mk, t, s_mm)

                @block.scalar
                def _(sc, it=it, b=b, h=h, mm0=mm0, ev0=ev0):
                    sc.wait_ge(s_mm.h, mm0 + 2 * it + 1)
                    if it >= 2:
                        sc.wait_ge(s_ev.h, ev0 + it - 1)
                    if QK_LEVEL >= 1:
                        s_sc.inc(sc.activation(qsb[b][:], ps_p[b][:, :], AF.Identity,
                                               bias=mk.b_qkv[:, h:h + 1]))
                    else:
                        s_sc.inc(sc.activation(qsb[b][:], qsb[b][:], AF.Silu))

                @block.tensor
                def _(t, it=it, b=b, sc0=sc0, ev0=ev0):
                    t.wait_ge(s_sc.h, sc0 + it + 1)
                    if it >= 2 and QK_LEVEL >= 3:
                        t.wait_ge(s_ev.h, ev0 + it - 1)
                    t.matmul(ps_r[b][:, :], mk.c_rot0[:], qsb[b][:],
                             start=True, stop=True)
                    fence(mk, t, s_mm)

                @block.vector
                def _(v, it=it, b=b, ns=ns, mm0=mm0, st0=st0):
                    v.wait_ge(s_mm.h, mm0 + 2 * it + 2)
                    if it >= 2:
                        v.wait_ge(s_st.h, st0 + 16 * (it - 1))
                    cs = slice(ns * 512, (ns + 1) * 512)
                    if QK_LEVEL >= 3:
                        v.tensor_tensor(t1[b][:], ps_r[b][:, :], ropeS[:, cs], ALU.mult)
                        v.tensor_tensor(t2[b][:], qsb[b][:], ropeC[:, cs], ALU.mult)
                        s_ev.inc(v.tensor_tensor(qrb[b][:], t1[b][:], t2[b][:], ALU.add))
                    else:
                        s_ev.inc(v.tensor_copy(qrb[b][:], qsb[b][:]))

                @block.sync
                def _(sy, it=it, b=b, h=h, ns=ns, n0=n0, ev0=ev0):
                    sy.wait_ge(s_ev.h, ev0 + it + 1)
                    s_st.dma(sy.dma_start(
                        qr_d[h, :, n0 + ns * 512:n0 + (ns + 1) * 512], qrb[b][:]))

        mm1, sc1, ev1, st1 = s_mm.n, s_sc.n, s_ev.n, s_st.n

        for hd in (range(8) if "v" in parts else []):
            sbuf_i = hd % 2

            @block.gpsimd
            def _(g, hd=hd, sbuf_i=sbuf_i, mm1=mm1):
                if hd < 2:
                    g.wait_ge(s_mm.h, mm1)
                else:
                    g.wait_ge(s_mm.h, mm1 + (hd - 1) * NS * 5)
                w = mk.dram["wsc_L0_attn_Wv"][hd, :, :, :].rearrange(
                    "kd dd ol -> dd kd ol")
                s_ld.dma(g.dma_start(strip[sbuf_i][:].rearrange(
                    "dd (kd ol) -> dd kd ol", ol=128), w))

            for ns in range(NS):
                it = hd * NS + ns
                b = it % 2

                @block.tensor
                def _(t, hd=hd, ns=ns, b=b, it=it, sbuf_i=sbuf_i,
                      ld1=None, sc1=sc1, xn_ready=xn_ready):
                    t.wait_ge(s_ld.h, xn_ready + 16 * 24 + 16 * (hd + 1))
                    if it >= 2:
                        t.wait_ge(s_sc.h, sc1 + it - 1)
                    for kd in range(16):
                        t.matmul(ps_p[b][:, :],
                                 strip[sbuf_i][:, kd * 128:(kd + 1) * 128],
                                 xnf[:, kd * HN + ns * 512: kd * HN + (ns + 1) * 512],
                                 start=(kd == 0), stop=(kd == 15))
                    fence(mk, t, s_mm)

                @block.scalar
                def _(sc, hd=hd, b=b, it=it, mm1=mm1, ev1=ev1):
                    sc.wait_ge(s_mm.h, mm1 + it * 5 + 1)
                    if it >= 1:
                        sc.wait_ge(s_ev.h, ev1 + it * 4)
                    s_sc.inc(sc.activation(vsb[:], ps_p[b][:, :], AF.Identity,
                                           bias=mk.b_qkv[:, 24 + hd:25 + hd]))

                for t4 in range(4):
                    tb = t4 % 2

                    @block.tensor
                    def _(t, t4=t4, it=it, sc1=sc1, ev1=ev1):
                        t.wait_ge(s_sc.h, sc1 + it + 1)
                        cnt = it * 4 + t4
                        if cnt >= 1:
                            t.wait_ge(s_ev.h, ev1 + cnt)
                        t.transpose(ps_t[:, 0:128],
                                    vsb[:, t4 * 128:(t4 + 1) * 128],
                                    mk.c_ident[:])
                        fence(mk, t, s_mm)

                    @block.vector
                    def _(v, t4=t4, tb=tb, it=it, mm1=mm1, st1=st1):
                        v.wait_ge(s_mm.h, mm1 + it * 5 + 2 + t4)
                        cnt = it * 4 + t4
                        if cnt >= 2:
                            v.wait_ge(s_st.h, st1 + 16 * (cnt - 1))
                        s_ev.inc(v.tensor_copy(vtb[tb][:], ps_t[:, 0:128]))

                    @block.sync
                    def _(sy, hd=hd, ns=ns, t4=t4, tb=tb, it=it, n0=n0, ev1=ev1):
                        sy.wait_ge(s_ev.h, ev1 + it * 4 + t4 + 1)
                        nn = n0 + ns * 512 + t4 * 128
                        s_st.dma(sy.dma_start(Vt[hd, nn:nn + 128, :], vtb[tb][:]))

        mm2, sc2, ev2, st2 = s_mm.n, s_sc.n, s_ev.n, s_st.n

        NTT = HN // 128
        for tt in (range(NTT) if "f1" in parts else []):
            @block.tensor
            def _(t, tt=tt, xn_ready=xn_ready):
                if tt == 0:
                    t.wait_ge(s_ld.h, xn_ready)
                    s_st.wait(t)
                s_sc.wait(t)
                s_ev.wait(t)
                for kd in range(16):
                    t.matmul(ps_l[:, 0:8],
                             xnf[:, kd * HN + tt * 128: kd * HN + (tt + 1) * 128],
                             mk.rwT[:, kd * 8:(kd + 1) * 8],
                             start=(kd == 0), stop=(kd == 15))
                fence(mk, t, s_mm)

            @block.scalar
            def _(sc, tt=tt):
                s_mm.wait(sc)
                s_sc.inc(sc.activation(esb[:], ps_l[:, 0:8], AF.Exp))

            @block.vector
            def _(v, tt=tt):
                s_sc.wait(v)

                def S(ins):
                    s_ev.inc(v.drain())
                    v.wait_ge(s_ev.h, s_ev.n)
                S(v.tensor_reduce(v1c[:], esb[:], mybir.AxisListType.X, ALU.max))
                S(v.tensor_scalar(m1[:], esb[:], v1c[:], None, ALU.is_equal))
                S(v.tensor_scalar(tdum[:], m1[:], -1.0, 1.0, ALU.mult, ALU.add))
                S(v.tensor_tensor(p2[:], tdum[:], esb[:], ALU.mult))
                S(v.tensor_reduce(v2c[:], p2[:], mybir.AxisListType.X, ALU.max))
                S(v.tensor_scalar(m2[:], p2[:], v2c[:], None, ALU.is_equal))
                S(v.tensor_tensor(s12[:], v1c[:], v2c[:], ALU.add))
                S(v.reciprocal(rs12[:], s12[:]))
                S(v.tensor_tensor(mm12[:], m1[:], m2[:], ALU.add))
                S(v.tensor_tensor(tdum[:], mm12[:], esb[:], ALU.mult))
                S(v.tensor_scalar(wdt[:], tdum[:], rs12[:], None, ALU.mult))

            if tt == 0 and half == 0 and "f1dbg_d" in mk.dbg:
                @block.vector
                def _(v):
                    dbgsb = f1dbg
                    v.tensor_copy(dbgsb[:, 0:8], ps_l[:, 0:8])
                    v.tensor_copy(dbgsb[:, 8:16], esb[:])
                    v.tensor_copy(dbgsb[:, 16:24], probs[:])
                    s_ev.inc(v.tensor_copy(dbgsb[:, 24:32], wdt[:]))

                @block.sync
                def _(sy):
                    sy.wait_ge(s_ev.h, s_ev.n)
                    s_st.dma(sy.dma_start(mk.dram["f1dbg_d"][:, :], f1dbg[:]))

            if "f1all_d" in mk.dbg:
                @block.sync
                def _(sy, tt=tt, n0=n0, esb_cnt=s_sc.n):
                    sy.wait_ge(s_sc.h, esb_cnt)
                    s_st.dma(sy.dma_start(
                        mk.dram["f1all_d"][n0 + tt * 128:n0 + (tt + 1) * 128, 8:16],
                        esb[:]))
                    sy.wait_ge(s_ev.h, s_ev.n)
                    s_st.dma(sy.dma_start(
                        mk.dram["f1all_d"][n0 + tt * 128:n0 + (tt + 1) * 128, 0:8],
                        wdt[:]))

            @block.tensor
            def _(t, tt=tt):
                s_ev.wait(t)
                s_sc.wait(t)
                t.transpose(ps_w[0:8, 0:128], wdt[:], mk.c_identf[:])
                fence(mk, t, s_mm)

            @block.vector
            def _(v, tt=tt):
                s_mm.wait(v)
                s_ev.inc(v.tensor_copy(wfm[0:8, tt * 128:(tt + 1) * 128],
                                       ps_w[0:8, 0:128]))

            @block.sync
            def _(sy, tt=tt):
                sy.wait_ge(s_ev.h, s_ev.n)
                s_st.dma(sy.dma_start(mk.dram["f1q_d"][:, :], wdt[:]))

            @block.vector
            def _(v, tt=tt):
                v.wait_ge(s_st.h, s_st.n)

        if "wfm_d" in mk.dbg:
            @block.sync
            def _(sy, n0=n0):
                sy.wait_ge(s_ev.h, s_ev.n)
                sy.wait_ge(s_sc.h, s_sc.n)
                s_st.dma(sy.dma_start(mk.dram["wfm_d"][:, n0:n0 + HN], wfm[0:8, :]))

        mm3, sc3, ev3, st3 = s_mm.n, s_sc.n, s_ev.n, s_st.n

        for ns in (range(NS) if "f2" in parts else []):
            b = ns % 2

            @block.tensor
            def _(t, ns=ns, b=b, ev3=ev3, sc3=sc3):
                if ns == 0:
                    t.wait_ge(s_ev.h, ev3)
                    t.wait_ge(s_sc.h, sc3)
                if ns >= 2:
                    t.wait_ge(s_ev.h, ev3 + ns - 1)
                for kd in range(16):
                    t.matmul(ps_p[b][:, :], mk.laT[:, kd * 128:(kd + 1) * 128],
                             xnf[:, kd * HN + ns * 512: kd * HN + (ns + 1) * 512],
                             start=(kd == 0), stop=(kd == 15))
                t.matmul(ps_r[b][:, :], mk.c_e16[0:8, :],
                         wfm[0:8, ns * 512:(ns + 1) * 512], start=True, stop=True)
                fence(mk, t, s_mm, 2)

            @block.vector
            def _(v, ns=ns, b=b, mm3=mm3):
                v.wait_ge(s_mm.h, mm3 + 2 * (ns + 1))
                v.tensor_copy(wbsb[b][:], ps_r[b][:, :])
                s_ev.inc(v.tensor_tensor(Axw[:, ns * 512:(ns + 1) * 512],
                                         ps_p[b][:, :], wbsb[b][:], ALU.mult))

        if "axw_d" in mk.dbg:
            @block.sync
            def _(sy, n0=n0):
                sy.wait_ge(s_ev.h, s_ev.n)
                s_st.dma(sy.dma_start(mk.dram["axw_d"][:, n0:n0 + HN], Axw[:]))

        mm4, sc4, ev4, st4, ld4 = s_mm.n, s_sc.n, s_ev.n, s_st.n, s_ld.n

        for os_ in (range(16) if "f3" in parts else []):
            sbuf_i = os_ % 2

            @block.gpsimd
            def _(g, os_=os_, sbuf_i=sbuf_i, mm4=mm4):
                if os_ < 2:
                    g.wait_ge(s_mm.h, mm4)
                else:
                    g.wait_ge(s_mm.h, mm4 + (os_ - 1) * NS)
                w = mk.dram["wsc_L0_ffn_weight"][os_, :, :, :].rearrange(
                    "kd dd ol -> dd kd ol")
                s_ld.dma(g.dma_start(strip[sbuf_i][:].rearrange(
                    "dd (kd ol) -> dd kd ol", ol=128), w))

            for ns in range(NS):
                b = ns % 2
                itf = os_ * NS + ns

                @block.tensor
                def _(t, os_=os_, ns=ns, b=b, itf=itf, sbuf_i=sbuf_i,
                      ld4=ld4, ev4=ev4, sc4=sc4):
                    t.wait_ge(s_ld.h, ld4 + 16 * (os_ + 1))
                    if itf == 0:
                        t.wait_ge(s_ev.h, ev4)
                    if itf >= 2:
                        t.wait_ge(s_sc.h, sc4 + itf - 1)
                    for kd in range(16):
                        t.matmul(ps_p[b][:, :],
                                 strip[sbuf_i][:, kd * 128:(kd + 1) * 128],
                                 xnf[:, kd * HN + ns * 512: kd * HN + (ns + 1) * 512],
                                 start=(kd == 0), stop=False)
                    t.matmul(ps_p[b][:, :], mk.BT[:, os_ * 128:(os_ + 1) * 128],
                             Axw[:, ns * 512:(ns + 1) * 512], start=False, stop=True)
                    fence(mk, t, s_mm)

                @block.scalar
                def _(sc, b=b, itf=itf, mm4=mm4, st4=st4):
                    sc.wait_ge(s_mm.h, mm4 + itf + 1)
                    if itf >= 2:
                        sc.wait_ge(s_st.h, st4 + 16 * (itf - 1))
                    s_sc.inc(sc.activation(gsb[b][:], ps_p[b][:, :], AF.Silu))

                @block.sync
                def _(sy, os_=os_, ns=ns, b=b, itf=itf, n0=n0, sc4=sc4):
                    sy.wait_ge(s_sc.h, sc4 + itf + 1)
                    s_st.dma(sy.dma_start(
                        g0_d[os_ * 128:(os_ + 1) * 128,
                             n0 + ns * 512:n0 + (ns + 1) * 512],
                        gsb[b][:]))

    mk.passA_done = [s_st, s_ev, s_sc, s_mm, s_ld]
    ph.close()
    phase_end(mk, mk.passA_done)


def emit_l0_passB(mk, block):
    nc = mk.nc
    ph = contextlib.ExitStack()
    sb = lambda n, s, d: ph.enter_context(nc.sbuf_tensor(n, list(s), d))
    phase_start(mk, block)

    qr_sub = sb("b_qr", [128, 24 * 512], BF16)
    qp_sub = sb("b_qp", [128, 24 * 512], BF16)
    v4 = [sb(f"b_v4{i}", [8, 128 * 128], BF16) for i in range(2)]
    A_sb = sb("b_A", [8, 16 * 512], BF16)
    O2_sb = sb("b_O2", [128, 16 * 512], BF16)
    invb_sb = sb("b_invb", [128, 512], F32)
    tp = sb("b_tp", [128, 512], F32)
    tq = sb("b_tq", [128, 512], F32)
    n2t = sb("b_n2t", [128, 512], F32)
    rr = sb("b_rr", [128, 512], F32)
    nsq = sb("b_nsq", [24, 512], F32)
    inv2 = sb("b_inv2", [24, 512], F32)
    ps_n = ph.enter_context(nc.psum_tensor("b_psn", [24, 512], F32))
    ps_p = ph.enter_context(nc.psum_tensor("b_psp", [128, 512], F32))
    ps_i = ph.enter_context(nc.psum_tensor("b_psi", [128, 512], F32))
    ps_A = [ph.enter_context(nc.psum_tensor(f"b_psA{i}", [8, 512], F32))
            for i in range(2)]
    ps_O = [ph.enter_context(nc.psum_tensor(f"b_psO{i}", [128, 512], F32))
            for i in range(2)]

    s_ld = mk.sem("b_ld")
    s_mm = mk.sem("b_mm")
    s_sc = mk.sem("b_sc")
    s_ev = mk.sem("b_ev")
    s_st = mk.sem("b_st")
    O2_d = mk.dram["O2_d"]

    NSUB = SEQ // 512
    for su in range(NSUB):
        n0 = su * 512

        @block.sync
        def _(sy, su=su, n0=n0):
            if su > 0:
                sy.wait_ge(s_ev.h, s_ev.n)
                sy.wait_ge(s_st.h, s_st.n)
            qr = mk.dram["qr_d"][:, :, n0:n0 + 512].rearrange("h p t -> p h t")
            s_ld.dma(sy.dma_start(
                qr_sub[:].rearrange("p (h t) -> p h t", h=24), qr))
            vsrc = mk.dram["Vt"][:, n0:n0 + 128, :].rearrange("hd tq dd -> hd (tq dd)")
            s_ld.dma(sy.dma_start(v4[0][:, :], vsrc))

        @block.vector
        def _(v, su=su):
            v.wait_ge(s_ld.h, s_ld.n)
            s_ev.inc(v.scalar_tensor_tensor(qp_sub[:], qr_sub[:], 1.0,
                                            qr_sub[:], ALU.mult, ALU.mult))

        @block.tensor
        def _(t, su=su):
            s_ev.wait(t)
            for h in range(24):
                t.matmul(ps_n[:, :], mk.hot24[:, h * 24:(h + 1) * 24],
                         qp_sub[:, h * 512:(h + 1) * 512],
                         start=(h == 0), stop=(h == 23))
            fence(mk, t, s_mm)

        @block.vector
        def _(v, su=su):
            s_mm.wait(v)
            v.tensor_copy(nsq[:], ps_n[:, :])
            s_ev.inc(v.drain())
            v.wait_ge(s_ev.h, s_ev.n)
            v.tensor_scalar(nsq[:], nsq[:], 1e-12, None, ALU.max)
            s_ev.inc(v.drain())

        @block.scalar
        def _(sc, su=su):
            s_ev.wait(sc)
            sc.activation(inv2[:], nsq[:], AF.Ln)
            sc.drain()
            sc.activation(inv2[:], inv2[:], AF.Exp, scale=-0.5,
                          bias=mk.ln2pi[0:24, :])
            s_sc.inc(sc.drain())

        for h in range(24):
            @block.tensor
            def _(t, h=h, su=su):
                if h == 0:
                    s_sc.wait(t)
                else:
                    s_sc.wait(t)
                    s_ev.wait(t)
                t.matmul(ps_p[:, :], mk.RRT[:], qr_sub[:, h * 512:(h + 1) * 512],
                         start=True, stop=True)
                t.matmul(ps_i[:, :], mk.hotrow24[:, h * 128:(h + 1) * 128],
                         inv2[:], start=True, stop=True)
                fence(mk, t, s_mm)

            @block.vector
            def _(v, h=h, su=su):
                s_mm.wait(v)

                def S():
                    s_ev.inc(v.drain())
                    v.wait_ge(s_ev.h, s_ev.n)
                v.tensor_copy(invb_sb[:], ps_i[:, :])
                S()
                v.tensor_tensor(tp[:], ps_p[:, :], invb_sb[:], ALU.mult)
                S()
                v.tensor_scalar(tq[:], tp[:], 1.0, mk.dmagic[:], ALU.mult, ALU.add)
                S()
                v.tensor_scalar(n2t[:], tq[:], -MAGIC, None, ALU.add)
                S()
                v.scalar_tensor_tensor(rr[:], n2t[:], -1.0, tp[:], ALU.mult, ALU.add)
                S()

            @block.scalar
            def _(sc, h=h, su=su):
                s_ev.wait(sc)
                sc.activation(qp_sub[:, h * 512:(h + 1) * 512], rr[:], AF.Sin,
                              scale=TWO_PI, bias=mk.b2pi[:])
                s_sc.inc(sc.drain())

        qpr = qp_sub[:].rearrange("p (h t) -> p t h", h=24)
        for g32 in range(16):
            bb = g32 % 2

            @block.tensor
            def _(t, su=su, g32=g32, bb=bb):
                if g32 == 0:
                    s_sc.wait(t)
                if g32 >= 2:
                    t.wait_ge(s_ev.h, s_ev.n - 1)
                for tl in range(32):
                    tok = g32 * 32 + tl
                    t.matmul(ps_A[bb][0:8, tl * 16:(tl + 1) * 16],
                             qpr[:, tok, 16:24], qpr[:, tok, 0:16],
                             start=True, stop=True, skip_group_check=True)
                fence(mk, t, s_mm)

            @block.vector
            def _(v, su=su, g32=g32, bb=bb):
                s_mm.wait(v)
                s_ev.inc(v.tensor_scalar(A_sb[0:8, g32 * 512:(g32 + 1) * 512],
                                         ps_A[bb][0:8, :], 1.0 / 64.0, None,
                                         ALU.mult))

        for q in range(4):
            qb = q % 2
            if q < 3:
                @block.sync
                def _(sy, su=su, q=q, n0=n0):
                    if q >= 1:
                        sy.wait_ge(s_mm.h, s_mm.n)
                    vsrc = mk.dram["Vt"][:, n0 + (q + 1) * 128:n0 + (q + 2) * 128, :]
                    s_ld.dma(sy.dma_start(
                        v4[(q + 1) % 2][:, :],
                        vsrc.rearrange("hd tq dd -> hd (tq dd)")))

            for g32 in range(4):
                bb = g32 % 2

                @block.tensor
                def _(t, su=su, q=q, g32=g32, bb=bb, qb=qb):
                    t.wait_ge(s_ld.h, s_ld.n)
                    if not (q == 0 and g32 == 0):
                        t.wait_ge(s_ev.h, s_ev.n - 1)
                    for tl in range(32):
                        tok_l = g32 * 32 + tl
                        t.matmul(ps_O[bb][:, tl * 16:(tl + 1) * 16],
                                 v4[qb][0:8, tok_l * 128:(tok_l + 1) * 128],
                                 A_sb[0:8,
                                      (q * 128 + tok_l) * 16:(q * 128 + tok_l + 1) * 16],
                                 start=True, stop=True, skip_group_check=True)
                    fence(mk, t, s_mm)

                @block.vector
                def _(v, su=su, q=q, g32=g32, bb=bb):
                    s_mm.wait(v)
                    t0 = q * 128 + g32 * 32
                    tw0 = t0 // 16
                    dst = O2_sb[:].rearrange("p (j h tw) -> p tw j h",
                                             j=16, h=16)[:, tw0:tw0 + 2, :, :]
                    src = ps_O[bb][:, :].rearrange("p (t2 j h) -> p t2 j h",
                                                   t2=2, j=16)
                    s_ev.inc(v.tensor_copy(dst, src))

        @block.sync
        def _(sy, su=su, n0=n0):
            sy.wait_ge(s_ev.h, s_ev.n)
            for j in range(16):
                dst = O2_d[j * 128:(j + 1) * 128, :].rearrange(
                    "dd (h t) -> dd h t", h=16)[:, :, su * 32:(su + 1) * 32]
                src = O2_sb[:, j * 512:(j + 1) * 512].rearrange(
                    "p (h tw) -> p h tw", h=16)
                s_st.dma(sy.dma_start(dst, src))

    mk.passB_done = [s_st, s_ev, s_sc, s_mm, s_ld]
    ph.close()
    phase_end(mk, mk.passB_done)


def emit_l0_passC(mk, block):
    nc = mk.nc
    ph = contextlib.ExitStack()
    sb = lambda n, s, d: ph.enter_context(nc.sbuf_tensor(n, list(s), d))
    phase_start(mk, block)

    wot = sb("c_wot", [128, 16 * 2048], BF16)
    o2c = [sb(f"c_o2{i}", [128, 16 * 512], BF16) for i in range(2)]
    g0c = [sb(f"c_g0{i}", [128, 16 * 512], BF16) for i in range(2)]
    sgc = [sb(f"c_sg{i}", [128, 16 * 512], BF16) for i in range(2)]
    ps = [ph.enter_context(nc.psum_tensor(f"c_ps{i}", [128, 512], F32))
          for i in range(2)]
    s_ld = mk.sem("c_ld")
    s_mm = mk.sem("c_mm")
    s_ev = mk.sem("c_ev")
    s_st = mk.sem("c_st")
    sg_d = mk.dram["sg_d"]

    @block.sync
    def _(sy):
        for kd in range(16):
            s_ld.dma(sy.dma_start(
                wot[:, kd * 2048:(kd + 1) * 2048],
                mk.dram["L0_attn_Wo_bf"][0:2048, kd * 128:(kd + 1) * 128],
                transpose=True))
    wot_ready = s_ld.n

    for nch in range(8):
        cb = nch % 2
        n0 = nch * 512

        @block.sync
        def _(sy, nch=nch, cb=cb, n0=n0):
            if nch >= 2:
                sy.wait_ge(s_st.h, 16 * (nch - 1))
                sy.wait_ge(s_mm.h, s_mm.n)
            src = mk.dram["O2_d"].rearrange("(h dd) n -> dd h n", dd=128)
            s_ld.dma(sy.dma_start(
                o2c[cb][:].rearrange("p (h t) -> p h t", h=16),
                src[:, :, n0:n0 + 512]))
            src2 = mk.dram["g0_d"].rearrange("(os dd) n -> dd os n", dd=128)
            s_ld.dma(sy.dma_start(
                g0c[cb][:].rearrange("p (os t) -> p os t", os=16),
                src2[:, :, n0:n0 + 512]))

        for os_ in range(16):
            @block.tensor
            def _(t, nch=nch, cb=cb, os_=os_):
                if nch == 0 and os_ == 0:
                    t.wait_ge(s_ld.h, wot_ready)
                t.wait_ge(s_ld.h, s_ld.n)
                pb = os_ % 2
                if not (nch == 0 and os_ <= 1):
                    t.wait_ge(s_ev.h, s_ev.n - 1)
                for kd in range(16):
                    t.matmul(ps[pb][:, :],
                             wot[:, kd * 2048 + os_ * 128: kd * 2048 + (os_ + 1) * 128],
                             o2c[cb][:, kd * 512:(kd + 1) * 512],
                             start=(kd == 0), stop=(kd == 15))
                fence(mk, t, s_mm)

            @block.vector
            def _(v, nch=nch, cb=cb, os_=os_):
                s_mm.wait(v)
                pb = os_ % 2
                s_ev.inc(v.scalar_tensor_tensor(
                    sgc[cb][:, os_ * 512:(os_ + 1) * 512], ps[pb][:, :],
                    mk.b_o[:, os_:os_ + 1], g0c[cb][:, os_ * 512:(os_ + 1) * 512],
                    ALU.add, ALU.mult))

        @block.sync
        def _(sy, nch=nch, cb=cb, n0=n0):
            sy.wait_ge(s_ev.h, s_ev.n)
            dst = sg_d.rearrange("(os dd) n -> dd os n", dd=128)
            s_st.dma(sy.dma_start(
                dst[:, :, n0:n0 + 512],
                sgc[cb][:].rearrange("p (os t) -> p os t", os=16)))

    mk.passC_done = [s_st, s_ev, s_mm, s_ld]
    ph.close()
    phase_end(mk, mk.passC_done)


def emit_combine(mk, block, x_name, sg_name, x1_name):
    nc = mk.nc
    ph = contextlib.ExitStack()
    sb = lambda n, s, d: ph.enter_context(nc.sbuf_tensor(n, list(s), d))
    phase_start(mk, block)
    tag = x1_name
    sgt = [sb(f"m_sg{i}_{tag}", [128, D], BF16) for i in range(2)]
    xt = [sb(f"m_x{i}_{tag}", [128, D], F32) for i in range(2)]
    x1t = [sb(f"m_x1{i}_{tag}", [128, D], F32) for i in range(2)]
    s_ld = mk.sem(f"m_ld_{tag}")
    s_ev = mk.sem(f"m_ev_{tag}")
    s_st = mk.sem(f"m_st_{tag}")
    xd = mk.dram[x_name]
    sgd = mk.dram[sg_name]
    x1d = mk.dram[x1_name]

    for tt in range(NT):
        b = tt % 2

        @block.sync
        def _(sy, tt=tt, b=b):
            if tt >= 2:
                sy.wait_ge(s_ev.h, tt - 1)
            s_ld.dma(sy.dma_start(sgt[b][:], sgd[0:D, tt * 128:(tt + 1) * 128],
                                  transpose=True))
            s_ld.dma(sy.dma_start(xt[b][:], xd[tt * 128:(tt + 1) * 128, :]))

        @block.vector
        def _(v, tt=tt, b=b):
            v.wait_ge(s_ld.h, 32 * (tt + 1))
            if tt >= 2:
                v.wait_ge(s_st.h, 16 * (tt - 1))
            s_ev.inc(v.tensor_tensor(x1t[b][:], xt[b][:], sgt[b][:], ALU.add))

        @block.sync
        def _(sy, tt=tt, b=b):
            sy.wait_ge(s_ev.h, tt + 1)
            s_st.dma(sy.dma_start(x1d[tt * 128:(tt + 1) * 128, :], x1t[b][:]))

    done = [s_st, s_ev, s_ld]
    ph.close()
    phase_end(mk, done)
    return Sem(s_st.h) if False else done[0]



def emit_l1_sp(mk, block):
    nc = mk.nc
    s = mk.sem("p1_ld")
    s_mm = mk.sem("p1_mm")
    mk.pq_sb = mk.sb("pq_sb", [128, 16 * 4], BF16)
    mk.b1_qkv = mk.sb("b1_qkv", [128, 48], F32)
    mk.b1_lmqkv = mk.sb("b1_lmqkv", [128, 48], F32)
    mk.b1_lmout = mk.sb("b1_lmout", [128, 16], F32)
    mk.b1_l2tq = mk.sb("b1_l2tq", [128, 16], F32)
    mk.b1_l2to = mk.sb("b1_l2to", [128, 16], F32)
    mk.b1_lmkv = mk.sb("b1_lmkv", [128, 32], F32)
    mk.b1_out = mk.sb("b1_out", [128, 16], F32)
    mk.b1_gate = mk.sb("b1_gate", [32, 1], F32)
    mk.cmask = mk.sb("c_cmask", [128, 512], F32)
    mk.hot64 = mk.sb("c_hot64", [128, 2], BF16)
    mk.hot64T = mk.sb("c_hot64T", [2, 128], BF16)
    mk.c_h128 = mk.sb("c_h128", [128, 128], BF16)
    mk.c_sw64 = mk.sb("c_sw64", [128, 128], BF16)
    phase_start(mk, block)
    ph = contextlib.ExitStack()
    tmp = ph.enter_context(nc.sbuf_tensor("p1_tmp", [4, 2048], BF16))
    ps_t = ph.enter_context(nc.psum_tensor("p1_ps", [128, 1024], BF16))

    @block.sync
    def _(sy):
        mk.s_wprep.wait(sy)
        s.dma(sy.dma_start(mk.cmask[:], mk.dram["cmask"][:, :]))
        for nm, dst, nstr in [("L1_attn_b_qkv", mk.b1_qkv, 48),
                              ("L1_attn_b_lm_qkv", mk.b1_lmqkv, 48),
                              ("L1_attn_b_lm_out", mk.b1_lmout, 16),
                              ("L1_attn_b_l2t_q", mk.b1_l2tq, 16),
                              ("L1_attn_b_l2t_out", mk.b1_l2to, 16),
                              ("L1_attn_b_lm_kv", mk.b1_lmkv, 32),
                              ("L1_attn_b_out", mk.b1_out, 16)]:
            v = mk.dram[nm].rearrange("(s p) -> p s", p=128)
            s.dma(sy.dma_start(dst[:, 0:nstr], v, allow_slow_non_contiguous=True))
        v = mk.dram["L1_attn_b_gate"].rearrange("(s p) -> p s", p=32)
        s.dma(sy.dma_start(mk.b1_gate[0:32, :], v, allow_slow_non_contiguous=True))
        s.dma(sy.dma_start(tmp[0:4, :], mk.dram["L1_attn_pool_q_bf"][:, :]))

    @block.gpsimd
    def _(g):
        s.dma(g.dma_start(mk.hot64[:], mk.dram["hot64"][:, :]))
        s.dma(g.dma_start(mk.hot64T[0:2, :], mk.dram["hot64T"][:, :]))
        s.dma(g.dma_start(mk.c_h128[:], mk.dram["h128"][:, :]))
        s.dma(g.dma_start(mk.c_sw64[:], mk.dram["swap64"][:, :]))

    for ht in range(16):
        @block.tensor
        def _(t, ht=ht):
            if ht == 0:
                s.wait(t)
                mk.s_const.wait(t)
            else:
                s_mm.wait(t)
            t.transpose(ps_t[:, 0:4], tmp[0:4, ht * 128:(ht + 1) * 128],
                        mk.c_ident[0:4, 0:4])
            fence(mk, t, s_mm)

        @block.vector
        def _(v, ht=ht):
            s_mm.wait(v)
            s_mm.inc(v.tensor_copy(mk.pq_sb[:, ht * 4:(ht + 1) * 4], ps_t[:, 0:4]))

    ph.close()
    phase_end(mk, [s, s_mm])
    mk.l1sp_done = s


def emit_l1_qkv(mk, block):
    nc = mk.nc
    ph = contextlib.ExitStack()
    sb = lambda n, s, d: ph.enter_context(nc.sbuf_tensor(n, list(s), d))
    phase_start(mk, block)
    HN = SEQ // 2
    xnf = sb("e_xnf", [128, 16 * HN], BF16)
    ropeC = sb("e_ropeC", [128, HN], F32)
    ropeS = sb("e_ropeS", [128, HN], F32)
    strip = [sb(f"e_strip{i}", [128, 16 * 128], BF16) for i in range(2)]
    qsb = [sb(f"e_qsb{i}", [128, 512], BF16) for i in range(2)]
    t1 = [sb(f"e_t1{i}", [128, 512], F32) for i in range(2)]
    t2 = [sb(f"e_t2{i}", [128, 512], F32) for i in range(2)]
    qrb = [sb(f"e_qrb{i}", [128, 512], BF16) for i in range(2)]
    ps_p = [ph.enter_context(nc.psum_tensor(f"e_psp{i}", [128, 512], F32))
            for i in range(2)]
    ps_r = [ph.enter_context(nc.psum_tensor(f"e_psr{i}", [128, 512], F32))
            for i in range(2)]
    s_ld = mk.sem("e_ld")
    s_mm = mk.sem("e_mm")
    s_sc = mk.sem("e_sc")
    s_ev = mk.sem("e_ev")
    s_st = mk.sem("e_st")
    outd = {"q": mk.dram["q1_d"], "k": mk.dram["k1_d"], "v": mk.dram["v1_d"]}

    NS = HN // 512
    for half in range(2):
        n0 = half * HN
        mm0, sc0, ev0, st0 = s_mm.n, s_sc.n, s_ev.n, s_st.n
        if half == 1:
            barrier(mk, block, [s_ld, s_mm, s_sc, s_ev, s_st])

        @block.sync
        def _(sy, n0=n0):
            for kd in range(16):
                s_ld.dma(sy.dma_start(
                    xnf[:, kd * HN:(kd + 1) * HN],
                    mk.dram["xn1"][n0:n0 + HN, kd * 128:(kd + 1) * 128],
                    transpose=True))
            s_ld.dma(sy.dma_start(ropeC[:], mk.dram["rope1_cos"][:, n0:n0 + HN]))
            s_ld.dma(sy.dma_start(ropeS[:], mk.dram["rope1_sin"][:, n0:n0 + HN]))
        xn_ready = s_ld.n

        for os_ in range(48):
            kind = "q" if os_ < 16 else ("k" if os_ < 32 else "v")
            dst = outd[kind]
            orow = (os_ % 16) * 128
            sbuf_i = os_ % 2

            @block.gpsimd
            def _(g, os_=os_, sbuf_i=sbuf_i, mm0=mm0):
                if os_ >= 2:
                    g.wait_ge(s_mm.h, mm0 + (os_ - 1) * NS * 2)
                else:
                    g.wait_ge(s_mm.h, mm0)
                w = mk.dram["L1_attn_w_qkv_bf"][:, os_ * 128:(os_ + 1) * 128]
                s_ld.dma(g.dma_start(strip[sbuf_i][:].rearrange(
                    "dd (kd ol) -> dd kd ol", ol=128),
                    w.rearrange("(kd dd) ol -> dd kd ol", dd=128)))

            for ns in range(NS):
                it = os_ * NS + ns
                b = it % 2

                @block.tensor
                def _(t, os_=os_, ns=ns, b=b, it=it, sbuf_i=sbuf_i,
                      xn_ready=xn_ready, sc0=sc0):
                    if ns == 0:
                        t.wait_ge(s_ld.h, xn_ready + 16 * (os_ + 1))
                    if it >= 2:
                        t.wait_ge(s_sc.h, sc0 + it - 1)
                    for kd in range(16):
                        t.matmul(ps_p[b][:, :],
                                 strip[sbuf_i][:, kd * 128:(kd + 1) * 128],
                                 xnf[:, kd * HN + ns * 512: kd * HN + (ns + 1) * 512],
                                 start=(kd == 0), stop=(kd == 15))
                    fence(mk, t, s_mm)

                @block.scalar
                def _(sc, it=it, b=b, os_=os_, mm0=mm0, ev0=ev0):
                    sc.wait_ge(s_mm.h, mm0 + 2 * it + 1)
                    if it >= 2:
                        sc.wait_ge(s_ev.h, ev0 + it - 1)
                    s_sc.inc(sc.activation(qsb[b][:], ps_p[b][:, :], AF.Identity,
                                           bias=mk.b1_qkv[:, os_:os_ + 1]))

                @block.tensor
                def _(t, it=it, b=b, sc0=sc0, ev0=ev0, kind=kind):
                    t.wait_ge(s_sc.h, sc0 + it + 1)
                    if it >= 2:
                        t.wait_ge(s_ev.h, ev0 + it - 1)
                    if kind in ("q", "k"):
                        t.matmul(ps_r[b][:, :], mk.c_rot1[:], qsb[b][:],
                                 start=True, stop=True)
                    fence(mk, t, s_mm)

                @block.vector
                def _(v, it=it, b=b, ns=ns, mm0=mm0, st0=st0, kind=kind):
                    v.wait_ge(s_mm.h, mm0 + 2 * it + 2)
                    if it >= 2:
                        v.wait_ge(s_st.h, st0 + 16 * (it - 1))
                    cs = slice(ns * 512, (ns + 1) * 512)
                    if kind in ("q", "k"):
                        v.tensor_tensor(t1[b][:], ps_r[b][:, :], ropeS[:, cs],
                                        ALU.mult)
                        v.tensor_tensor(t2[b][:], qsb[b][:], ropeC[:, cs],
                                        ALU.mult)
                        s_ev.inc(v.tensor_tensor(qrb[b][:], t1[b][:], t2[b][:],
                                                 ALU.add))
                    else:
                        s_ev.inc(v.tensor_copy(qrb[b][:], qsb[b][:]))

                @block.sync
                def _(sy, it=it, b=b, ns=ns, n0=n0, ev0=ev0, dst=dst, orow=orow):
                    sy.wait_ge(s_ev.h, ev0 + it + 1)
                    s_st.dma(sy.dma_start(
                        dst[orow:orow + 128, n0 + ns * 512:n0 + (ns + 1) * 512],
                        qrb[b][:]))

    mk.l1qkv_done = [s_st, s_ev, s_sc, s_mm, s_ld]
    ph.close()
    phase_end(mk, mk.l1qkv_done)


def emit_l1_loc(mk, block):
    nc = mk.nc
    mk.attn_es = contextlib.ExitStack()
    mk.lmfA = mk.attn_es.enter_context(mk.nc.sbuf_tensor("lmfA", [128, 16 * 256], BF16))
    mk.lmfB = mk.attn_es.enter_context(mk.nc.sbuf_tensor("lmfB", [128, 16 * 256], BF16))
    ph = contextlib.ExitStack()
    sb = lambda n, s, d: ph.enter_context(nc.sbuf_tensor(n, list(s), d))
    phase_start(mk, block)

    qt = [sb(f"g_qt{i}", [128, SEQ], BF16) for i in range(2)]
    kt = [sb(f"g_kt{i}", [128, SEQ], BF16) for i in range(2)]
    vt = [sb(f"g_vt{i}", [128, SEQ], BF16) for i in range(2)]
    vT = sb("g_vT", [128, SEQ], BF16)
    sc_sb = sb("g_sc", [128, 512], F32)
    ex_sb = sb("g_ex", [128, 512], F32)
    ex2_sb = sb("g_ex2", [128, 512], F32)
    pr_sb = [sb(f"g_pr{i}", [128, 512], BF16) for i in range(2)]
    PT_sb = [sb(f"g_PT{i}", [128, 512], BF16) for i in range(2)]
    loc_sb = sb("g_loc", [128, SEQ], BF16)
    mx8 = sb("g_mx8", [128, 8], F32)
    nm8 = sb("g_nm8", [128, 8], F32)
    s8 = sb("g_s8", [128, 8], F32)
    r8 = sb("g_r8", [128, 8], F32)
    pe_sb = sb("g_pe", [128, 256], BF16)
    s2 = sb("g_s2", [2, 256], F32)
    rs2 = sb("g_rs2", [2, 256], BF16)
    pa_sb = sb("g_pa", [128, 256], BF16)
    pa_sw = sb("g_pasw", [128, 256], BF16)
    ps_sc = [ph.enter_context(nc.psum_tensor(f"g_pssc{i}", [128, 512], F32))
             for i in range(2)]
    ps_T = ph.enter_context(nc.psum_tensor("g_psT", [128, 1024], BF16))
    ps_o = ph.enter_context(nc.psum_tensor("g_pso", [128, 512], F32))
    ps_pa = ph.enter_context(nc.psum_tensor("g_pspa", [128, 512], F32))
    ps_ms = ph.enter_context(nc.psum_tensor("g_psms", [128, 512], F32))
    ps_lmA = ph.enter_context(nc.psum_tensor("g_pslma", [128, 512], F32))
    ps_lmB = ph.enter_context(nc.psum_tensor("g_pslmb", [128, 512], F32))
    s_ld = mk.sem("g_ld")
    s_mm = mk.sem("g_mm")
    s_sc = mk.sem("g_sc")
    s_ev = mk.sem("g_ev")
    s_st = mk.sem("g_st")

    sc_fence = {}
    c1_done = {}
    e_done = {}
    p_done = {}
    tr_fence = {}
    ptev_done = {}
    og_fence = {}
    ev_done = {}
    ld_mark = {}
    loc_st_mark = {}
    ptev_last = [0]
    vtcopy_last = [0]

    def prologue(ht):
        hb = ht % 2

        @block.sync
        def _(sy, ht=ht, hb=hb):
            if ht >= 2:
                sy.wait_ge(s_ev.h, s_ev.n)
                sy.wait_ge(s_mm.h, s_mm.n)
            s_ld.dma(sy.dma_start(qt[hb][:], mk.dram["q1_d"][ht * 128:(ht + 1) * 128, :]))
            s_ld.dma(sy.dma_start(kt[hb][:], mk.dram["k1_d"][ht * 128:(ht + 1) * 128, :]))
            s_ld.dma(sy.dma_start(vt[hb][:], mk.dram["v1_d"][ht * 128:(ht + 1) * 128, :]))
            ld_mark[ht] = s_ld.n

        for grp in range(4):
            @block.tensor
            def _(t, ht=ht, hb=hb, grp=grp):
                t.wait_ge(s_ld.h, ld_mark[ht])
                t.wait_ge(s_sc.h, ptev_last[0])
                s_ev.wait(t)
                for jj in range(8):
                    np_ = grp * 8 + jj
                    t.transpose(ps_T[:, jj * 128:(jj + 1) * 128],
                                vt[hb][:, np_ * 128:(np_ + 1) * 128],
                                mk.c_ident[:])
                fence(mk, t, s_mm)

            @block.vector
            def _(v, grp=grp):
                s_mm.wait(v)
                s_ev.inc(v.tensor_copy(vT[:, grp * 1024:(grp + 1) * 1024],
                                       ps_T[:, 0:1024]))
                vtcopy_last[0] = s_ev.n

        @block.sync
        def _(sy, ht=ht):
            sy.wait_ge(s_ev.h, s_ev.n)
            s_st.dma(sy.dma_start(mk.dram["vT1_d"][ht, :, :], vT[:]))

    def a_part(u):
        ht, nbg = u // 8, u % 8
        hb = ht % 2

        @block.tensor
        def _(t, u=u, ht=ht, hb=hb, nbg=nbg):
            t.wait_ge(s_ld.h, ld_mark[ht])
            if u - 2 in c1_done:
                t.wait_ge(s_ev.h, c1_done[u - 2])
            for j in range(8):
                nb = nbg * 8 + j
                for h2 in range(2):
                    t.matmul(ps_sc[u % 2][h2 * 64:(h2 + 1) * 64,
                                          j * 64:(j + 1) * 64],
                             qt[hb][h2 * 64:(h2 + 1) * 64, nb * 64:(nb + 1) * 64],
                             kt[hb][h2 * 64:(h2 + 1) * 64, nb * 64:(nb + 1) * 64],
                             start=True, stop=True, skip_group_check=True)
            fence(mk, t, s_mm)
            sc_fence[u] = s_mm.n

        @block.vector
        def _(v, u=u):
            v.wait_ge(s_mm.h, sc_fence[u])
            if u - 1 in e_done:
                v.wait_ge(s_sc.h, e_done[u - 1])

            def S():
                s_ev.inc(v.drain())
                v.wait_ge(s_ev.h, s_ev.n)
            v.scalar_tensor_tensor(sc_sb[:], ps_sc[u % 2][:, :], 0.125,
                                   mk.cmask[:], ALU.mult, ALU.add)
            vv = sc_sb[:].rearrange("p (b s) -> p b s", s=64)
            v.tensor_reduce(mx8[:], vv, mybir.AxisListType.X, ALU.max)
            S()
            v.tensor_scalar(nm8[:], mx8[:], -1.0, None, ALU.mult)
            S()
            bview = nm8[:].unsqueeze(2).broadcast_to([128, 8, 64])
            s_ev.inc(v.tensor_tensor(
                ex_sb[:].rearrange("p (b s) -> p b s", s=64), vv, bview,
                ALU.add))
            c1_done[u] = s_ev.n

        @block.scalar
        def _(sc, u=u):
            sc.wait_ge(s_ev.h, c1_done[u])
            if u - 1 in p_done:
                sc.wait_ge(s_ev.h, p_done[u - 1])
            s_sc.inc(sc.activation(ex2_sb[:], ex_sb[:], AF.Exp))
            e_done[u] = s_sc.n

    def a_part2(u):
        @block.vector
        def _(v, u=u):
            v.wait_ge(s_sc.h, e_done[u])
            if u - 2 in tr_fence:
                v.wait_ge(s_mm.h, tr_fence[u - 2])

            def S():
                s_ev.inc(v.drain())
                v.wait_ge(s_ev.h, s_ev.n)
            ev_ = ex2_sb[:].rearrange("p (b s) -> p b s", s=64)
            v.tensor_reduce(s8[:], ev_, mybir.AxisListType.X, ALU.add)
            S()
            v.reciprocal(r8[:], s8[:])
            S()
            rview = r8[:].unsqueeze(2).broadcast_to([128, 8, 64])
            s_ev.inc(v.tensor_tensor(
                pr_sb[u % 2][:].rearrange("p (b s) -> p b s", s=64), ev_,
                rview, ALU.mult))
            p_done[u] = s_ev.n

    def b_part(u):
        ht, nbg = u // 8, u % 8
        hb = ht % 2

        @block.tensor
        def _(t, u=u):
            t.wait_ge(s_ev.h, p_done[u])
            t.wait_ge(s_ev.h, vtcopy_last[0])
            if u - 2 in ptev_done:
                t.wait_ge(s_sc.h, ptev_done[u - 2])
            for tp in range(4):
                t.transpose(ps_T[:, tp * 128:(tp + 1) * 128],
                            pr_sb[u % 2][:, tp * 128:(tp + 1) * 128],
                            mk.c_ident[:])
            fence(mk, t, s_mm)
            tr_fence[u] = s_mm.n

        @block.scalar
        def _(sc, u=u):
            sc.wait_ge(s_mm.h, tr_fence[u])
            if u - 2 in og_fence:
                sc.wait_ge(s_mm.h, og_fence[u - 2])
            s_sc.inc(sc.activation(PT_sb[u % 2][:], ps_T[:, 0:512], AF.Copy))
            ptev_done[u] = s_sc.n
            ptev_last[0] = s_sc.n

        for jg in range(2):
            @block.tensor
            def _(t, u=u, ht=ht, hb=hb, nbg=nbg, jg=jg):
                t.wait_ge(s_sc.h, ptev_done[u])
                mark = (u, jg - 1) if jg == 1 else (u - 1, 1)
                if mark in ev_done:
                    t.wait_ge(s_ev.h, ev_done[mark])
                for jj in range(4):
                    j = jg * 4 + jj
                    nb = nbg * 8 + j
                    par = j % 2
                    if jj > 0:
                        t.drain()
                    t.matmul(ps_o[:, jj * 128:(jj + 1) * 128],
                             vT[par * 64:(par + 1) * 64,
                                (nb // 2) * 128:(nb // 2 + 1) * 128],
                             PT_sb[u % 2][par * 64:(par + 1) * 64,
                                          (j // 2) * 128:(j // 2 + 1) * 128],
                             start=True, stop=True, skip_group_check=True)
                fence(mk, t, s_mm)
                og_fence[u] = s_mm.n

            @block.vector
            def _(v, u=u, ht=ht, nbg=nbg, jg=jg):
                v.wait_ge(s_mm.h, og_fence[u])
                if nbg == 0 and jg == 0 and ht - 1 in loc_st_mark:
                    v.wait_ge(s_st.h, loc_st_mark[ht - 1])
                c0 = (nbg * 8 + jg * 4) * 64
                pv = ps_o[:, 0:512].rearrange("p (j two s) -> p j two s",
                                              two=2, s=64)
                v.tensor_copy(
                    loc_sb[0:64, c0:c0 + 256].rearrange("p (j s) -> p j s", s=64),
                    pv[0:64, :, 0, :])
                s_ev.inc(v.tensor_copy(
                    loc_sb[64:128, c0:c0 + 256].rearrange("p (j s) -> p j s", s=64),
                    pv[64:128, :, 1, :]))
                ev_done[(u, jg)] = s_ev.n

    def loc_store(ht):
        @block.sync
        def _(sy, ht=ht):
            sy.wait_ge(s_ev.h, ev_done[(ht * 8 + 7, 1)])
            s_st.dma(sy.dma_start(mk.dram["loc_d"][ht * 128:(ht + 1) * 128, :],
                                  loc_sb[:]))
            loc_st_mark[ht] = s_st.n

    def pa_lmh(ht):
        hb = ht % 2

        @block.tensor
        def _(t, ht=ht, hb=hb):
            s_ev.wait(t)
            for nb in range(64):
                for h2 in range(2):
                    t.matmul(ps_pa[h2 * 64:(h2 + 1) * 64, nb * 4:(nb + 1) * 4],
                             kt[hb][h2 * 64:(h2 + 1) * 64, nb * 64:(nb + 1) * 64],
                             mk.pq_sb[h2 * 64:(h2 + 1) * 64, ht * 4:(ht + 1) * 4],
                             start=True, stop=True, skip_group_check=True)
            fence(mk, t, s_mm)

        @block.scalar
        def _(sc, ht=ht):
            s_mm.wait(sc)
            s_sc.inc(sc.activation(pe_sb[:], ps_pa[:, 0:256], AF.Exp,
                                   scale=0.125))

        @block.tensor
        def _(t, ht=ht):
            s_sc.wait(t)
            t.matmul(ps_ms[0:2, 0:256], mk.hot64[:], pe_sb[:],
                     start=True, stop=True)
            fence(mk, t, s_mm)

        @block.vector
        def _(v, ht=ht):
            s_mm.wait(v)

            def S():
                s_ev.inc(v.drain())
                v.wait_ge(s_ev.h, s_ev.n)
            v.tensor_copy(s2[0:2, :], ps_ms[0:2, 0:256])
            S()
            v.reciprocal(s2[0:2, :], s2[0:2, :])
            S()
            v.tensor_copy(rs2[0:2, :], s2[0:2, :])
            S()

        @block.tensor
        def _(t, ht=ht):
            s_ev.wait(t)
            t.matmul(ps_ms[:, 256:512], mk.hot64T[0:2, :], rs2[0:2, :],
                     start=True, stop=True)
            fence(mk, t, s_mm)

        @block.vector
        def _(v, ht=ht):
            s_mm.wait(v)
            s_ev.inc(v.tensor_tensor(pa_sb[:], pe_sb[:], ps_ms[:, 256:512],
                                     ALU.mult))

        @block.tensor
        def _(t, ht=ht):
            s_ev.wait(t)
            t.matmul(ps_ms[:, 0:256], mk.c_sw64[:], pa_sb[:],
                     start=True, stop=True)
            fence(mk, t, s_mm)

        @block.vector
        def _(v, ht=ht):
            s_mm.wait(v)
            s_ev.inc(v.tensor_copy(pa_sw[:], ps_ms[:, 0:256]))

        @block.tensor
        def _(t, ht=ht):
            s_ev.wait(t)
            for nb in range(64):
                par = nb % 2
                for h2 in range(2):
                    dst = ps_lmA if h2 == 0 else ps_lmB
                    srcpa = pa_sb if h2 == par else pa_sw
                    t.matmul(dst[:, nb * 4:(nb + 1) * 4],
                             vT[par * 64:(par + 1) * 64, (nb // 2) * 128:(nb // 2 + 1) * 128],
                             srcpa[par * 64:(par + 1) * 64, nb * 4:(nb + 1) * 4],
                             start=True, stop=True, skip_group_check=True)
            fence(mk, t, s_mm)

        @block.vector
        def _(v, ht=ht):
            s_mm.wait(v)
            v.tensor_copy(mk.lmfA[0:64, ht * 256:(ht + 1) * 256],
                          ps_lmA[0:64, 0:256])
            s_ev.inc(v.tensor_copy(mk.lmfA[64:128, ht * 256:(ht + 1) * 256],
                                   ps_lmB[64:128, 0:256]))

    NU = 16 * 8
    for u in range(NU + 1):
        ht, nbg = u // 8, u % 8
        if u < NU and nbg == 0:
            if u - 1 >= 0:
                b_part(u - 1)
                loc_store(ht - 1)
            prologue(ht)
        if u < NU:
            a_part(u)
            a_part2(u)
        if u - 1 >= 0 and (u - 1) % 8 != 7 and u <= NU - 1:
            b_part(u - 1)
        if u < NU and nbg == 7:
            pa_lmh(ht)
    b_part(NU - 1)
    loc_store(15)

    mk.l1loc_done = [s_st, s_ev, s_sc, s_mm, s_ld]
    ph.close()
    phase_end(mk, mk.l1loc_done)


def _strip_matmuls_direct(mk, t, ps, strip, rhs_sb, rhs_off, n):
    for kd in range(16):
        t.matmul(ps[:, 0:n], strip[:, kd * 128:(kd + 1) * 128],
                 rhs_sb[:, kd * 256 + rhs_off: kd * 256 + rhs_off + n],
                 start=(kd == 0), stop=(kd == 15))


def emit_l1_lm(mk, block):
    nc = mk.nc
    mk.gk_sb = mk.attn_es.enter_context(mk.nc.sbuf_tensor("gk_sb", [128, 16 * 256], BF16))
    mk.gv_sb = mk.attn_es.enter_context(mk.nc.sbuf_tensor("gv_sb", [128, 16 * 256], BF16))
    ph = contextlib.ExitStack()
    sb = lambda n, s, d: ph.enter_context(nc.sbuf_tensor(n, list(s), d))
    phase_start(mk, block)

    lq = sb("l_lq", [128, 16 * 256], BF16)
    lk = sb("l_lk", [128, 16 * 256], BF16)
    lv = sb("l_lv", [128, 16 * 256], BF16)
    lvT = sb("l_lvT", [128, 16 * 2 * 128], BF16)
    lo = sb("l_lo", [128, 16 * 256], BF16)
    l2q = sb("l_l2q", [128, 16 * 256], BF16)
    ro = sb("l_ro", [128, 16 * 256], BF16)
    strip = [sb(f"l_strip{i}", [128, 16 * 128], BF16) for i in range(2)]
    lmm = sb("l_lmm", [128, 2 * 256], F32)
    sc2 = sb("l_sc2", [128, 512], F32)
    ex2 = sb("l_ex2", [128, 512], F32)
    pr2 = sb("l_pr2", [128, 512], BF16)
    PT2 = sb("l_PT2", [128, 512], BF16)
    mx1 = sb("l_mx1", [128, 4], F32)
    nmx1 = sb("l_nmx1", [128, 4], F32)
    sum1 = sb("l_sum1", [128, 4], F32)
    rs1 = sb("l_rs1", [128, 4], F32)
    edump = sb("l_edump", [128, 256], F32)
    kt = sb("l_kt", [128, SEQ], BF16)
    vT = sb("l_vT", [128, SEQ], BF16)
    rsc = sb("l_rsc", [128, 256], F32)
    rpr = sb("l_rpr", [128, 256], BF16)
    rpw = sb("l_rpw", [128, 256], BF16)
    mxs = sb("l_mxs", [128, 2], F32)
    ps_a = ph.enter_context(nc.psum_tensor("l_psa", [128, 512], F32))
    ps_b = ph.enter_context(nc.psum_tensor("l_psb", [128, 512], F32))
    ps_T = ph.enter_context(nc.psum_tensor("l_psT", [128, 1024], BF16))
    ps_lmA = ph.enter_context(nc.psum_tensor("l_pslma", [128, 512], F32))
    ps_lmB = ph.enter_context(nc.psum_tensor("l_pslmb", [128, 512], F32))
    ps_Tf = ph.enter_context(nc.psum_tensor("l_psTf", [128, 512], F32))
    s_ld = mk.sem("l_ld")
    s_mm = mk.sem("l_mm")
    s_sc = mk.sem("l_sc")
    s_ev = mk.sem("l_ev")

    @block.sync
    def _(sy):
        s_ld.dma(sy.dma_start(lmm[:, 0:256], mk.dram["lmmask"][0:128, :]))
        s_ld.dma(sy.dma_start(lmm[:, 256:512], mk.dram["lmmask"][128:256, :]))

    def proj_256(wname, bias, dsts, rhs, nstrips):
        for os_ in range(nstrips):
            sbuf_i = os_ % 2

            @block.gpsimd
            def _(g, os_=os_, sbuf_i=sbuf_i, wname=wname):
                g.wait_ge(s_mm.h, s_mm.n)
                w = mk.dram[wname][:, os_ * 128:(os_ + 1) * 128]
                s_ld.dma(g.dma_start(strip[sbuf_i][:].rearrange(
                    "dd (kd ol) -> dd kd ol", ol=128),
                    w.rearrange("(kd dd) ol -> dd kd ol", dd=128)))

            @block.tensor
            def _(t, os_=os_, sbuf_i=sbuf_i, rhs=rhs):
                t.wait_ge(s_ld.h, s_ld.n)
                s_ev.wait(t)
                s_sc.wait(t)
                for kd in range(16):
                    t.matmul(ps_a[:, 0:256], strip[sbuf_i][:, kd * 128:(kd + 1) * 128],
                             rhs[:, kd * 256:(kd + 1) * 256],
                             start=(kd == 0), stop=(kd == 15))
                fence(mk, t, s_mm)

            @block.vector
            def _(v, os_=os_, bias=bias, dsts=dsts):
                s_mm.wait(v)
                dst, base = None, None
                for dsb, b0, b1 in dsts:
                    if b0 <= os_ < b1:
                        dst, base = dsb, b0
                s_sc.inc(v.tensor_scalar(
                    dst[:, (os_ - base) * 256:(os_ - base + 1) * 256],
                    ps_a[:, 0:256], bias[:, os_:os_ + 1], None, ALU.add))

    proj_256("L1_attn_w_lm_qkv_bf", mk.b1_lmqkv,
             [(lq, 0, 16), (lk, 16, 32), (lv, 32, 48)], mk.lmfA, 48)

    for grp in range(4):
        @block.tensor
        def _(t, grp=grp):
            s_sc.wait(t)
            s_ev.wait(t)
            for jj in range(8):
                c = grp * 8 + jj
                t.transpose(ps_T[:, jj * 128:(jj + 1) * 128],
                            lv[:, c * 128:(c + 1) * 128], mk.c_ident[:])
            fence(mk, t, s_mm)

        @block.vector
        def _(v, grp=grp):
            s_mm.wait(v)
            s_ev.inc(v.tensor_copy(lvT[:, grp * 1024:(grp + 1) * 1024],
                                   ps_T[:, 0:1024]))

    for ht in range(16):
        for h2 in range(2):
            @block.tensor
            def _(t, ht=ht, h2=h2):
                s_ev.wait(t)
                s_sc.wait(t)
                for qh in range(2):
                    t.matmul(ps_a[:, qh * 256:(qh + 1) * 256],
                             lq[h2 * 64:(h2 + 1) * 64,
                                ht * 256 + qh * 128: ht * 256 + (qh + 1) * 128],
                             lk[h2 * 64:(h2 + 1) * 64, ht * 256:(ht + 1) * 256],
                             start=True, stop=True, skip_group_check=True)
                fence(mk, t, s_mm)

            @block.vector
            def _(v, h2=h2):
                s_mm.wait(v)
                s_sc.wait(v)

                def S():
                    s_ev.inc(v.drain())
                    v.wait_ge(s_ev.h, s_ev.n)
                for qh in range(2):
                    v.scalar_tensor_tensor(sc2[:, qh * 256:(qh + 1) * 256],
                                           ps_a[:, qh * 256:(qh + 1) * 256],
                                           0.125, lmm[:, qh * 256:(qh + 1) * 256],
                                           ALU.mult, ALU.add)
                vv = sc2[:].rearrange("p (q k) -> p q k", k=256)
                v.tensor_reduce(mx1[:, 0:2], vv, mybir.AxisListType.X, ALU.max)
                S()
                v.tensor_scalar(nmx1[:, 0:2], mx1[:, 0:2], -1.0, None, ALU.mult)
                S()
                bv = nmx1[:, 0:2].unsqueeze(2).broadcast_to([128, 2, 256])
                s_ev.inc(v.tensor_tensor(
                    ex2[:].rearrange("p (q k) -> p q k", k=256), vv, bv, ALU.add))

            @block.scalar
            def _(sc, h2=h2):
                s_ev.wait(sc)
                s_sc.inc(sc.activation(sc2[:], ex2[:], AF.Exp))

            @block.vector
            def _(v, h2=h2):
                s_sc.wait(v)

                def S():
                    s_ev.inc(v.drain())
                    v.wait_ge(s_ev.h, s_ev.n)
                ev_ = sc2[:].rearrange("p (q k) -> p q k", k=256)
                v.tensor_reduce(sum1[:, 0:2], ev_, mybir.AxisListType.X, ALU.add)
                S()
                v.reciprocal(rs1[:, 0:2], sum1[:, 0:2])
                S()
                rv = rs1[:, 0:2].unsqueeze(2).broadcast_to([128, 2, 256])
                s_ev.inc(v.tensor_tensor(
                    pr2[:].rearrange("p (q k) -> p q k", k=256), ev_, rv,
                    ALU.mult))

            @block.tensor
            def _(t, h2=h2):
                s_ev.wait(t)
                for c in range(4):
                    t.transpose(ps_T[:, c * 128:(c + 1) * 128],
                                pr2[:, c * 128:(c + 1) * 128], mk.c_ident[:])
                fence(mk, t, s_mm)

            @block.vector
            def _(v, h2=h2):
                s_mm.wait(v)
                s_ev.inc(v.tensor_copy(PT2[:], ps_T[:, 0:512]))

            @block.tensor
            def _(t, ht=ht, h2=h2):
                s_ev.wait(t)
                for qh in range(2):
                    for kt_ in range(2):
                        t.matmul(ps_b[h2 * 64:(h2 + 1) * 64,
                                      qh * 128:(qh + 1) * 128],
                                 lvT[:, (ht * 2 + kt_) * 128 + h2 * 64:
                                     (ht * 2 + kt_) * 128 + (h2 + 1) * 64],
                                 PT2[:, (qh * 2 + kt_) * 128:
                                     (qh * 2 + kt_ + 1) * 128],
                                 start=(kt_ == 0), stop=(kt_ == 1),
                                 skip_group_check=True)
                fence(mk, t, s_mm)

        @block.vector
        def _(v, ht=ht):
            s_mm.wait(v)
            s_ev.inc(v.tensor_copy(lo[:, ht * 256:(ht + 1) * 256],
                                   ps_b[:, 0:256]))

    for os_ in range(16):
        sbuf_i = os_ % 2

        @block.gpsimd
        def _(g, os_=os_, sbuf_i=sbuf_i):
            g.wait_ge(s_mm.h, s_mm.n)
            w = mk.dram["L1_attn_w_lm_out_bf"][:, os_ * 128:(os_ + 1) * 128]
            s_ld.dma(g.dma_start(strip[sbuf_i][:].rearrange(
                "dd (kd ol) -> dd kd ol", ol=128),
                w.rearrange("(kd dd) ol -> dd kd ol", dd=128)))

        @block.tensor
        def _(t, os_=os_, sbuf_i=sbuf_i):
            t.wait_ge(s_ld.h, s_ld.n)
            s_ev.wait(t)
            for kd in range(16):
                t.matmul(ps_a[:, 0:256], strip[sbuf_i][:, kd * 128:(kd + 1) * 128],
                         lo[:, kd * 256:(kd + 1) * 256],
                         start=(kd == 0), stop=(kd == 15))
            fence(mk, t, s_mm)

        @block.vector
        def _(v, os_=os_):
            s_mm.wait(v)
            s_ev.inc(v.scalar_tensor_tensor(
                mk.lmfB[:, os_ * 256:(os_ + 1) * 256], ps_a[:, 0:256],
                mk.b1_lmout[:, os_:os_ + 1],
                mk.lmfA[:, os_ * 256:(os_ + 1) * 256], ALU.add, ALU.add))

    proj_256("L1_attn_w_l2t_q_bf", mk.b1_l2tq, [(l2q, 0, 16)], mk.lmfB, 16)

    for ht in range(16):
        @block.sync
        def _(sy, ht=ht):
            sy.wait_ge(s_ev.h, s_ev.n)
            sy.wait_ge(s_mm.h, s_mm.n)
            s_ld.dma(sy.dma_start(kt[:], mk.dram["k1_d"][ht * 128:(ht + 1) * 128, :]))
            s_ld.dma(sy.dma_start(vT[:], mk.dram["vT1_d"][ht, :, :]))

        @block.tensor
        def _(t, ht=ht):
            t.wait_ge(s_ld.h, s_ld.n)
            s_ev.wait(t)
            s_sc.wait(t)
            for nb in range(64):
                for h2 in range(2):
                    t.matmul(ps_a[h2 * 64:(h2 + 1) * 64, nb * 4:(nb + 1) * 4],
                             kt[h2 * 64:(h2 + 1) * 64, nb * 64:(nb + 1) * 64],
                             l2q[h2 * 64:(h2 + 1) * 64,
                                 ht * 256 + nb * 4: ht * 256 + (nb + 1) * 4],
                             start=True, stop=True, skip_group_check=True)
            fence(mk, t, s_mm)

        @block.vector
        def _(v, ht=ht):
            s_mm.wait(v)
            s_ev.inc(v.tensor_scalar(rsc[:], ps_a[:, 0:256], 0.125, None,
                                     ALU.mult))

        for chb in range(2):
            @block.tensor
            def _(t, chb=chb):
                s_ev.wait(t)
                t.transpose(ps_Tf[:, 0:128], rsc[:, chb * 128:(chb + 1) * 128],
                            mk.c_identf[:])
                fence(mk, t, s_mm)

            @block.vector
            def _(v, chb=chb):
                s_mm.wait(v)

                def S():
                    s_ev.inc(v.drain())
                    v.wait_ge(s_ev.h, s_ev.n)
                v.tensor_copy(ex2[:, 128:256], ps_Tf[:, 0:128])
                S()
                vv = ex2[:, 128:256].rearrange("p (h s) -> p h s", s=64)
                v.tensor_reduce(mxs[:], vv, mybir.AxisListType.X, ALU.max)
                S()
                v.tensor_scalar(mx1[:, 0:2], mxs[:], -1.0, None, ALU.mult)
                S()
                bv = mx1[:, 0:2].unsqueeze(2).broadcast_to([128, 2, 64])
                v.tensor_tensor(ex2[:, 0:128].rearrange("p (h s) -> p h s", s=64),
                                vv, bv, ALU.add)
                S()

            @block.scalar
            def _(sc, chb=chb):
                s_ev.wait(sc)
                s_sc.inc(sc.activation(sc2[:, 0:128], ex2[:, 0:128], AF.Exp))

            @block.vector
            def _(v, chb=chb):
                s_sc.wait(v)

                def S():
                    s_ev.inc(v.drain())
                    v.wait_ge(s_ev.h, s_ev.n)
                ev_ = sc2[:, 0:128].rearrange("p (h s) -> p h s", s=64)
                v.tensor_reduce(mxs[:], ev_, mybir.AxisListType.X, ALU.add)
                S()
                v.reciprocal(sum1[:, 0:2], mxs[:])
                S()
                rv = sum1[:, 0:2].unsqueeze(2).broadcast_to([128, 2, 64])
                v.tensor_tensor(pr2[:, 0:128].rearrange("p (h s) -> p h s", s=64),
                                ev_, rv, ALU.mult)
                S()

            @block.tensor
            def _(t, chb=chb):
                s_ev.wait(t)
                t.transpose(ps_T[:, 128:256], pr2[:, 0:128], mk.c_ident[:])
                fence(mk, t, s_mm)

            @block.vector
            def _(v, chb=chb):
                s_mm.wait(v)
                s_ev.inc(v.tensor_copy(rpr[:, chb * 128:(chb + 1) * 128],
                                       ps_T[:, 128:256]))

        @block.tensor
        def _(t, ht=ht):
            s_ev.wait(t)
            t.matmul(ps_b[:, 0:256], mk.c_sw64[:], rpr[:, 0:256],
                     start=True, stop=True)
            fence(mk, t, s_mm)

        @block.vector
        def _(v, ht=ht):
            s_mm.wait(v)
            s_ev.inc(v.tensor_copy(rpw[:], ps_b[:, 0:256]))

        @block.tensor
        def _(t, ht=ht):
            s_ev.wait(t)
            for nb in range(64):
                par = nb % 2
                for h2 in range(2):
                    dst = ps_lmA if h2 == 0 else ps_lmB
                    srcpa = rpr if h2 == par else rpw
                    t.matmul(dst[:, nb * 4:(nb + 1) * 4],
                             vT[par * 64:(par + 1) * 64,
                                (nb // 2) * 128:(nb // 2 + 1) * 128],
                             srcpa[par * 64:(par + 1) * 64, nb * 4:(nb + 1) * 4],
                             start=True, stop=True, skip_group_check=True)
            fence(mk, t, s_mm)

        @block.vector
        def _(v, ht=ht):
            s_mm.wait(v)
            v.tensor_copy(ro[0:64, ht * 256:(ht + 1) * 256], ps_lmA[0:64, 0:256])
            s_ev.inc(v.tensor_copy(ro[64:128, ht * 256:(ht + 1) * 256],
                                   ps_lmB[64:128, 0:256]))

    for os_ in range(16):
        sbuf_i = os_ % 2

        @block.gpsimd
        def _(g, os_=os_, sbuf_i=sbuf_i):
            g.wait_ge(s_mm.h, s_mm.n)
            w = mk.dram["L1_attn_w_l2t_out_bf"][:, os_ * 128:(os_ + 1) * 128]
            s_ld.dma(g.dma_start(strip[sbuf_i][:].rearrange(
                "dd (kd ol) -> dd kd ol", ol=128),
                w.rearrange("(kd dd) ol -> dd kd ol", dd=128)))

        @block.tensor
        def _(t, os_=os_, sbuf_i=sbuf_i):
            t.wait_ge(s_ld.h, s_ld.n)
            s_ev.wait(t)
            for kd in range(16):
                t.matmul(ps_a[:, 0:256], strip[sbuf_i][:, kd * 128:(kd + 1) * 128],
                         ro[:, kd * 256:(kd + 1) * 256],
                         start=(kd == 0), stop=(kd == 15))
            fence(mk, t, s_mm)

        @block.vector
        def _(v, os_=os_):
            s_mm.wait(v)
            s_ev.inc(v.scalar_tensor_tensor(
                mk.lmfA[:, os_ * 256:(os_ + 1) * 256], ps_a[:, 0:256],
                mk.b1_l2to[:, os_:os_ + 1],
                mk.lmfB[:, os_ * 256:(os_ + 1) * 256], ALU.add, ALU.add))

    for os_ in range(32):
        sbuf_i = os_ % 2

        @block.gpsimd
        def _(g, os_=os_, sbuf_i=sbuf_i):
            g.wait_ge(s_mm.h, s_mm.n)
            w = mk.dram["L1_attn_w_lm_kv_bf"][:, os_ * 128:(os_ + 1) * 128]
            s_ld.dma(g.dma_start(strip[sbuf_i][:].rearrange(
                "dd (kd ol) -> dd kd ol", ol=128),
                w.rearrange("(kd dd) ol -> dd kd ol", dd=128)))

        @block.tensor
        def _(t, os_=os_, sbuf_i=sbuf_i):
            t.wait_ge(s_ld.h, s_ld.n)
            s_sc.wait(t)
            s_ev.wait(t)
            for kd in range(16):
                t.matmul(ps_a[:, 0:256], strip[sbuf_i][:, kd * 128:(kd + 1) * 128],
                         mk.lmfA[:, kd * 256:(kd + 1) * 256],
                         start=(kd == 0), stop=(kd == 15))
            fence(mk, t, s_mm)

        @block.scalar
        def _(sc, os_=os_):
            s_mm.wait(sc)
            dst = mk.gk_sb if os_ < 16 else mk.gv_sb
            b0 = os_ % 16
            s_sc.inc(sc.activation(dst[:, b0 * 256:(b0 + 1) * 256],
                                   ps_a[:, 0:256], AF.Identity,
                                   bias=mk.b1_lmkv[:, os_:os_ + 1]))

    if "lmf_d" in mk.dbg:
        @block.sync
        def _(sy):
            sy.wait_ge(s_sc.h, s_sc.n)
            sy.wait_ge(s_ev.h, s_ev.n)
            dd = mk.dram["lmf_d"]
            s_ld.dma(sy.dma_start(dd[0:128, :], mk.lmfA[:]))
            s_ld.dma(sy.dma_start(dd[128:256, :], mk.lmfB[:]))
            s_ld.dma(sy.dma_start(dd[256:384, :], mk.gk_sb[:]))
            s_ld.dma(sy.dma_start(dd[384:512, :], mk.gv_sb[:]))
            s_ld.dma(sy.dma_start(dd[512:640, :], lo[:]))
            s_ld.dma(sy.dma_start(dd[640:768, :], ro[:]))

    mk.l1lm_done = [s_sc, s_ev, s_mm, s_ld]
    ph.close()
    phase_end(mk, mk.l1lm_done)


def emit_l1_glob(mk, block, parts=None):
    nc = mk.nc
    ph = contextlib.ExitStack()
    sb = lambda n, s, d: ph.enter_context(nc.sbuf_tensor(n, list(s), d))
    phase_start(mk, block)

    gate_sb = sb("gl_gate", [32, SEQ], BF16)
    selht = sb("gl_selht", [32, 16 * 128], BF16)
    gmsk = sb("gl_gmsk", [128, 32 * 256], F32)
    xnf = sb("gl_xnf", [128, 16 * 512], BF16)
    qt = [sb(f"gl_qt{i}", [128, SEQ], BF16) for i in range(2)]
    locb = [sb(f"gl_locb{i}", [128, SEQ], BF16) for i in range(2)]
    gvT = [sb(f"gl_gvT{i}", [128, 256], BF16) for i in range(2)]
    gsc = sb("gl_gsc", [128, 1024], F32)
    ge = sb("gl_ge", [128, 1024], F32)
    ge2 = sb("gl_ge2", [128, 1024], F32)
    gp = [sb(f"gl_gp{i}", [128, 1024], BF16) for i in range(2)]
    gpT = [sb(f"gl_gpT{i}", [128, 1024], BF16) for i in range(2)]
    dsb = sb("gl_dsb", [128, 128], F32)
    msb = sb("gl_msb", [128, 128], F32)
    att = sb("gl_att", [128, 512], BF16)
    mx1 = sb("gl_mx1", [128, 4], F32)
    nmx = sb("gl_nmx", [128, 4], F32)
    sm1 = sb("gl_sm1", [128, 4], F32)
    rs1 = sb("gl_rs1", [128, 4], F32)
    wg_sb = sb("gl_wg", [128, 16 * 32], BF16)
    ps_s = [[ph.enter_context(nc.psum_tensor(f"gl_pss{i}{j}", [128, 512], F32))
             for j in range(2)] for i in range(2)]
    ps_T = ph.enter_context(nc.psum_tensor("gl_psT", [128, 1024], BF16))
    ps_og = ph.enter_context(nc.psum_tensor("gl_psog", [128, 512], F32))
    ps_gb = ph.enter_context(nc.psum_tensor("gl_psgb", [128, 512], F32))
    s_ld = mk.sem("gl_ld")
    s_mm = mk.sem("gl_mm")
    s_sc = mk.sem("gl_sc")
    s_ev = mk.sem("gl_ev")
    s_st = mk.sem("gl_st")

    @block.gpsimd
    def _(g):
        s_ld.dma(g.dma_start(selht[0:32, :], mk.dram["selht"][:, :]))

    @block.sync
    def _(sy):
        s_ld.dma(sy.dma_start(
            gmsk[:].rearrange("p (tt nl) -> p tt nl", nl=256),
            mk.dram["gmask"][:, :].rearrange("(tt p) nl -> p tt nl", p=128)))
        s_ld.dma(sy.dma_start(
            wg_sb[:].rearrange("p (kd o) -> p kd o", o=32),
            mk.dram["L1_attn_w_gate_bf"][:, :].rearrange("(kd p) o -> p kd o", p=128)))

    for nch in range(8):
        @block.sync
        def _(sy, nch=nch):
            sy.wait_ge(s_mm.h, s_mm.n)
            for kd in range(16):
                s_ld.dma(sy.dma_start(
                    xnf[:, kd * 512:(kd + 1) * 512],
                    mk.dram["xn1"][nch * 512:(nch + 1) * 512,
                                   kd * 128:(kd + 1) * 128], transpose=True))

        @block.tensor
        def _(t, nch=nch):
            t.wait_ge(s_ld.h, s_ld.n)
            s_sc.wait(t)
            for kd in range(16):
                t.matmul(ps_s[0][0][0:32, 0:512],
                         wg_sb[:, kd * 32:(kd + 1) * 32],
                         xnf[:, kd * 512:(kd + 1) * 512],
                         start=(kd == 0), stop=(kd == 15))
            fence(mk, t, s_mm)

        @block.scalar
        def _(sc, nch=nch):
            s_mm.wait(sc)
            s_sc.inc(sc.activation(gate_sb[0:32, nch * 512:(nch + 1) * 512],
                                   ps_s[0][0][0:32, 0:512], AF.Sigmoid,
                                   bias=mk.b1_gate[0:32, 0:1]))

    sc_fence = {}
    c1_done = {}
    e_done = {}
    p_done = {}
    tr_fence = {}
    gpT_done = {}
    og_fence = {}
    cmb_done = {}
    st_done = {}
    ld_mark = {}
    gvT_done = {}
    gvT_last = [0]

    def prefetch(ht):
        hb = ht % 2

        @block.sync
        def _(sy, ht=ht, hb=hb):
            gq = (ht - 2) * 16 + 15
            if gq in sc_fence:
                sy.wait_ge(s_mm.h, sc_fence[gq])
            if gq in cmb_done:
                sy.wait_ge(s_ev.h, cmb_done[gq])
            s_ld.dma(sy.dma_start(qt[hb][:], mk.dram["q1_d"][ht * 128:(ht + 1) * 128, :]))
            s_ld.dma(sy.dma_start(locb[hb][:], mk.dram["loc_d"][ht * 128:(ht + 1) * 128, :]))
            ld_mark[ht] = s_ld.n

    prefetch(0)
    prefetch(1)

    NG = 16 * 16
    for g in range(NG + 1):
        ht, ttp = g // 16, g % 16
        hb = ht % 2

        if g < NG and ttp == 1 and 2 <= ht + 1 < 16:
            prefetch(ht + 1)

        if g < NG:
            if ttp == 0:
                @block.tensor
                def _(t, ht=ht, hb=hb, g=g):
                    gq = (ht - 2) * 16 + 15
                    if gq in og_fence:
                        t.wait_ge(s_mm.h, og_fence[gq])
                    if g - 1 in gpT_done:
                        t.wait_ge(s_sc.h, gpT_done[g - 1])
                    s_sc.wait(t)
                    for kt_ in range(2):
                        t.transpose(ps_T[:, kt_ * 128:(kt_ + 1) * 128],
                                    mk.gv_sb[:, ht * 256 + kt_ * 128:
                                             ht * 256 + (kt_ + 1) * 128],
                                    mk.c_ident[:])
                    fence(mk, t, s_mm)
                    gvT_done[ht] = s_mm.n

                @block.scalar
                def _(sc, ht=ht, hb=hb):
                    sc.wait_ge(s_mm.h, gvT_done[ht])
                    sc.activation(gvT[hb][:], ps_T[:, 0:256], AF.Copy)
                    s_sc.inc(sc.drain())
                    gvT_done[ht] = ("sc", s_sc.n)
                    gvT_last[0] = s_sc.n

            @block.tensor
            def _(t, g=g, ht=ht, ttp=ttp, hb=hb):
                t.wait_ge(s_ld.h, ld_mark[ht])
                if g == 0:
                    s_sc.wait(t)
                if g - 2 in c1_done:
                    t.wait_ge(s_ev.h, c1_done[g - 2])
                first = True
                for tt2 in range(2):
                    tt = ttp * 2 + tt2
                    for h2 in range(2):
                        if not first:
                            t.drain()
                        first = False
                        t.matmul(ps_s[g % 2][tt2][:, h2 * 256:(h2 + 1) * 256],
                                 qt[hb][h2 * 64:(h2 + 1) * 64,
                                        tt * 128:(tt + 1) * 128],
                                 mk.gk_sb[h2 * 64:(h2 + 1) * 64,
                                          ht * 256:(ht + 1) * 256],
                                 start=True, stop=True, skip_group_check=True)
                fence(mk, t, s_mm)
                sc_fence[g] = s_mm.n

            @block.vector
            def _(v, g=g, ht=ht, ttp=ttp):
                v.wait_ge(s_mm.h, sc_fence[g])
                if g - 1 in e_done:
                    v.wait_ge(s_sc.h, e_done[g - 1])

                def S():
                    s_ev.inc(v.drain())
                    v.wait_ge(s_ev.h, s_ev.n)
                for tt2 in range(2):
                    tt = ttp * 2 + tt2
                    for h2 in range(2):
                        v.scalar_tensor_tensor(
                            gsc[:, (tt2 * 2 + h2) * 256:(tt2 * 2 + h2 + 1) * 256],
                            ps_s[g % 2][tt2][:, h2 * 256:(h2 + 1) * 256],
                            0.125, gmsk[:, tt * 256:(tt + 1) * 256],
                            ALU.mult, ALU.add)
                vv = gsc[:].rearrange("p (c k) -> p c k", k=256)
                v.tensor_reduce(mx1[:], vv, mybir.AxisListType.X, ALU.max)
                S()
                v.tensor_scalar(nmx[:], mx1[:], -1.0, None, ALU.mult)
                S()
                bv = nmx[:].unsqueeze(2).broadcast_to([128, 4, 256])
                s_ev.inc(v.tensor_tensor(ge[:].rearrange("p (c k) -> p c k", k=256),
                                         vv, bv, ALU.add))
                c1_done[g] = s_ev.n

            @block.scalar
            def _(sc, g=g):
                sc.wait_ge(s_ev.h, c1_done[g])
                if g - 1 in p_done:
                    sc.wait_ge(s_ev.h, p_done[g - 1])
                s_sc.inc(sc.activation(ge2[:], ge[:], AF.Exp))
                e_done[g] = s_sc.n

            @block.vector
            def _(v, g=g, ttp=ttp):
                v.wait_ge(s_sc.h, e_done[g])
                if g - 2 in tr_fence:
                    v.wait_ge(s_mm.h, tr_fence[g - 2])

                def S():
                    s_ev.inc(v.drain())
                    v.wait_ge(s_ev.h, s_ev.n)
                ev_ = ge2[:].rearrange("p (c k) -> p c k", k=256)
                v.tensor_reduce(sm1[:], ev_, mybir.AxisListType.X, ALU.add)
                S()
                v.reciprocal(rs1[:], sm1[:])
                S()
                rv = rs1[:].unsqueeze(2).broadcast_to([128, 4, 256])
                v.tensor_tensor(gp[g % 2][:].rearrange("p (c k) -> p c k", k=256),
                                ev_, rv, ALU.mult)
                if ttp == 0:
                    s_ev.inc(v.drain())
                    v.wait_ge(s_ev.h, s_ev.n)
                    s_ev.inc(v.memset(gp[g % 2][0:64, 0:512], 0.0))
                else:
                    s_ev.inc(v.drain())
                p_done[g] = s_ev.n

        p = g - 1
        if p < 0:
            continue
        pht, pttp = p // 16, p % 16
        phb = pht % 2

        gvl = gvT_last[0]

        @block.tensor
        def _(t, p=p, gvl=gvl):
            t.wait_ge(s_ev.h, p_done[p])
            if p - 1 in gpT_done:
                t.wait_ge(s_sc.h, gpT_done[p - 1])
            t.wait_ge(s_sc.h, gvl)
            for c in range(8):
                t.transpose(ps_T[:, c * 128:(c + 1) * 128],
                            gp[p % 2][:, c * 128:(c + 1) * 128], mk.c_ident[:])
            fence(mk, t, s_mm)
            tr_fence[p] = s_mm.n

        @block.scalar
        def _(sc, p=p):
            sc.wait_ge(s_mm.h, tr_fence[p])
            if p - 2 in og_fence:
                sc.wait_ge(s_mm.h, og_fence[p - 2])
            s_sc.inc(sc.activation(gpT[p % 2][:], ps_T[:, 0:1024], AF.Copy))
            gpT_done[p] = s_sc.n

        @block.tensor
        def _(t, p=p, pht=pht, pttp=pttp, phb=phb):
            t.wait_ge(s_sc.h, gpT_done[p])
            t.wait_ge(s_sc.h, gvT_done[pht][1])
            if p - 1 in cmb_done:
                t.wait_ge(s_ev.h, cmb_done[p - 1])
            for tt2 in range(2):
                for h2 in range(2):
                    for kt_ in range(2):
                        c = (tt2 * 2 + h2) * 2 + kt_
                        t.matmul(ps_og[h2 * 64:(h2 + 1) * 64,
                                       tt2 * 128:(tt2 + 1) * 128],
                                 gvT[phb][:, kt_ * 128 + h2 * 64:
                                       kt_ * 128 + (h2 + 1) * 64],
                                 gpT[p % 2][:, c * 128:(c + 1) * 128],
                                 start=(kt_ == 0), stop=(kt_ == 1),
                                 skip_group_check=True)
            t.matmul(ps_gb[:, 0:256], selht[0:32, pht * 128:(pht + 1) * 128],
                     gate_sb[0:32, pttp * 256:(pttp + 1) * 256],
                     start=True, stop=True, skip_group_check=True)
            fence(mk, t, s_mm)
            og_fence[p] = s_mm.n

        @block.vector
        def _(v, p=p, pht=pht, pttp=pttp, phb=phb):
            v.wait_ge(s_mm.h, og_fence[p])
            if p % 2 == 0 and p - 1 in st_done:
                v.wait_ge(s_st.h, st_done[p - 1])
            for tt2 in range(2):
                tt = pttp * 2 + tt2
                lsl = locb[phb][:, tt * 128:(tt + 1) * 128]
                v.tensor_tensor(dsb[:], ps_og[:, tt2 * 128:(tt2 + 1) * 128],
                                lsl, ALU.subtract)
                v.tensor_tensor(msb[:], dsb[:],
                                ps_gb[:, tt2 * 128:(tt2 + 1) * 128], ALU.mult)
                v.tensor_tensor(att[:, (p % 2) * 256 + tt2 * 128:
                                    (p % 2) * 256 + (tt2 + 1) * 128],
                                msb[:], lsl, ALU.add)
            s_ev.inc(v.drain())
            cmb_done[p] = s_ev.n

        if p % 2 == 1:
            @block.sync
            def _(sy, p=p, pht=pht, pttp=pttp):
                sy.wait_ge(s_ev.h, cmb_done[p])
                c0 = (pttp - 1) * 256
                s_st.dma(sy.dma_start(
                    mk.dram["att1_d"][pht * 128:(pht + 1) * 128, c0:c0 + 512],
                    att[:]))
                st_done[p] = s_st.n

    mk.l1glob_done = [s_sc, s_ev, s_mm, s_ld, s_st]
    ph.close()
    phase_end(mk, mk.l1glob_done)


def emit_w2prep(mk, block):
    nc = mk.nc
    W2p = mk.dtmp("W2p_bf", [D, 8192], BF16)
    ph = contextlib.ExitStack()
    sb = lambda n, s, d: ph.enter_context(nc.sbuf_tensor(n, list(s), d))
    phase_start(mk, block)
    bufA = sb("wp_bufA", [128, 8192], F32)
    bufB = sb("wp_bufB", [128, 8192], F32)
    obuf = sb("wp_obuf", [128, 8192], BF16)
    s_ld = mk.sem("wp_ld")
    s_ev = mk.sem("wp_ev")
    s_st = mk.sem("wp_st")

    for ot in range(16):
        @block.sync
        def _(sy, ot=ot):
            if ot >= 1:
                sy.wait_ge(s_ev.h, s_ev.n)
            s_ld.dma(sy.dma_start(bufA[:], mk.dram["L1_ffn_W2"][ot * 128:(ot + 1) * 128, :]))

        @block.vector
        def _(v, ot=ot):
            v.wait_ge(s_ld.h, s_ld.n)
            if ot >= 1:
                v.wait_ge(s_st.h, s_st.n)

            def S():
                s_ev.inc(v.drain())
                v.wait_ge(s_ev.h, s_ev.n)
            cur = bufA
            for st in range(6):
                Bs = 128 << st
                G = 8192 // (2 * Bs)
                dst = obuf if st == 5 else (bufB if cur is bufA else bufA)
                vi = cur[:].rearrange("p (g j b) -> p g j b", j=2, b=Bs)
                vo = dst[:].rearrange("p (g j b) -> p g j b", j=2, b=Bs)
                v.tensor_tensor(vo[:, :, 0, :], vi[:, :, 0, :], vi[:, :, 1, :],
                                ALU.add)
                S()
                v.tensor_tensor(vo[:, :, 1, :], vi[:, :, 0, :], vi[:, :, 1, :],
                                ALU.subtract)
                S()
                cur = dst

        @block.sync
        def _(sy, ot=ot):
            sy.wait_ge(s_ev.h, s_ev.n)
            s_st.dma(sy.dma_start(W2p[ot * 128:(ot + 1) * 128, :], obuf[:]))

    mk.w2p_done = [s_st, s_ev, s_ld]
    ph.close()
    phase_end(mk, mk.w2p_done)


def emit_l1_ffn(mk, block):
    nc = mk.nc
    hid_d = mk.dtmp("hid_d", [8192, SEQ], BF16)
    ph = contextlib.ExitStack()
    sb = lambda n, s, d: ph.enter_context(nc.sbuf_tensor(n, list(s), d))
    phase_start(mk, block)
    xnf = [sb(f"f_xnf{i}", [128, 16 * 512], BF16) for i in range(2)]
    w1s = [sb(f"f_w1s{i}", [128, 16 * 128], BF16) for i in range(2)]
    w3s = [sb(f"f_w3s{i}", [128, 16 * 128], BF16) for i in range(2)]
    sU = [sb(f"f_sU{i}", [128, 512], BF16) for i in range(2)]
    hidb = [sb(f"f_hid{i}", [128, 512], BF16) for i in range(2)]
    hid2 = [sb(f"f_hid2{i}", [128, 512], BF16) for i in range(2)]
    psU = [ph.enter_context(nc.psum_tensor(f"f_psU{i}", [128, 512], F32))
           for i in range(2)]
    psG = [ph.enter_context(nc.psum_tensor(f"f_psG{i}", [128, 512], F32))
           for i in range(2)]
    psH = [ph.enter_context(nc.psum_tensor(f"f_psH{i}", [128, 512], F32))
           for i in range(2)]
    s_ld = mk.sem("f_ld")
    s_ws = mk.sem("f_ws")
    s_mm = mk.sem("f_mm")
    s_sc = mk.sem("f_sc")
    s_ev = mk.sem("f_ev")
    s_st = mk.sem("f_st")
    ug_fence = {}
    h_fence = {}
    sil_done = {}
    tt_done = {}
    ev2_done = {}
    st_done = {}

    def emit_h128(t, it):
        b = it % 2
        t.wait_ge(s_ev.h, tt_done[it])
        if it - 2 in ev2_done:
            t.wait_ge(s_ev.h, ev2_done[it - 2])
        t.matmul(psH[b][:, :], mk.c_h128[:], hidb[b][:],
                 start=True, stop=True)
        fence(mk, t, s_mm)
        h_fence[it] = s_mm.n

    for ns in range(8):
        @block.sync
        def _(sy, ns=ns):
            xb = ns % 2
            gate = ns * 64 - 65
            if gate in ug_fence:
                sy.wait_ge(s_mm.h, ug_fence[gate])
            for kd in range(16):
                s_ld.dma(sy.dma_start(
                    xnf[xb][:, kd * 512:(kd + 1) * 512],
                    mk.dram["xn1"][ns * 512:(ns + 1) * 512,
                                   kd * 128:(kd + 1) * 128], transpose=True))

        for os_ in range(64):
            b = os_ % 2
            xb = ns % 2
            it = ns * 64 + os_

            @block.gpsimd
            def _(g, ns=ns, os_=os_, b=b, it=it):
                if it - 2 in ug_fence:
                    g.wait_ge(s_mm.h, ug_fence[it - 2])
                for nm, dstb in (("wsc_L1_ffn_W1", w1s), ("wsc_L1_ffn_W3", w3s)):
                    w = mk.dram[nm][os_, :, :, :].rearrange("kd dd ol -> dd kd ol")
                    s_ws.dma(g.dma_start(dstb[b][:].rearrange(
                        "dd (kd ol) -> dd kd ol", ol=128), w))

            @block.tensor
            def _(t, ns=ns, os_=os_, b=b, xb=xb, it=it):
                t.wait_ge(s_ws.h, s_ws.n)
                t.wait_ge(s_ld.h, s_ld.n)
                if it - 2 in sil_done:
                    t.wait_ge(s_sc.h, sil_done[it - 2])
                if it - 2 in tt_done:
                    t.wait_ge(s_ev.h, tt_done[it - 2])
                for kd in range(16):
                    t.matmul(psU[b][:, :], w1s[b][:, kd * 128:(kd + 1) * 128],
                             xnf[xb][:, kd * 512:(kd + 1) * 512],
                             start=(kd == 0), stop=(kd == 15))
                for kd in range(16):
                    t.matmul(psG[b][:, :], w3s[b][:, kd * 128:(kd + 1) * 128],
                             xnf[xb][:, kd * 512:(kd + 1) * 512],
                             start=(kd == 0), stop=(kd == 15))
                fence(mk, t, s_mm)
                ug_fence[it] = s_mm.n
                if it - 1 in tt_done:
                    emit_h128(t, it - 1)

            @block.scalar
            def _(sc, ns=ns, os_=os_, b=b, it=it):
                sc.wait_ge(s_mm.h, ug_fence[it])
                if it - 2 in tt_done:
                    sc.wait_ge(s_ev.h, tt_done[it - 2])
                s_sc.inc(sc.activation(sU[b][:], psU[b][:, :], AF.Silu))
                sil_done[it] = s_sc.n

            @block.vector
            def _(v, ns=ns, os_=os_, b=b, it=it):
                v.wait_ge(s_sc.h, sil_done[it])
                if it - 2 in h_fence:
                    v.wait_ge(s_mm.h, h_fence[it - 2])
                s_ev.inc(v.tensor_tensor(hidb[b][:], sU[b][:], psG[b][:, :],
                                         ALU.mult))
                tt_done[it] = s_ev.n

            @block.vector
            def _(v, ns=ns, os_=os_, it=it):
                pit = it - 1
                if pit not in tt_done or pit < 0:
                    return
                pb = pit % 2
                v.wait_ge(s_mm.h, h_fence.get(pit, 0))
                if pit - 2 in st_done:
                    v.wait_ge(s_st.h, st_done[pit - 2])
                s_ev.inc(v.tensor_copy(hid2[pb][:], psH[pb][:, :]))
                ev2_done[pit] = s_ev.n

            @block.sync
            def _(sy, ns=ns, os_=os_, it=it):
                pit = it - 1
                if pit not in ev2_done:
                    return
                pns, pos = pit // 64, pit % 64
                pb = pit % 2
                sy.wait_ge(s_ev.h, ev2_done[pit])
                s_st.dma(sy.dma_start(
                    mk.dram["hid_d"][pos * 128:(pos + 1) * 128,
                                     pns * 512:(pns + 1) * 512], hid2[pb][:]))
                st_done[pit] = s_st.n

    last = 8 * 64 - 1

    @block.tensor
    def _(t):
        emit_h128(t, last)

    @block.vector
    def _(v):
        pb = last % 2
        v.wait_ge(s_mm.h, h_fence[last])
        s_ev.inc(v.tensor_copy(hid2[pb][:], psH[pb][:, :]))
        ev2_done[last] = s_ev.n

    @block.sync
    def _(sy):
        pb = last % 2
        sy.wait_ge(s_ev.h, ev2_done[last])
        s_st.dma(sy.dma_start(
            mk.dram["hid_d"][63 * 128:64 * 128, 7 * 512:8 * 512], hid2[pb][:]))
        st_done[last] = s_st.n

    mk.ffn_done = [s_st, s_ev, s_sc, s_mm, s_ws, s_ld]
    ph.close()
    phase_end(mk, mk.ffn_done)


def emit_l1_ffn2(mk, block):
    nc = mk.nc
    g1_d = mk.dtmp("g1_d", [D, SEQ], BF16)
    ph = contextlib.ExitStack()
    sb = lambda n, s, d: ph.enter_context(nc.sbuf_tensor(n, list(s), d))
    phase_start(mk, block)
    hch = sb("f2_hch", [128, 64 * 512], BF16)
    w2s = [sb(f"f2_w2s{i}", [128, 64 * 128], BF16) for i in range(2)]
    g1b = [sb(f"f2_g1b{i}", [128, 512], BF16) for i in range(2)]
    psW = [ph.enter_context(nc.psum_tensor(f"f2_psW{i}", [128, 512], F32))
           for i in range(2)]
    s_ld = mk.sem("f2_ld")
    s_ws = mk.sem("f2_ws")
    s_mm = mk.sem("f2_mm")
    s_sc = mk.sem("f2_sc")
    s_st = mk.sem("f2_st")
    mm_fence = {}
    sil2_done = {}
    st2_done = {}

    for nch in range(8):
        @block.sync
        def _(sy, nch=nch):
            sy.wait_ge(s_mm.h, s_mm.n)
            s_ld.dma(sy.dma_start(
                hch[:].rearrange("p (kd j) -> p kd j", j=512),
                mk.dram["hid_d"][:, nch * 512:(nch + 1) * 512].rearrange(
                    "(kd p) j -> p kd j", p=128)))

        for os_ in range(16):
            b = os_ % 2
            it = nch * 16 + os_

            @block.gpsimd
            def _(g, nch=nch, os_=os_, b=b, it=it):
                if it - 2 in mm_fence:
                    g.wait_ge(s_mm.h, mm_fence[it - 2])
                w = mk.dram["wsc_W2p"][os_, :, :, :].rearrange("kd dd ol -> dd kd ol")
                s_ws.dma(g.dma_start(w2s[b][:].rearrange(
                    "dd (kd ol) -> dd kd ol", ol=128), w))

            @block.tensor
            def _(t, nch=nch, os_=os_, b=b, it=it):
                t.wait_ge(s_ws.h, s_ws.n)
                t.wait_ge(s_ld.h, s_ld.n)
                if it - 2 in sil2_done:
                    t.wait_ge(s_sc.h, sil2_done[it - 2])
                for kd in range(64):
                    t.matmul(psW[b][:, :], w2s[b][:, kd * 128:(kd + 1) * 128],
                             hch[:, kd * 512:(kd + 1) * 512],
                             start=(kd == 0), stop=(kd == 63))
                fence(mk, t, s_mm)
                mm_fence[it] = s_mm.n

            @block.scalar
            def _(sc, nch=nch, os_=os_, b=b, it=it):
                sc.wait_ge(s_mm.h, mm_fence[it])
                if it - 2 in st2_done:
                    sc.wait_ge(s_st.h, st2_done[it - 2])
                s_sc.inc(sc.activation(g1b[b][:], psW[b][:, :], AF.Silu))
                sil2_done[it] = s_sc.n

            @block.sync
            def _(sy, nch=nch, os_=os_, b=b, it=it):
                sy.wait_ge(s_sc.h, sil2_done[it])
                s_st.dma(sy.dma_start(
                    mk.dram["g1_d"][os_ * 128:(os_ + 1) * 128,
                                    nch * 512:(nch + 1) * 512], g1b[b][:]))
                st2_done[it] = s_st.n

    mk.ffn2_done = [s_st, s_sc, s_mm, s_ws, s_ld]
    ph.close()
    phase_end(mk, mk.ffn2_done)


def emit_l1_wo(mk, block):
    nc = mk.nc
    sg1_d = mk.dtmp("sg1_d", [D, SEQ], BF16)
    ph = contextlib.ExitStack()
    sb = lambda n, s, d: ph.enter_context(nc.sbuf_tensor(n, list(s), d))
    phase_start(mk, block)
    wot = sb("o_wot", [128, 16 * 2048], BF16)
    a1c = [sb(f"o_a1{i}", [128, 16 * 512], BF16) for i in range(2)]
    g1c = [sb(f"o_g1{i}", [128, 16 * 512], BF16) for i in range(2)]
    sgc = [sb(f"o_sg{i}", [128, 16 * 512], BF16) for i in range(2)]
    ps = [ph.enter_context(nc.psum_tensor(f"o_ps{i}", [128, 512], F32))
          for i in range(2)]
    s_ld = mk.sem("o_ld")
    s_mm = mk.sem("o_mm")
    s_ev = mk.sem("o_ev")
    s_st = mk.sem("o_st")

    @block.sync
    def _(sy):
        for kd in range(16):
            s_ld.dma(sy.dma_start(
                wot[:, kd * 2048:(kd + 1) * 2048],
                mk.dram["L1_attn_w_out_bf"][kd * 128:(kd + 1) * 128, 0:2048]))
    wot_ready = s_ld.n

    for nch in range(8):
        cb = nch % 2
        n0 = nch * 512

        @block.sync
        def _(sy, nch=nch, cb=cb, n0=n0):
            if nch >= 2:
                sy.wait_ge(s_st.h, 16 * (nch - 1))
                sy.wait_ge(s_mm.h, s_mm.n)
            src = mk.dram["att1_d"].rearrange("(h dd) n -> dd h n", dd=128)
            s_ld.dma(sy.dma_start(
                a1c[cb][:].rearrange("p (h t) -> p h t", h=16),
                src[:, :, n0:n0 + 512]))
            src2 = mk.dram["g1_d"].rearrange("(os dd) n -> dd os n", dd=128)
            s_ld.dma(sy.dma_start(
                g1c[cb][:].rearrange("p (os t) -> p os t", os=16),
                src2[:, :, n0:n0 + 512]))

        for os_ in range(16):
            @block.tensor
            def _(t, nch=nch, cb=cb, os_=os_):
                if nch == 0 and os_ == 0:
                    t.wait_ge(s_ld.h, wot_ready)
                t.wait_ge(s_ld.h, s_ld.n)
                pb = os_ % 2
                if not (nch == 0 and os_ <= 1):
                    t.wait_ge(s_ev.h, s_ev.n - 1)
                for kd in range(16):
                    t.matmul(ps[pb][:, :],
                             wot[:, kd * 2048 + os_ * 128: kd * 2048 + (os_ + 1) * 128],
                             a1c[cb][:, kd * 512:(kd + 1) * 512],
                             start=(kd == 0), stop=(kd == 15))
                fence(mk, t, s_mm)

            @block.vector
            def _(v, nch=nch, cb=cb, os_=os_):
                s_mm.wait(v)
                pb = os_ % 2
                s_ev.inc(v.scalar_tensor_tensor(
                    sgc[cb][:, os_ * 512:(os_ + 1) * 512], ps[pb][:, :],
                    mk.b1_out[:, os_:os_ + 1], g1c[cb][:, os_ * 512:(os_ + 1) * 512],
                    ALU.add, ALU.mult))

        @block.sync
        def _(sy, nch=nch, cb=cb, n0=n0):
            sy.wait_ge(s_ev.h, s_ev.n)
            dst = mk.dram["sg1_d"].rearrange("(os dd) n -> dd os n", dd=128)
            s_st.dma(sy.dma_start(
                dst[:, :, n0:n0 + 512],
                sgc[cb][:].rearrange("p (os t) -> p os t", os=16)))

    mk.wo_done = [s_st, s_ev, s_mm, s_ld]
    ph.close()
    phase_end(mk, mk.wo_done)


def build_program(dbg=(), stages=("prep", "ln0")):
    mk = MK(dbg=dbg)
    nc = mk.nc
    mk.din("x", [SEQ, D], F32)
    for n, shp in INPUT_SHAPES.items():
        if n != "L0_attn_hidden_states":
            mk.din(n, shp, F32)
    for n, arr in HOST_CONSTS.items():
        mk.din(n, list(arr.shape), F32)
    out = mk.dout("out", [SEQ, D], F32)

    mk.dtmp("xn0", [SEQ, D], BF16)
    mk.dtmp("qr_d", [24, 128, SEQ], BF16)
    mk.dtmp("Vt", [8, SEQ, 128], BF16)
    mk.dtmp("g0_d", [D, SEQ], BF16)
    mk.dtmp("f1q_d", [128, 8], F32)
    mk.dtmp("O2_d", [D, SEQ], BF16)
    mk.dtmp("sg_d", [D, SEQ], BF16)
    mk.dtmp("x1_d", [SEQ, D], F32)
    mk.dtmp("xn1", [SEQ, D], BF16)
    mk.dtmp("q1_d", [D, SEQ], BF16)
    mk.dtmp("k1_d", [D, SEQ], BF16)
    mk.dtmp("v1_d", [D, SEQ], BF16)
    mk.dtmp("loc_d", [D, SEQ], BF16)
    mk.dtmp("att1_d", [D, SEQ], BF16)
    mk.dtmp("vT1_d", [16, 128, SEQ], BF16)
    if "wfm_d" in mk.dbg:
        mk.dtmp("wfm_d", [8, SEQ], BF16)
    if "f1dbg_d" in mk.dbg:
        mk.dtmp("f1dbg_d", [128, 32], F32)
    if "f1all_d" in mk.dbg:
        mk.dtmp("f1all_d", [SEQ, 24], F32)
    if "xnf_d" in mk.dbg:
        mk.dtmp("xnf_d", [128, 16 * SEQ], BF16)
    if "axw_d" in mk.dbg:
        mk.dtmp("axw_d", [128, SEQ], BF16)
    if "ln_stats" in mk.dbg:
        mk.dtmp("ln_stats", [SEQ, 4], F32)
    if "lmf_d" in mk.dbg:
        mk.dtmp("lmf_d", [768, 4096], BF16)

    mk.c_eps = None

    mk.s_wprep = mk.sem("wprep")
    mk.s_const = mk.sem("constld")

    with nc.Block() as block:
        mk.c_eps = mk.sb("c_eps", [128, 1], F32)
        mk.zero_col = mk.sb("c_zero", [128, 1], F32)
        mk.s_eps = mk.sem("eps")

        @block.vector
        def _(v):
            v.memset(mk.zero_col[:], 0.0)
            mk.s_eps.inc(v.memset(mk.c_eps[:], 1e-5))

        if "prep" in stages:
            emit_weight_prep(mk, block)
        emit_const_load(mk, block)

        if "ln0" in stages:
            mk.ln_done = emit_layernorm(mk, block, "x", "xn0")
        if "wtrans" in stages:
            emit_wtrans(mk, block, ["L0_attn_Wq", "L0_attn_Wk", "L0_attn_Wv",
                                    "L0_attn_Wo", "L0_ffn_weight",
                                    "L1_ffn_W1", "L1_ffn_W3"])
        l0sp_st = [s for s in stages if s.startswith("l0sp")]
        if l0sp_st:
            spec = l0sp_st[0]
            parts = tuple(spec.split(":", 1)[1].split(",")) if ":" in spec else ("ld", "rw", "la", "bt")
            emit_l0_prep_small(mk, block, parts=parts)
        l0a_st = [s for s in stages if s.startswith("l0a")]
        if l0a_st:
            spec = l0a_st[0]
            parts = tuple(spec.split(":", 1)[1].split(",")) if ":" in spec else ("qk", "v", "f1", "f2", "f3")
            emit_l0_passA(mk, block, parts=parts)
        if "l0b" in stages:
            emit_l0_passB(mk, block)
        if "l0c" in stages:
            emit_l0_passC(mk, block)
        if "cmb0" in stages:
            emit_combine(mk, block, "x", "sg_d", "x1_d")
        if "ln1" in stages:
            mk.ln1_done = emit_layernorm(mk, block, "x1_d", "xn1")
        if "l1sp" in stages:
            emit_l1_sp(mk, block)
        if "l1qkv" in stages:
            emit_l1_qkv(mk, block)
        if "l1loc" in stages:
            emit_l1_loc(mk, block)
        if "l1lm" in stages:
            emit_l1_lm(mk, block)
        if "w2p" in stages:
            emit_w2prep(mk, block)
            emit_wtrans(mk, block, ["W2p"], tag="2")
        l1g_st = [s for s in stages if s.startswith("l1glob")]
        if l1g_st:
            spec = l1g_st[0]
            gparts = tuple(spec.split(":", 1)[1].split(",")) if ":" in spec else ("gate", "gv", "sc", "out", "gb", "cmb")
            emit_l1_glob(mk, block, parts=gparts)
        if hasattr(mk, "attn_es"):
            mk.attn_es.close()
        if "l1ffn" in stages:
            emit_l1_ffn(mk, block)
        if "l1ffn2" in stages:
            emit_l1_ffn2(mk, block)
        if "l1wo" in stages:
            emit_l1_wo(mk, block)
        if "cmb1" in stages:
            emit_combine(mk, block, "x1_d", "sg1_d", "out")

        s_fin = mk.sem("fin")

        @block.gpsimd
        def _(g):
            g.wait_ge(mk.s_wprep.h, mk.s_wprep.n)
            for s in mk.phase_sems:
                if s.n > 0:
                    g.wait_ge(s.h, s.n)
            if "cmb1" not in stages:
                z = mk.sb("fin_z", [128, D], F32)
                for t in range(NT):
                    s_fin.dma(g.dma_start(z[:], mk.dram["x"][t * 128:(t + 1) * 128, :]))
                    g.wait_ge(s_fin.h, s_fin.n)
                    s_fin.dma(g.dma_start(out[t * 128:(t + 1) * 128, :], z[:]))
                    g.wait_ge(s_fin.h, s_fin.n)
            else:
                z = mk.sb("fin_z", [128, D], F32)
                s_fin.dma(g.dma_start(z[:], out[0:128, :]))
                g.wait_ge(s_fin.h, s_fin.n)

    mk.es.close()
    return mk


INPUT_SHAPES = {
    "L0_attn_hidden_states": [N_CORES, SEQ, D],
    "L0_attn_Wq": [2048, 2048], "L0_attn_bq": [2048],
    "L0_attn_Wk": [1024, 2048], "L0_attn_bk": [1024],
    "L0_attn_Wv": [1024, 2048], "L0_attn_bv": [1024],
    "L0_attn_Wo": [2048, 2048], "L0_attn_bo": [2048],
    "L0_attn_rand_matrix": [64, 128],
    "L0_ffn_weight": [2048, 2048],
    "L0_ffn_lora_A": [8, 16, 2048], "L0_ffn_lora_B": [8, 2048, 16],
    "L0_ffn_router_w": [8, 2048],
    "L1_attn_w_qkv": [2048, 6144], "L1_attn_b_qkv": [6144],
    "L1_attn_pool_q": [4, 2048],
    "L1_attn_w_lm_qkv": [2048, 6144], "L1_attn_b_lm_qkv": [6144],
    "L1_attn_w_lm_out": [2048, 2048], "L1_attn_b_lm_out": [2048],
    "L1_attn_w_l2t_q": [2048, 2048], "L1_attn_b_l2t_q": [2048],
    "L1_attn_w_l2t_out": [2048, 2048], "L1_attn_b_l2t_out": [2048],
    "L1_attn_w_lm_kv": [2048, 4096], "L1_attn_b_lm_kv": [4096],
    "L1_attn_w_out": [2048, 2048], "L1_attn_b_out": [2048],
    "L1_attn_w_gate": [2048, 32], "L1_attn_b_gate": [32],
    "L1_ffn_W1": [8192, 2048], "L1_ffn_W2": [2048, 8192], "L1_ffn_W3": [8192, 2048],
}

HOST_CONSTS = host_constants()


ALL_STAGES = ("prep", "ln0", "wtrans", "l0sp", "l0a", "l0b", "l0c", "cmb0",
              "ln1", "l1sp", "l1qkv", "l1loc", "l1lm", "l1glob", "w2p",
              "l1ffn", "l1ffn2", "l1wo", "cmb1")


def kernel(**inputs):
    mk = build_program(dbg=(), stages=ALL_STAGES)
    in_maps = []
    x_full = np.ascontiguousarray(inputs["L0_attn_hidden_states"])
    base = {k: np.ascontiguousarray(v) for k, v in inputs.items()
            if k != "L0_attn_hidden_states"}
    base.update({k: v for k, v in HOST_CONSTS.items()})
    for c in range(N_CORES):
        m = dict(base)
        m["x"] = x_full[c]
        in_maps.append(m)
    res = run_bass_kernel_spmd(mk.nc, in_maps, core_ids=list(range(N_CORES)))
    return np.stack([r["out"] for r in res.results], axis=0)
